# version 03
# baseline (speedup 1.0000x reference)
import numpy as np
import concourse.bass as bass
import concourse.mybir as mybir
from concourse.bass_utils import run_bass_kernel_spmd

F32 = mybir.dt.float32
BF16 = mybir.dt.bfloat16
AF = mybir.ActivationFunctionType
ALU = mybir.AluOpType
AX = mybir.AxisListType

MAGIC = float(np.float32(1.5 * 2 ** 23))
EXP_SCALE = 1.0 / float(np.sqrt(128.0))
LN_EPS = 1e-5
RMS_EPS = 1e-6
Q_EPS = 1e-5
INV2048 = 1.0 / 2048.0
RSQ2048 = float(2048.0 ** -0.5)
RSQ8192 = float(8192.0 ** -0.5)

DM, KVC, QC, INNER = 2048, 512, 1536, 8192
S, T = 4096, 1024


class Regions:
    BASE = 17408

    def __init__(self, nc, limit=229376):
        self.nc = nc
        self.limit = limit
        self.items = []

    def add(self, name, off, shape, dt, phases):
        off = off + self.BASE
        esz = 2 if dt == BF16 else 4
        size = int(np.prod(shape[1:])) * esz
        assert off + size <= self.limit, f"{name}: {off}+{size} > {self.limit}"
        for (n2, o2, s2, p2) in self.items:
            if phases & p2 and not (off + size <= o2 or o2 + s2 <= off):
                raise AssertionError(f"overlap {name}[{off}:{off+size}] vs "
                                     f"{n2}[{o2}:{o2+s2}] phases {phases & p2}")
        self.items.append((name, off, size, phases))
        return self.nc.alloc_sbuf_tensor_at(name, shape, dt, offset=off)


def build_program(taps=()):
    nc = bass.Bass("TRN2", target_bir_lowering=False, detect_race_conditions=False)

    xT = nc.dram_tensor("xT", [DM, S], F32, kind="ExternalInput")
    xoT = nc.dram_tensor("xoT", [DM, T], F32, kind="ExternalInput")
    wdkv = nc.dram_tensor("wdkv", [DM, KVC], F32, kind="ExternalInput")
    wuk = nc.dram_tensor("wuk", [KVC, DM], F32, kind="ExternalInput")
    wuv = nc.dram_tensor("wuv", [KVC, DM], F32, kind="ExternalInput")
    wdq = nc.dram_tensor("wdq", [DM, QC], F32, kind="ExternalInput")
    wuq = nc.dram_tensor("wuq", [QC, DM], F32, kind="ExternalInput")
    wo = nc.dram_tensor("wo", [DM, DM], F32, kind="ExternalInput")
    w1 = nc.dram_tensor("w1", [DM, INNER], F32, kind="ExternalInput")
    w2 = nc.dram_tensor("w2", [INNER, DM], F32, kind="ExternalInput")
    biasd = {}
    for nm, n in (("bdkv", KVC), ("buk", DM), ("buv", DM), ("bdq", QC),
                  ("buq", DM), ("bo", DM), ("b1", INNER), ("b2", DM)):
        biasd[nm] = nc.dram_tensor(nm, [n], F32, kind="ExternalInput")
    outT = nc.dram_tensor("outT", [DM, T], F32, kind="ExternalOutput")
    t1sp = nc.dram_tensor("t1sp", [DM, T], F32, kind="Internal")
    tap_t = {nm: nc.dram_tensor("tap_" + nm, sh, F32 if nm in ("s12", "preacc", "dbg") else BF16,
                                 kind="ExternalOutput")
             for nm, sh in taps}

    R = Regions(nc)
    AB = frozenset("AB")
    xbf = R.add("xbf", 0, [128, 16 * 512], BF16, frozenset("A"))
    cqT = R.add("cqT", 16384, [128, 12 * 512], BF16, frozenset("A"))
    arow_f = R.add("arow_f", 28672, [1, 1024], F32, frozenset("A"))
    arb = R.add("arb", 32768, [128, 512], F32, frozenset("A"))
    asq = R.add("asq", 34816, [128, 512], BF16, frozenset("A"))
    atmp = R.add("atmp", 35840, [128, 512], F32, frozenset("A"))
    wsdq = R.add("wsdq", 37888, [1, 1536], BF16, frozenset("A"))
    wdq_sb = R.add("wdq_sb", 40960, [128, 16 * 1536], BF16, frozenset("A"))
    wuq_sb = R.add("wuq_sb", 90112, [128, 2 * 12 * 128], BF16, frozenset("A"))
    wdkv_sb = R.add("wdkv_sb", 96256, [128, 16 * 512], BF16, frozenset("A"))
    arow_b = R.add("arow_b", 112640, [1, 512], BF16, frozenset("A"))
    arow_b2 = R.add("arow_b2", 113664, [1, 512], BF16, frozenset("A"))
    arow_b2a = R.add("arow_b2a", 114688, [1, 512], BF16, frozenset("A"))

    ktb = R.add("ktb", 0, [128, 2 * 4096], BF16, frozenset("B"))
    vtb = R.add("vtb", 16384, [128, 2 * 4096], BF16, frozenset("B"))
    prb = R.add("prb", 32768, [128, 2 * 1024], BF16, frozenset("B"))
    invb = R.add("invb", 36864, [128, 1024], F32, frozenset("B"))
    wuk_sb = R.add("wuk_sb", 40960, [128, 4 * 2048], BF16, frozenset("B"))
    wuv_sb = R.add("wuv_sb", 57344, [128, 4 * 2048], BF16, frozenset("B"))
    pre_f = R.add("pre_f", 73728, [128, 2 * 2048], F32, frozenset("B"))
    brow_f = R.add("brow_f", 90112, [1, 1024], F32, frozenset("B"))
    brow_h = R.add("brow_h", 94208, [1, 1024], BF16, frozenset("B"))
    brow_l = R.add("brow_l", 96256, [1, 1024], BF16, frozenset("B"))
    attn = R.add("attn", 98304, [128, 16 * 1024], BF16, frozenset("BP"))

    wo_sb = R.add("wo_sb", 73728, [128, 2 * 16 * 128], BF16, frozenset("P"))
    aoT = R.add("aoT", 163840, [128, 16 * 1024], BF16, frozenset("PL"))

    t1f = R.add("t1f", 0, [128, 16 * 1024], F32, frozenset("L"))
    lrow_f = R.add("lrow_f", 65536, [1, 1024], F32, frozenset("PL"))
    lrow_h = R.add("lrow_h", 69632, [1, 1024], BF16, frozenset("PL"))
    lrow_l = R.add("lrow_l", 71680, [1, 1024], BF16, frozenset("PL"))
    lrow_f2 = R.add("lrow_f2", 81920, [1, 1024], F32, frozenset("PL"))
    sqtmp = R.add("sqtmp", 86016, [128, 1024], BF16, frozenset("PL"))
    sqmax = R.add("sqmax", 88064, [128, 1024], F32, frozenset("L"))
    xob = R.add("xob", 92160, [128, 2 * 1024], F32, frozenset("L"))
    ltmp1 = R.add("ltmp1", 100352, [128, 1024], F32, frozenset("L"))
    ltmp2 = R.add("ltmp2", 104448, [128, 1024], F32, frozenset("L"))

    h2 = R.add("h2", 0, [128, 64 * 1024], BF16, frozenset(["C1", "C2"]))
    coutb = R.add("coutb", 0, [128, 2 * 1024], F32, frozenset("F"))
    t1rd = R.add("t1rd", 8192, [128, 2 * 1024], F32, frozenset("F"))
    frow_f = R.add("frow_f", 16384, [1, 1024], F32, frozenset("F"))
    frow_h = R.add("frow_h", 20480, [1, 1024], BF16, frozenset("F"))
    frow_l = R.add("frow_l", 22528, [1, 1024], BF16, frozenset("F"))
    fr_sraw = R.add("fr_sraw", 24576, [1, 1024], F32, frozenset(["F"]))
    fr_ssq = R.add("fr_ssq", 28672, [1, 1024], F32, frozenset(["F"]))
    fr_sb2r = R.add("fr_sb2r", 32768, [1, 1024], F32, frozenset(["F"]))
    fr_mp = R.add("fr_mp", 36864, [1, 1024], F32, frozenset("F"))
    fr_r3 = R.add("fr_r3", 40960, [1, 1024], F32, frozenset("F"))
    fr_c1 = R.add("fr_c1", 45056, [1, 1024], F32, frozenset("F"))
    nb16 = R.add("nb16", 49152, [1, 2048], BF16, frozenset("F"))
    ftmp1 = R.add("ftmp1", 57344, [128, 1024], F32, frozenset("F"))
    ftmp2 = R.add("ftmp2", 61440, [128, 1024], F32, frozenset("F"))
    frow_h2 = R.add("frow_h2", 65536, [1, 1024], BF16, frozenset("F"))
    frow_l2 = R.add("frow_l2", 67584, [1, 1024], BF16, frozenset("F"))

    R1 = 131072
    ckv = R.add("ckv", R1, [128, 4 * 4096], BF16, frozenset("AB"))
    t1b = R.add("t1b", R1, [128, 16 * 1024], BF16, frozenset("L"))
    w1db = R.add("w1db", R1, [128, 2 * 16 * 128], BF16, frozenset("C1"))
    crow_h = R.add("crow_h", R1, [1, 1024], BF16, frozenset(["Cq"]))
    crow_l = R.add("crow_l", R1 + 2048, [1, 1024], BF16, frozenset(["Cq"]))
    b2rb = R.add("b2rb", R1, [1, 2048], BF16, frozenset(["F"]))
    w2db = R.add("w2db", R1 + 8192, [128, 2 * 16 * 128], BF16, frozenset("C2"))
    csq = R.add("csq", R1 + 16384, [128, 1024], BF16, frozenset(["C1", "C2"]))
    csqmax = R.add("csqmax", R1 + 18432, [128, 1024], F32, frozenset(["C1", "Cq"]))
    ctmp1 = R.add("ctmp1", R1 + 22528, [128, 1024], F32,
                  frozenset(["C1", "Cq", "C2", "F"]))
    ctmp2 = R.add("ctmp2", R1 + 26624, [128, 512], F32, frozenset(["C2", "F"]))
    crow_f = R.add("crow_f", R1 + 28672, [1, 1024], F32, frozenset(["Cq"]))

    bf2r = R.add("bf2r", 163840, [128, 16 * 1024], BF16, frozenset(["C2", "F"]))
    qT = R.add("qT", 163840, [128, 16 * 1024], BF16, AB)

    bcA = R.add("bcA", 196608, [128, 1024], F32, frozenset(["L", "Cq", "C1"]))
    bcB = R.add("bcB", 200704, [128, 1024], F32, frozenset(["L", "Cq", "C1"]))
    bcC = R.add("bcC", 204800, [128, 1024], F32, frozenset(["F"]))
    dbgB = R.add("dbgB", 204800 - 4096, [128, 16], F32, frozenset(["Bdbg"]))
    qrow_f = R.add("qrow_f", 204800, [1, 1024], F32, frozenset(["L", "Cq"]))

    P0 = 208896
    onesr = R.add("onesr", P0, [1, 128], BF16, frozenset("ABPLC1C2F"))
    onesc = R.add("onesc", P0 + 256, [128, 1], BF16, frozenset("ABPLC1C2F"))
    wsdkv = R.add("wsdkv", P0 + 320, [1, 512], BF16, frozenset("A"))
    bias_sb = R.add("bias_sb", P0 + 1344, [128, 148], F32, frozenset("ABPLC1C2F"))
    s12 = R.add("s12", P0 + 1952, [128, 4], F32, frozenset("BPLC1C2F"))
    b2cb = R.add("b2cb", P0 + 1984, [128, 16], BF16, frozenset("ABPLC1C2F"))
    rowp_f = R.add("rowp_f", P0 + 2016, [1, 128], F32, frozenset("ABPLC1C2F"))
    pre_acc = R.add("pre_acc", P0 + 2528, [128, 2], F32, frozenset("AB"))
    fence_c = R.add("fence_c", P0 + 2624, [1, 8], F32,
                    frozenset("ABPLC1C2F"))
    pre_r = R.add("pre_r", P0 + 2560, [128, 2], F32, frozenset("AB"))
    prehl_b = R.add("prehl_b", P0 + 2592, [128, 4], BF16, frozenset("AB"))
    buvr = R.add("buvr", 196608, [1, 2048], BF16, frozenset("AB"))
    pre_s = R.add("pre_s", 196608 + 4096, [128, 128], F32, frozenset("AB"))


    BIG1 = nc.alloc_psum_tensor("BIG1", [128, 1024], F32)
    BIG2 = nc.alloc_psum_tensor("BIG2", [128, 1024], F32)
    BIG3 = nc.alloc_psum_tensor("BIG3", [128, 1024], F32)
    SM1 = nc.alloc_psum_tensor("SM1", [128, 512], F32)
    SM2 = nc.alloc_psum_tensor("SM2", [128, 512], F32)

    st = nc.alloc_semaphore("st")
    sv = nc.alloc_semaphore("sv")
    sc = nc.alloc_semaphore("sc")
    ds = nc.alloc_semaphore("ds")
    NDG = 8
    dg = [nc.alloc_semaphore("dg%d" % i) for i in range(NDG)]
    NYS = 8
    ys = [nc.alloc_semaphore("ys%d" % i) for i in range(NYS)]

    ops = []
    EMIT = {k: (lambda fn, kk=k: ops.append((kk, fn))) for k in "TVSGY"}
    cnt = {"t": 0, "v": 0, "s": 0, "ds": 0}
    gslot, guse = [0], [0] * NDG
    yslot, yuse = [0], [0] * NYS
    last_wait = {}
    SEMH = {"t": st, "v": sv, "s": sc}

    def wait(ek, ev):
        if ev is None:
            return
        key, val = ev
        if val is None or val <= 0:
            return
        k = (ek, key)
        if last_wait.get(k, 0) >= val:
            return
        last_wait[k] = val
        if key.startswith("dg"):
            semh, tgt = dg[int(key[2:])], val * 16
        elif key.startswith("ys"):
            semh, tgt = ys[int(key[2:])], val * 16
        elif key == "ds":
            semh, tgt = ds, val * 16
        else:
            semh, tgt = SEMH[key], val
        EMIT[ek](lambda eng, s=semh, v=tgt: eng.wait_ge(s, v))

    def g_dma(dst_ap, src_ap, waits=()):
        slot = gslot[0]
        gslot[0] = (slot + 1) % NDG
        prev = guse[slot]
        guse[slot] = prev + 1
        for w in waits:
            wait("G", w)
        if prev > 0:
            wait("G", ("dg%d" % slot, prev))
        EMIT["G"](lambda eng, d=dst_ap, s=src_ap, sm=dg[slot]:
                  eng.dma_start(d, s).then_inc(sm, 16))
        return ("dg%d" % slot, guse[slot])

    def y_dma(dst_ap, src_ap, waits=(), slow=False):
        for w in waits:
            wait("Y", w)
        cnt["ds"] += 1
        EMIT["Y"](lambda eng, d=dst_ap, s=src_ap, sl=slow:
                  eng.dma_start(d, s, allow_slow_non_contiguous=sl).then_inc(ds, 16))
        return ("ds", cnt["ds"])

    def y_load(dst_ap, src_ap, waits=(), slow=False):
        slot = yslot[0]
        yslot[0] = (slot + 1) % NYS
        prev = yuse[slot]
        yuse[slot] = prev + 1
        for w in waits:
            wait("Y", w)
        if prev > 0:
            wait("Y", ("ys%d" % slot, prev))
        EMIT["Y"](lambda eng, d=dst_ap, s=src_ap, sm=ys[slot], sl=slow:
                  eng.dma_start(d, s, allow_slow_non_contiguous=sl).then_inc(sm, 16))
        return ("ys%d" % slot, yuse[slot])

    tap_names = {nm for nm, sh in taps}

    def tap(nm, src_t, ncols, ev):
        if nm in tap_names:
            y_dma(bass.AP(tap_t[nm], 0, [[ncols, 128], [1, ncols]]),
                  bass.AP(src_t, 0, [[ncols, 128], [1, ncols]]), waits=[ev])

    def t_mm(out_ap, lhs_ap, rhs_ap, start, stop, inc=False, waits=()):
        for w in waits:
            wait("T", w)
        if inc:
            cnt["t"] += 1
        EMIT["T"](lambda eng, o=out_ap, l=lhs_ap, r=rhs_ap, a=start, b=stop, i=inc:
                  (eng.matmul(o, l, r, start=a, stop=b).then_inc(st, 1) if i
                   else eng.matmul(o, l, r, start=a, stop=b)))
        return ("t", cnt["t"]) if inc else None

    def v_op(builder, inc=False, waits=()):
        for w in waits:
            wait("V", w)
        if inc:
            cnt["v"] += 1
        EMIT["V"](lambda eng, b=builder, i=inc:
                  (b(eng).then_inc(sv, 1) if i else b(eng)))
        return ("v", cnt["v"]) if inc else None

    def s_op(builder, inc=False, waits=()):
        for w in waits:
            wait("S", w)
        if inc:
            cnt["s"] += 1
        EMIT["S"](lambda eng, b=builder, i=inc:
                  (b(eng).then_inc(sc, 1) if i else b(eng)))
        return ("s", cnt["s"]) if inc else None

    def vfence(ev):
        return s_op(lambda eng: eng.copy(fence_c[0:1, 0:1], fence_c[0:1, 0:1]),
                    inc=True, waits=[ev])

    def sfence(ev):
        return v_op(lambda eng: eng.tensor_copy(fence_c[0:1, 1:2],
                                                fence_c[0:1, 1:2]),
                    inc=True, waits=[ev])

    def bcast(src_row_ap, n, psum_t, scr_f_t, scr_h_t, scr_l_t, waits=()):
        def prep(eng):
            eng.tensor_copy(scr_h_t[0:1, 0:n], src_row_ap)
            eng.tensor_tensor(scr_f_t[0:1, 0:n], src_row_ap, scr_h_t[0:1, 0:n],
                              ALU.subtract)
            return eng.tensor_copy(scr_l_t[0:1, 0:n], scr_f_t[0:1, 0:n])
        pe = v_op(prep, inc=True, waits=waits)
        ev = None
        for c0 in range(0, n, 512):
            w = min(512, n - c0)
            t_mm(psum_t[:, c0:c0 + w], onesr[0:1, :], scr_h_t[0:1, c0:c0 + w],
                 True, False, waits=[pe])
            ev = t_mm(psum_t[:, c0:c0 + w], onesr[0:1, :],
                      scr_l_t[0:1, c0:c0 + w], False, True, inc=True)
        return ev

    def pfold(buf, scratch, n, op):
        def fn(eng, b=buf, s=scratch, nn=n, o=op):
            ins = None
            for p in (64, 32):
                eng.tensor_copy(s[0:p, 0:nn], b[p:2 * p, 0:nn])
                ins = eng.tensor_tensor(b[0:p, 0:nn], b[0:p, 0:nn], s[0:p, 0:nn], o)
            p = 16
            while p >= 1:
                mask = [(i + p) % 32 for i in range(32)]
                eng.stream_shuffle(s[0:32, 0:nn], b[0:32, 0:nn], mask)
                ins = eng.tensor_tensor(b[0:32, 0:nn], b[0:32, 0:nn], s[0:32, 0:nn], o)
                p //= 2
            return ins
        return fn

    ev_ones = v_op(lambda eng: (eng.memset(onesr[0:1, :], 1.0),
                                eng.memset(onesc[:, :], 1.0))[-1], inc=True)
    ev_bias = None
    for nm, col, k in (("bdkv", 0, 4), ("buk", 4, 16), ("bdq", 20, 12),
                       ("buq", 32, 16), ("bo", 48, 16), ("b1", 64, 64),
                       ("b2", 128, 16)):
        ev_bias = y_load(bass.AP(bias_sb, col, [[148, 128], [1, k]]),
                         bass.AP(biasd[nm], 0, [[1, 128], [128, k]]), slow=True)
    e1 = y_load(arow_f[0:1, 0:1024], bass.AP(biasd["buv"], 0, [[1024, 1], [1, 1024]]))
    ec1 = v_op(lambda eng: eng.tensor_copy(buvr[0:1, 0:1024], arow_f[0:1, 0:1024]),
               inc=True, waits=[e1])
    e2 = y_load(arow_f[0:1, 0:1024],
                bass.AP(biasd["buv"], 1024, [[1024, 1], [1, 1024]]), waits=[ec1])
    ev_buvr = v_op(lambda eng: eng.tensor_copy(buvr[0:1, 1024:2048],
                                               arow_f[0:1, 0:1024]),
                   inc=True, waits=[e2])
    ev_b2cb = v_op(lambda eng: eng.tensor_copy(b2cb[:, :], bias_sb[:, 128:144]),
                   inc=True, waits=[ev_bias])
    ev_pre0 = v_op(lambda eng: eng.memset(fence_c[0:1, 0:8], 0.0), inc=True)

    def wload(dst_sb, dst_row, src_t, src_cols, itile0, ntiles, col0, ncols,
              dst_col0, waits=()):
        dst = bass.AP(dst_sb, dst_col0, [[dst_row, 128], [ncols, ntiles], [1, ncols]])
        src = bass.AP(src_t, itile0 * 128 * src_cols + col0,
                      [[src_cols, 128], [128 * src_cols, ntiles], [1, ncols]])
        return g_dma(dst, src, waits=waits)

    ev_wdkv = wload(wdkv_sb, 8192, wdkv, KVC, 0, 16, 0, KVC, 0)
    ev_wdq = None
    for g in range(4):
        ev_wdq = wload(wdq_sb, 16 * 1536, wdq, QC, 4 * g, 4, 0, QC, 4 * g * 1536)

    ev = None
    for k in range(16):
        ev = t_mm(SM1[0:1, 0:512], onesc[:, :], wdkv_sb[:, k * 512:(k + 1) * 512],
                  k == 0, k == 15, inc=(k == 15),
                  waits=[ev_wdkv, ev_ones] if k == 0 else ())
    ev_wsdkv = v_op(lambda eng: eng.tensor_copy(wsdkv[0:1, :], SM1[0:1, 0:512]),
                    inc=True, waits=[ev])
    rows3 = [(SM2, 0, 0), (BIG3, 0, 512), (BIG3, 512, 1024)]
    evs3 = []
    for (pt, off, dcol) in rows3:
        ev = None
        for k in range(16):
            ev = t_mm(pt[0:1, off:off + 512], onesc[:, :],
                      wdq_sb[:, k * 1536 + dcol: k * 1536 + dcol + 512],
                      k == 0, k == 15, inc=(k == 15),
                      waits=[ev_wdq] if k == 0 else ())
        evs3.append(ev)

    def cp_wsdq(eng):
        eng.tensor_copy(wsdq[0:1, 0:512], SM2[0:1, 0:512])
        eng.tensor_copy(wsdq[0:1, 512:1024], BIG3[0:1, 0:512])
        return eng.tensor_copy(wsdq[0:1, 1024:1536], BIG3[0:1, 512:1024])
    ev_wsdq = v_op(cp_wsdq, inc=True, waits=evs3)

    pslots = [(BIG1, 0), (BIG1, 512), (BIG2, 0), (BIG2, 512)]
    pslot_ev = [None] * 4
    psi = [0]

    def next_slot():
        i = psi[0] % 4
        psi[0] += 1
        return i

    prev_fin = [ev_wsdq]
    prev_rb = [ev_wsdq]
    prev_mm2 = [None]
    prev_atmp_rd = [None]

    def chunk_stats(xev):
        sq_ev = None
        mm2 = prev_mm2[0]
        for k in range(16):
            sq_ev = s_op(lambda eng, kk=k: eng.activation(
                asq[:, 0:512], xbf[:, kk * 512:(kk + 1) * 512], AF.Square),
                inc=True, waits=[xev, mm2])
            t_mm(SM1[0:1, 0:512], onesc[:, :], xbf[:, k * 512:(k + 1) * 512],
                 k == 0, k == 15,
                 waits=[xev, prev_fin[0]] if k == 0 else ())
            mm2 = t_mm(SM2[0:1, 0:512], onesc[:, :], asq[:, 0:512],
                       k == 0, k == 15, inc=True, waits=[sq_ev])
        prev_mm2[0] = mm2

        def fin(eng):
            m = arow_f[0:1, 0:512]
            v2 = arow_f[0:1, 512:1024]
            eng.tensor_scalar(m, SM1[0:1, 0:512], INV2048, None, ALU.mult)
            eng.tensor_scalar(v2, SM2[0:1, 0:512], INV2048, None, ALU.mult)
            eng.tensor_scalar(atmp[0:1, 0:512], m, -1.0, None, ALU.mult)
            eng.tensor_copy(arow_b[0:1, 0:512], atmp[0:1, 0:512])
            eng.tensor_tensor(atmp[0:1, 0:512], m, m, ALU.mult)
            eng.tensor_tensor(v2, v2, atmp[0:1, 0:512], ALU.subtract)
            eng.tensor_scalar(v2, v2, LN_EPS, None, ALU.add)
            return eng.reciprocal(atmp[0:1, 0:512], v2)
        e_f = v_op(fin, inc=True, waits=[mm2, prev_atmp_rd[0]])
        prev_fin[0] = e_f
        e_r = s_op(lambda eng: eng.activation(arow_f[0:1, 0:512],
                                              atmp[0:1, 0:512], AF.Sqrt),
                   inc=True, waits=[e_f])
        ebc = bcast(arow_f[0:1, 0:512], 512, BIG3, atmp, arow_b2, arow_b2a,
                    waits=[e_r, prev_rb[0]])
        e_rb = v_op(lambda eng: eng.tensor_copy(arb[:, :], BIG3[:, 0:512]),
                    inc=True, waits=[ebc])
        prev_rb[0] = e_rb
        return e_rb

    a_last_s = [None]
    last_q_mm = [None]

    def emit_fold_linear(n_ot, w_sb, w_tcols, ws_t, bias_col, out_cb, e_rb, xev):
        evs = []
        for ot in range(n_ot):
            si = next_slot()
            pt, off = pslots[si]
            t_mm(pt[:, off:off + 512], ws_t[0:1, ot * 128:(ot + 1) * 128],
                 arow_b[0:1, 0:512], True, False,
                 waits=[e_rb, pslot_ev[si]])
            for k in range(16):
                t_mm(pt[:, off:off + 512],
                     w_sb[:, k * w_tcols + ot * 128: k * w_tcols + (ot + 1) * 128],
                     xbf[:, k * 512:(k + 1) * 512],
                     False, k == 15, inc=(k == 15))
            mev = ("t", cnt["t"])
            e_v = v_op(lambda eng, p=pt, o=off: eng.tensor_tensor(
                atmp[:, :], p[:, o:o + 512], arb[:, :], ALU.mult),
                inc=True, waits=[mev, prev_atmp_rd[0]])
            e_s = s_op(lambda eng, oo=ot, dst=out_cb(ot), bc=bias_col + ot:
                       eng.activation(dst, atmp[:, :], AF.Identity,
                                      bias=bias_sb[:, bc:bc + 1]),
                       inc=True, waits=[e_v])
            prev_atmp_rd[0] = e_s
            pslot_ev[si] = e_s
            evs.append(e_s)
        return evs

    last_chunk_done = [None]
    for c in range(8):
        xev = g_dma(bass.AP(xbf, 0, [[8192, 128], [512, 16], [1, 512]]),
                    bass.AP(xT, c * 512, [[S, 128], [128 * S, 16], [1, 512]]),
                    waits=[last_chunk_done[0]])
        e_rb = chunk_stats(xev)
        evs = emit_fold_linear(
            4, wdkv_sb, 512, wsdkv, 0,
            lambda ot, cc=c: ckv[:, ot * 4096 + cc * 512: ot * 4096 + (cc + 1) * 512],
            e_rb, xev)
        last_chunk_done[0] = evs[-1]
        a_last_s[0] = evs[-1]

    for oc in range(2):
        xev = g_dma(bass.AP(xbf, 0, [[8192, 128], [512, 16], [1, 512]]),
                    bass.AP(xoT, oc * 512, [[T, 128], [128 * T, 16], [1, 512]]),
                    waits=[last_chunk_done[0]])
        e_rb = chunk_stats(xev)
        cq_evs = []
        for ot in range(12):
            si = next_slot()
            pt, off = pslots[si]
            t_mm(pt[:, off:off + 512], wsdq[0:1, ot * 128:(ot + 1) * 128],
                 arow_b[0:1, 0:512], True, False, waits=[e_rb, pslot_ev[si]])
            for k in range(16):
                t_mm(pt[:, off:off + 512],
                     wdq_sb[:, k * 1536 + ot * 128: k * 1536 + (ot + 1) * 128],
                     xbf[:, k * 512:(k + 1) * 512], False, k == 15, inc=(k == 15))
            mev = ("t", cnt["t"])
            e_v = v_op(lambda eng, p=pt, o=off: eng.tensor_tensor(
                atmp[:, :], p[:, o:o + 512], arb[:, :], ALU.mult),
                inc=True, waits=[mev, prev_atmp_rd[0]])
            e_s = s_op(lambda eng, oo=ot: eng.activation(
                cqT[:, oo * 512:(oo + 1) * 512], atmp[:, :], AF.Identity,
                bias=bias_sb[:, 20 + oo:21 + oo]), inc=True,
                waits=[e_v, last_q_mm[0] if ot == 0 else None])
            prev_atmp_rd[0] = e_s
            pslot_ev[si] = e_s
            cq_evs.append(e_s)
        qw_prev = [None, None]
        for ot in range(16):
            wev = wload(wuq_sb, 2 * 12 * 128, wuq, DM, 0, 12, ot * 128, 128,
                        (ot % 2) * 12 * 128, waits=[qw_prev[ot % 2]])
            si = next_slot()
            pt, off = pslots[si]
            for k in range(12):
                t_mm(pt[:, off:off + 512],
                     wuq_sb[:, (ot % 2) * 1536 + k * 128:(ot % 2) * 1536 + (k + 1) * 128],
                     cqT[:, k * 512:(k + 1) * 512], k == 0, k == 11,
                     inc=(k == 11),
                     waits=[wev, pslot_ev[si], cq_evs[k]] if k == 0 else [cq_evs[k]])
            mev = ("t", cnt["t"])
            qw_prev[ot % 2] = mev
            e_s = s_op(lambda eng, p=pt, o=off, oo=ot, occ=oc: eng.activation(
                qT[:, oo * 1024 + occ * 512: oo * 1024 + occ * 512 + 512],
                p[:, o:o + 512], AF.Identity, bias=bias_sb[:, 32 + oo:33 + oo]),
                inc=True, waits=[mev])
            pslot_ev[si] = e_s
            a_last_s[0] = e_s
        last_q_mm[0] = qw_prev[15 % 2]
        last_chunk_done[0] = a_last_s[0]
    a_last = a_last_s[0]
    tap("ckv", ckv, 16384, a_last)
    tap("qT", qT, 16384, a_last)

    ev_wuk = wload(wuk_sb, 4 * 2048, wuk, DM, 0, 4, 0, DM, 0, waits=[a_last])
    ev_wuv = wload(wuv_sb, 4 * 2048, wuv, DM, 0, 4, 0, DM, 0)

    pre_jobs = [(0, w1, ch) for ch in range(64)] + [(1, w2, ch) for ch in range(64)]
    pre_ptr = [0]
    pre_vev = [a_last, a_last]

    def emit_prestream(n):
        for _ in range(n):
            if pre_ptr[0] >= len(pre_jobs):
                return
            wi, wt, ch = pre_jobs[pre_ptr[0]]
            slot = pre_ptr[0] % 2
            pre_ptr[0] += 1
            dev = y_load(bass.AP(pre_f, slot * 2048, [[4096, 128], [1, 2048]]),
                         bass.AP(wt, ch * 128 * 2048, [[2048, 128], [1, 2048]]),
                         waits=[pre_vev[slot]])

            col = wi * 64 + ch
            pre_vev[slot] = v_op(
                lambda eng, sl=slot, c=col: eng.tensor_reduce(
                    pre_s[:, c:c + 1], pre_f[:, sl * 2048:(sl + 1) * 2048],
                    AX.X, ALU.add, apply_absolute_value=True),
                inc=True, waits=[dev])

    def emit_kgen(h, slot, first_waits):
        out = []
        for cc in range(8):
            pt, off = (BIG2, 0) if cc % 2 == 0 else (BIG2, 512)
            for j in range(4):
                w = []
                if cc < 2 and j == 0:
                    w = list(first_waits) + [ev_wuk]
                elif j == 0 and cc >= 2:
                    w = [out[cc - 2]]
                t_mm(pt[:, off:off + 512],
                     wuk_sb[:, j * 2048 + h * 128: j * 2048 + (h + 1) * 128],
                     ckv[:, j * 4096 + cc * 512: j * 4096 + (cc + 1) * 512],
                     j == 0, j == 3, inc=(j == 3), waits=w)
            mev = ("t", cnt["t"])
            out.append(v_op(lambda eng, p=pt, o=off, c2=cc, sl=slot, hh=h:
                            eng.tensor_scalar(
                                ktb[:, sl * 4096 + c2 * 512: sl * 4096 + (c2 + 1) * 512],
                                p[:, o:o + 512], bias_sb[:, 4 + hh:5 + hh],
                                None, ALU.add),
                            inc=True, waits=[mev]))
        return out

    def emit_vgen(h, slot, first_waits):
        out = []
        for tt in range(32):
            pt, off = (BIG3, 0) if tt % 2 == 0 else (BIG3, 512)
            w = (list(first_waits) + [ev_wuv, ev_buvr]) if tt < 2 else [out[tt - 2]]
            t_mm(pt[:, off:off + 128], onesr[0:1, :],
                 buvr[0:1, h * 128:(h + 1) * 128], True, False, waits=w)
            for j in range(4):
                t_mm(pt[:, off:off + 128],
                     ckv[:, j * 4096 + tt * 128: j * 4096 + (tt + 1) * 128],
                     wuv_sb[:, j * 2048 + h * 128: j * 2048 + (h + 1) * 128],
                     False, j == 3, inc=(j == 3))
            mev = ("t", cnt["t"])
            out.append(v_op(lambda eng, p=pt, o=off, t2=tt, sl=slot:
                            eng.tensor_copy(
                                vtb[:, sl * 4096 + t2 * 128: sl * 4096 + (t2 + 1) * 128],
                                p[:, o:o + 128]),
                            inc=True, waits=[mev]))
        return out

    head_kv = [(emit_kgen(0, 0, [a_last]), emit_vgen(0, 0, [a_last]))]
    attn_ev = []
    prev_inv = [prev_fin[0]]
    for h in range(16):
        k_evs, v_evs = head_kv[h]
        sl = h % 2
        exp_evs = [None] * 32
        pv_evs = [None] * 32
        qk_last = None
        for kt in range(32):
            spt = BIG1 if kt % 2 == 0 else BIG2
            if kt < 2:
                w0 = [k_evs[-1], v_evs[-1]]
                if attn_ev:
                    w0.append(attn_ev[-1])
            else:
                w0 = [exp_evs[kt - 2]]
            t_mm(spt[:, 0:512],
                 ktb[:, sl * 4096 + kt * 128: sl * 4096 + (kt + 1) * 128],
                 qT[:, h * 1024: h * 1024 + 512], True, True, waits=w0)
            qk_last = t_mm(spt[:, 512:1024],
                           ktb[:, sl * 4096 + kt * 128: sl * 4096 + (kt + 1) * 128],
                           qT[:, h * 1024 + 512:(h + 1) * 1024], True, True,
                           inc=True)
            exp_evs[kt] = s_op(
                lambda eng, p=spt, k2=kt: eng.activation(
                    prb[:, (k2 % 2) * 1024:(k2 % 2 + 1) * 1024],
                    p[:, 0:1024], AF.Exp, scale=EXP_SCALE),
                inc=True,
                waits=[qk_last, pv_evs[kt - 2] if kt >= 2 else None])
            first = kt == 0
            pw = [exp_evs[kt]]
            if first:
                pw.append(attn_ev[-1] if attn_ev else a_last)
            t_mm(BIG3[:, 0:512],
                 vtb[:, sl * 4096 + kt * 128: sl * 4096 + (kt + 1) * 128],
                 prb[:, (kt % 2) * 1024:(kt % 2) * 1024 + 512],
                 first, kt == 31, waits=pw)
            t_mm(BIG3[:, 512:1024],
                 vtb[:, sl * 4096 + kt * 128: sl * 4096 + (kt + 1) * 128],
                 prb[:, (kt % 2) * 1024 + 512:(kt % 2 + 1) * 1024],
                 first, kt == 31)
            t_mm(SM1[0:1, 0:512], onesc[:, :],
                 prb[:, (kt % 2) * 1024:(kt % 2) * 1024 + 512],
                 first, kt == 31, waits=[prev_inv[0]] if first else ())
            pv_evs[kt] = t_mm(SM2[0:1, 0:512], onesc[:, :],
                              prb[:, (kt % 2) * 1024 + 512:(kt % 2 + 1) * 1024],
                              first, kt == 31, inc=True)

        def inv_prep(eng):
            eng.reciprocal(brow_f[0:1, 0:512], SM1[0:1, 0:512])
            return eng.reciprocal(brow_f[0:1, 512:1024], SM2[0:1, 0:512])
        e_inv = v_op(inv_prep, inc=True, waits=[pv_evs[31]])
        prev_inv[0] = e_inv
        ebc = bcast(brow_f[0:1, 0:1024], 1024, BIG1, brow_f,
                    brow_h, brow_l, waits=[e_inv, exp_evs[30]])
        e_cp = v_op(lambda eng: eng.tensor_copy(invb[:, :], BIG1[:, 0:1024]),
                    inc=True, waits=[ebc])
        e_at = v_op(lambda eng, hh=h: eng.tensor_tensor(
            attn[:, hh * 1024:(hh + 1) * 1024], BIG3[:, 0:1024], invb[:, :],
            ALU.mult), inc=True, waits=[e_cp])
        attn_ev.append(e_at)
        if h < 15:
            nk = emit_kgen(h + 1, (h + 1) % 2, [exp_evs[31]])
            nv = emit_vgen(h + 1, (h + 1) % 2, [e_at])
            head_kv.append((nk, nv))
        emit_prestream(8)
    emit_prestream(len(pre_jobs))

    tap("attn", attn, 16384, attn_ev[15])

    e_pv = s_op(lambda eng: eng.copy(fence_c[0:1, 0:1], fence_c[0:1, 0:1]),
                inc=True, waits=[pre_vev[0], pre_vev[1]])
    e_red = v_op(lambda eng: (eng.tensor_reduce(pre_acc[:, 0:1], pre_s[:, 0:64],
                                                AX.X, ALU.add),
                              eng.tensor_reduce(pre_acc[:, 1:2], pre_s[:, 64:128],
                                                AX.X, ALU.add))[-1],
                 inc=True, waits=[e_pv, attn_ev[15]])
    e_h1 = v_op(lambda eng: (eng.tensor_copy(prehl_b[:, 0:1], pre_acc[:, 0:1]),
                             eng.tensor_copy(prehl_b[:, 1:2], pre_acc[:, 1:2]))[-1],
                inc=True, waits=[vfence(e_red)])
    e_fh1 = vfence(e_h1)
    e_l1 = s_op(lambda eng: (eng.activation(pre_r[:, 0:1], prehl_b[:, 0:1],
                                            AF.Identity, scale=-1.0,
                                            bias=pre_acc[:, 0:1]),
                             eng.activation(pre_r[:, 1:2], prehl_b[:, 1:2],
                                            AF.Identity, scale=-1.0,
                                            bias=pre_acc[:, 1:2]))[-1],
                inc=True, waits=[e_fh1])
    e_l1b = v_op(lambda eng: (eng.tensor_copy(prehl_b[:, 2:3], pre_r[:, 0:1]),
                              eng.tensor_copy(prehl_b[:, 3:4], pre_r[:, 1:2]))[-1],
                 inc=True, waits=[sfence(e_l1)])
    e_sum = None
    for c in range(2):
        t_mm(SM1[0:1, c:c + 1], onesc[:, :], prehl_b[:, c:c + 1], True, False,
             waits=[e_l1b, prev_inv[0]] if c == 0 else ())
        e_sum = t_mm(SM1[0:1, c:c + 1], onesc[:, :], prehl_b[:, 2 + c:3 + c],
                     False, True, inc=True)
    e_rp = v_op(lambda eng: (eng.tensor_scalar(rowp_f[0:1, 0:1], SM1[0:1, 0:1],
                                               1.0 / 16777216.0, None, ALU.mult),
                             eng.tensor_scalar(rowp_f[0:1, 1:2], SM1[0:1, 1:2],
                                               1.0 / 16777216.0, None, ALU.mult))[-1],
                inc=True, waits=[e_sum])
    e_bh = s_op(lambda eng: (eng.copy(brow_h[0:1, 0:1], rowp_f[0:1, 0:1]),
                             eng.copy(brow_h[0:1, 1:2], rowp_f[0:1, 1:2]))[-1],
                inc=True, waits=[vfence(e_rp)])
    e_bl = v_op(lambda eng: (eng.tensor_tensor(rowp_f[0:1, 2:3], rowp_f[0:1, 0:1],
                                               brow_h[0:1, 0:1], ALU.subtract),
                             eng.tensor_tensor(rowp_f[0:1, 3:4], rowp_f[0:1, 1:2],
                                               brow_h[0:1, 1:2], ALU.subtract))[-1],
                inc=True, waits=[sfence(e_bh)])
    e_blc = s_op(lambda eng: (eng.copy(brow_l[0:1, 0:1], rowp_f[0:1, 2:3]),
                              eng.copy(brow_l[0:1, 1:2], rowp_f[0:1, 3:4]))[-1],
                 inc=True, waits=[vfence(e_bl)])
    e_s12m = None
    for c in range(2):
        t_mm(SM2[:, c:c + 1], onesr[0:1, :], brow_h[0:1, c:c + 1], True, False,
             waits=[e_blc] if c == 0 else ())
        e_s12m = t_mm(SM2[:, c:c + 1], onesr[0:1, :], brow_l[0:1, c:c + 1],
                      False, True, inc=True)

    def cp_s12(eng):
        eng.tensor_copy(s12[:, 0:1], SM2[:, 0:1])
        return eng.tensor_copy(s12[:, 1:2], SM2[:, 1:2])
    e_s12 = v_op(cp_s12, inc=True, waits=[e_s12m])
    if "dbg" in tap_names:
        def cp_dbg(eng):
            eng.memset(dbgB[:, :], 0.0)
            for c in range(4):
                eng.tensor_copy(dbgB[:, c:c + 1], prehl_b[:, c:c + 1])
            eng.tensor_copy(dbgB[0:1, 4:5], SM1[0:1, 0:1])
            eng.tensor_copy(dbgB[0:1, 5:6], SM1[0:1, 1:2])
            eng.tensor_copy(dbgB[0:1, 6:7], rowp_f[0:1, 0:1])
            eng.tensor_copy(dbgB[0:1, 7:8], rowp_f[0:1, 1:2])
            eng.tensor_copy(dbgB[0:1, 8:9], brow_h[0:1, 0:1])
            eng.tensor_copy(dbgB[0:1, 9:10], brow_h[0:1, 1:2])
            eng.tensor_copy(dbgB[0:1, 10:11], brow_l[0:1, 0:1])
            eng.tensor_copy(dbgB[0:1, 11:12], brow_l[0:1, 1:2])
            eng.tensor_copy(dbgB[:, 12:13], SM2[:, 0:1])
            return eng.tensor_copy(dbgB[:, 13:14], SM2[:, 1:2])
        e_dbg = v_op(cp_dbg, inc=True, waits=[e_s12])
        y_dma(bass.AP(tap_t["dbg"], 0, [[16, 128], [1, 16]]), dbgB[:, :],
              waits=[e_dbg])

    op_evs = []
    stat_mm = [None]
    wo_prev = [None, None]
    for ot in range(16):
        wev = wload(wo_sb, 2 * 16 * 128, wo, DM, 0, 16, ot * 128, 128,
                    (ot % 2) * 2048,
                    waits=[wo_prev[ot % 2]] if ot >= 2 else
                    [pre_vev[0], pre_vev[1], attn_ev[15], e_s12])
        for half in range(2):
            si = next_slot()
            pt, off = pslots[si]
            for k in range(16):
                w = [wev, pslot_ev[si]] if k == 0 else ()
                t_mm(pt[:, off:off + 512],
                     wo_sb[:, (ot % 2) * 2048 + k * 128:(ot % 2) * 2048 + (k + 1) * 128],
                     attn[:, k * 1024 + half * 512: k * 1024 + half * 512 + 512],
                     k == 0, k == 15, inc=(k == 15), waits=w)
            mev = ("t", cnt["t"])
            if half == 1:
                wo_prev[ot % 2] = mev
            e_s = s_op(lambda eng, p=pt, o=off, oo=ot, hf=half: eng.activation(
                aoT[:, oo * 1024 + hf * 512: oo * 1024 + hf * 512 + 512],
                p[:, o:o + 512], AF.Identity, bias=bias_sb[:, 48 + oo:49 + oo]),
                inc=True, waits=[mev])
            pslot_ev[si] = e_s
            op_evs.append(e_s)
        e_sq = s_op(lambda eng, oo=ot: eng.activation(
            sqtmp[:, 0:1024], aoT[:, oo * 1024:(oo + 1) * 1024], AF.Square),
            inc=True, waits=[op_evs[-1], stat_mm[0]])
        t_mm(SM1[0:1, 0:512], onesc[:, :], aoT[:, ot * 1024: ot * 1024 + 512],
             ot == 0, ot == 15,
             waits=[op_evs[-1], e_s12] if ot == 0 else [op_evs[-1]])
        t_mm(SM2[0:1, 0:512], onesc[:, :],
             aoT[:, ot * 1024 + 512:(ot + 1) * 1024], ot == 0, ot == 15)
        t_mm(BIG3[0:1, 0:512], onesc[:, :], sqtmp[:, 0:512], ot == 0, ot == 15,
             waits=[e_sq, attn_ev[15]] if ot == 0 else [e_sq])
        stat_mm[0] = t_mm(BIG3[0:1, 512:1024], onesc[:, :], sqtmp[:, 512:1024],
                          ot == 0, ot == 15, inc=True)

    def ln_fin(n_inv, eps, r_out, mr_out, waits):
        def fn(eng):
            m = mr_out
            eng.tensor_scalar(m[0:1, 0:512], SM1[0:1, 0:512], n_inv, None, ALU.mult)
            eng.tensor_scalar(m[0:1, 512:1024], SM2[0:1, 0:512], n_inv, None, ALU.mult)
            e2 = lrow_f2[0:1, 0:1024]
            eng.tensor_copy(e2[0:1, 0:512], BIG3[0:1, 0:512])
            eng.tensor_copy(e2[0:1, 512:1024], BIG3[0:1, 512:1024])
            eng.tensor_scalar(e2, e2, n_inv, None, ALU.mult)
            eng.tensor_tensor(r_out, m, m, ALU.mult)
            eng.tensor_tensor(e2, e2, r_out, ALU.subtract)
            eng.tensor_scalar(e2, e2, eps, None, ALU.add)
            return eng.reciprocal(r_out, e2)
        e1 = v_op(fn, inc=True, waits=waits)
        e2 = s_op(lambda eng: eng.activation(r_out, r_out, AF.Sqrt),
                  inc=True, waits=[e1])
        e3 = v_op(lambda eng: eng.tensor_tensor(mr_out, mr_out, r_out, ALU.mult),
                  inc=True, waits=[e2])
        return e3

    r1r = lrow_f[0:1, 0:1024]
    mr1r = ltmp1[0:1, 0:1024]
    e_ln1b = ln_fin(INV2048, LN_EPS, r1r, mr1r, [stat_mm[0]])
    ebc = bcast(r1r, 1024, BIG3, lrow_f2, lrow_h, lrow_l,
                waits=[e_ln1b, stat_mm[0]])
    e_rb1 = v_op(lambda eng: eng.tensor_copy(bcA[:, :], BIG3[:, 0:1024]),
                 inc=True, waits=[ebc])
    ebc = bcast(mr1r, 1024, BIG3, lrow_f2, lrow_h, lrow_l, waits=[e_rb1])
    e_mrb1 = v_op(lambda eng: eng.tensor_copy(bcB[:, :], BIG3[:, 0:1024]),
                  inc=True, waits=[ebc])

    t1_evs = []
    sp_evs = []
    for ft in range(16):
        dev = y_load(bass.AP(xob, (ft % 2) * 1024, [[2048, 128], [1, 1024]]),
                     bass.AP(xoT, ft * 128 * T, [[T, 128], [1, 1024]]),
                     waits=[e_mrb1] if ft < 2 else [t1_evs[ft - 2]])

        def mk(eng, f=ft):
            eng.tensor_tensor(ltmp2[:, :], aoT[:, f * 1024:(f + 1) * 1024],
                              bcA[:, :], ALU.mult)
            eng.tensor_tensor(ltmp2[:, :], ltmp2[:, :], bcB[:, :], ALU.subtract)
            eng.tensor_tensor(t1f[:, f * 1024:(f + 1) * 1024], ltmp2[:, :],
                              xob[:, (f % 2) * 1024:(f % 2 + 1) * 1024], ALU.add)
            return eng.tensor_copy(t1b[:, f * 1024:(f + 1) * 1024],
                                   t1f[:, f * 1024:(f + 1) * 1024])
        e_t1 = v_op(mk, inc=True, waits=[dev, e_mrb1])
        t1_evs.append(e_t1)
        sp_evs.append(y_load(bass.AP(t1sp, ft * 128 * T, [[T, 128], [1, 1024]]),
                             bass.AP(t1f, ft * 1024, [[16384, 128], [1, 1024]]),
                             waits=[e_t1]))

    tap("t1b", t1b, 16384, t1_evs[15])

    stat2 = [None]
    for ft in range(16):
        e_sq = s_op(lambda eng, f=ft: eng.activation(
            sqtmp[:, 0:1024], t1b[:, f * 1024:(f + 1) * 1024], AF.Square),
            inc=True, waits=[t1_evs[ft], stat2[0]])
        t_mm(SM1[0:1, 0:512], onesc[:, :], t1b[:, ft * 1024: ft * 1024 + 512],
             ft == 0, ft == 15,
             waits=[t1_evs[ft], e_mrb1] if ft == 0 else [t1_evs[ft]])
        t_mm(SM2[0:1, 0:512], onesc[:, :],
             t1b[:, ft * 1024 + 512:(ft + 1) * 1024], ft == 0, ft == 15)
        t_mm(BIG3[0:1, 0:512], onesc[:, :], sqtmp[:, 0:512], ft == 0, ft == 15,
             waits=[e_sq, e_mrb1] if ft == 0 else [e_sq])
        stat2[0] = t_mm(BIG3[0:1, 512:1024], onesc[:, :], sqtmp[:, 512:1024],
                        ft == 0, ft == 15, inc=True)
    r2r = lrow_f[0:1, 0:1024]
    mr2r = ltmp1[0:1, 0:1024]
    e_ln2 = ln_fin(INV2048, LN_EPS, r2r, mr2r, [stat2[0], e_rb1])
    ebc = bcast(r2r, 1024, BIG3, lrow_f2, lrow_h, lrow_l, waits=[e_ln2])
    e_rb2 = v_op(lambda eng: eng.tensor_copy(bcA[:, :], BIG3[:, 0:1024]),
                 inc=True, waits=[ebc, t1_evs[15]])
    ebc = bcast(mr2r, 1024, BIG3, lrow_f2, lrow_h, lrow_l, waits=[e_rb2])
    e_mrb2 = v_op(lambda eng: eng.tensor_copy(bcB[:, :], BIG3[:, 0:1024]),
                  inc=True, waits=[ebc])

    e_sqm0 = v_op(lambda eng: eng.memset(sqmax[:, :], 0.0), inc=True,
                  waits=[e_mrb2])
    ln2_evs = []
    rms_mm = [None]
    mx_prev = [e_sqm0]
    for ft in range(16):
        def mk(eng, f=ft):
            eng.tensor_tensor(ltmp2[:, :], t1f[:, f * 1024:(f + 1) * 1024],
                              bcA[:, :], ALU.mult)
            eng.tensor_tensor(ltmp2[:, :], ltmp2[:, :], bcB[:, :], ALU.subtract)
            return eng.tensor_copy(aoT[:, f * 1024:(f + 1) * 1024], ltmp2[:, :])
        e_l = v_op(mk, inc=True, waits=[e_mrb2, sp_evs[ft]])
        e_sq = s_op(lambda eng, f=ft: eng.activation(
            sqtmp[:, 0:1024], aoT[:, f * 1024:(f + 1) * 1024], AF.Square),
            inc=True, waits=[e_l, rms_mm[0], mx_prev[0]])
        t_mm(SM1[0:1, 0:512], onesc[:, :], sqtmp[:, 0:512], ft == 0, ft == 15,
             waits=[e_sq, e_mrb2] if ft == 0 else [e_sq])
        rms_mm[0] = t_mm(SM2[0:1, 0:512], onesc[:, :], sqtmp[:, 512:1024],
                         ft == 0, ft == 15, inc=True)
        mx_prev[0] = v_op(lambda eng: eng.tensor_tensor(
            sqmax[:, :], sqmax[:, :], sqtmp[:, 0:1024], ALU.max),
            inc=True, waits=[e_sq, mx_prev[0]])
        ln2_evs.append(e_l)

    def quant_rows(n_inv, rsqn, rms_mm_ev, fold_ev, bcast_wait):
        def r1(eng):
            r = qrow_f[0:1, 0:1024]
            eng.tensor_scalar(r[0:1, 0:512], SM1[0:1, 0:512], n_inv, None, ALU.mult)
            eng.tensor_scalar(r[0:1, 512:1024], SM2[0:1, 0:512], n_inv, None, ALU.mult)
            eng.tensor_scalar(r, r, RMS_EPS, None, ALU.add)
            return eng.reciprocal(r, r)
        e1 = v_op(r1, inc=True, waits=[rms_mm_ev, fold_ev])
        e2 = s_op(lambda eng: eng.activation(qrow_f[0:1, 0:1024],
                                             qrow_f[0:1, 0:1024], AF.Sqrt),
                  inc=True, waits=[e1])
        return e2

    e_q = quant_rows(INV2048, RSQ2048, rms_mm[0],
                     v_op(pfold(sqmax, ltmp2, 1024, ALU.max), inc=True,
                          waits=[mx_prev[0], ln2_evs[15]]), None)
    e_mxr = v_op(lambda eng: eng.tensor_copy(lrow_f[0:1, 0:1024],
                                             sqmax[0:1, 0:1024]),
                 inc=True, waits=[e_q])
    e_mxs = s_op(lambda eng: eng.activation(lrow_f[0:1, 0:1024],
                                            lrow_f[0:1, 0:1024], AF.Sqrt),
                 inc=True, waits=[e_mxr])

    def q_rows3(eng, rsqn=RSQ2048):
        f = qrow_f[0:1, 0:1024]
        mx = lrow_f[0:1, 0:1024]
        eng.tensor_scalar(f, f, rsqn, None, ALU.mult)
        eng.tensor_tensor(mx, mx, f, ALU.mult)
        eng.tensor_scalar(mx, mx, Q_EPS, None, ALU.max)
        eng.reciprocal(lrow_f2[0:1, 0:1024], mx)
        eng.tensor_scalar(lrow_f2[0:1, 0:1024], lrow_f2[0:1, 0:1024], 127.0,
                          None, ALU.mult)
        eng.tensor_tensor(f, f, lrow_f2[0:1, 0:1024], ALU.mult)
        return eng.tensor_scalar(mx, mx, 1.0 / 127.0, None, ALU.mult)
    e_q3 = v_op(q_rows3, inc=True, waits=[e_mxs])
    ebc = bcast(qrow_f[0:1, 0:1024], 1024, BIG3, lrow_f2, lrow_h, lrow_l,
                waits=[e_q3])
    e_A1 = v_op(lambda eng: eng.tensor_copy(bcA[:, :], BIG3[:, 0:1024]),
                inc=True, waits=[ebc])
    ebc = bcast(lrow_f[0:1, 0:1024], 1024, BIG3, lrow_f2, lrow_h, lrow_l,
                waits=[e_A1])
    e_B1 = v_op(lambda eng: eng.tensor_copy(bcB[:, :], BIG3[:, 0:1024]),
                inc=True, waits=[ebc])

    xq_evs = []
    for ft in range(16):
        def mk(eng, f=ft):
            eng.tensor_tensor(ltmp2[:, :], aoT[:, f * 1024:(f + 1) * 1024],
                              bcA[:, :], ALU.mult)
            eng.tensor_scalar(ltmp2[:, :], ltmp2[:, :], MAGIC, -MAGIC,
                              ALU.add, ALU.add)
            return eng.tensor_tensor(aoT[:, f * 1024:(f + 1) * 1024],
                                     ltmp2[:, :], bcB[:, :], ALU.mult)
        xq_evs.append(v_op(mk, inc=True, waits=[e_B1]))

    tap("xq", aoT, 16384, xq_evs[15])
    if "s12" in tap_names:
        y_dma(bass.AP(tap_t["s12"], 0, [[2, 128], [1, 2]]), s12[:, 0:2],
              waits=[e_s12])
    if "preacc" in tap_names:
        y_dma(bass.AP(tap_t["preacc"], 0, [[2, 128], [1, 2]]), pre_acc[:, :],
              waits=[e_h1])

    sg1_prev = [None, None]
    h2_evs = []
    rms2_mm = [None]
    mx2_prev = [None]
    e_sqm2 = v_op(lambda eng: eng.memset(csqmax[:, :], 0.0), inc=True,
                  waits=[e_B1])
    mx2_prev[0] = e_sqm2
    sq_prev = [None]
    for ot in range(64):
        wev = wload(w1db, 2 * 16 * 128, w1, INNER, 0, 16, ot * 128, 128,
                    (ot % 2) * 2048,
                    waits=[sg1_prev[ot % 2]] if ot >= 2 else
                    ([xq_evs[15]] + sp_evs[8:]))
        e_sg = s_op(lambda eng, s2=(ot % 2): eng.activation(
            w1db[:, s2 * 2048:(s2 + 1) * 2048], w1db[:, s2 * 2048:(s2 + 1) * 2048],
            AF.Sign), inc=True, waits=[wev])
        for half in range(2):
            si = next_slot()
            pt, off = pslots[si]
            for k in range(16):
                w = []
                if k == 0:
                    w = [e_sg, pslot_ev[si]]
                    if ot == 0 and half == 0:
                        w += [e_s12] + xq_evs
                t_mm(pt[:, off:off + 512],
                     w1db[:, (ot % 2) * 2048 + k * 128:(ot % 2) * 2048 + (k + 1) * 128],
                     aoT[:, k * 1024 + half * 512: k * 1024 + half * 512 + 512],
                     k == 0, k == 15, inc=(k == 15), waits=w)
            mev = ("t", cnt["t"])
            if half == 1:
                sg1_prev[ot % 2] = mev
            e_h = s_op(lambda eng, p=pt, o=off, oo=ot, hf=half: eng.activation(
                h2[:, oo * 1024 + hf * 512: oo * 1024 + hf * 512 + 512],
                p[:, o:o + 512], AF.Gelu, bias=bias_sb[:, 64 + oo:65 + oo],
                scale=s12[:, 0:1]), inc=True, waits=[mev])
            pslot_ev[si] = e_h
            h2_evs.append(e_h)
        e_sq = s_op(lambda eng, oo=ot: eng.activation(
            csq[:, 0:1024], h2[:, oo * 1024:(oo + 1) * 1024], AF.Square),
            inc=True, waits=[h2_evs[-1], rms2_mm[0], mx2_prev[0]])
        sq_prev[0] = e_sq
        t_mm(SM1[0:1, 0:512], onesc[:, :], csq[:, 0:512], ot == 0, ot == 63,
             waits=[e_sq, e_B1] if ot == 0 else [e_sq])
        rms2_mm[0] = t_mm(SM2[0:1, 0:512], onesc[:, :], csq[:, 512:1024],
                          ot == 0, ot == 63, inc=True)
        mx2_prev[0] = v_op(lambda eng: eng.tensor_tensor(
            csqmax[:, :], csqmax[:, :], csq[:, 0:1024], ALU.max),
            inc=True, waits=[e_sq, mx2_prev[0]])

    e_fold2 = v_op(pfold(csqmax, ctmp1, 1024, ALU.max), inc=True,
                   waits=[mx2_prev[0]])

    def r2a(eng):
        r = qrow_f[0:1, 0:1024]
        eng.tensor_scalar(r[0:1, 0:512], SM1[0:1, 0:512], 1.0 / 8192.0, None, ALU.mult)
        eng.tensor_scalar(r[0:1, 512:1024], SM2[0:1, 0:512], 1.0 / 8192.0, None, ALU.mult)
        eng.tensor_scalar(r, r, RMS_EPS, None, ALU.add)
        return eng.reciprocal(r, r)
    e_ra = v_op(r2a, inc=True, waits=[rms2_mm[0], e_fold2])
    e_rb_ = s_op(lambda eng: eng.activation(qrow_f[0:1, 0:1024],
                                            qrow_f[0:1, 0:1024], AF.Sqrt),
                 inc=True, waits=[e_ra])
    e_mxr2 = v_op(lambda eng: eng.tensor_copy(crow_f[0:1, 0:1024],
                                              csqmax[0:1, 0:1024]),
                  inc=True, waits=[e_rb_])
    e_mxs2 = s_op(lambda eng: eng.activation(crow_f[0:1, 0:1024],
                                             crow_f[0:1, 0:1024], AF.Sqrt),
                  inc=True, waits=[e_mxr2])

    def q2_rows3(eng):
        f = qrow_f[0:1, 0:1024]
        mx = crow_f[0:1, 0:1024]
        eng.tensor_scalar(f, f, RSQ8192, None, ALU.mult)
        eng.tensor_tensor(mx, mx, f, ALU.mult)
        eng.tensor_scalar(mx, mx, Q_EPS, None, ALU.max)
        eng.reciprocal(ctmp1[0:1, 0:1024], mx)
        eng.tensor_scalar(ctmp1[0:1, 0:1024], ctmp1[0:1, 0:1024], 127.0,
                          None, ALU.mult)
        eng.tensor_tensor(f, f, ctmp1[0:1, 0:1024], ALU.mult)
        return eng.tensor_scalar(mx, mx, 1.0 / 127.0, None, ALU.mult)
    e_q23 = v_op(q2_rows3, inc=True, waits=[e_mxs2])
    ebc = bcast(qrow_f[0:1, 0:1024], 1024, BIG3, ctmp1, crow_h, crow_l,
                waits=[e_q23])
    e_A2 = v_op(lambda eng: eng.tensor_copy(bcA[:, :], BIG3[:, 0:1024]),
                inc=True, waits=[ebc])
    ebc = bcast(crow_f[0:1, 0:1024], 1024, BIG3, ctmp1, crow_h, crow_l,
                waits=[e_A2])
    e_B2b = v_op(lambda eng: eng.tensor_copy(bcB[:, :], BIG3[:, 0:1024]),
                 inc=True, waits=[ebc])

    xq2_evs = []
    for f2 in range(64):
        def mk(eng, f=f2):
            eng.tensor_tensor(ctmp1[:, :], h2[:, f * 1024:(f + 1) * 1024],
                              bcA[:, :], ALU.mult)
            eng.tensor_scalar(ctmp1[:, :], ctmp1[:, :], MAGIC, -MAGIC,
                              ALU.add, ALU.add)
            return eng.tensor_tensor(h2[:, f * 1024:(f + 1) * 1024],
                                     ctmp1[:, :], bcB[:, :], ALU.mult)
        xq2_evs.append(v_op(mk, inc=True, waits=[e_B2b, h2_evs[2 * f2 + 1]]))

    tap("h2q", h2, 65536, xq2_evs[63])

    sg2_prev = [None, None]
    raw_evs = []
    statr = [None]
    sqr_prev = [None]
    for ot2 in range(16):
        siA = next_slot()
        ptA, offA = pslots[siA]
        siB = next_slot()
        ptB, offB = pslots[siB]
        mevA = mevB = None
        for blk in range(4):
            sl = blk % 2
            wev = wload(w2db, 2 * 16 * 128, w2, DM, blk * 16, 16, ot2 * 128,
                        128, sl * 2048,
                        waits=[sg2_prev[sl]] if sg2_prev[sl] else [])
            e_sg = s_op(lambda eng, s2=sl: eng.activation(
                w2db[:, s2 * 2048:(s2 + 1) * 2048],
                w2db[:, s2 * 2048:(s2 + 1) * 2048], AF.Sign),
                inc=True, waits=[wev])
            for ki in range(16):
                k = blk * 16 + ki
                wA = []
                wB = []
                if ki == 0:
                    wA = [e_sg]
                    if ot2 == 0:
                        wA.append(xq2_evs[blk * 16 + 15])
                    if k == 0:
                        wA.append(pslot_ev[siA])
                        wB = [pslot_ev[siB]]
                t_mm(ptA[:, offA:offA + 512],
                     w2db[:, sl * 2048 + ki * 128: sl * 2048 + (ki + 1) * 128],
                     h2[:, k * 1024: k * 1024 + 512],
                     k == 0, k == 63, inc=(k == 63), waits=wA)
                if k == 63:
                    mevA = ("t", cnt["t"])
                t_mm(ptB[:, offB:offB + 512],
                     w2db[:, sl * 2048 + ki * 128: sl * 2048 + (ki + 1) * 128],
                     h2[:, k * 1024 + 512:(k + 1) * 1024],
                     k == 0, k == 63, inc=(ki == 15), waits=wB)
                if ki == 15:
                    sg2_prev[sl] = ("t", cnt["t"])
                    if k == 63:
                        mevB = ("t", cnt["t"])
        e_rA = s_op(lambda eng, p=ptA, o=offA, oo=ot2: eng.activation(
            bf2r[:, oo * 1024: oo * 1024 + 512], p[:, o:o + 512], AF.Identity),
            inc=True, waits=[mevA])
        e_rB = s_op(lambda eng, p=ptB, o=offB, oo=ot2: eng.activation(
            bf2r[:, oo * 1024 + 512:(oo + 1) * 1024], p[:, o:o + 512],
            AF.Identity), inc=True, waits=[mevB])
        pslot_ev[siA] = e_rA
        pslot_ev[siB] = e_rB
        raw_evs.append(e_rA)
        raw_evs.append(e_rB)
        e_sq = s_op(lambda eng, oo=ot2: eng.activation(
            csq[:, 0:1024], bf2r[:, oo * 1024:(oo + 1) * 1024], AF.Square),
            inc=True, waits=[e_rB, statr[0], sqr_prev[0]])
        sqr_prev[0] = e_sq
        t_mm(SM1[0:1, 0:512], onesc[:, :], bf2r[:, ot2 * 1024: ot2 * 1024 + 512],
             ot2 == 0, ot2 == 15,
             waits=[e_rA, e_B2b] if ot2 == 0 else [e_rA])
        t_mm(SM2[0:1, 0:512], onesc[:, :],
             bf2r[:, ot2 * 1024 + 512:(ot2 + 1) * 1024], ot2 == 0, ot2 == 15,
             waits=[e_rB])
        t_mm(BIG3[0:1, 0:512], onesc[:, :], csq[:, 0:512], ot2 == 0, ot2 == 15,
             waits=[e_sq, e_B2b] if ot2 == 0 else [e_sq])
        statr[0] = t_mm(BIG3[0:1, 512:1024], onesc[:, :], csq[:, 512:1024],
                        ot2 == 0, ot2 == 15, inc=True)

    tap("bf2r", bf2r, 16384, statr[0])

    e3 = y_dma(fr_sraw[0:1, 0:1024], bass.AP(biasd["b2"], 0, [[1024, 1], [1, 1024]]),
               waits=[statr[0]])
    ec3 = v_op(lambda eng: eng.tensor_copy(b2rb[0:1, 0:1024], fr_sraw[0:1, 0:1024]),
               inc=True, waits=[e3, statr[0]])
    e4 = y_load(fr_sraw[0:1, 0:1024],
                bass.AP(biasd["b2"], 1024, [[1024, 1], [1, 1024]]), waits=[ec3])
    e_b2rb = v_op(lambda eng: eng.tensor_copy(b2rb[0:1, 1024:2048],
                                              fr_sraw[0:1, 0:1024]),
                  inc=True, waits=[e4])


    def cp_rows(eng):
        eng.tensor_copy(fr_sraw[0:1, 0:512], SM1[0:1, 0:512])
        eng.tensor_copy(fr_sraw[0:1, 512:1024], SM2[0:1, 0:512])
        eng.tensor_copy(fr_ssq[0:1, 0:512], BIG3[0:1, 0:512])
        return eng.tensor_copy(fr_ssq[0:1, 512:1024], BIG3[0:1, 512:1024])
    e_cpr = v_op(cp_rows, inc=True, waits=[statr[0], e_b2rb])
    b2r_mm = [None]
    for ot2 in range(16):
        t_mm(SM1[0:1, 0:512], b2cb[:, ot2:ot2 + 1],
             bf2r[:, ot2 * 1024: ot2 * 1024 + 512], ot2 == 0, ot2 == 15,
             waits=[e_cpr, ev_b2cb] if ot2 == 0 else ())
        b2r_mm[0] = t_mm(SM2[0:1, 0:512], b2cb[:, ot2:ot2 + 1],
                         bf2r[:, ot2 * 1024 + 512:(ot2 + 1) * 1024],
                         ot2 == 0, ot2 == 15, inc=(ot2 == 15))

    def cp_b2r(eng):
        eng.tensor_copy(fr_sb2r[0:1, 0:512], SM1[0:1, 0:512])
        return eng.tensor_copy(fr_sb2r[0:1, 512:1024], SM2[0:1, 0:512])
    e_cb2r = v_op(cp_b2r, inc=True, waits=[b2r_mm[0]])

    e_q0 = v_op(lambda eng: eng.tensor_tensor(ctmp2[:, 0:16], bias_sb[:, 128:144],
                                              bias_sb[:, 128:144], ALU.mult),
                inc=True, waits=[ev_bias, e_b2rb, sqr_prev[0], statr[0]])
    e_q1 = v_op(lambda eng: (eng.tensor_reduce(ctmp2[:, 16:17], bias_sb[:, 128:144],
                                               AX.X, ALU.add),
                             eng.tensor_reduce(ctmp2[:, 17:18], ctmp2[:, 0:16],
                                               AX.X, ALU.add))[-1],
                inc=True, waits=[vfence(e_q0)])
    e_q2h = s_op(lambda eng: (eng.copy(csq[:, 0:1], ctmp2[:, 16:17]),
                              eng.copy(csq[:, 1:2], ctmp2[:, 17:18]))[-1],
                 inc=True, waits=[vfence(e_q1)])
    e_q2l = s_op(lambda eng: (eng.activation(ctmp2[:, 18:19], csq[:, 0:1],
                                             AF.Identity, scale=-1.0,
                                             bias=ctmp2[:, 16:17]),
                              eng.activation(ctmp2[:, 19:20], csq[:, 1:2],
                                             AF.Identity, scale=-1.0,
                                             bias=ctmp2[:, 17:18]))[-1],
                 inc=True, waits=[e_q2h])
    e_q2lb = v_op(lambda eng: (eng.tensor_copy(csq[:, 2:3], ctmp2[:, 18:19]),
                               eng.tensor_copy(csq[:, 3:4], ctmp2[:, 19:20]))[-1],
                  inc=True, waits=[sfence(e_q2l)])
    e_b2m = None
    for c in range(2):
        t_mm(SM1[0:1, 16 + c:17 + c], onesc[:, :], csq[:, c:c + 1], True, False,
             waits=[e_q2lb, e_cb2r] if c == 0 else ())
        e_b2m = t_mm(SM1[0:1, 16 + c:17 + c], onesc[:, :], csq[:, 2 + c:3 + c],
                     False, True, inc=True)

    def cp_b2f(eng):
        eng.tensor_copy(ctmp2[0:1, 16:17], SM1[0:1, 16:17])
        return eng.tensor_copy(ctmp2[0:1, 17:18], SM1[0:1, 17:18])
    e_b2f0 = v_op(cp_b2f, inc=True, waits=[e_b2m])
    e_b2f = vfence(e_b2f0)

    e_means = v_op(lambda eng: (eng.tensor_scalar(frow_f[0:1, 0:1],
                                                  ctmp2[0:1, 16:17], INV2048,
                                                  None, ALU.mult),
                                eng.tensor_scalar(frow_f[0:1, 1:2],
                                                  ctmp2[0:1, 17:18], INV2048,
                                                  None, ALU.mult))[-1],
                    inc=True, waits=[e_b2f, sfence(e_b2f)])
    e_meansf = vfence(e_means)

    def f_rows1(eng):
        s2v = s12[0:1, 1:2]
        mp = fr_mp[0:1, 0:1024]
        eng.tensor_scalar(mp, fr_sraw[0:1, 0:1024], INV2048, None, ALU.mult)
        eng.tensor_scalar(mp, mp, s2v, None, ALU.mult)
        eng.tensor_scalar(mp, mp, frow_f[0:1, 0:1], None, ALU.add)
        ey = ftmp1[0:1, 0:1024]
        eng.tensor_scalar(ey, fr_ssq[0:1, 0:1024], INV2048, None, ALU.mult)
        eng.tensor_scalar(ey, ey, s2v, None, ALU.mult)
        eng.tensor_scalar(ey, ey, s2v, None, ALU.mult)
        eng.tensor_scalar(ftmp2[0:1, 0:1024], fr_sb2r[0:1, 0:1024],
                          2.0 * INV2048, None, ALU.mult)
        eng.tensor_scalar(ftmp2[0:1, 0:1024], ftmp2[0:1, 0:1024], s2v, None,
                          ALU.mult)
        eng.tensor_tensor(ey, ey, ftmp2[0:1, 0:1024], ALU.add)
        eng.tensor_scalar(ey, ey, frow_f[0:1, 1:2], None, ALU.add)
        eng.tensor_tensor(ftmp2[0:1, 0:1024], mp, mp, ALU.mult)
        eng.tensor_tensor(ey, ey, ftmp2[0:1, 0:1024], ALU.subtract)
        eng.tensor_scalar(ey, ey, LN_EPS, None, ALU.add)
        return eng.reciprocal(fr_r3[0:1, 0:1024], ey)
    e_f1 = v_op(f_rows1, inc=True, waits=[e_cb2r, sfence(e_meansf), e_s12])
    e_f2 = s_op(lambda eng: eng.activation(fr_r3[0:1, 0:1024],
                                           fr_r3[0:1, 0:1024], AF.Sqrt),
                inc=True, waits=[e_f1])

    def f_rows2(eng):
        s2v = s12[0:1, 1:2]
        r3 = fr_r3[0:1, 0:1024]
        eng.tensor_scalar(fr_c1[0:1, 0:1024], r3, s2v, None, ALU.mult)
        eng.tensor_tensor(ftmp1[0:1, 0:1024], fr_mp[0:1, 0:1024], r3, ALU.mult)
        eng.tensor_scalar(ftmp1[0:1, 0:1024], ftmp1[0:1, 0:1024], -1.0, None,
                          ALU.mult)
        eng.tensor_copy(nb16[0:1, 0:1024], ftmp1[0:1, 0:1024])
        eng.tensor_tensor(ftmp2[0:1, 0:1024], ftmp1[0:1, 0:1024],
                          nb16[0:1, 0:1024], ALU.subtract)
        eng.tensor_copy(nb16[0:1, 1024:2048], ftmp2[0:1, 0:1024])
        eng.tensor_copy(frow_h[0:1, 0:1024], r3)
        eng.tensor_tensor(ftmp2[0:1, 0:1024], r3, frow_h[0:1, 0:1024],
                          ALU.subtract)
        return eng.tensor_copy(frow_l[0:1, 0:1024], ftmp2[0:1, 0:1024])
    e_f3 = v_op(f_rows2, inc=True, waits=[e_f2])
    ebc = bcast(fr_c1[0:1, 0:1024], 1024, BIG3, frow_f, frow_h2, frow_l2,
                waits=[e_f3, statr[0]])
    e_C1 = v_op(lambda eng: eng.tensor_copy(bcC[:, :], BIG3[:, 0:1024]),
                inc=True, waits=[ebc])

    out_evs = []
    for ot2 in range(16):
        trd = y_load(bass.AP(t1rd, (ot2 % 2) * 1024, [[2048, 128], [1, 1024]]),
                     bass.AP(t1sp, ot2 * 128 * T, [[T, 128], [1, 1024]]),
                     waits=[e_f3] if ot2 < 2 else [out_evs[ot2 - 2]])
        si = next_slot()
        pt, off = pslots[si]
        for half in range(2):
            o2 = off if False else None
        si2 = next_slot()
        pt2, off2 = pslots[si2]
        dmm = []
        for (p_, o_, half) in ((pt, off, 0), (pt2, off2, 1)):
            t_mm(p_[:, o_:o_ + 512], b2rb[0:1, ot2 * 128:(ot2 + 1) * 128],
                 frow_h[0:1, half * 512: half * 512 + 512], True, False,
                 waits=[pslot_ev[si if half == 0 else si2], e_f3,
                        e_C1 if ot2 == 0 else None])
            t_mm(p_[:, o_:o_ + 512], b2rb[0:1, ot2 * 128:(ot2 + 1) * 128],
                 frow_l[0:1, half * 512: half * 512 + 512], False, False)
            t_mm(p_[:, o_:o_ + 512], onesr[0:1, :],
                 nb16[0:1, half * 512: half * 512 + 512], False, False)
            dmm.append(t_mm(p_[:, o_:o_ + 512], onesr[0:1, :],
                            nb16[0:1, 1024 + half * 512: 1024 + half * 512 + 512],
                            False, True, inc=True))

        def mk(eng, oo=ot2, pa=pt, oa=off, pb=pt2, ob=off2):
            eng.tensor_tensor(ftmp1[:, :], bf2r[:, oo * 1024:(oo + 1) * 1024],
                              bcC[:, :], ALU.mult)
            eng.tensor_tensor(ftmp1[:, 0:512], ftmp1[:, 0:512],
                              pa[:, oa:oa + 512], ALU.add)
            eng.tensor_tensor(ftmp1[:, 512:1024], ftmp1[:, 512:1024],
                              pb[:, ob:ob + 512], ALU.add)
            return eng.tensor_tensor(
                coutb[:, (oo % 2) * 1024:(oo % 2 + 1) * 1024], ftmp1[:, :],
                t1rd[:, (oo % 2) * 1024:(oo % 2 + 1) * 1024], ALU.add)
        e_o = v_op(mk, inc=True, waits=[trd, dmm[0], dmm[1], e_C1,
                                        raw_evs[2 * ot2 + 1]])
        pslot_ev[si] = e_o
        pslot_ev[si2] = e_o
        out_evs.append(e_o)
        y_dma(bass.AP(outT, ot2 * 128 * T, [[T, 128], [1, 1024]]),
              bass.AP(coutb, (ot2 % 2) * 1024, [[2048, 128], [1, 1024]]),
              waits=[e_o])

    wait("Y", ("ds", cnt["ds"]))

    with nc.Block() as block:
        @block.tensor
        def _(eng):
            for k, fn in ops:
                if k == "T":
                    fn(eng)

        @block.vector
        def _(eng):
            for k, fn in ops:
                if k == "V":
                    fn(eng)

        @block.scalar
        def _(eng):
            for k, fn in ops:
                if k == "S":
                    fn(eng)

        @block.gpsimd
        def _(eng):
            for k, fn in ops:
                if k == "G":
                    fn(eng)

        @block.sync
        def _(eng):
            for k, fn in ops:
                if k == "Y":
                    fn(eng)

    return nc


def kernel(**inputs):
    ip = {k: np.asarray(v) for k, v in inputs.items()}
    x = ip["L0_attn_hidden_states"]

    def tr(a):
        return np.ascontiguousarray(a.T)

    w = {
        "wdkv": tr(ip["L0_attn_W_DKV_w"]), "wuk": tr(ip["L0_attn_W_UK_w"]),
        "wuv": tr(ip["L0_attn_W_UV_w"]), "wdq": tr(ip["L0_attn_W_DQ_w"]),
        "wuq": tr(ip["L0_attn_W_UQ_w"]), "wo": tr(ip["L0_attn_W_O_w"]),
        "w1": tr(ip["L0_ffn_w1"]), "w2": tr(ip["L0_ffn_w2"]),
        "bdkv": np.ascontiguousarray(ip["L0_attn_W_DKV_b"]),
        "buk": np.ascontiguousarray(ip["L0_attn_W_UK_b"]),
        "buv": np.ascontiguousarray(ip["L0_attn_W_UV_b"]),
        "bdq": np.ascontiguousarray(ip["L0_attn_W_DQ_b"]),
        "buq": np.ascontiguousarray(ip["L0_attn_W_UQ_b"]),
        "bo": np.ascontiguousarray(ip["L0_attn_W_O_b"]),
        "b1": np.ascontiguousarray(ip["L0_ffn_b1"]),
        "b2": np.ascontiguousarray(ip["L0_ffn_b2"]),
    }
    xTb = [np.ascontiguousarray(x[b].T) for b in range(2)]
    nc = build_program()
    in_maps = []
    for c in range(8):
        b, qt = c // 4, c % 4
        im = dict(w)
        im["xT"] = xTb[b]
        im["xoT"] = np.ascontiguousarray(xTb[b][:, qt * 1024:(qt + 1) * 1024])
        in_maps.append(im)
    res = run_bass_kernel_spmd(nc, in_maps, core_ids=list(range(8)))
    out = np.empty((2, 4096, 2048), dtype=np.float32)
    for c in range(8):
        b, qt = c // 4, c % 4
        out[b, qt * 1024:(qt + 1) * 1024, :] = res.results[c]["outT"].T
    return out


# version 05
# speedup vs baseline: 1.1678x; 1.1678x over previous
import numpy as np
import concourse.bass as bass
import concourse.mybir as mybir
from concourse.bass_utils import run_bass_kernel_spmd

F32 = mybir.dt.float32
BF16 = mybir.dt.bfloat16
AF = mybir.ActivationFunctionType
ALU = mybir.AluOpType
AX = mybir.AxisListType

MAGIC = float(np.float32(1.5 * 2 ** 23))
EXP_SCALE = 1.0 / float(np.sqrt(128.0))
LN_EPS = 1e-5
RMS_EPS = 1e-6
Q_EPS = 1e-5
INV2048 = 1.0 / 2048.0
RSQ2048 = float(2048.0 ** -0.5)
RSQ8192 = float(8192.0 ** -0.5)

DM, KVC, QC, INNER = 2048, 512, 1536, 8192
S, T = 4096, 1024


class Regions:
    BASE = 17408

    def __init__(self, nc, limit=229376):
        self.nc = nc
        self.limit = limit
        self.items = []

    def add(self, name, off, shape, dt, phases):
        off = off + self.BASE
        esz = 2 if dt == BF16 else 4
        size = int(np.prod(shape[1:])) * esz
        assert off + size <= self.limit, f"{name}: {off}+{size} > {self.limit}"
        for (n2, o2, s2, p2) in self.items:
            if phases & p2 and not (off + size <= o2 or o2 + s2 <= off):
                raise AssertionError(f"overlap {name}[{off}:{off+size}] vs "
                                     f"{n2}[{o2}:{o2+s2}] phases {phases & p2}")
        self.items.append((name, off, size, phases))
        return self.nc.alloc_sbuf_tensor_at(name, shape, dt, offset=off)


def build_program(taps=()):
    nc = bass.Bass("TRN2", target_bir_lowering=False, detect_race_conditions=False)

    xT = nc.dram_tensor("xT", [DM, S], F32, kind="ExternalInput")
    xoT = nc.dram_tensor("xoT", [DM, T], F32, kind="ExternalInput")
    wdkv = nc.dram_tensor("wdkv", [DM, KVC], F32, kind="ExternalInput")
    wuk = nc.dram_tensor("wuk", [KVC, DM], F32, kind="ExternalInput")
    wuv = nc.dram_tensor("wuv", [KVC, DM], F32, kind="ExternalInput")
    wdq = nc.dram_tensor("wdq", [DM, QC], F32, kind="ExternalInput")
    wuq = nc.dram_tensor("wuq", [QC, DM], F32, kind="ExternalInput")
    wo = nc.dram_tensor("wo", [DM, DM], F32, kind="ExternalInput")
    w1 = nc.dram_tensor("w1", [DM, INNER], F32, kind="ExternalInput")
    w2 = nc.dram_tensor("w2", [INNER, DM], F32, kind="ExternalInput")
    biasd = {}
    for nm, n in (("bdkv", KVC), ("buk", DM), ("buv", DM), ("bdq", QC),
                  ("buq", DM), ("bo", DM), ("b1", INNER), ("b2", DM)):
        biasd[nm] = nc.dram_tensor(nm, [n], F32, kind="ExternalInput")
    outT = nc.dram_tensor("outT", [DM, T], F32, kind="ExternalOutput")
    t1sp = nc.dram_tensor("t1sp", [DM, T], F32, kind="Internal")
    tap_t = {nm: nc.dram_tensor("tap_" + nm, sh, F32 if nm in ("s12", "preacc", "dbg") else BF16,
                                 kind="ExternalOutput")
             for nm, sh in taps}

    R = Regions(nc)
    AB = frozenset("AB")
    xbf = R.add("xbf", 0, [128, 2 * 16 * 512], BF16, frozenset("A"))
    arow_f = R.add("arow_f", 32768, [1, 1024], F32, frozenset("A"))
    arb0 = R.add("arb0", 36864, [128, 512], F32, frozenset("A"))
    arb1 = R.add("arb1", 38912, [128, 512], F32, frozenset("A"))
    wdq_sb = R.add("wdq_sb", 40960, [128, 16 * 1536], BF16, frozenset("A"))
    wuq_sb = R.add("wuq_sb", 90112, [128, 2 * 12 * 128], BF16, frozenset("A"))
    wdkv_sb = R.add("wdkv_sb", 96256, [128, 16 * 512], BF16, frozenset("A"))
    cqT = R.add("cqT", 112640, [128, 12 * 512], BF16, frozenset("A"))
    asq = R.add("asq", 124928, [128, 512], BF16, frozenset("A"))
    atmp = R.add("atmp", 125952, [128, 512], F32, frozenset("A"))
    wsdq = R.add("wsdq", 128000, [1, 1536], BF16, frozenset("A"))
    arow_b_0 = R.add("arow_b_0", 196608 + 4608, [1, 512], BF16, frozenset("A"))
    arow_b_1 = R.add("arow_b_1", 196608 + 5632, [1, 512], BF16, frozenset("A"))
    arow_b2 = R.add("arow_b2", 196608 + 6656, [1, 512], BF16, frozenset("A"))
    arow_b2a = R.add("arow_b2a", 196608 + 7680, [1, 512], BF16, frozenset("A"))

    ktb = R.add("ktb", 0, [128, 2 * 4096], BF16, frozenset("B"))
    vtb = R.add("vtb", 16384, [128, 2 * 4096], BF16, frozenset("B"))
    prb = R.add("prb", 32768, [128, 2 * 1024], BF16, frozenset("B"))
    invb = R.add("invb", 36864, [128, 1024], F32, frozenset("B"))
    wuk_sb = R.add("wuk_sb", 40960, [128, 4 * 2048], BF16, frozenset("B"))
    wuv_sb = R.add("wuv_sb", 57344, [128, 4 * 2048], BF16, frozenset("B"))
    pre_f = R.add("pre_f", 73728, [128, 2 * 2048], F32, frozenset("B"))
    brow_f = R.add("brow_f", 90112, [1, 1024], F32, frozenset("B"))
    brow_h = R.add("brow_h", 94208, [1, 1024], BF16, frozenset("B"))
    brow_l = R.add("brow_l", 96256, [1, 1024], BF16, frozenset("B"))
    attn = R.add("attn", 98304, [128, 16 * 1024], BF16, frozenset("BP"))

    wo_sb = R.add("wo_sb", 73728, [128, 2 * 16 * 128], BF16, frozenset("P"))
    aoT = R.add("aoT", 163840, [128, 16 * 1024], BF16, frozenset("PL"))

    t1f = R.add("t1f", 0, [128, 16 * 1024], F32, frozenset("L"))
    lrow_f = R.add("lrow_f", 65536, [1, 1024], F32, frozenset("PL"))
    lrow_h = R.add("lrow_h", 69632, [1, 1024], BF16, frozenset("PL"))
    lrow_l = R.add("lrow_l", 71680, [1, 1024], BF16, frozenset("PL"))
    lrow_f2 = R.add("lrow_f2", 81920, [1, 1024], F32, frozenset("PL"))
    sqtmp = R.add("sqtmp", 86016, [128, 1024], BF16, frozenset("PL"))
    sqmax = R.add("sqmax", 88064, [128, 1024], F32, frozenset("L"))
    xob = R.add("xob", 92160, [128, 2 * 1024], F32, frozenset("L"))
    ltmp1 = R.add("ltmp1", 100352, [128, 1024], F32, frozenset("L"))
    ltmp2 = R.add("ltmp2", 104448, [128, 1024], F32, frozenset("L"))

    h2 = R.add("h2", 0, [128, 64 * 1024], BF16, frozenset(["C1", "C2"]))
    coutb = R.add("coutb", 0, [128, 2 * 1024], F32, frozenset("F"))
    t1rd = R.add("t1rd", 8192, [128, 2 * 1024], F32, frozenset("F"))
    frow_f = R.add("frow_f", 16384, [1, 1024], F32, frozenset("F"))
    frow_h = R.add("frow_h", 20480, [1, 1024], BF16, frozenset("F"))
    frow_l = R.add("frow_l", 22528, [1, 1024], BF16, frozenset("F"))
    fr_sraw = R.add("fr_sraw", 24576, [1, 1024], F32, frozenset(["F"]))
    fr_ssq = R.add("fr_ssq", 28672, [1, 1024], F32, frozenset(["F"]))
    fr_sb2r = R.add("fr_sb2r", 32768, [1, 1024], F32, frozenset(["F"]))
    fr_mp = R.add("fr_mp", 36864, [1, 1024], F32, frozenset("F"))
    fr_r3 = R.add("fr_r3", 40960, [1, 1024], F32, frozenset("F"))
    fr_c1 = R.add("fr_c1", 45056, [1, 1024], F32, frozenset("F"))
    nb16 = R.add("nb16", 49152, [1, 2048], BF16, frozenset("F"))
    ftmp1 = R.add("ftmp1", 57344, [128, 1024], F32, frozenset("F"))
    ftmp2 = R.add("ftmp2", 61440, [128, 1024], F32, frozenset("F"))
    frow_h2 = R.add("frow_h2", 65536, [1, 1024], BF16, frozenset("F"))
    frow_l2 = R.add("frow_l2", 67584, [1, 1024], BF16, frozenset("F"))

    R1 = 131072
    ckv = R.add("ckv", R1, [128, 4 * 4096], BF16, frozenset("AB"))
    t1b = R.add("t1b", R1, [128, 16 * 1024], BF16, frozenset("L"))
    w1db = R.add("w1db", R1, [128, 2 * 16 * 128], BF16, frozenset("C1"))
    crow_h = R.add("crow_h", R1, [1, 1024], BF16, frozenset(["Cq"]))
    crow_l = R.add("crow_l", R1 + 2048, [1, 1024], BF16, frozenset(["Cq"]))
    b2rb = R.add("b2rb", R1, [1, 2048], BF16, frozenset(["F"]))
    w2db = R.add("w2db", R1 + 8192, [128, 2 * 16 * 128], BF16, frozenset("C2"))
    csq = R.add("csq", R1 + 16384, [128, 1024], BF16, frozenset(["C1", "C2"]))
    csqmax = R.add("csqmax", R1 + 18432, [128, 1024], F32, frozenset(["C1", "Cq"]))
    ctmp1 = R.add("ctmp1", R1 + 22528, [128, 1024], F32,
                  frozenset(["C1", "Cq", "C2", "F"]))
    ctmp2 = R.add("ctmp2", R1 + 26624, [128, 512], F32, frozenset(["C2", "F"]))
    crow_f = R.add("crow_f", R1 + 28672, [1, 1024], F32, frozenset(["Cq"]))

    bf2r = R.add("bf2r", 163840, [128, 16 * 1024], BF16, frozenset(["C2", "F"]))
    qT = R.add("qT", 163840, [128, 16 * 1024], BF16, AB)

    bcA = R.add("bcA", 196608, [128, 1024], F32, frozenset(["L", "Cq", "C1"]))
    bcB = R.add("bcB", 200704, [128, 1024], F32, frozenset(["L", "Cq", "C1"]))
    bcC = R.add("bcC", 204800, [128, 1024], F32, frozenset(["F"]))
    dbgB = R.add("dbgB", 204800 - 4096, [128, 16], F32, frozenset(["Bdbg"]))
    qrow_f = R.add("qrow_f", 204800, [1, 1024], F32, frozenset(["L", "Cq"]))

    P0 = 208896
    onesr = R.add("onesr", P0, [1, 128], BF16, frozenset("ABPLC1C2F"))
    onesc = R.add("onesc", P0 + 256, [128, 1], BF16, frozenset("ABPLC1C2F"))
    wsdkv = R.add("wsdkv", P0 + 320, [1, 512], BF16, frozenset("A"))
    bias_sb = R.add("bias_sb", P0 + 1344, [128, 148], F32, frozenset("ABPLC1C2F"))
    s12 = R.add("s12", P0 + 1952, [128, 4], F32, frozenset("BPLC1C2F"))
    b2cb = R.add("b2cb", P0 + 1984, [128, 16], BF16, frozenset("ABPLC1C2F"))
    rowp_f = R.add("rowp_f", P0 + 2016, [1, 128], F32, frozenset("ABPLC1C2F"))
    pre_acc = R.add("pre_acc", P0 + 2528, [128, 2], F32, frozenset("AB"))
    fence_c = R.add("fence_c", P0 + 2624, [1, 8], F32,
                    frozenset("ABPLC1C2F"))
    pre_r = R.add("pre_r", P0 + 2560, [128, 2], F32, frozenset("AB"))
    prehl_b = R.add("prehl_b", P0 + 2592, [128, 4], BF16, frozenset("AB"))
    buvr = R.add("buvr", 196608, [1, 2048], BF16, frozenset("AB"))
    pre_s = R.add("pre_s", 196608 + 4096, [128, 128], F32, frozenset("AB"))


    BIG1 = nc.alloc_psum_tensor("BIG1", [128, 1024], F32)
    BIG2 = nc.alloc_psum_tensor("BIG2", [128, 1024], F32)
    BIG3 = nc.alloc_psum_tensor("BIG3", [128, 1024], F32)
    SM1 = nc.alloc_psum_tensor("SM1", [128, 512], F32)
    SM2 = nc.alloc_psum_tensor("SM2", [128, 512], F32)

    nc.const_aps.register(F32, MAGIC, s12[:, 2:3])
    nc.const_aps.register(F32, -MAGIC, s12[:, 3:4])

    st = nc.alloc_semaphore("st")
    sv = nc.alloc_semaphore("sv")
    sc = nc.alloc_semaphore("sc")
    ds = nc.alloc_semaphore("ds")
    NDG = 8
    dg = [nc.alloc_semaphore("dg%d" % i) for i in range(NDG)]
    NYS = 8
    ys = [nc.alloc_semaphore("ys%d" % i) for i in range(NYS)]

    ops = []
    EMIT = {k: (lambda fn, kk=k: ops.append((kk, fn))) for k in "TVSGY"}
    cnt = {"t": 0, "v": 0, "s": 0, "ds": 0}
    gslot, guse = [0], [0] * NDG
    yslot, yuse = [0], [0] * NYS
    last_wait = {}
    SEMH = {"t": st, "v": sv, "s": sc}

    def wait(ek, ev):
        if ev is None:
            return
        key, val = ev
        if val is None or val <= 0:
            return
        k = (ek, key)
        if last_wait.get(k, 0) >= val:
            return
        last_wait[k] = val
        if key.startswith("dg"):
            semh, tgt = dg[int(key[2:])], val * 16
        elif key.startswith("ys"):
            semh, tgt = ys[int(key[2:])], val * 16
        elif key == "ds":
            semh, tgt = ds, val * 16
        else:
            semh, tgt = SEMH[key], val
        EMIT[ek](lambda eng, s=semh, v=tgt: eng.wait_ge(s, v))

    def g_dma(dst_ap, src_ap, waits=()):
        slot = gslot[0]
        gslot[0] = (slot + 1) % NDG
        prev = guse[slot]
        guse[slot] = prev + 1
        for w in waits:
            wait("G", w)
        if prev > 0:
            wait("G", ("dg%d" % slot, prev))
        EMIT["G"](lambda eng, d=dst_ap, s=src_ap, sm=dg[slot]:
                  eng.dma_start(d, s).then_inc(sm, 16))
        return ("dg%d" % slot, guse[slot])

    def y_dma(dst_ap, src_ap, waits=(), slow=False):
        for w in waits:
            wait("Y", w)
        cnt["ds"] += 1
        EMIT["Y"](lambda eng, d=dst_ap, s=src_ap, sl=slow:
                  eng.dma_start(d, s, allow_slow_non_contiguous=sl).then_inc(ds, 16))
        return ("ds", cnt["ds"])

    def y_load(dst_ap, src_ap, waits=(), slow=False):
        slot = yslot[0]
        yslot[0] = (slot + 1) % NYS
        prev = yuse[slot]
        yuse[slot] = prev + 1
        for w in waits:
            wait("Y", w)
        if prev > 0:
            wait("Y", ("ys%d" % slot, prev))
        EMIT["Y"](lambda eng, d=dst_ap, s=src_ap, sm=ys[slot], sl=slow:
                  eng.dma_start(d, s, allow_slow_non_contiguous=sl).then_inc(sm, 16))
        return ("ys%d" % slot, yuse[slot])

    tap_names = {nm for nm, sh in taps}

    def tap(nm, src_t, ncols, ev):
        if nm in tap_names:
            y_dma(bass.AP(tap_t[nm], 0, [[ncols, 128], [1, ncols]]),
                  bass.AP(src_t, 0, [[ncols, 128], [1, ncols]]), waits=[ev])

    def t_mm(out_ap, lhs_ap, rhs_ap, start, stop, inc=False, waits=()):
        for w in waits:
            wait("T", w)
        if inc:
            cnt["t"] += 1
        EMIT["T"](lambda eng, o=out_ap, l=lhs_ap, r=rhs_ap, a=start, b=stop, i=inc:
                  (eng.matmul(o, l, r, start=a, stop=b).then_inc(st, 1) if i
                   else eng.matmul(o, l, r, start=a, stop=b)))
        return ("t", cnt["t"]) if inc else None

    def v_op(builder, inc=False, waits=()):
        for w in waits:
            wait("V", w)
        if inc:
            cnt["v"] += 1
        EMIT["V"](lambda eng, b=builder, i=inc:
                  (b(eng).then_inc(sv, 1) if i else b(eng)))
        return ("v", cnt["v"]) if inc else None

    def s_op(builder, inc=False, waits=()):
        for w in waits:
            wait("S", w)
        if inc:
            cnt["s"] += 1
        EMIT["S"](lambda eng, b=builder, i=inc:
                  (b(eng).then_inc(sc, 1) if i else b(eng)))
        return ("s", cnt["s"]) if inc else None

    def vfence(ev):
        return s_op(lambda eng: eng.copy(fence_c[0:1, 0:1], fence_c[0:1, 0:1]),
                    inc=True, waits=[ev])

    def sfence(ev):
        return v_op(lambda eng: eng.tensor_copy(fence_c[0:1, 1:2],
                                                fence_c[0:1, 1:2]),
                    inc=True, waits=[ev])

    def bcast(src_row_ap, n, psum_t, scr_f_t, scr_h_t, scr_l_t, waits=(),
              psum_off=0):
        def prep(eng):
            eng.tensor_copy(scr_h_t[0:1, 0:n], src_row_ap)
            eng.tensor_tensor(scr_f_t[0:1, 0:n], src_row_ap, scr_h_t[0:1, 0:n],
                              ALU.subtract)
            return eng.tensor_copy(scr_l_t[0:1, 0:n], scr_f_t[0:1, 0:n])
        pe = v_op(prep, inc=True, waits=waits)
        ev = None
        for c0 in range(0, n, 512):
            w = min(512, n - c0)
            t_mm(psum_t[:, psum_off + c0:psum_off + c0 + w], onesr[0:1, :],
                 scr_h_t[0:1, c0:c0 + w], True, False, waits=[pe])
            ev = t_mm(psum_t[:, psum_off + c0:psum_off + c0 + w], onesr[0:1, :],
                      scr_l_t[0:1, c0:c0 + w], False, True, inc=True)
        return ev

    def pfold(buf, scratch, n, op):
        def fn(eng, b=buf, s=scratch, nn=n, o=op):
            ins = None
            for p in (64, 32):
                eng.tensor_copy(s[0:p, 0:nn], b[p:2 * p, 0:nn])
                ins = eng.tensor_tensor(b[0:p, 0:nn], b[0:p, 0:nn], s[0:p, 0:nn], o)
            p = 16
            while p >= 1:
                mask = [(i + p) % 32 for i in range(32)]
                eng.stream_shuffle(s[0:32, 0:nn], b[0:32, 0:nn], mask)
                ins = eng.tensor_tensor(b[0:32, 0:nn], b[0:32, 0:nn], s[0:32, 0:nn], o)
                p //= 2
            return ins
        return fn

    def init_ones(eng):
        eng.memset(onesr[0:1, :], 1.0)
        eng.memset(fence_c[0:1, 0:8], 0.0)
        eng.memset(s12[:, 2:3], MAGIC)
        eng.memset(s12[:, 3:4], -MAGIC)
        return eng.memset(onesc[:, :], 1.0)
    ev_ones = v_op(init_ones, inc=True)
    ev_bias = None
    for nm, col, k in (("bdkv", 0, 4), ("buk", 4, 16), ("bdq", 20, 12),
                       ("buq", 32, 16), ("bo", 48, 16), ("b1", 64, 64),
                       ("b2", 128, 16)):
        ev_bias = y_load(bass.AP(bias_sb, col, [[148, 128], [1, k]]),
                         bass.AP(biasd[nm], 0, [[1, 128], [128, k]]), slow=True)
    e1 = y_load(arow_f[0:1, 0:1024], bass.AP(biasd["buv"], 0, [[1024, 1], [1, 1024]]))
    ec1 = v_op(lambda eng: eng.tensor_copy(buvr[0:1, 0:1024], arow_f[0:1, 0:1024]),
               inc=True, waits=[e1])
    e2 = y_load(arow_f[0:1, 0:1024],
                bass.AP(biasd["buv"], 1024, [[1024, 1], [1, 1024]]), waits=[ec1])
    ev_buvr = v_op(lambda eng: eng.tensor_copy(buvr[0:1, 1024:2048],
                                               arow_f[0:1, 0:1024]),
                   inc=True, waits=[e2])
    ev_b2cb = v_op(lambda eng: eng.tensor_copy(b2cb[:, :], bias_sb[:, 128:144]),
                   inc=True, waits=[ev_bias])
    ev_pre0 = v_op(lambda eng: eng.memset(fence_c[0:1, 4:6], 0.0), inc=True)

    def wload(dst_sb, dst_row, src_t, src_cols, itile0, ntiles, col0, ncols,
              dst_col0, waits=()):
        dst = bass.AP(dst_sb, dst_col0, [[dst_row, 128], [ncols, ntiles], [1, ncols]])
        src = bass.AP(src_t, itile0 * 128 * src_cols + col0,
                      [[src_cols, 128], [128 * src_cols, ntiles], [1, ncols]])
        return g_dma(dst, src, waits=waits)

    ev_wdkv = wload(wdkv_sb, 8192, wdkv, KVC, 0, 16, 0, KVC, 0)
    ev_wdq = None
    for g in range(4):
        ev_wdq = wload(wdq_sb, 16 * 1536, wdq, QC, 4 * g, 4, 0, QC, 4 * g * 1536)

    ev = None
    for k in range(16):
        ev = t_mm(SM1[0:1, 0:512], onesc[:, :], wdkv_sb[:, k * 512:(k + 1) * 512],
                  k == 0, k == 15, inc=(k == 15),
                  waits=[ev_wdkv, ev_ones] if k == 0 else ())
    ev_wsdkv = v_op(lambda eng: eng.tensor_copy(wsdkv[0:1, :], SM1[0:1, 0:512]),
                    inc=True, waits=[ev])
    rows3 = [(SM2, 0, 0), (BIG3, 0, 512), (BIG3, 512, 1024)]
    evs3 = []
    for (pt, off, dcol) in rows3:
        ev = None
        for k in range(16):
            ev = t_mm(pt[0:1, off:off + 512], onesc[:, :],
                      wdq_sb[:, k * 1536 + dcol: k * 1536 + dcol + 512],
                      k == 0, k == 15, inc=(k == 15),
                      waits=[ev_wdq] if k == 0 else ())
        evs3.append(ev)

    def cp_wsdq(eng):
        eng.tensor_copy(wsdq[0:1, 0:512], SM2[0:1, 0:512])
        eng.tensor_copy(wsdq[0:1, 512:1024], BIG3[0:1, 0:512])
        return eng.tensor_copy(wsdq[0:1, 1024:1536], BIG3[0:1, 512:1024])
    ev_wsdq = v_op(cp_wsdq, inc=True, waits=evs3)

    pslots = [(BIG1, 0), (BIG1, 512), (BIG2, 0), (BIG2, 512)]
    pslot_ev = [None] * 4
    psi = [0]

    def next_slot():
        i = psi[0] % 4
        psi[0] += 1
        return i

    prev_fin = [ev_wsdq]
    prev_rb = [ev_wsdq, ev_wsdq]
    prev_mm2 = [None]
    prev_atmp_rd = [None]

    ARB = [arb0, arb1]
    NEGM = [arow_b_0, arow_b_1]

    def chunk_stats(par, xoff, xev):
        sq_ev = None
        mm2 = prev_mm2[0]
        for k in range(16):
            sq_ev = s_op(lambda eng, kk=k: eng.activation(
                asq[:, 0:512], xbf[:, xoff + kk * 512: xoff + (kk + 1) * 512],
                AF.Square), inc=True, waits=[xev, mm2])
            t_mm(SM1[0:1, 0:512], onesc[:, :],
                 xbf[:, xoff + k * 512: xoff + (k + 1) * 512], k == 0, k == 15,
                 waits=[xev, prev_fin[0]] if k == 0 else ())
            mm2 = t_mm(SM2[0:1, 0:512], onesc[:, :], asq[:, 0:512],
                       k == 0, k == 15, inc=True, waits=[sq_ev])
        prev_mm2[0] = mm2

        def fin(eng):
            m = arow_f[0:1, 0:512]
            v2 = arow_f[0:1, 512:1024]
            eng.tensor_scalar(m, SM1[0:1, 0:512], INV2048, None, ALU.mult)
            eng.tensor_scalar(v2, SM2[0:1, 0:512], INV2048, None, ALU.mult)
            eng.tensor_scalar(atmp[0:1, 0:512], m, -1.0, None, ALU.mult)
            eng.tensor_copy(NEGM[par][0:1, 0:512], atmp[0:1, 0:512])
            eng.tensor_tensor(atmp[0:1, 0:512], m, m, ALU.mult)
            eng.tensor_tensor(v2, v2, atmp[0:1, 0:512], ALU.subtract)
            eng.tensor_scalar(v2, v2, LN_EPS, None, ALU.add)
            return eng.reciprocal(atmp[0:1, 0:512], v2)
        e_f = v_op(fin, inc=True, waits=[mm2, prev_atmp_rd[0]])
        prev_fin[0] = e_f
        e_r = s_op(lambda eng: eng.activation(arow_f[0:1, 0:512],
                                              atmp[0:1, 0:512], AF.Sqrt),
                   inc=True, waits=[e_f])
        ebc = bcast(arow_f[0:1, 0:512], 512, BIG3, atmp, arow_b2,
                    arow_b2a, waits=[e_r, prev_rb[par], prev_atmp_rd[0]],
                    psum_off=par * 512)
        e_rb = v_op(lambda eng, p=par: eng.tensor_copy(ARB[p][:, :],
                                                       BIG3[:, p * 512:(p + 1) * 512]),
                    inc=True, waits=[ebc])
        prev_rb[par] = e_rb
        return e_rb

    a_done = []
    a_last_s = [None]
    last_q_mm = [None]
    qw_prev = [None, None]

    def emit_fold_linear(par, xoff, n_ot, w_sb, w_tcols, ws_t, bias_col,
                         out_cb, e_rb):
        evs = []
        last_mm = [None]
        for ot in range(n_ot):
            si = next_slot()
            pt, off = pslots[si]
            t_mm(pt[:, off:off + 512], ws_t[0:1, ot * 128:(ot + 1) * 128],
                 NEGM[par][0:1, 0:512], True, False,
                 waits=[pslot_ev[si], e_rb if ot == 0 else None])
            for k in range(16):
                t_mm(pt[:, off:off + 512],
                     w_sb[:, k * w_tcols + ot * 128: k * w_tcols + (ot + 1) * 128],
                     xbf[:, xoff + k * 512: xoff + (k + 1) * 512],
                     False, k == 15, inc=(k == 15))
            mev = ("t", cnt["t"])
            last_mm[0] = mev
            e_v = v_op(lambda eng, p=pt, o=off, pr=par: eng.tensor_tensor(
                atmp[:, :], p[:, o:o + 512], ARB[pr][:, :], ALU.mult),
                inc=True, waits=[mev, prev_atmp_rd[0]])
            e_s = s_op(lambda eng, oo=ot, dst=out_cb(ot), bc=bias_col + ot:
                       eng.activation(dst, atmp[:, :], AF.Identity,
                                      bias=bias_sb[:, bc:bc + 1]),
                       inc=True, waits=[e_v])
            prev_atmp_rd[0] = e_s
            pslot_ev[si] = e_s
            evs.append(e_s)
        return evs, last_mm[0]

    consumed = [None] * 10
    xevs = [None] * 10

    def chunk_dma(i):
        par = i % 2
        w = []
        if i >= 2:
            w = [consumed[i - 2]]
        if i < 8:
            src = bass.AP(xT, i * 512, [[S, 128], [128 * S, 16], [1, 512]])
        else:
            src = bass.AP(xoT, (i - 8) * 512, [[T, 128], [128 * T, 16], [1, 512]])
        return g_dma(bass.AP(xbf, par * 8192, [[16384, 128], [512, 16], [1, 512]]),
                     src, waits=w)

    xevs[0] = chunk_dma(0)
    xevs[1] = chunk_dma(1)
    for c in range(8):
        par = c % 2
        xoff = par * 8192
        e_rb = chunk_stats(par, xoff, xevs[c])
        evs, lmm = emit_fold_linear(
            par, xoff, 4, wdkv_sb, 512, wsdkv, 0,
            lambda ot, cc=c: ckv[:, ot * 4096 + cc * 512: ot * 4096 + (cc + 1) * 512],
            e_rb)
        consumed[c] = lmm
        prev_rb[par] = evs[-1]
        a_done.extend(evs)
        a_last_s[0] = evs[-1]
        if c + 2 < 10:
            xevs[c + 2] = chunk_dma(c + 2)

    for oc in range(2):
        i = 8 + oc
        par = i % 2
        xoff = par * 8192
        e_rb = chunk_stats(par, xoff, xevs[i])
        cq_evs = []
        lmm = None
        for ot in range(12):
            si = next_slot()
            pt, off = pslots[si]
            t_mm(pt[:, off:off + 512], wsdq[0:1, ot * 128:(ot + 1) * 128],
                 NEGM[par][0:1, 0:512], True, False,
                 waits=[pslot_ev[si], e_rb if ot == 0 else None])
            for k in range(16):
                t_mm(pt[:, off:off + 512],
                     wdq_sb[:, k * 1536 + ot * 128: k * 1536 + (ot + 1) * 128],
                     xbf[:, xoff + k * 512: xoff + (k + 1) * 512],
                     False, k == 15, inc=(k == 15))
            mev = ("t", cnt["t"])
            lmm = mev
            e_v = v_op(lambda eng, p=pt, o=off, pr=par: eng.tensor_tensor(
                atmp[:, :], p[:, o:o + 512], ARB[pr][:, :], ALU.mult),
                inc=True, waits=[mev, prev_atmp_rd[0]])
            e_s = s_op(lambda eng, oo=ot: eng.activation(
                cqT[:, oo * 512:(oo + 1) * 512], atmp[:, :], AF.Identity,
                bias=bias_sb[:, 20 + oo:21 + oo]), inc=True,
                waits=[e_v, last_q_mm[0] if ot == 0 else None])
            prev_atmp_rd[0] = e_s
            pslot_ev[si] = e_s
            cq_evs.append(e_s)
        consumed[i] = lmm
        prev_rb[par] = cq_evs[-1]
        for ot in range(16):
            wev = wload(wuq_sb, 2 * 12 * 128, wuq, DM, 0, 12, ot * 128, 128,
                        (ot % 2) * 12 * 128, waits=[qw_prev[ot % 2]])
            si = next_slot()
            pt, off = pslots[si]
            for k in range(12):
                t_mm(pt[:, off:off + 512],
                     wuq_sb[:, (ot % 2) * 1536 + k * 128:(ot % 2) * 1536 + (k + 1) * 128],
                     cqT[:, k * 512:(k + 1) * 512], k == 0, k == 11,
                     inc=(k == 11),
                     waits=[wev, pslot_ev[si], cq_evs[k]] if k == 0 else [cq_evs[k]])
            mev = ("t", cnt["t"])
            qw_prev[ot % 2] = mev
            e_s = s_op(lambda eng, p=pt, o=off, oo=ot, occ=oc: eng.activation(
                qT[:, oo * 1024 + occ * 512: oo * 1024 + occ * 512 + 512],
                p[:, o:o + 512], AF.Identity, bias=bias_sb[:, 32 + oo:33 + oo]),
                inc=True, waits=[mev])
            pslot_ev[si] = e_s
            a_last_s[0] = e_s
        last_q_mm[0] = qw_prev[15 % 2]
    a_last = a_last_s[0]
    tap("ckv", ckv, 16384, a_last)
    tap("qT", qT, 16384, a_last)

    ev_wuk = wload(wuk_sb, 4 * 2048, wuk, DM, 0, 4, 0, DM, 0, waits=[a_last])
    ev_wuv = wload(wuv_sb, 4 * 2048, wuv, DM, 0, 4, 0, DM, 0)

    pre_jobs = [(0, w1, ch) for ch in range(64)] + [(1, w2, ch) for ch in range(64)]
    pre_ptr = [0]
    pre_vev = [a_last, a_last]

    def emit_prestream(n):
        for _ in range(n):
            if pre_ptr[0] >= len(pre_jobs):
                return
            wi, wt, ch = pre_jobs[pre_ptr[0]]
            slot = pre_ptr[0] % 2
            pre_ptr[0] += 1
            dev = y_load(bass.AP(pre_f, slot * 2048, [[4096, 128], [1, 2048]]),
                         bass.AP(wt, ch * 128 * 2048, [[2048, 128], [1, 2048]]),
                         waits=[pre_vev[slot]])

            col = wi * 64 + ch
            pre_vev[slot] = v_op(
                lambda eng, sl=slot, c=col: eng.tensor_reduce(
                    pre_s[:, c:c + 1], pre_f[:, sl * 2048:(sl + 1) * 2048],
                    AX.X, ALU.add, apply_absolute_value=True),
                inc=True, waits=[dev])

    def emit_kgen(h, slot, first_waits):
        out = []
        for cc in range(8):
            pt, off = (BIG2, 0) if cc % 2 == 0 else (BIG2, 512)
            for j in range(4):
                w = []
                if cc < 2 and j == 0:
                    w = list(first_waits) + [ev_wuk]
                elif j == 0 and cc >= 2:
                    w = [out[cc - 2]]
                t_mm(pt[:, off:off + 512],
                     wuk_sb[:, j * 2048 + h * 128: j * 2048 + (h + 1) * 128],
                     ckv[:, j * 4096 + cc * 512: j * 4096 + (cc + 1) * 512],
                     j == 0, j == 3, inc=(j == 3), waits=w)
            mev = ("t", cnt["t"])
            out.append(v_op(lambda eng, p=pt, o=off, c2=cc, sl=slot, hh=h:
                            eng.tensor_scalar(
                                ktb[:, sl * 4096 + c2 * 512: sl * 4096 + (c2 + 1) * 512],
                                p[:, o:o + 512], bias_sb[:, 4 + hh:5 + hh],
                                None, ALU.add),
                            inc=True, waits=[mev]))
        return out

    def emit_vgen(h, slot, first_waits):
        out = []
        for tt in range(32):
            pt, off = (BIG3, 0) if tt % 2 == 0 else (BIG3, 512)
            w = (list(first_waits) + [ev_wuv, ev_buvr]) if tt < 2 else [out[tt - 2]]
            t_mm(pt[:, off:off + 128], onesr[0:1, :],
                 buvr[0:1, h * 128:(h + 1) * 128], True, False, waits=w)
            for j in range(4):
                t_mm(pt[:, off:off + 128],
                     ckv[:, j * 4096 + tt * 128: j * 4096 + (tt + 1) * 128],
                     wuv_sb[:, j * 2048 + h * 128: j * 2048 + (h + 1) * 128],
                     False, j == 3, inc=(j == 3))
            mev = ("t", cnt["t"])
            out.append(v_op(lambda eng, p=pt, o=off, t2=tt, sl=slot:
                            eng.tensor_copy(
                                vtb[:, sl * 4096 + t2 * 128: sl * 4096 + (t2 + 1) * 128],
                                p[:, o:o + 128]),
                            inc=True, waits=[mev]))
        return out

    head_kv = [(emit_kgen(0, 0, [a_last]), emit_vgen(0, 0, [a_last]))]
    attn_ev = []
    prev_inv = [prev_fin[0]]
    for h in range(16):
        k_evs, v_evs = head_kv[h]
        sl = h % 2
        exp_a = [None] * 32
        exp_b = [None] * 32
        pv_evs = [None] * 32
        qk_evs = [None] * 32

        def emit_pvden(j):
            first = j == 0
            pa = [exp_a[j]]
            if first:
                pa.append(attn_ev[-1] if attn_ev else a_last)
            t_mm(BIG3[:, 0:512],
                 vtb[:, sl * 4096 + j * 128: sl * 4096 + (j + 1) * 128],
                 prb[:, (j % 2) * 1024:(j % 2) * 1024 + 512],
                 first, j == 31, waits=pa)
            t_mm(BIG3[:, 512:1024],
                 vtb[:, sl * 4096 + j * 128: sl * 4096 + (j + 1) * 128],
                 prb[:, (j % 2) * 1024 + 512:(j % 2 + 1) * 1024],
                 first, j == 31, waits=[exp_b[j]])
            t_mm(SM1[0:1, 0:512], onesc[:, :],
                 prb[:, (j % 2) * 1024:(j % 2) * 1024 + 512],
                 first, j == 31, waits=[prev_inv[0]] if first else ())
            pv_evs[j] = t_mm(SM2[0:1, 0:512], onesc[:, :],
                             prb[:, (j % 2) * 1024 + 512:(j % 2 + 1) * 1024],
                             first, j == 31, inc=True)

        for kt in range(32):
            spt = BIG1 if kt % 2 == 0 else BIG2
            if kt < 2:
                w0 = [k_evs[-1], v_evs[-1]]
                if attn_ev:
                    w0.append(attn_ev[-1])
            else:
                w0 = [exp_b[kt - 2]]
            t_mm(spt[:, 0:512],
                 ktb[:, sl * 4096 + kt * 128: sl * 4096 + (kt + 1) * 128],
                 qT[:, h * 1024: h * 1024 + 512], True, True, waits=w0)
            qk_evs[kt] = t_mm(spt[:, 512:1024],
                              ktb[:, sl * 4096 + kt * 128: sl * 4096 + (kt + 1) * 128],
                              qT[:, h * 1024 + 512:(h + 1) * 1024], True, True,
                              inc=True)
            if kt >= 1:
                emit_pvden(kt - 1)
            exp_a[kt] = s_op(
                lambda eng, p=spt, k2=kt: eng.activation(
                    prb[:, (k2 % 2) * 1024:(k2 % 2) * 1024 + 512],
                    p[:, 0:512], AF.Exp, scale=EXP_SCALE),
                inc=True,
                waits=[qk_evs[kt], pv_evs[kt - 2] if kt >= 2 else None])
            exp_b[kt] = s_op(
                lambda eng, p=spt, k2=kt: eng.activation(
                    prb[:, (k2 % 2) * 1024 + 512:(k2 % 2 + 1) * 1024],
                    p[:, 512:1024], AF.Exp, scale=EXP_SCALE),
                inc=True)
        emit_pvden(31)
        exp_evs = exp_b

        def inv_prep(eng):
            eng.reciprocal(brow_f[0:1, 0:512], SM1[0:1, 0:512])
            return eng.reciprocal(brow_f[0:1, 512:1024],
                                              SM2[0:1, 0:512])
        e_inv = v_op(inv_prep, inc=True, waits=[pv_evs[31]])
        prev_inv[0] = e_inv
        ebc = bcast(brow_f[0:1, 0:1024], 1024, BIG1, brow_f,
                    brow_h, brow_l, waits=[e_inv, exp_evs[30]])
        e_cp = v_op(lambda eng: eng.tensor_copy(invb[:, :], BIG1[:, 0:1024]),
                    inc=True, waits=[ebc])
        e_at = v_op(lambda eng, hh=h: eng.tensor_tensor(
            attn[:, hh * 1024:(hh + 1) * 1024], BIG3[:, 0:1024], invb[:, :],
            ALU.mult), inc=True, waits=[e_cp])
        attn_ev.append(e_at)
        if h < 15:
            nk = emit_kgen(h + 1, (h + 1) % 2, [exp_evs[31]])
            nv = emit_vgen(h + 1, (h + 1) % 2, [e_at])
            head_kv.append((nk, nv))
        emit_prestream(8)
    emit_prestream(len(pre_jobs))

    tap("attn", attn, 16384, attn_ev[15])

    e_pv = s_op(lambda eng: eng.copy(fence_c[0:1, 0:1], fence_c[0:1, 0:1]),
                inc=True, waits=[pre_vev[0], pre_vev[1]])
    e_red = v_op(lambda eng: (eng.tensor_reduce(pre_acc[:, 0:1], pre_s[:, 0:64],
                                                AX.X, ALU.add),
                              eng.tensor_reduce(pre_acc[:, 1:2], pre_s[:, 64:128],
                                                AX.X, ALU.add))[-1],
                 inc=True, waits=[e_pv, attn_ev[15]])
    e_h1 = v_op(lambda eng: (eng.tensor_copy(prehl_b[:, 0:1], pre_acc[:, 0:1]),
                             eng.tensor_copy(prehl_b[:, 1:2], pre_acc[:, 1:2]))[-1],
                inc=True, waits=[vfence(e_red)])
    e_fh1 = vfence(e_h1)
    e_l1 = s_op(lambda eng: (eng.activation(pre_r[:, 0:1], prehl_b[:, 0:1],
                                            AF.Identity, scale=-1.0,
                                            bias=pre_acc[:, 0:1]),
                             eng.activation(pre_r[:, 1:2], prehl_b[:, 1:2],
                                            AF.Identity, scale=-1.0,
                                            bias=pre_acc[:, 1:2]))[-1],
                inc=True, waits=[e_fh1])
    e_l1b = v_op(lambda eng: (eng.tensor_copy(prehl_b[:, 2:3], pre_r[:, 0:1]),
                              eng.tensor_copy(prehl_b[:, 3:4], pre_r[:, 1:2]))[-1],
                 inc=True, waits=[sfence(e_l1)])
    e_sum = None
    for c in range(2):
        t_mm(SM1[0:1, c:c + 1], onesc[:, :], prehl_b[:, c:c + 1], True, False,
             waits=[e_l1b, prev_inv[0]] if c == 0 else ())
        e_sum = t_mm(SM1[0:1, c:c + 1], onesc[:, :], prehl_b[:, 2 + c:3 + c],
                     False, True, inc=True)
    e_rp = v_op(lambda eng: (eng.tensor_scalar(rowp_f[0:1, 0:1], SM1[0:1, 0:1],
                                               1.0 / 16777216.0, None, ALU.mult),
                             eng.tensor_scalar(rowp_f[0:1, 1:2], SM1[0:1, 1:2],
                                               1.0 / 16777216.0, None, ALU.mult))[-1],
                inc=True, waits=[e_sum])
    e_bh = s_op(lambda eng: (eng.copy(brow_h[0:1, 0:1], rowp_f[0:1, 0:1]),
                             eng.copy(brow_h[0:1, 1:2], rowp_f[0:1, 1:2]))[-1],
                inc=True, waits=[vfence(e_rp)])
    e_bl = v_op(lambda eng: (eng.tensor_tensor(rowp_f[0:1, 2:3], rowp_f[0:1, 0:1],
                                               brow_h[0:1, 0:1], ALU.subtract),
                             eng.tensor_tensor(rowp_f[0:1, 3:4], rowp_f[0:1, 1:2],
                                               brow_h[0:1, 1:2], ALU.subtract))[-1],
                inc=True, waits=[sfence(e_bh)])
    e_blc = s_op(lambda eng: (eng.copy(brow_l[0:1, 0:1], rowp_f[0:1, 2:3]),
                              eng.copy(brow_l[0:1, 1:2], rowp_f[0:1, 3:4]))[-1],
                 inc=True, waits=[vfence(e_bl)])
    e_s12m = None
    for c in range(2):
        t_mm(SM2[:, c:c + 1], onesr[0:1, :], brow_h[0:1, c:c + 1], True, False,
             waits=[e_blc] if c == 0 else ())
        e_s12m = t_mm(SM2[:, c:c + 1], onesr[0:1, :], brow_l[0:1, c:c + 1],
                      False, True, inc=True)

    def cp_s12(eng):
        eng.tensor_copy(s12[:, 0:1], SM2[:, 0:1])
        return eng.tensor_copy(s12[:, 1:2], SM2[:, 1:2])
    e_s12 = v_op(cp_s12, inc=True, waits=[e_s12m])
    if "dbg" in tap_names:
        def cp_dbg(eng):
            eng.memset(dbgB[:, :], 0.0)
            for c in range(4):
                eng.tensor_copy(dbgB[:, c:c + 1], prehl_b[:, c:c + 1])
            eng.tensor_copy(dbgB[0:1, 4:5], SM1[0:1, 0:1])
            eng.tensor_copy(dbgB[0:1, 5:6], SM1[0:1, 1:2])
            eng.tensor_copy(dbgB[0:1, 6:7], rowp_f[0:1, 0:1])
            eng.tensor_copy(dbgB[0:1, 7:8], rowp_f[0:1, 1:2])
            eng.tensor_copy(dbgB[0:1, 8:9], brow_h[0:1, 0:1])
            eng.tensor_copy(dbgB[0:1, 9:10], brow_h[0:1, 1:2])
            eng.tensor_copy(dbgB[0:1, 10:11], brow_l[0:1, 0:1])
            eng.tensor_copy(dbgB[0:1, 11:12], brow_l[0:1, 1:2])
            eng.tensor_copy(dbgB[:, 12:13], SM2[:, 0:1])
            return eng.tensor_copy(dbgB[:, 13:14], SM2[:, 1:2])
        e_dbg = v_op(cp_dbg, inc=True, waits=[e_s12])
        y_dma(bass.AP(tap_t["dbg"], 0, [[16, 128], [1, 16]]), dbgB[:, :],
              waits=[e_dbg])

    op_evs = []
    stat_mm = [None]
    wo_prev = [None, None]
    for ot in range(16):
        wev = wload(wo_sb, 2 * 16 * 128, wo, DM, 0, 16, ot * 128, 128,
                    (ot % 2) * 2048,
                    waits=[wo_prev[ot % 2]] if ot >= 2 else
                    [pre_vev[0], pre_vev[1], attn_ev[15], e_s12])
        for half in range(2):
            si = next_slot()
            pt, off = pslots[si]
            for k in range(16):
                w = [wev, pslot_ev[si]] if k == 0 else ()
                t_mm(pt[:, off:off + 512],
                     wo_sb[:, (ot % 2) * 2048 + k * 128:(ot % 2) * 2048 + (k + 1) * 128],
                     attn[:, k * 1024 + half * 512: k * 1024 + half * 512 + 512],
                     k == 0, k == 15, inc=(k == 15), waits=w)
            mev = ("t", cnt["t"])
            if half == 1:
                wo_prev[ot % 2] = mev
            e_s = s_op(lambda eng, p=pt, o=off, oo=ot, hf=half: eng.activation(
                aoT[:, oo * 1024 + hf * 512: oo * 1024 + hf * 512 + 512],
                p[:, o:o + 512], AF.Identity, bias=bias_sb[:, 48 + oo:49 + oo]),
                inc=True, waits=[mev])
            pslot_ev[si] = e_s
            op_evs.append(e_s)
        e_sq = s_op(lambda eng, oo=ot: eng.activation(
            sqtmp[:, 0:1024], aoT[:, oo * 1024:(oo + 1) * 1024], AF.Square),
            inc=True, waits=[op_evs[-1], stat_mm[0]])
        t_mm(SM1[0:1, 0:512], onesc[:, :], aoT[:, ot * 1024: ot * 1024 + 512],
             ot == 0, ot == 15,
             waits=[op_evs[-1], e_s12] if ot == 0 else [op_evs[-1]])
        t_mm(SM2[0:1, 0:512], onesc[:, :],
             aoT[:, ot * 1024 + 512:(ot + 1) * 1024], ot == 0, ot == 15)
        t_mm(BIG3[0:1, 0:512], onesc[:, :], sqtmp[:, 0:512], ot == 0, ot == 15,
             waits=[e_sq, attn_ev[15]] if ot == 0 else [e_sq])
        stat_mm[0] = t_mm(BIG3[0:1, 512:1024], onesc[:, :], sqtmp[:, 512:1024],
                          ot == 0, ot == 15, inc=True)

    def ln_fin(n_inv, eps, r_out, mr_out, waits):
        def fn(eng):
            m = mr_out
            eng.tensor_scalar(m[0:1, 0:512], SM1[0:1, 0:512], n_inv, None, ALU.mult)
            eng.tensor_scalar(m[0:1, 512:1024], SM2[0:1, 0:512], n_inv, None, ALU.mult)
            e2 = lrow_f2[0:1, 0:1024]
            eng.tensor_copy(e2[0:1, 0:512], BIG3[0:1, 0:512])
            eng.tensor_copy(e2[0:1, 512:1024], BIG3[0:1, 512:1024])
            eng.tensor_scalar(e2, e2, n_inv, None, ALU.mult)
            eng.tensor_tensor(r_out, m, m, ALU.mult)
            eng.tensor_tensor(e2, e2, r_out, ALU.subtract)
            eng.tensor_scalar(e2, e2, eps, None, ALU.add)
            return eng.reciprocal(r_out, e2)
        e1 = v_op(fn, inc=True, waits=waits)
        e2 = s_op(lambda eng: eng.activation(r_out, r_out, AF.Sqrt),
                  inc=True, waits=[e1])
        e3 = v_op(lambda eng: eng.tensor_tensor(mr_out, mr_out, r_out, ALU.mult),
                  inc=True, waits=[e2])
        return e3

    r1r = lrow_f[0:1, 0:1024]
    mr1r = ltmp1[0:1, 0:1024]
    e_ln1b = ln_fin(INV2048, LN_EPS, r1r, mr1r, [stat_mm[0]])
    ebc = bcast(r1r, 1024, BIG3, lrow_f2, lrow_h, lrow_l,
                waits=[e_ln1b, stat_mm[0]])
    e_rb1 = v_op(lambda eng: eng.tensor_copy(bcA[:, :], BIG3[:, 0:1024]),
                 inc=True, waits=[ebc])
    ebc = bcast(mr1r, 1024, BIG3, lrow_f2, lrow_h, lrow_l, waits=[e_rb1])
    e_mrb1 = v_op(lambda eng: eng.tensor_copy(bcB[:, :], BIG3[:, 0:1024]),
                  inc=True, waits=[ebc])

    t1_evs = []
    sp_evs = []
    for ft in range(16):
        dev = y_load(bass.AP(xob, (ft % 2) * 1024, [[2048, 128], [1, 1024]]),
                     bass.AP(xoT, ft * 128 * T, [[T, 128], [1, 1024]]),
                     waits=[e_mrb1] if ft < 2 else [t1_evs[ft - 2]])

        def mk(eng, f=ft):
            eng.tensor_tensor(ltmp2[:, :], aoT[:, f * 1024:(f + 1) * 1024],
                              bcA[:, :], ALU.mult)
            eng.tensor_tensor(ltmp2[:, :], ltmp2[:, :], bcB[:, :], ALU.subtract)
            eng.tensor_tensor(t1f[:, f * 1024:(f + 1) * 1024], ltmp2[:, :],
                              xob[:, (f % 2) * 1024:(f % 2 + 1) * 1024], ALU.add)
            return eng.tensor_copy(t1b[:, f * 1024:(f + 1) * 1024],
                                   t1f[:, f * 1024:(f + 1) * 1024])
        e_t1 = v_op(mk, inc=True, waits=[dev, e_mrb1])
        t1_evs.append(e_t1)
        sp_evs.append(y_load(bass.AP(t1sp, ft * 128 * T, [[T, 128], [1, 1024]]),
                             bass.AP(t1f, ft * 1024, [[16384, 128], [1, 1024]]),
                             waits=[e_t1]))

    tap("t1b", t1b, 16384, t1_evs[15])

    stat2 = [None]
    for ft in range(16):
        e_sq = s_op(lambda eng, f=ft: eng.activation(
            sqtmp[:, 0:1024], t1b[:, f * 1024:(f + 1) * 1024], AF.Square),
            inc=True, waits=[t1_evs[ft], stat2[0]])
        t_mm(SM1[0:1, 0:512], onesc[:, :], t1b[:, ft * 1024: ft * 1024 + 512],
             ft == 0, ft == 15,
             waits=[t1_evs[ft], e_mrb1] if ft == 0 else [t1_evs[ft]])
        t_mm(SM2[0:1, 0:512], onesc[:, :],
             t1b[:, ft * 1024 + 512:(ft + 1) * 1024], ft == 0, ft == 15)
        t_mm(BIG3[0:1, 0:512], onesc[:, :], sqtmp[:, 0:512], ft == 0, ft == 15,
             waits=[e_sq, e_mrb1] if ft == 0 else [e_sq])
        stat2[0] = t_mm(BIG3[0:1, 512:1024], onesc[:, :], sqtmp[:, 512:1024],
                        ft == 0, ft == 15, inc=True)
    r2r = lrow_f[0:1, 0:1024]
    mr2r = ltmp1[0:1, 0:1024]
    e_ln2 = ln_fin(INV2048, LN_EPS, r2r, mr2r, [stat2[0], e_rb1])
    ebc = bcast(r2r, 1024, BIG3, lrow_f2, lrow_h, lrow_l, waits=[e_ln2])
    e_rb2 = v_op(lambda eng: eng.tensor_copy(bcA[:, :], BIG3[:, 0:1024]),
                 inc=True, waits=[ebc, t1_evs[15]])
    ebc = bcast(mr2r, 1024, BIG3, lrow_f2, lrow_h, lrow_l, waits=[e_rb2])
    e_mrb2 = v_op(lambda eng: eng.tensor_copy(bcB[:, :], BIG3[:, 0:1024]),
                  inc=True, waits=[ebc])

    e_sqm0 = v_op(lambda eng: eng.memset(sqmax[:, :], 0.0), inc=True,
                  waits=[e_mrb2])
    ln2_evs = []
    rms_mm = [None]
    mx_prev = [e_sqm0]
    for ft in range(16):
        def mk(eng, f=ft):
            eng.tensor_tensor(ltmp2[:, :], t1f[:, f * 1024:(f + 1) * 1024],
                              bcA[:, :], ALU.mult)
            eng.tensor_tensor(ltmp2[:, :], ltmp2[:, :], bcB[:, :], ALU.subtract)
            return eng.tensor_copy(aoT[:, f * 1024:(f + 1) * 1024], ltmp2[:, :])
        e_l = v_op(mk, inc=True, waits=[e_mrb2, sp_evs[ft]])
        e_sq = s_op(lambda eng, f=ft: eng.activation(
            sqtmp[:, 0:1024], aoT[:, f * 1024:(f + 1) * 1024], AF.Square),
            inc=True, waits=[e_l, rms_mm[0], mx_prev[0]])
        t_mm(SM1[0:1, 0:512], onesc[:, :], sqtmp[:, 0:512], ft == 0, ft == 15,
             waits=[e_sq, e_mrb2] if ft == 0 else [e_sq])
        rms_mm[0] = t_mm(SM2[0:1, 0:512], onesc[:, :], sqtmp[:, 512:1024],
                         ft == 0, ft == 15, inc=True)
        mx_prev[0] = v_op(lambda eng: eng.tensor_tensor(
            sqmax[:, :], sqmax[:, :], sqtmp[:, 0:1024], ALU.max),
            inc=True, waits=[e_sq, mx_prev[0]])
        ln2_evs.append(e_l)

    def quant_rows(n_inv, rsqn, rms_mm_ev, fold_ev, bcast_wait):
        def r1(eng):
            t = lrow_f2[0:1, 0:1024]
            eng.tensor_scalar(t[0:1, 0:512], SM1[0:1, 0:512], n_inv, None, ALU.mult)
            eng.tensor_scalar(t[0:1, 512:1024], SM2[0:1, 0:512], n_inv, None, ALU.mult)
            eng.tensor_scalar(t, t, RMS_EPS, None, ALU.add)
            return eng.reciprocal(qrow_f[0:1, 0:1024], t)
        e1 = v_op(r1, inc=True, waits=[rms_mm_ev, fold_ev])
        e2 = s_op(lambda eng: eng.activation(qrow_f[0:1, 0:1024],
                                             qrow_f[0:1, 0:1024], AF.Sqrt),
                  inc=True, waits=[e1])
        return e2

    e_q = quant_rows(INV2048, RSQ2048, rms_mm[0],
                     v_op(pfold(sqmax, ltmp2, 1024, ALU.max), inc=True,
                          waits=[mx_prev[0], ln2_evs[15]]), None)
    e_mxr = v_op(lambda eng: eng.tensor_copy(lrow_f[0:1, 0:1024],
                                             sqmax[0:1, 0:1024]),
                 inc=True, waits=[e_q])
    e_mxs = s_op(lambda eng: eng.activation(lrow_f[0:1, 0:1024],
                                            lrow_f[0:1, 0:1024], AF.Sqrt),
                 inc=True, waits=[e_mxr])

    def q_rows3(eng, rsqn=RSQ2048):
        f = qrow_f[0:1, 0:1024]
        mx = lrow_f[0:1, 0:1024]
        eng.tensor_scalar(f, f, rsqn, None, ALU.mult)
        eng.tensor_tensor(mx, mx, f, ALU.mult)
        eng.tensor_scalar(mx, mx, Q_EPS, None, ALU.max)
        eng.reciprocal(lrow_f2[0:1, 0:1024], mx)
        eng.tensor_scalar(lrow_f2[0:1, 0:1024], lrow_f2[0:1, 0:1024], 127.0,
                          None, ALU.mult)
        eng.tensor_tensor(f, f, lrow_f2[0:1, 0:1024], ALU.mult)
        return eng.tensor_scalar(mx, mx, 1.0 / 127.0, None, ALU.mult)
    e_q3 = v_op(q_rows3, inc=True, waits=[e_mxs])
    ebc = bcast(qrow_f[0:1, 0:1024], 1024, BIG3, lrow_f2, lrow_h, lrow_l,
                waits=[e_q3])
    e_A1 = v_op(lambda eng: eng.tensor_copy(bcA[:, :], BIG3[:, 0:1024]),
                inc=True, waits=[ebc])
    ebc = bcast(lrow_f[0:1, 0:1024], 1024, BIG3, lrow_f2, lrow_h, lrow_l,
                waits=[e_A1])
    e_B1 = v_op(lambda eng: eng.tensor_copy(bcB[:, :], BIG3[:, 0:1024]),
                inc=True, waits=[ebc])

    xq_evs = []
    for ft in range(16):
        e_m = v_op(lambda eng, f=ft: eng.tensor_tensor(
            ltmp2[:, :], aoT[:, f * 1024:(f + 1) * 1024], bcA[:, :], ALU.mult),
            inc=True, waits=[e_B1, xq_evs[-1] if xq_evs else None])
        e_r = s_op(lambda eng: (eng.activation(ltmp2[:, :], ltmp2[:, :],
                                               AF.Identity, bias=MAGIC),
                                eng.activation(ltmp2[:, :], ltmp2[:, :],
                                               AF.Identity, bias=-MAGIC))[-1],
                   inc=True, waits=[e_m])
        xq_evs.append(v_op(lambda eng, f=ft: eng.tensor_tensor(
            aoT[:, f * 1024:(f + 1) * 1024], ltmp2[:, :], bcB[:, :], ALU.mult),
            inc=True, waits=[e_r]))

    tap("xq", aoT, 16384, xq_evs[15])
    if "s12" in tap_names:
        y_dma(bass.AP(tap_t["s12"], 0, [[2, 128], [1, 2]]), s12[:, 0:2],
              waits=[e_s12])
    if "preacc" in tap_names:
        y_dma(bass.AP(tap_t["preacc"], 0, [[2, 128], [1, 2]]), pre_acc[:, :],
              waits=[e_h1])

    sg1_prev = [None, None]
    h2_evs = []
    rms2_mm = [None]
    mx2_prev = [None]
    e_sqm2 = v_op(lambda eng: eng.memset(csqmax[:, :], 0.0), inc=True,
                  waits=[e_B1])
    mx2_prev[0] = e_sqm2
    sq_prev = [None]
    for ot in range(64):
        wev = wload(w1db, 2 * 16 * 128, w1, INNER, 0, 16, ot * 128, 128,
                    (ot % 2) * 2048,
                    waits=[sg1_prev[ot % 2]] if ot >= 2 else
                    ([xq_evs[15]] + sp_evs[8:]))
        e_sg = v_op(lambda eng, s2=(ot % 2): (
            eng.tensor_scalar(w1db[:, s2 * 2048:(s2 + 1) * 2048],
                              w1db[:, s2 * 2048:(s2 + 1) * 2048],
                              1e10, 1.0, ALU.mult, ALU.min),
            eng.tensor_scalar(w1db[:, s2 * 2048:(s2 + 1) * 2048],
                              w1db[:, s2 * 2048:(s2 + 1) * 2048],
                              -1.0, None, ALU.max))[-1], inc=True, waits=[wev])
        for half in range(2):
            si = next_slot()
            pt, off = pslots[si]
            for k in range(16):
                w = []
                if k == 0:
                    w = [e_sg, pslot_ev[si]]
                    if ot == 0 and half == 0:
                        w += [e_s12] + xq_evs
                t_mm(pt[:, off:off + 512],
                     w1db[:, (ot % 2) * 2048 + k * 128:(ot % 2) * 2048 + (k + 1) * 128],
                     aoT[:, k * 1024 + half * 512: k * 1024 + half * 512 + 512],
                     k == 0, k == 15, inc=(k == 15), waits=w)
            mev = ("t", cnt["t"])
            if half == 1:
                sg1_prev[ot % 2] = mev
            e_h = s_op(lambda eng, p=pt, o=off, oo=ot, hf=half: eng.activation(
                h2[:, oo * 1024 + hf * 512: oo * 1024 + hf * 512 + 512],
                p[:, o:o + 512], AF.Gelu, bias=bias_sb[:, 64 + oo:65 + oo],
                scale=s12[:, 0:1]), inc=True, waits=[mev])
            pslot_ev[si] = e_h
            h2_evs.append(e_h)
        e_sq = s_op(lambda eng, oo=ot: eng.activation(
            csq[:, 0:1024], h2[:, oo * 1024:(oo + 1) * 1024], AF.Square),
            inc=True, waits=[h2_evs[-1], rms2_mm[0], mx2_prev[0]])
        sq_prev[0] = e_sq
        t_mm(SM1[0:1, 0:512], onesc[:, :], csq[:, 0:512], ot == 0, ot == 63,
             waits=[e_sq, e_B1] if ot == 0 else [e_sq])
        rms2_mm[0] = t_mm(SM2[0:1, 0:512], onesc[:, :], csq[:, 512:1024],
                          ot == 0, ot == 63, inc=True)
        mx2_prev[0] = v_op(lambda eng: eng.tensor_tensor(
            csqmax[:, :], csqmax[:, :], csq[:, 0:1024], ALU.max),
            inc=True, waits=[e_sq, mx2_prev[0]])

    e_fold2 = v_op(pfold(csqmax, ctmp1, 1024, ALU.max), inc=True,
                   waits=[mx2_prev[0]])

    def r2a(eng):
        t = ctmp1[0:1, 0:1024]
        eng.tensor_scalar(t[0:1, 0:512], SM1[0:1, 0:512], 1.0 / 8192.0, None, ALU.mult)
        eng.tensor_scalar(t[0:1, 512:1024], SM2[0:1, 0:512], 1.0 / 8192.0, None, ALU.mult)
        eng.tensor_scalar(t, t, RMS_EPS, None, ALU.add)
        return eng.reciprocal(qrow_f[0:1, 0:1024], t)
    e_ra = v_op(r2a, inc=True, waits=[rms2_mm[0], e_fold2])
    e_rb_ = s_op(lambda eng: eng.activation(qrow_f[0:1, 0:1024],
                                            qrow_f[0:1, 0:1024], AF.Sqrt),
                 inc=True, waits=[e_ra])
    e_mxr2 = v_op(lambda eng: eng.tensor_copy(crow_f[0:1, 0:1024],
                                              csqmax[0:1, 0:1024]),
                  inc=True, waits=[e_rb_])
    e_mxs2 = s_op(lambda eng: eng.activation(crow_f[0:1, 0:1024],
                                             crow_f[0:1, 0:1024], AF.Sqrt),
                  inc=True, waits=[e_mxr2])

    def q2_rows3(eng):
        f = qrow_f[0:1, 0:1024]
        mx = crow_f[0:1, 0:1024]
        eng.tensor_scalar(f, f, RSQ8192, None, ALU.mult)
        eng.tensor_tensor(mx, mx, f, ALU.mult)
        eng.tensor_scalar(mx, mx, Q_EPS, None, ALU.max)
        eng.reciprocal(ctmp1[0:1, 0:1024], mx)
        eng.tensor_scalar(ctmp1[0:1, 0:1024], ctmp1[0:1, 0:1024], 127.0,
                          None, ALU.mult)
        eng.tensor_tensor(f, f, ctmp1[0:1, 0:1024], ALU.mult)
        return eng.tensor_scalar(mx, mx, 1.0 / 127.0, None, ALU.mult)
    e_q23 = v_op(q2_rows3, inc=True, waits=[e_mxs2])
    ebc = bcast(qrow_f[0:1, 0:1024], 1024, BIG3, ctmp1, crow_h, crow_l,
                waits=[e_q23])
    e_A2 = v_op(lambda eng: eng.tensor_copy(bcA[:, :], BIG3[:, 0:1024]),
                inc=True, waits=[ebc])
    ebc = bcast(crow_f[0:1, 0:1024], 1024, BIG3, ctmp1, crow_h, crow_l,
                waits=[e_A2])
    e_B2b = v_op(lambda eng: eng.tensor_copy(bcB[:, :], BIG3[:, 0:1024]),
                 inc=True, waits=[ebc])

    xq2_evs = []
    for f2 in range(64):
        e_m = v_op(lambda eng, f=f2: eng.tensor_tensor(
            ctmp1[:, :], h2[:, f * 1024:(f + 1) * 1024], bcA[:, :], ALU.mult),
            inc=True, waits=[e_B2b, h2_evs[2 * f2 + 1],
                             xq2_evs[-1] if xq2_evs else None])
        e_r = s_op(lambda eng: (eng.activation(ctmp1[:, :], ctmp1[:, :],
                                               AF.Identity, bias=MAGIC),
                                eng.activation(ctmp1[:, :], ctmp1[:, :],
                                               AF.Identity, bias=-MAGIC))[-1],
                   inc=True, waits=[e_m])
        xq2_evs.append(v_op(lambda eng, f=f2: eng.tensor_tensor(
            h2[:, f * 1024:(f + 1) * 1024], ctmp1[:, :], bcB[:, :], ALU.mult),
            inc=True, waits=[e_r]))

    tap("h2q", h2, 65536, xq2_evs[63])

    sg2_prev = [None, None]
    raw_evs = []
    statr = [None]
    sqr_prev = [None]
    for ot2 in range(16):
        siA = next_slot()
        ptA, offA = pslots[siA]
        siB = next_slot()
        ptB, offB = pslots[siB]
        mevA = mevB = None
        for blk in range(4):
            sl = blk % 2
            wev = wload(w2db, 2 * 16 * 128, w2, DM, blk * 16, 16, ot2 * 128,
                        128, sl * 2048,
                        waits=[sg2_prev[sl]] if sg2_prev[sl] else [])
            e_sg = v_op(lambda eng, s2=sl: (
                eng.tensor_scalar(w2db[:, s2 * 2048:(s2 + 1) * 2048],
                                  w2db[:, s2 * 2048:(s2 + 1) * 2048],
                                  1e10, 1.0, ALU.mult, ALU.min),
                eng.tensor_scalar(w2db[:, s2 * 2048:(s2 + 1) * 2048],
                                  w2db[:, s2 * 2048:(s2 + 1) * 2048],
                                  -1.0, None, ALU.max))[-1], inc=True,
                waits=[wev])
            for ki in range(16):
                k = blk * 16 + ki
                wA = []
                wB = []
                if ki == 0:
                    wA = [e_sg]
                    if ot2 == 0:
                        wA.append(xq2_evs[blk * 16 + 15])
                    if k == 0:
                        wA.append(pslot_ev[siA])
                        wB = [pslot_ev[siB]]
                t_mm(ptA[:, offA:offA + 512],
                     w2db[:, sl * 2048 + ki * 128: sl * 2048 + (ki + 1) * 128],
                     h2[:, k * 1024: k * 1024 + 512],
                     k == 0, k == 63, inc=(k == 63), waits=wA)
                if k == 63:
                    mevA = ("t", cnt["t"])
                t_mm(ptB[:, offB:offB + 512],
                     w2db[:, sl * 2048 + ki * 128: sl * 2048 + (ki + 1) * 128],
                     h2[:, k * 1024 + 512:(k + 1) * 1024],
                     k == 0, k == 63, inc=(ki == 15), waits=wB)
                if ki == 15:
                    sg2_prev[sl] = ("t", cnt["t"])
                    if k == 63:
                        mevB = ("t", cnt["t"])
        e_rA = s_op(lambda eng, p=ptA, o=offA, oo=ot2: eng.activation(
            bf2r[:, oo * 1024: oo * 1024 + 512], p[:, o:o + 512], AF.Identity),
            inc=True, waits=[mevA])
        e_rB = s_op(lambda eng, p=ptB, o=offB, oo=ot2: eng.activation(
            bf2r[:, oo * 1024 + 512:(oo + 1) * 1024], p[:, o:o + 512],
            AF.Identity), inc=True, waits=[mevB])
        pslot_ev[siA] = e_rA
        pslot_ev[siB] = e_rB
        raw_evs.append(e_rA)
        raw_evs.append(e_rB)
        e_sq = s_op(lambda eng, oo=ot2: eng.activation(
            csq[:, 0:1024], bf2r[:, oo * 1024:(oo + 1) * 1024], AF.Square),
            inc=True, waits=[e_rB, statr[0], sqr_prev[0]])
        sqr_prev[0] = e_sq
        t_mm(SM1[0:1, 0:512], onesc[:, :], bf2r[:, ot2 * 1024: ot2 * 1024 + 512],
             ot2 == 0, ot2 == 15,
             waits=[e_rA, e_B2b] if ot2 == 0 else [e_rA])
        t_mm(SM2[0:1, 0:512], onesc[:, :],
             bf2r[:, ot2 * 1024 + 512:(ot2 + 1) * 1024], ot2 == 0, ot2 == 15,
             waits=[e_rB])
        t_mm(BIG3[0:1, 0:512], onesc[:, :], csq[:, 0:512], ot2 == 0, ot2 == 15,
             waits=[e_sq, e_B2b] if ot2 == 0 else [e_sq])
        statr[0] = t_mm(BIG3[0:1, 512:1024], onesc[:, :], csq[:, 512:1024],
                        ot2 == 0, ot2 == 15, inc=True)

    tap("bf2r", bf2r, 16384, statr[0])

    e3 = y_dma(fr_sraw[0:1, 0:1024], bass.AP(biasd["b2"], 0, [[1024, 1], [1, 1024]]),
               waits=[statr[0]])
    ec3 = v_op(lambda eng: eng.tensor_copy(b2rb[0:1, 0:1024], fr_sraw[0:1, 0:1024]),
               inc=True, waits=[e3, statr[0]])
    e4 = y_load(fr_sraw[0:1, 0:1024],
                bass.AP(biasd["b2"], 1024, [[1024, 1], [1, 1024]]), waits=[ec3])
    e_b2rb = v_op(lambda eng: eng.tensor_copy(b2rb[0:1, 1024:2048],
                                              fr_sraw[0:1, 0:1024]),
                  inc=True, waits=[e4])


    def cp_rows(eng):
        eng.tensor_copy(fr_sraw[0:1, 0:512], SM1[0:1, 0:512])
        eng.tensor_copy(fr_sraw[0:1, 512:1024], SM2[0:1, 0:512])
        eng.tensor_copy(fr_ssq[0:1, 0:512], BIG3[0:1, 0:512])
        return eng.tensor_copy(fr_ssq[0:1, 512:1024], BIG3[0:1, 512:1024])
    e_cpr = v_op(cp_rows, inc=True, waits=[statr[0], e_b2rb])
    b2r_mm = [None]
    for ot2 in range(16):
        t_mm(SM1[0:1, 0:512], b2cb[:, ot2:ot2 + 1],
             bf2r[:, ot2 * 1024: ot2 * 1024 + 512], ot2 == 0, ot2 == 15,
             waits=[e_cpr, ev_b2cb] if ot2 == 0 else ())
        b2r_mm[0] = t_mm(SM2[0:1, 0:512], b2cb[:, ot2:ot2 + 1],
                         bf2r[:, ot2 * 1024 + 512:(ot2 + 1) * 1024],
                         ot2 == 0, ot2 == 15, inc=(ot2 == 15))

    def cp_b2r(eng):
        eng.tensor_copy(fr_sb2r[0:1, 0:512], SM1[0:1, 0:512])
        return eng.tensor_copy(fr_sb2r[0:1, 512:1024], SM2[0:1, 0:512])
    e_cb2r = v_op(cp_b2r, inc=True, waits=[b2r_mm[0]])

    e_q0 = v_op(lambda eng: eng.tensor_tensor(ctmp2[:, 0:16], bias_sb[:, 128:144],
                                              bias_sb[:, 128:144], ALU.mult),
                inc=True, waits=[ev_bias, e_b2rb, sqr_prev[0], statr[0]])
    e_q1 = v_op(lambda eng: (eng.tensor_reduce(ctmp2[:, 16:17], bias_sb[:, 128:144],
                                               AX.X, ALU.add),
                             eng.tensor_reduce(ctmp2[:, 17:18], ctmp2[:, 0:16],
                                               AX.X, ALU.add))[-1],
                inc=True, waits=[vfence(e_q0)])
    e_q2h = s_op(lambda eng: (eng.copy(csq[:, 0:1], ctmp2[:, 16:17]),
                              eng.copy(csq[:, 1:2], ctmp2[:, 17:18]))[-1],
                 inc=True, waits=[vfence(e_q1)])
    e_q2l = s_op(lambda eng: (eng.activation(ctmp2[:, 18:19], csq[:, 0:1],
                                             AF.Identity, scale=-1.0,
                                             bias=ctmp2[:, 16:17]),
                              eng.activation(ctmp2[:, 19:20], csq[:, 1:2],
                                             AF.Identity, scale=-1.0,
                                             bias=ctmp2[:, 17:18]))[-1],
                 inc=True, waits=[e_q2h])
    e_q2lb = v_op(lambda eng: (eng.tensor_copy(csq[:, 2:3], ctmp2[:, 18:19]),
                               eng.tensor_copy(csq[:, 3:4], ctmp2[:, 19:20]))[-1],
                  inc=True, waits=[sfence(e_q2l)])
    e_b2m = None
    for c in range(2):
        t_mm(SM1[0:1, 16 + c:17 + c], onesc[:, :], csq[:, c:c + 1], True, False,
             waits=[e_q2lb, e_cb2r] if c == 0 else ())
        e_b2m = t_mm(SM1[0:1, 16 + c:17 + c], onesc[:, :], csq[:, 2 + c:3 + c],
                     False, True, inc=True)

    def cp_b2f(eng):
        eng.tensor_copy(ctmp2[0:1, 16:17], SM1[0:1, 16:17])
        return eng.tensor_copy(ctmp2[0:1, 17:18], SM1[0:1, 17:18])
    e_b2f0 = v_op(cp_b2f, inc=True, waits=[e_b2m])
    e_b2f = vfence(e_b2f0)

    e_means = v_op(lambda eng: (eng.tensor_scalar(frow_f[0:1, 0:1],
                                                  ctmp2[0:1, 16:17], INV2048,
                                                  None, ALU.mult),
                                eng.tensor_scalar(frow_f[0:1, 1:2],
                                                  ctmp2[0:1, 17:18], INV2048,
                                                  None, ALU.mult))[-1],
                    inc=True, waits=[e_b2f, sfence(e_b2f)])
    e_meansf = vfence(e_means)

    def f_rows1(eng):
        s2v = s12[0:1, 1:2]
        mp = fr_mp[0:1, 0:1024]
        eng.tensor_scalar(mp, fr_sraw[0:1, 0:1024], INV2048, None, ALU.mult)
        eng.tensor_scalar(mp, mp, s2v, None, ALU.mult)
        eng.tensor_scalar(mp, mp, frow_f[0:1, 0:1], None, ALU.add)
        ey = ftmp1[0:1, 0:1024]
        eng.tensor_scalar(ey, fr_ssq[0:1, 0:1024], INV2048, None, ALU.mult)
        eng.tensor_scalar(ey, ey, s2v, None, ALU.mult)
        eng.tensor_scalar(ey, ey, s2v, None, ALU.mult)
        eng.tensor_scalar(ftmp2[0:1, 0:1024], fr_sb2r[0:1, 0:1024],
                          2.0 * INV2048, None, ALU.mult)
        eng.tensor_scalar(ftmp2[0:1, 0:1024], ftmp2[0:1, 0:1024], s2v, None,
                          ALU.mult)
        eng.tensor_tensor(ey, ey, ftmp2[0:1, 0:1024], ALU.add)
        eng.tensor_scalar(ey, ey, frow_f[0:1, 1:2], None, ALU.add)
        eng.tensor_tensor(ftmp2[0:1, 0:1024], mp, mp, ALU.mult)
        eng.tensor_tensor(ey, ey, ftmp2[0:1, 0:1024], ALU.subtract)
        eng.tensor_scalar(ey, ey, LN_EPS, None, ALU.add)
        return eng.reciprocal(fr_r3[0:1, 0:1024], ey)
    e_f1 = v_op(f_rows1, inc=True, waits=[e_cb2r, sfence(e_meansf), e_s12])
    e_f2 = s_op(lambda eng: eng.activation(fr_r3[0:1, 0:1024],
                                           fr_r3[0:1, 0:1024], AF.Sqrt),
                inc=True, waits=[e_f1])

    def f_rows2(eng):
        s2v = s12[0:1, 1:2]
        r3 = fr_r3[0:1, 0:1024]
        eng.tensor_scalar(fr_c1[0:1, 0:1024], r3, s2v, None, ALU.mult)
        eng.tensor_tensor(ftmp1[0:1, 0:1024], fr_mp[0:1, 0:1024], r3, ALU.mult)
        eng.tensor_scalar(ftmp1[0:1, 0:1024], ftmp1[0:1, 0:1024], -1.0, None,
                          ALU.mult)
        eng.tensor_copy(nb16[0:1, 0:1024], ftmp1[0:1, 0:1024])
        eng.tensor_tensor(ftmp2[0:1, 0:1024], ftmp1[0:1, 0:1024],
                          nb16[0:1, 0:1024], ALU.subtract)
        eng.tensor_copy(nb16[0:1, 1024:2048], ftmp2[0:1, 0:1024])
        eng.tensor_copy(frow_h[0:1, 0:1024], r3)
        eng.tensor_tensor(ftmp2[0:1, 0:1024], r3, frow_h[0:1, 0:1024],
                          ALU.subtract)
        return eng.tensor_copy(frow_l[0:1, 0:1024], ftmp2[0:1, 0:1024])
    e_f3 = v_op(f_rows2, inc=True, waits=[e_f2])
    ebc = bcast(fr_c1[0:1, 0:1024], 1024, BIG3, frow_f, frow_h2, frow_l2,
                waits=[e_f3, statr[0]])
    e_C1 = v_op(lambda eng: eng.tensor_copy(bcC[:, :], BIG3[:, 0:1024]),
                inc=True, waits=[ebc])

    out_evs = []
    for ot2 in range(16):
        trd = y_load(bass.AP(t1rd, (ot2 % 2) * 1024, [[2048, 128], [1, 1024]]),
                     bass.AP(t1sp, ot2 * 128 * T, [[T, 128], [1, 1024]]),
                     waits=[e_f3] if ot2 < 2 else [out_evs[ot2 - 2]])
        si = next_slot()
        pt, off = pslots[si]
        for half in range(2):
            o2 = off if False else None
        si2 = next_slot()
        pt2, off2 = pslots[si2]
        dmm = []
        for (p_, o_, half) in ((pt, off, 0), (pt2, off2, 1)):
            t_mm(p_[:, o_:o_ + 512], b2rb[0:1, ot2 * 128:(ot2 + 1) * 128],
                 frow_h[0:1, half * 512: half * 512 + 512], True, False,
                 waits=[pslot_ev[si if half == 0 else si2], e_f3,
                        e_C1 if ot2 == 0 else None])
            t_mm(p_[:, o_:o_ + 512], b2rb[0:1, ot2 * 128:(ot2 + 1) * 128],
                 frow_l[0:1, half * 512: half * 512 + 512], False, False)
            t_mm(p_[:, o_:o_ + 512], onesr[0:1, :],
                 nb16[0:1, half * 512: half * 512 + 512], False, False)
            dmm.append(t_mm(p_[:, o_:o_ + 512], onesr[0:1, :],
                            nb16[0:1, 1024 + half * 512: 1024 + half * 512 + 512],
                            False, True, inc=True))

        def mk(eng, oo=ot2, pa=pt, oa=off, pb=pt2, ob=off2):
            eng.tensor_tensor(ftmp1[:, :], bf2r[:, oo * 1024:(oo + 1) * 1024],
                              bcC[:, :], ALU.mult)
            eng.tensor_tensor(ftmp1[:, 0:512], ftmp1[:, 0:512],
                              pa[:, oa:oa + 512], ALU.add)
            eng.tensor_tensor(ftmp1[:, 512:1024], ftmp1[:, 512:1024],
                              pb[:, ob:ob + 512], ALU.add)
            return eng.tensor_tensor(
                coutb[:, (oo % 2) * 1024:(oo % 2 + 1) * 1024], ftmp1[:, :],
                t1rd[:, (oo % 2) * 1024:(oo % 2 + 1) * 1024], ALU.add)
        e_o = v_op(mk, inc=True, waits=[trd, dmm[0], dmm[1], e_C1,
                                        raw_evs[2 * ot2 + 1]])
        pslot_ev[si] = e_o
        pslot_ev[si2] = e_o
        out_evs.append(e_o)
        y_dma(bass.AP(outT, ot2 * 128 * T, [[T, 128], [1, 1024]]),
              bass.AP(coutb, (ot2 % 2) * 1024, [[2048, 128], [1, 1024]]),
              waits=[e_o])

    wait("Y", ("ds", cnt["ds"]))

    with nc.Block() as block:
        @block.tensor
        def _(eng):
            for k, fn in ops:
                if k == "T":
                    fn(eng)

        @block.vector
        def _(eng):
            for k, fn in ops:
                if k == "V":
                    fn(eng)

        @block.scalar
        def _(eng):
            for k, fn in ops:
                if k == "S":
                    fn(eng)

        @block.gpsimd
        def _(eng):
            for k, fn in ops:
                if k == "G":
                    fn(eng)

        @block.sync
        def _(eng):
            for k, fn in ops:
                if k == "Y":
                    fn(eng)

    return nc


def kernel(**inputs):
    ip = {k: np.asarray(v) for k, v in inputs.items()}
    x = ip["L0_attn_hidden_states"]

    def tr(a):
        return np.ascontiguousarray(a.T)

    w = {
        "wdkv": tr(ip["L0_attn_W_DKV_w"]), "wuk": tr(ip["L0_attn_W_UK_w"]),
        "wuv": tr(ip["L0_attn_W_UV_w"]), "wdq": tr(ip["L0_attn_W_DQ_w"]),
        "wuq": tr(ip["L0_attn_W_UQ_w"]), "wo": tr(ip["L0_attn_W_O_w"]),
        "w1": tr(ip["L0_ffn_w1"]), "w2": tr(ip["L0_ffn_w2"]),
        "bdkv": np.ascontiguousarray(ip["L0_attn_W_DKV_b"]),
        "buk": np.ascontiguousarray(ip["L0_attn_W_UK_b"]),
        "buv": np.ascontiguousarray(ip["L0_attn_W_UV_b"]),
        "bdq": np.ascontiguousarray(ip["L0_attn_W_DQ_b"]),
        "buq": np.ascontiguousarray(ip["L0_attn_W_UQ_b"]),
        "bo": np.ascontiguousarray(ip["L0_attn_W_O_b"]),
        "b1": np.ascontiguousarray(ip["L0_ffn_b1"]),
        "b2": np.ascontiguousarray(ip["L0_ffn_b2"]),
    }
    xTb = [np.ascontiguousarray(x[b].T) for b in range(2)]
    nc = build_program()
    in_maps = []
    for c in range(8):
        b, qt = c // 4, c % 4
        im = dict(w)
        im["xT"] = xTb[b]
        im["xoT"] = np.ascontiguousarray(xTb[b][:, qt * 1024:(qt + 1) * 1024])
        in_maps.append(im)
    res = run_bass_kernel_spmd(nc, in_maps, core_ids=list(range(8)))
    out = np.empty((2, 4096, 2048), dtype=np.float32)
    for c in range(8):
        b, qt = c // 4, c % 4
        out[b, qt * 1024:(qt + 1) * 1024, :] = res.results[c]["outT"].T
    return out


# version 06
# speedup vs baseline: 1.2206x; 1.0453x over previous
import numpy as np
import concourse.bass as bass
import concourse.mybir as mybir
from concourse.bass_utils import run_bass_kernel_spmd

F32 = mybir.dt.float32
BF16 = mybir.dt.bfloat16
AF = mybir.ActivationFunctionType
ALU = mybir.AluOpType
AX = mybir.AxisListType

MAGIC = float(np.float32(1.5 * 2 ** 23))
EXP_SCALE = 1.0 / float(np.sqrt(128.0))
LN_EPS = 1e-5
RMS_EPS = 1e-6
Q_EPS = 1e-5
INV2048 = 1.0 / 2048.0
RSQ2048 = float(2048.0 ** -0.5)
RSQ8192 = float(8192.0 ** -0.5)

DM, KVC, QC, INNER = 2048, 512, 1536, 8192
S, T = 4096, 1024


class Regions:
    BASE = 17408

    def __init__(self, nc, limit=229376):
        self.nc = nc
        self.limit = limit
        self.items = []

    def add(self, name, off, shape, dt, phases):
        off = off + self.BASE
        esz = 2 if dt == BF16 else 4
        size = int(np.prod(shape[1:])) * esz
        assert off + size <= self.limit, f"{name}: {off}+{size} > {self.limit}"
        for (n2, o2, s2, p2) in self.items:
            if phases & p2 and not (off + size <= o2 or o2 + s2 <= off):
                raise AssertionError(f"overlap {name}[{off}:{off+size}] vs "
                                     f"{n2}[{o2}:{o2+s2}] phases {phases & p2}")
        self.items.append((name, off, size, phases))
        return self.nc.alloc_sbuf_tensor_at(name, shape, dt, offset=off)


def build_program(taps=()):
    nc = bass.Bass("TRN2", target_bir_lowering=False, detect_race_conditions=False)

    xT = nc.dram_tensor("xT", [DM, S], F32, kind="ExternalInput")
    xoT = nc.dram_tensor("xoT", [DM, T], F32, kind="ExternalInput")
    wdkv = nc.dram_tensor("wdkv", [DM, KVC], F32, kind="ExternalInput")
    wuk = nc.dram_tensor("wuk", [KVC, DM], F32, kind="ExternalInput")
    wuv = nc.dram_tensor("wuv", [KVC, DM], F32, kind="ExternalInput")
    wdq = nc.dram_tensor("wdq", [DM, QC], F32, kind="ExternalInput")
    wuq = nc.dram_tensor("wuq", [QC, DM], F32, kind="ExternalInput")
    wo = nc.dram_tensor("wo", [DM, DM], F32, kind="ExternalInput")
    w1 = nc.dram_tensor("w1", [DM, INNER], F32, kind="ExternalInput")
    w2 = nc.dram_tensor("w2", [INNER, DM], F32, kind="ExternalInput")
    biasd = {}
    for nm, n in (("bdkv", KVC), ("buk", DM), ("buv", DM), ("bdq", QC),
                  ("buq", DM), ("bo", DM), ("b1", INNER), ("b2", DM)):
        biasd[nm] = nc.dram_tensor(nm, [n], F32, kind="ExternalInput")
    outT = nc.dram_tensor("outT", [DM, T], F32, kind="ExternalOutput")
    t1sp = nc.dram_tensor("t1sp", [DM, T], F32, kind="Internal")
    tap_t = {nm: nc.dram_tensor("tap_" + nm, sh, F32 if nm in ("s12", "preacc", "dbg") else BF16,
                                 kind="ExternalOutput")
             for nm, sh in taps}

    R = Regions(nc)
    AB = frozenset("AB")
    xbf = R.add("xbf", 0, [128, 2 * 16 * 512], BF16, frozenset("A"))
    arow_f = R.add("arow_f", 32768, [1, 1024], F32, frozenset("A"))
    arb0 = R.add("arb0", 36864, [128, 512], F32, frozenset("A"))
    arb1 = R.add("arb1", 38912, [128, 512], F32, frozenset("A"))
    wdq_sb = R.add("wdq_sb", 40960, [128, 16 * 1536], BF16, frozenset("A"))
    wuq_sb = R.add("wuq_sb", 90112, [128, 2 * 12 * 128], BF16, frozenset("A"))
    wdkv_sb = R.add("wdkv_sb", 96256, [128, 16 * 512], BF16, frozenset("A"))
    cqT = R.add("cqT", 112640, [128, 12 * 512], BF16, frozenset("A"))
    asq = R.add("asq", 124928, [128, 512], BF16, frozenset("A"))
    atmp = R.add("atmp", 125952, [128, 512], F32, frozenset("A"))
    wsdq = R.add("wsdq", 128000, [1, 1536], BF16, frozenset("A"))
    arow_b_0 = R.add("arow_b_0", 196608 + 4608, [1, 512], BF16, frozenset("A"))
    arow_b_1 = R.add("arow_b_1", 196608 + 5632, [1, 512], BF16, frozenset("A"))
    arow_b2 = R.add("arow_b2", 196608 + 6656, [1, 512], BF16, frozenset("A"))
    arow_b2a = R.add("arow_b2a", 196608 + 7680, [1, 512], BF16, frozenset("A"))

    ktb = R.add("ktb", 0, [128, 2 * 4096], BF16, frozenset("B"))
    vtb = R.add("vtb", 16384, [128, 2 * 4096], BF16, frozenset("B"))
    prb = R.add("prb", 32768, [128, 2 * 1024], BF16, frozenset("B"))
    invb = R.add("invb", 36864, [128, 1024], F32, frozenset("B"))
    wuk_sb = R.add("wuk_sb", 40960, [128, 4 * 2048], BF16, frozenset("B"))
    wuv_sb = R.add("wuv_sb", 57344, [128, 4 * 2048], BF16, frozenset("B"))
    pre_f = R.add("pre_f", 73728, [128, 2 * 2048], F32, frozenset("B"))
    brow_f = R.add("brow_f", 90112, [1, 1024], F32, frozenset("B"))
    brow_h = R.add("brow_h", 94208, [1, 1024], BF16, frozenset("B"))
    brow_l = R.add("brow_l", 96256, [1, 1024], BF16, frozenset("B"))
    attn = R.add("attn", 98304, [128, 16 * 1024], BF16, frozenset("BP"))

    wo_sb = R.add("wo_sb", 73728, [128, 2 * 16 * 128], BF16, frozenset("P"))
    aoT = R.add("aoT", 163840, [128, 16 * 1024], BF16, frozenset("PL"))

    t1f = R.add("t1f", 0, [128, 16 * 1024], F32, frozenset("L"))
    lrow_f = R.add("lrow_f", 65536, [1, 1024], F32, frozenset("PL"))
    lrow_h = R.add("lrow_h", 69632, [1, 1024], BF16, frozenset("PL"))
    lrow_l = R.add("lrow_l", 71680, [1, 1024], BF16, frozenset("PL"))
    lrow_f2 = R.add("lrow_f2", 81920, [1, 1024], F32, frozenset("PL"))
    sqtmp = R.add("sqtmp", 86016, [128, 1024], BF16, frozenset("PL"))
    sqmax = R.add("sqmax", 88064, [128, 1024], F32, frozenset("L"))
    xob = R.add("xob", 92160, [128, 2 * 1024], F32, frozenset("L"))
    ltmp1 = R.add("ltmp1", 100352, [128, 1024], F32, frozenset("L"))
    ltmp2 = R.add("ltmp2", 104448, [128, 1024], F32, frozenset("L"))

    h2 = R.add("h2", 0, [128, 64 * 1024], BF16, frozenset(["C1", "C2"]))
    coutb = R.add("coutb", 0, [128, 2 * 1024], F32, frozenset("F"))
    t1rd = R.add("t1rd", 8192, [128, 2 * 1024], F32, frozenset("F"))
    frow_f = R.add("frow_f", 16384, [1, 1024], F32, frozenset("F"))
    frow_h = R.add("frow_h", 20480, [1, 1024], BF16, frozenset("F"))
    frow_l = R.add("frow_l", 22528, [1, 1024], BF16, frozenset("F"))
    fr_sraw = R.add("fr_sraw", 24576, [1, 1024], F32, frozenset(["F"]))
    fr_ssq = R.add("fr_ssq", 28672, [1, 1024], F32, frozenset(["F"]))
    fr_sb2r = R.add("fr_sb2r", 32768, [1, 1024], F32, frozenset(["F"]))
    fr_mp = R.add("fr_mp", 36864, [1, 1024], F32, frozenset("F"))
    fr_r3 = R.add("fr_r3", 40960, [1, 1024], F32, frozenset("F"))
    fr_c1 = R.add("fr_c1", 45056, [1, 1024], F32, frozenset("F"))
    nb16 = R.add("nb16", 49152, [1, 2048], BF16, frozenset("F"))
    ftmp1 = R.add("ftmp1", 57344, [128, 1024], F32, frozenset("F"))
    ftmp2 = R.add("ftmp2", 61440, [128, 1024], F32, frozenset("F"))
    frow_h2 = R.add("frow_h2", 65536, [1, 1024], BF16, frozenset("F"))
    frow_l2 = R.add("frow_l2", 67584, [1, 1024], BF16, frozenset("F"))

    R1 = 131072
    ckv = R.add("ckv", R1, [128, 4 * 4096], BF16, frozenset("AB"))
    t1b = R.add("t1b", R1, [128, 16 * 1024], BF16, frozenset("L"))
    w1db = R.add("w1db", R1, [128, 2 * 16 * 128], BF16, frozenset("C1"))
    crow_h = R.add("crow_h", R1, [1, 1024], BF16, frozenset(["Cq"]))
    crow_l = R.add("crow_l", R1 + 2048, [1, 1024], BF16, frozenset(["Cq"]))
    b2rb = R.add("b2rb", R1, [1, 2048], BF16, frozenset(["F"]))
    w2db = R.add("w2db", R1 + 8192, [128, 2 * 16 * 128], BF16, frozenset("C2"))
    csq = R.add("csq", R1 + 16384, [128, 1024], BF16, frozenset(["C1", "C2"]))
    csqmax = R.add("csqmax", R1 + 18432, [128, 1024], F32, frozenset(["C1", "Cq"]))
    ctmp1 = R.add("ctmp1", R1 + 22528, [128, 1024], F32,
                  frozenset(["C1", "Cq", "C2", "F"]))
    ctmp2 = R.add("ctmp2", R1 + 26624, [128, 512], F32, frozenset(["C2", "F"]))
    crow_f = R.add("crow_f", R1 + 28672, [1, 1024], F32, frozenset(["Cq"]))

    bf2r = R.add("bf2r", 163840, [128, 16 * 1024], BF16, frozenset(["C2", "F"]))
    qT = R.add("qT", 163840, [128, 16 * 1024], BF16, AB)

    bcA = R.add("bcA", 196608, [128, 1024], F32, frozenset(["L", "Cq", "C1"]))
    bcB = R.add("bcB", 200704, [128, 1024], F32, frozenset(["L", "Cq", "C1"]))
    bcC = R.add("bcC", 204800, [128, 1024], F32, frozenset(["F"]))
    ctmp1b = R.add("ctmp1b", R1 + 4096, [128, 1024], F32, frozenset(["Cq"]))
    dbgB = R.add("dbgB", 204800 - 4096, [128, 16], F32, frozenset(["Bdbg"]))
    qrow_f = R.add("qrow_f", 204800, [1, 1024], F32, frozenset(["L", "Cq"]))

    P0 = 208896
    onesr = R.add("onesr", P0, [1, 128], BF16, frozenset("ABPLC1C2F"))
    onesc = R.add("onesc", P0 + 256, [128, 1], BF16, frozenset("ABPLC1C2F"))
    wsdkv = R.add("wsdkv", P0 + 320, [1, 512], BF16, frozenset("A"))
    bias_sb = R.add("bias_sb", P0 + 1344, [128, 148], F32, frozenset("ABPLC1C2F"))
    s12 = R.add("s12", P0 + 1952, [128, 4], F32, frozenset("BPLC1C2F"))
    b2cb = R.add("b2cb", P0 + 1984, [128, 16], BF16, frozenset("ABPLC1C2F"))
    rowp_f = R.add("rowp_f", P0 + 2016, [1, 128], F32, frozenset("ABPLC1C2F"))
    pre_acc = R.add("pre_acc", P0 + 2528, [128, 2], F32, frozenset("AB"))
    fence_c = R.add("fence_c", P0 + 2624, [1, 8], F32,
                    frozenset("ABPLC1C2F"))
    pre_r = R.add("pre_r", P0 + 2560, [128, 2], F32, frozenset("AB"))
    prehl_b = R.add("prehl_b", P0 + 2592, [128, 4], BF16, frozenset("AB"))
    buvr = R.add("buvr", 196608, [1, 2048], BF16, frozenset("AB"))
    pre_s = R.add("pre_s", 196608 + 4096, [128, 128], F32, frozenset("AB"))


    BIG1 = nc.alloc_psum_tensor("BIG1", [128, 1024], F32)
    BIG2 = nc.alloc_psum_tensor("BIG2", [128, 1024], F32)
    BIG3 = nc.alloc_psum_tensor("BIG3", [128, 1024], F32)
    SM1 = nc.alloc_psum_tensor("SM1", [128, 512], F32)
    SM2 = nc.alloc_psum_tensor("SM2", [128, 512], F32)

    nc.const_aps.register(F32, MAGIC, s12[:, 2:3])
    nc.const_aps.register(F32, -MAGIC, s12[:, 3:4])

    st = nc.alloc_semaphore("st")
    sv = nc.alloc_semaphore("sv")
    sc = nc.alloc_semaphore("sc")
    ds = nc.alloc_semaphore("ds")
    NDG = 8
    dg = [nc.alloc_semaphore("dg%d" % i) for i in range(NDG)]
    NYS = 8
    ys = [nc.alloc_semaphore("ys%d" % i) for i in range(NYS)]

    ops = []
    EMIT = {k: (lambda fn, kk=k: ops.append((kk, fn))) for k in "TVSGY"}
    cnt = {"t": 0, "v": 0, "s": 0, "ds": 0}
    gslot, guse = [0], [0] * NDG
    yslot, yuse = [0], [0] * NYS
    last_wait = {}
    SEMH = {"t": st, "v": sv, "s": sc}

    def wait(ek, ev):
        if ev is None:
            return
        key, val = ev
        if val is None or val <= 0:
            return
        k = (ek, key)
        if last_wait.get(k, 0) >= val:
            return
        last_wait[k] = val
        if key.startswith("dg"):
            semh, tgt = dg[int(key[2:])], val * 16
        elif key.startswith("ys"):
            semh, tgt = ys[int(key[2:])], val * 16
        elif key == "ds":
            semh, tgt = ds, val * 16
        else:
            semh, tgt = SEMH[key], val
        EMIT[ek](lambda eng, s=semh, v=tgt: eng.wait_ge(s, v))

    def g_dma(dst_ap, src_ap, waits=()):
        slot = gslot[0]
        gslot[0] = (slot + 1) % NDG
        prev = guse[slot]
        guse[slot] = prev + 1
        for w in waits:
            wait("G", w)
        if prev > 0:
            wait("G", ("dg%d" % slot, prev))
        EMIT["G"](lambda eng, d=dst_ap, s=src_ap, sm=dg[slot]:
                  eng.dma_start(d, s).then_inc(sm, 16))
        return ("dg%d" % slot, guse[slot])

    def y_dma(dst_ap, src_ap, waits=(), slow=False):
        for w in waits:
            wait("Y", w)
        cnt["ds"] += 1
        EMIT["Y"](lambda eng, d=dst_ap, s=src_ap, sl=slow:
                  eng.dma_start(d, s, allow_slow_non_contiguous=sl).then_inc(ds, 16))
        return ("ds", cnt["ds"])

    def y_load(dst_ap, src_ap, waits=(), slow=False):
        slot = yslot[0]
        yslot[0] = (slot + 1) % NYS
        prev = yuse[slot]
        yuse[slot] = prev + 1
        for w in waits:
            wait("Y", w)
        if prev > 0:
            wait("Y", ("ys%d" % slot, prev))
        EMIT["Y"](lambda eng, d=dst_ap, s=src_ap, sm=ys[slot], sl=slow:
                  eng.dma_start(d, s, allow_slow_non_contiguous=sl).then_inc(sm, 16))
        return ("ys%d" % slot, yuse[slot])

    tap_names = {nm for nm, sh in taps}

    def tap(nm, src_t, ncols, ev):
        if nm in tap_names:
            y_dma(bass.AP(tap_t[nm], 0, [[ncols, 128], [1, ncols]]),
                  bass.AP(src_t, 0, [[ncols, 128], [1, ncols]]), waits=[ev])

    def t_mm(out_ap, lhs_ap, rhs_ap, start, stop, inc=False, waits=()):
        for w in waits:
            wait("T", w)
        if inc:
            cnt["t"] += 1
        EMIT["T"](lambda eng, o=out_ap, l=lhs_ap, r=rhs_ap, a=start, b=stop, i=inc:
                  (eng.matmul(o, l, r, start=a, stop=b).then_inc(st, 1) if i
                   else eng.matmul(o, l, r, start=a, stop=b)))
        return ("t", cnt["t"]) if inc else None

    def v_op(builder, inc=False, waits=()):
        for w in waits:
            wait("V", w)
        if inc:
            cnt["v"] += 1
        EMIT["V"](lambda eng, b=builder, i=inc:
                  (b(eng).then_inc(sv, 1) if i else b(eng)))
        return ("v", cnt["v"]) if inc else None

    def s_op(builder, inc=False, waits=()):
        for w in waits:
            wait("S", w)
        if inc:
            cnt["s"] += 1
        EMIT["S"](lambda eng, b=builder, i=inc:
                  (b(eng).then_inc(sc, 1) if i else b(eng)))
        return ("s", cnt["s"]) if inc else None

    def vfence(ev):
        return s_op(lambda eng: eng.copy(fence_c[0:1, 0:1], fence_c[0:1, 0:1]),
                    inc=True, waits=[ev])

    def sfence(ev):
        return v_op(lambda eng: eng.tensor_copy(fence_c[0:1, 1:2],
                                                fence_c[0:1, 1:2]),
                    inc=True, waits=[ev])

    def bcast(src_row_ap, n, psum_t, scr_f_t, scr_h_t, scr_l_t, waits=(),
              psum_off=0):
        def prep(eng):
            eng.tensor_copy(scr_h_t[0:1, 0:n], src_row_ap)
            eng.tensor_tensor(scr_f_t[0:1, 0:n], src_row_ap, scr_h_t[0:1, 0:n],
                              ALU.subtract)
            return eng.tensor_copy(scr_l_t[0:1, 0:n], scr_f_t[0:1, 0:n])
        pe = v_op(prep, inc=True, waits=waits)
        ev = None
        for c0 in range(0, n, 512):
            w = min(512, n - c0)
            t_mm(psum_t[:, psum_off + c0:psum_off + c0 + w], onesr[0:1, :],
                 scr_h_t[0:1, c0:c0 + w], True, False, waits=[pe])
            ev = t_mm(psum_t[:, psum_off + c0:psum_off + c0 + w], onesr[0:1, :],
                      scr_l_t[0:1, c0:c0 + w], False, True, inc=True)
        return ev

    def pfold(buf, scratch, n, op):
        def fn(eng, b=buf, s=scratch, nn=n, o=op):
            ins = None
            for p in (64, 32):
                eng.tensor_copy(s[0:p, 0:nn], b[p:2 * p, 0:nn])
                ins = eng.tensor_tensor(b[0:p, 0:nn], b[0:p, 0:nn], s[0:p, 0:nn], o)
            p = 16
            while p >= 1:
                mask = [(i + p) % 32 for i in range(32)]
                eng.stream_shuffle(s[0:32, 0:nn], b[0:32, 0:nn], mask)
                ins = eng.tensor_tensor(b[0:32, 0:nn], b[0:32, 0:nn], s[0:32, 0:nn], o)
                p //= 2
            return ins
        return fn

    def init_ones(eng):
        eng.memset(onesr[0:1, :], 1.0)
        eng.memset(fence_c[0:1, 0:8], 0.0)
        eng.memset(s12[:, 2:3], MAGIC)
        eng.memset(s12[:, 3:4], -MAGIC)
        return eng.memset(onesc[:, :], 1.0)
    ev_ones = v_op(init_ones, inc=True)
    ev_bias = None
    for nm, col, k in (("bdkv", 0, 4), ("buk", 4, 16), ("bdq", 20, 12),
                       ("buq", 32, 16), ("bo", 48, 16), ("b1", 64, 64),
                       ("b2", 128, 16)):
        ev_bias = y_load(bass.AP(bias_sb, col, [[148, 128], [1, k]]),
                         bass.AP(biasd[nm], 0, [[1, 128], [128, k]]), slow=True)
    e1 = y_load(arow_f[0:1, 0:1024], bass.AP(biasd["buv"], 0, [[1024, 1], [1, 1024]]))
    ec1 = v_op(lambda eng: eng.tensor_copy(buvr[0:1, 0:1024], arow_f[0:1, 0:1024]),
               inc=True, waits=[e1])
    e2 = y_load(arow_f[0:1, 0:1024],
                bass.AP(biasd["buv"], 1024, [[1024, 1], [1, 1024]]), waits=[ec1])
    ev_buvr = v_op(lambda eng: eng.tensor_copy(buvr[0:1, 1024:2048],
                                               arow_f[0:1, 0:1024]),
                   inc=True, waits=[e2])
    ev_b2cb = v_op(lambda eng: eng.tensor_copy(b2cb[:, :], bias_sb[:, 128:144]),
                   inc=True, waits=[ev_bias])
    ev_pre0 = v_op(lambda eng: eng.memset(fence_c[0:1, 4:6], 0.0), inc=True)

    def wload(dst_sb, dst_row, src_t, src_cols, itile0, ntiles, col0, ncols,
              dst_col0, waits=()):
        dst = bass.AP(dst_sb, dst_col0, [[dst_row, 128], [ncols, ntiles], [1, ncols]])
        src = bass.AP(src_t, itile0 * 128 * src_cols + col0,
                      [[src_cols, 128], [128 * src_cols, ntiles], [1, ncols]])
        return g_dma(dst, src, waits=waits)

    ev_wdkv = wload(wdkv_sb, 8192, wdkv, KVC, 0, 16, 0, KVC, 0)
    ev_wdq = None
    for g in range(4):
        ev_wdq = wload(wdq_sb, 16 * 1536, wdq, QC, 4 * g, 4, 0, QC, 4 * g * 1536)

    ev = None
    for k in range(16):
        ev = t_mm(SM1[0:1, 0:512], onesc[:, :], wdkv_sb[:, k * 512:(k + 1) * 512],
                  k == 0, k == 15, inc=(k == 15),
                  waits=[ev_wdkv, ev_ones] if k == 0 else ())
    ev_wsdkv = v_op(lambda eng: eng.tensor_copy(wsdkv[0:1, :], SM1[0:1, 0:512]),
                    inc=True, waits=[ev])
    rows3 = [(SM2, 0, 0), (BIG3, 0, 512), (BIG3, 512, 1024)]
    evs3 = []
    for (pt, off, dcol) in rows3:
        ev = None
        for k in range(16):
            ev = t_mm(pt[0:1, off:off + 512], onesc[:, :],
                      wdq_sb[:, k * 1536 + dcol: k * 1536 + dcol + 512],
                      k == 0, k == 15, inc=(k == 15),
                      waits=[ev_wdq] if k == 0 else ())
        evs3.append(ev)

    def cp_wsdq(eng):
        eng.tensor_copy(wsdq[0:1, 0:512], SM2[0:1, 0:512])
        eng.tensor_copy(wsdq[0:1, 512:1024], BIG3[0:1, 0:512])
        return eng.tensor_copy(wsdq[0:1, 1024:1536], BIG3[0:1, 512:1024])
    ev_wsdq = v_op(cp_wsdq, inc=True, waits=evs3)

    pslots = [(BIG1, 0), (BIG1, 512), (BIG2, 0), (BIG2, 512)]
    pslot_ev = [None] * 4
    psi = [0]

    def next_slot():
        i = psi[0] % 4
        psi[0] += 1
        return i

    prev_fin = [ev_wsdq]
    prev_rb = [ev_wsdq, ev_wsdq]
    prev_mm2 = [None]
    prev_atmp_rd = [None]

    ARB = [arb0, arb1]
    NEGM = [arow_b_0, arow_b_1]

    def chunk_stats(par, xoff, xev):
        sq_ev = None
        mm2 = prev_mm2[0]
        for k in range(16):
            sq_ev = s_op(lambda eng, kk=k: eng.activation(
                asq[:, 0:512], xbf[:, xoff + kk * 512: xoff + (kk + 1) * 512],
                AF.Square), inc=True, waits=[xev, mm2])
            t_mm(SM1[0:1, 0:512], onesc[:, :],
                 xbf[:, xoff + k * 512: xoff + (k + 1) * 512], k == 0, k == 15,
                 waits=[xev, prev_fin[0]] if k == 0 else ())
            mm2 = t_mm(SM2[0:1, 0:512], onesc[:, :], asq[:, 0:512],
                       k == 0, k == 15, inc=True, waits=[sq_ev])
        prev_mm2[0] = mm2

        def fin(eng):
            m = arow_f[0:1, 0:512]
            v2 = arow_f[0:1, 512:1024]
            eng.tensor_scalar(m, SM1[0:1, 0:512], INV2048, None, ALU.mult)
            eng.tensor_scalar(v2, SM2[0:1, 0:512], INV2048, None, ALU.mult)
            eng.tensor_scalar(atmp[0:1, 0:512], m, -1.0, None, ALU.mult)
            eng.tensor_copy(NEGM[par][0:1, 0:512], atmp[0:1, 0:512])
            eng.tensor_tensor(atmp[0:1, 0:512], m, m, ALU.mult)
            eng.tensor_tensor(v2, v2, atmp[0:1, 0:512], ALU.subtract)
            eng.tensor_scalar(v2, v2, LN_EPS, None, ALU.add)
            return eng.reciprocal(atmp[0:1, 0:512], v2)
        e_f = v_op(fin, inc=True, waits=[mm2, prev_atmp_rd[0]])
        prev_fin[0] = e_f
        e_r = s_op(lambda eng: eng.activation(arow_f[0:1, 0:512],
                                              atmp[0:1, 0:512], AF.Sqrt),
                   inc=True, waits=[e_f])
        ebc = bcast(arow_f[0:1, 0:512], 512, BIG3, atmp, arow_b2,
                    arow_b2a, waits=[e_r, prev_rb[par], prev_atmp_rd[0]],
                    psum_off=par * 512)
        e_rb = v_op(lambda eng, p=par: eng.tensor_copy(ARB[p][:, :],
                                                       BIG3[:, p * 512:(p + 1) * 512]),
                    inc=True, waits=[ebc])
        prev_rb[par] = e_rb
        return e_rb

    a_done = []
    a_last_s = [None]
    last_q_mm = [None]
    qw_prev = [None, None]

    def emit_fold_linear(par, xoff, n_ot, w_sb, w_tcols, ws_t, bias_col,
                         out_cb, e_rb):
        evs = []
        last_mm = [None]
        for ot in range(n_ot):
            si = next_slot()
            pt, off = pslots[si]
            t_mm(pt[:, off:off + 512], ws_t[0:1, ot * 128:(ot + 1) * 128],
                 NEGM[par][0:1, 0:512], True, False,
                 waits=[pslot_ev[si], e_rb if ot == 0 else None])
            for k in range(16):
                t_mm(pt[:, off:off + 512],
                     w_sb[:, k * w_tcols + ot * 128: k * w_tcols + (ot + 1) * 128],
                     xbf[:, xoff + k * 512: xoff + (k + 1) * 512],
                     False, k == 15, inc=(k == 15))
            mev = ("t", cnt["t"])
            last_mm[0] = mev
            e_v = v_op(lambda eng, p=pt, o=off, pr=par: eng.tensor_tensor(
                atmp[:, :], p[:, o:o + 512], ARB[pr][:, :], ALU.mult),
                inc=True, waits=[mev, prev_atmp_rd[0]])
            e_s = s_op(lambda eng, oo=ot, dst=out_cb(ot), bc=bias_col + ot:
                       eng.activation(dst, atmp[:, :], AF.Identity,
                                      bias=bias_sb[:, bc:bc + 1]),
                       inc=True, waits=[e_v])
            prev_atmp_rd[0] = e_s
            pslot_ev[si] = e_s
            evs.append(e_s)
        return evs, last_mm[0]

    consumed = [None] * 10
    xevs = [None] * 10

    def chunk_dma(i):
        par = i % 2
        w = []
        if i >= 2:
            w = [consumed[i - 2]]
        if i < 8:
            src = bass.AP(xT, i * 512, [[S, 128], [128 * S, 16], [1, 512]])
        else:
            src = bass.AP(xoT, (i - 8) * 512, [[T, 128], [128 * T, 16], [1, 512]])
        return g_dma(bass.AP(xbf, par * 8192, [[16384, 128], [512, 16], [1, 512]]),
                     src, waits=w)

    xevs[0] = chunk_dma(0)
    xevs[1] = chunk_dma(1)
    for c in range(8):
        par = c % 2
        xoff = par * 8192
        e_rb = chunk_stats(par, xoff, xevs[c])
        evs, lmm = emit_fold_linear(
            par, xoff, 4, wdkv_sb, 512, wsdkv, 0,
            lambda ot, cc=c: ckv[:, ot * 4096 + cc * 512: ot * 4096 + (cc + 1) * 512],
            e_rb)
        consumed[c] = lmm
        prev_rb[par] = evs[-1]
        a_done.extend(evs)
        a_last_s[0] = evs[-1]
        if c + 2 < 10:
            xevs[c + 2] = chunk_dma(c + 2)

    for oc in range(2):
        i = 8 + oc
        par = i % 2
        xoff = par * 8192
        e_rb = chunk_stats(par, xoff, xevs[i])
        cq_evs = []
        lmm = None
        for ot in range(12):
            si = next_slot()
            pt, off = pslots[si]
            t_mm(pt[:, off:off + 512], wsdq[0:1, ot * 128:(ot + 1) * 128],
                 NEGM[par][0:1, 0:512], True, False,
                 waits=[pslot_ev[si], e_rb if ot == 0 else None])
            for k in range(16):
                t_mm(pt[:, off:off + 512],
                     wdq_sb[:, k * 1536 + ot * 128: k * 1536 + (ot + 1) * 128],
                     xbf[:, xoff + k * 512: xoff + (k + 1) * 512],
                     False, k == 15, inc=(k == 15))
            mev = ("t", cnt["t"])
            lmm = mev
            e_v = v_op(lambda eng, p=pt, o=off, pr=par: eng.tensor_tensor(
                atmp[:, :], p[:, o:o + 512], ARB[pr][:, :], ALU.mult),
                inc=True, waits=[mev, prev_atmp_rd[0]])
            e_s = s_op(lambda eng, oo=ot: eng.activation(
                cqT[:, oo * 512:(oo + 1) * 512], atmp[:, :], AF.Identity,
                bias=bias_sb[:, 20 + oo:21 + oo]), inc=True,
                waits=[e_v, last_q_mm[0] if ot == 0 else None])
            prev_atmp_rd[0] = e_s
            pslot_ev[si] = e_s
            cq_evs.append(e_s)
        consumed[i] = lmm
        prev_rb[par] = cq_evs[-1]
        for ot in range(16):
            wev = wload(wuq_sb, 2 * 12 * 128, wuq, DM, 0, 12, ot * 128, 128,
                        (ot % 2) * 12 * 128, waits=[qw_prev[ot % 2]])
            si = next_slot()
            pt, off = pslots[si]
            for k in range(12):
                t_mm(pt[:, off:off + 512],
                     wuq_sb[:, (ot % 2) * 1536 + k * 128:(ot % 2) * 1536 + (k + 1) * 128],
                     cqT[:, k * 512:(k + 1) * 512], k == 0, k == 11,
                     inc=(k == 11),
                     waits=[wev, pslot_ev[si], cq_evs[k]] if k == 0 else [cq_evs[k]])
            mev = ("t", cnt["t"])
            qw_prev[ot % 2] = mev
            e_s = s_op(lambda eng, p=pt, o=off, oo=ot, occ=oc: eng.activation(
                qT[:, oo * 1024 + occ * 512: oo * 1024 + occ * 512 + 512],
                p[:, o:o + 512], AF.Identity, bias=bias_sb[:, 32 + oo:33 + oo]),
                inc=True, waits=[mev])
            pslot_ev[si] = e_s
            a_last_s[0] = e_s
        last_q_mm[0] = qw_prev[15 % 2]
    a_last = a_last_s[0]
    tap("ckv", ckv, 16384, a_last)
    tap("qT", qT, 16384, a_last)

    ev_wuk = wload(wuk_sb, 4 * 2048, wuk, DM, 0, 4, 0, DM, 0, waits=[a_last])
    ev_wuv = wload(wuv_sb, 4 * 2048, wuv, DM, 0, 4, 0, DM, 0)

    pre_jobs = [(0, w1, ch) for ch in range(64)] + [(1, w2, ch) for ch in range(64)]
    pre_ptr = [0]
    pre_vev = [a_last, a_last]

    def emit_prestream(n):
        for _ in range(n):
            if pre_ptr[0] >= len(pre_jobs):
                return
            wi, wt, ch = pre_jobs[pre_ptr[0]]
            slot = pre_ptr[0] % 2
            pre_ptr[0] += 1
            dev = y_load(bass.AP(pre_f, slot * 2048, [[4096, 128], [1, 2048]]),
                         bass.AP(wt, ch * 128 * 2048, [[2048, 128], [1, 2048]]),
                         waits=[pre_vev[slot]])

            col = wi * 64 + ch
            pre_vev[slot] = v_op(
                lambda eng, sl=slot, c=col: eng.tensor_reduce(
                    pre_s[:, c:c + 1], pre_f[:, sl * 2048:(sl + 1) * 2048],
                    AX.X, ALU.add, apply_absolute_value=True),
                inc=True, waits=[dev])

    def emit_kgen(h, slot, first_waits):
        out = []
        for cc in range(8):
            pt, off = (BIG2, 0) if cc % 2 == 0 else (BIG2, 512)
            for j in range(4):
                w = []
                if cc < 2 and j == 0:
                    w = list(first_waits) + [ev_wuk]
                elif j == 0 and cc >= 2:
                    w = [out[cc - 2]]
                t_mm(pt[:, off:off + 512],
                     wuk_sb[:, j * 2048 + h * 128: j * 2048 + (h + 1) * 128],
                     ckv[:, j * 4096 + cc * 512: j * 4096 + (cc + 1) * 512],
                     j == 0, j == 3, inc=(j == 3), waits=w)
            mev = ("t", cnt["t"])
            out.append(v_op(lambda eng, p=pt, o=off, c2=cc, sl=slot, hh=h:
                            eng.tensor_scalar(
                                ktb[:, sl * 4096 + c2 * 512: sl * 4096 + (c2 + 1) * 512],
                                p[:, o:o + 512], bias_sb[:, 4 + hh:5 + hh],
                                None, ALU.add),
                            inc=True, waits=[mev]))
        return out

    def emit_vgen(h, slot, first_waits):
        out = []
        for tt in range(32):
            pt, off = (BIG3, 0) if tt % 2 == 0 else (BIG3, 512)
            w = (list(first_waits) + [ev_wuv, ev_buvr]) if tt < 2 else [out[tt - 2]]
            t_mm(pt[:, off:off + 128], onesr[0:1, :],
                 buvr[0:1, h * 128:(h + 1) * 128], True, False, waits=w)
            for j in range(4):
                t_mm(pt[:, off:off + 128],
                     ckv[:, j * 4096 + tt * 128: j * 4096 + (tt + 1) * 128],
                     wuv_sb[:, j * 2048 + h * 128: j * 2048 + (h + 1) * 128],
                     False, j == 3, inc=(j == 3))
            mev = ("t", cnt["t"])
            out.append(v_op(lambda eng, p=pt, o=off, t2=tt, sl=slot:
                            eng.tensor_copy(
                                vtb[:, sl * 4096 + t2 * 128: sl * 4096 + (t2 + 1) * 128],
                                p[:, o:o + 128]),
                            inc=True, waits=[mev]))
        return out

    head_kv = [(emit_kgen(0, 0, [a_last]), emit_vgen(0, 0, [a_last]))]
    attn_ev = []
    prev_inv = [prev_fin[0]]
    for h in range(16):
        k_evs, v_evs = head_kv[h]
        sl = h % 2
        exp_a = [None] * 32
        exp_b = [None] * 32
        pv_evs = [None] * 32
        qk_evs = [None] * 32

        def emit_pvden(j):
            first = j == 0
            pa = [exp_a[j]]
            if first:
                pa.append(attn_ev[-1] if attn_ev else a_last)
            t_mm(BIG3[:, 0:512],
                 vtb[:, sl * 4096 + j * 128: sl * 4096 + (j + 1) * 128],
                 prb[:, (j % 2) * 1024:(j % 2) * 1024 + 512],
                 first, j == 31, waits=pa)
            t_mm(BIG3[:, 512:1024],
                 vtb[:, sl * 4096 + j * 128: sl * 4096 + (j + 1) * 128],
                 prb[:, (j % 2) * 1024 + 512:(j % 2 + 1) * 1024],
                 first, j == 31, waits=[exp_b[j]])
            t_mm(SM1[0:1, 0:512], onesc[:, :],
                 prb[:, (j % 2) * 1024:(j % 2) * 1024 + 512],
                 first, j == 31, waits=[prev_inv[0]] if first else ())
            pv_evs[j] = t_mm(SM2[0:1, 0:512], onesc[:, :],
                             prb[:, (j % 2) * 1024 + 512:(j % 2 + 1) * 1024],
                             first, j == 31, inc=True)

        for kt in range(32):
            spt = BIG1 if kt % 2 == 0 else BIG2
            if kt < 2:
                w0 = [k_evs[-1], v_evs[-1]]
                if attn_ev:
                    w0.append(attn_ev[-1])
            else:
                w0 = [exp_b[kt - 2]]
            t_mm(spt[:, 0:512],
                 ktb[:, sl * 4096 + kt * 128: sl * 4096 + (kt + 1) * 128],
                 qT[:, h * 1024: h * 1024 + 512], True, True, waits=w0)
            qk_evs[kt] = t_mm(spt[:, 512:1024],
                              ktb[:, sl * 4096 + kt * 128: sl * 4096 + (kt + 1) * 128],
                              qT[:, h * 1024 + 512:(h + 1) * 1024], True, True,
                              inc=True)
            if kt >= 1:
                emit_pvden(kt - 1)
            exp_a[kt] = s_op(
                lambda eng, p=spt, k2=kt: eng.activation(
                    prb[:, (k2 % 2) * 1024:(k2 % 2) * 1024 + 512],
                    p[:, 0:512], AF.Exp, scale=EXP_SCALE),
                inc=True,
                waits=[qk_evs[kt], pv_evs[kt - 2] if kt >= 2 else None])
            exp_b[kt] = s_op(
                lambda eng, p=spt, k2=kt: eng.activation(
                    prb[:, (k2 % 2) * 1024 + 512:(k2 % 2 + 1) * 1024],
                    p[:, 512:1024], AF.Exp, scale=EXP_SCALE),
                inc=True)
        emit_pvden(31)
        exp_evs = exp_b

        def inv_prep(eng):
            eng.reciprocal(brow_f[0:1, 0:512], SM1[0:1, 0:512])
            return eng.reciprocal(brow_f[0:1, 512:1024],
                                              SM2[0:1, 0:512])
        e_inv = v_op(inv_prep, inc=True, waits=[pv_evs[31]])
        prev_inv[0] = e_inv
        ebc = bcast(brow_f[0:1, 0:1024], 1024, BIG1, brow_f,
                    brow_h, brow_l, waits=[e_inv, exp_evs[30]])
        e_cp = v_op(lambda eng: eng.tensor_copy(invb[:, :], BIG1[:, 0:1024]),
                    inc=True, waits=[ebc])
        e_at = v_op(lambda eng, hh=h: eng.tensor_tensor(
            attn[:, hh * 1024:(hh + 1) * 1024], BIG3[:, 0:1024], invb[:, :],
            ALU.mult), inc=True, waits=[e_cp])
        attn_ev.append(e_at)
        if h < 15:
            nk = emit_kgen(h + 1, (h + 1) % 2, [exp_evs[31]])
            nv = emit_vgen(h + 1, (h + 1) % 2, [e_at])
            head_kv.append((nk, nv))
        emit_prestream(8)
    emit_prestream(len(pre_jobs))

    tap("attn", attn, 16384, attn_ev[15])

    e_pv = s_op(lambda eng: eng.copy(fence_c[0:1, 0:1], fence_c[0:1, 0:1]),
                inc=True, waits=[pre_vev[0], pre_vev[1]])
    e_red = v_op(lambda eng: (eng.tensor_reduce(pre_acc[:, 0:1], pre_s[:, 0:64],
                                                AX.X, ALU.add),
                              eng.tensor_reduce(pre_acc[:, 1:2], pre_s[:, 64:128],
                                                AX.X, ALU.add))[-1],
                 inc=True, waits=[e_pv, attn_ev[15]])
    e_h1 = v_op(lambda eng: (eng.tensor_copy(prehl_b[:, 0:1], pre_acc[:, 0:1]),
                             eng.tensor_copy(prehl_b[:, 1:2], pre_acc[:, 1:2]))[-1],
                inc=True, waits=[vfence(e_red)])
    e_fh1 = vfence(e_h1)
    e_l1 = s_op(lambda eng: (eng.activation(pre_r[:, 0:1], prehl_b[:, 0:1],
                                            AF.Identity, scale=-1.0,
                                            bias=pre_acc[:, 0:1]),
                             eng.activation(pre_r[:, 1:2], prehl_b[:, 1:2],
                                            AF.Identity, scale=-1.0,
                                            bias=pre_acc[:, 1:2]))[-1],
                inc=True, waits=[e_fh1])
    e_l1b = v_op(lambda eng: (eng.tensor_copy(prehl_b[:, 2:3], pre_r[:, 0:1]),
                              eng.tensor_copy(prehl_b[:, 3:4], pre_r[:, 1:2]))[-1],
                 inc=True, waits=[sfence(e_l1)])
    e_sum = None
    for c in range(2):
        t_mm(SM1[0:1, c:c + 1], onesc[:, :], prehl_b[:, c:c + 1], True, False,
             waits=[e_l1b, prev_inv[0]] if c == 0 else ())
        e_sum = t_mm(SM1[0:1, c:c + 1], onesc[:, :], prehl_b[:, 2 + c:3 + c],
                     False, True, inc=True)
    e_rp = v_op(lambda eng: (eng.tensor_scalar(rowp_f[0:1, 0:1], SM1[0:1, 0:1],
                                               1.0 / 16777216.0, None, ALU.mult),
                             eng.tensor_scalar(rowp_f[0:1, 1:2], SM1[0:1, 1:2],
                                               1.0 / 16777216.0, None, ALU.mult))[-1],
                inc=True, waits=[e_sum])
    e_bh = s_op(lambda eng: (eng.copy(brow_h[0:1, 0:1], rowp_f[0:1, 0:1]),
                             eng.copy(brow_h[0:1, 1:2], rowp_f[0:1, 1:2]))[-1],
                inc=True, waits=[vfence(e_rp)])
    e_bl = v_op(lambda eng: (eng.tensor_tensor(rowp_f[0:1, 2:3], rowp_f[0:1, 0:1],
                                               brow_h[0:1, 0:1], ALU.subtract),
                             eng.tensor_tensor(rowp_f[0:1, 3:4], rowp_f[0:1, 1:2],
                                               brow_h[0:1, 1:2], ALU.subtract))[-1],
                inc=True, waits=[sfence(e_bh)])
    e_blc = s_op(lambda eng: (eng.copy(brow_l[0:1, 0:1], rowp_f[0:1, 2:3]),
                              eng.copy(brow_l[0:1, 1:2], rowp_f[0:1, 3:4]))[-1],
                 inc=True, waits=[vfence(e_bl)])
    e_s12m = None
    for c in range(2):
        t_mm(SM2[:, c:c + 1], onesr[0:1, :], brow_h[0:1, c:c + 1], True, False,
             waits=[e_blc] if c == 0 else ())
        e_s12m = t_mm(SM2[:, c:c + 1], onesr[0:1, :], brow_l[0:1, c:c + 1],
                      False, True, inc=True)

    def cp_s12(eng):
        eng.tensor_copy(s12[:, 0:1], SM2[:, 0:1])
        return eng.tensor_copy(s12[:, 1:2], SM2[:, 1:2])
    e_s12 = v_op(cp_s12, inc=True, waits=[e_s12m])
    if "dbg" in tap_names:
        def cp_dbg(eng):
            eng.memset(dbgB[:, :], 0.0)
            for c in range(4):
                eng.tensor_copy(dbgB[:, c:c + 1], prehl_b[:, c:c + 1])
            eng.tensor_copy(dbgB[0:1, 4:5], SM1[0:1, 0:1])
            eng.tensor_copy(dbgB[0:1, 5:6], SM1[0:1, 1:2])
            eng.tensor_copy(dbgB[0:1, 6:7], rowp_f[0:1, 0:1])
            eng.tensor_copy(dbgB[0:1, 7:8], rowp_f[0:1, 1:2])
            eng.tensor_copy(dbgB[0:1, 8:9], brow_h[0:1, 0:1])
            eng.tensor_copy(dbgB[0:1, 9:10], brow_h[0:1, 1:2])
            eng.tensor_copy(dbgB[0:1, 10:11], brow_l[0:1, 0:1])
            eng.tensor_copy(dbgB[0:1, 11:12], brow_l[0:1, 1:2])
            eng.tensor_copy(dbgB[:, 12:13], SM2[:, 0:1])
            return eng.tensor_copy(dbgB[:, 13:14], SM2[:, 1:2])
        e_dbg = v_op(cp_dbg, inc=True, waits=[e_s12])
        y_dma(bass.AP(tap_t["dbg"], 0, [[16, 128], [1, 16]]), dbgB[:, :],
              waits=[e_dbg])

    op_evs = []
    stat_mm = [None]
    wo_prev = [None, None]
    for ot in range(16):
        wev = wload(wo_sb, 2 * 16 * 128, wo, DM, 0, 16, ot * 128, 128,
                    (ot % 2) * 2048,
                    waits=[wo_prev[ot % 2]] if ot >= 2 else
                    [pre_vev[0], pre_vev[1], attn_ev[15], e_s12])
        for half in range(2):
            si = next_slot()
            pt, off = pslots[si]
            for k in range(16):
                w = [wev, pslot_ev[si]] if k == 0 else ()
                t_mm(pt[:, off:off + 512],
                     wo_sb[:, (ot % 2) * 2048 + k * 128:(ot % 2) * 2048 + (k + 1) * 128],
                     attn[:, k * 1024 + half * 512: k * 1024 + half * 512 + 512],
                     k == 0, k == 15, inc=(k == 15), waits=w)
            mev = ("t", cnt["t"])
            if half == 1:
                wo_prev[ot % 2] = mev
            e_s = s_op(lambda eng, p=pt, o=off, oo=ot, hf=half: eng.activation(
                aoT[:, oo * 1024 + hf * 512: oo * 1024 + hf * 512 + 512],
                p[:, o:o + 512], AF.Identity, bias=bias_sb[:, 48 + oo:49 + oo]),
                inc=True, waits=[mev])
            pslot_ev[si] = e_s
            op_evs.append(e_s)
        e_sq = s_op(lambda eng, oo=ot: eng.activation(
            sqtmp[:, 0:1024], aoT[:, oo * 1024:(oo + 1) * 1024], AF.Square),
            inc=True, waits=[op_evs[-1], stat_mm[0]])
        t_mm(SM1[0:1, 0:512], onesc[:, :], aoT[:, ot * 1024: ot * 1024 + 512],
             ot == 0, ot == 15,
             waits=[op_evs[-1], e_s12] if ot == 0 else [op_evs[-1]])
        t_mm(SM2[0:1, 0:512], onesc[:, :],
             aoT[:, ot * 1024 + 512:(ot + 1) * 1024], ot == 0, ot == 15)
        t_mm(BIG3[0:1, 0:512], onesc[:, :], sqtmp[:, 0:512], ot == 0, ot == 15,
             waits=[e_sq, attn_ev[15]] if ot == 0 else [e_sq])
        stat_mm[0] = t_mm(BIG3[0:1, 512:1024], onesc[:, :], sqtmp[:, 512:1024],
                          ot == 0, ot == 15, inc=True)

    def ln_fin(n_inv, eps, r_out, mr_out, waits):
        def fn(eng):
            m = mr_out
            eng.tensor_scalar(m[0:1, 0:512], SM1[0:1, 0:512], n_inv, None, ALU.mult)
            eng.tensor_scalar(m[0:1, 512:1024], SM2[0:1, 0:512], n_inv, None, ALU.mult)
            e2 = lrow_f2[0:1, 0:1024]
            eng.tensor_copy(e2[0:1, 0:512], BIG3[0:1, 0:512])
            eng.tensor_copy(e2[0:1, 512:1024], BIG3[0:1, 512:1024])
            eng.tensor_scalar(e2, e2, n_inv, None, ALU.mult)
            eng.tensor_tensor(r_out, m, m, ALU.mult)
            eng.tensor_tensor(e2, e2, r_out, ALU.subtract)
            eng.tensor_scalar(e2, e2, eps, None, ALU.add)
            return eng.reciprocal(r_out, e2)
        e1 = v_op(fn, inc=True, waits=waits)
        e2 = s_op(lambda eng: eng.activation(r_out, r_out, AF.Sqrt),
                  inc=True, waits=[e1])
        e3 = v_op(lambda eng: eng.tensor_tensor(mr_out, mr_out, r_out, ALU.mult),
                  inc=True, waits=[e2])
        return e3

    r1r = lrow_f[0:1, 0:1024]
    mr1r = ltmp1[0:1, 0:1024]
    e_ln1b = ln_fin(INV2048, LN_EPS, r1r, mr1r, [stat_mm[0]])
    ebc = bcast(r1r, 1024, BIG3, lrow_f2, lrow_h, lrow_l,
                waits=[e_ln1b, stat_mm[0]])
    e_rb1 = v_op(lambda eng: eng.tensor_copy(bcA[:, :], BIG3[:, 0:1024]),
                 inc=True, waits=[ebc])
    ebc = bcast(mr1r, 1024, BIG3, lrow_f2, lrow_h, lrow_l, waits=[e_rb1])
    e_mrb1 = v_op(lambda eng: eng.tensor_copy(bcB[:, :], BIG3[:, 0:1024]),
                  inc=True, waits=[ebc])

    t1_evs = []
    sp_evs = []
    for ft in range(16):
        dev = y_load(bass.AP(xob, (ft % 2) * 1024, [[2048, 128], [1, 1024]]),
                     bass.AP(xoT, ft * 128 * T, [[T, 128], [1, 1024]]),
                     waits=[e_mrb1] if ft < 2 else [t1_evs[ft - 2]])

        def mk(eng, f=ft):
            eng.tensor_tensor(ltmp2[:, :], aoT[:, f * 1024:(f + 1) * 1024],
                              bcA[:, :], ALU.mult)
            eng.tensor_tensor(ltmp2[:, :], ltmp2[:, :], bcB[:, :], ALU.subtract)
            eng.tensor_tensor(t1f[:, f * 1024:(f + 1) * 1024], ltmp2[:, :],
                              xob[:, (f % 2) * 1024:(f % 2 + 1) * 1024], ALU.add)
            return eng.tensor_copy(t1b[:, f * 1024:(f + 1) * 1024],
                                   t1f[:, f * 1024:(f + 1) * 1024])
        e_t1 = v_op(mk, inc=True, waits=[dev, e_mrb1])
        t1_evs.append(e_t1)
        sp_evs.append(y_load(bass.AP(t1sp, ft * 128 * T, [[T, 128], [1, 1024]]),
                             bass.AP(t1f, ft * 1024, [[16384, 128], [1, 1024]]),
                             waits=[e_t1]))

    tap("t1b", t1b, 16384, t1_evs[15])

    stat2 = [None]
    for ft in range(16):
        e_sq = s_op(lambda eng, f=ft: eng.activation(
            sqtmp[:, 0:1024], t1b[:, f * 1024:(f + 1) * 1024], AF.Square),
            inc=True, waits=[t1_evs[ft], stat2[0]])
        t_mm(SM1[0:1, 0:512], onesc[:, :], t1b[:, ft * 1024: ft * 1024 + 512],
             ft == 0, ft == 15,
             waits=[t1_evs[ft], e_mrb1] if ft == 0 else [t1_evs[ft]])
        t_mm(SM2[0:1, 0:512], onesc[:, :],
             t1b[:, ft * 1024 + 512:(ft + 1) * 1024], ft == 0, ft == 15)
        t_mm(BIG3[0:1, 0:512], onesc[:, :], sqtmp[:, 0:512], ft == 0, ft == 15,
             waits=[e_sq, e_mrb1] if ft == 0 else [e_sq])
        stat2[0] = t_mm(BIG3[0:1, 512:1024], onesc[:, :], sqtmp[:, 512:1024],
                        ft == 0, ft == 15, inc=True)
    r2r = lrow_f[0:1, 0:1024]
    mr2r = ltmp1[0:1, 0:1024]
    e_ln2 = ln_fin(INV2048, LN_EPS, r2r, mr2r, [stat2[0], e_rb1])
    ebc = bcast(r2r, 1024, BIG3, lrow_f2, lrow_h, lrow_l, waits=[e_ln2])
    e_rb2 = v_op(lambda eng: eng.tensor_copy(bcA[:, :], BIG3[:, 0:1024]),
                 inc=True, waits=[ebc, t1_evs[15]])
    ebc = bcast(mr2r, 1024, BIG3, lrow_f2, lrow_h, lrow_l, waits=[e_rb2])
    e_mrb2 = v_op(lambda eng: eng.tensor_copy(bcB[:, :], BIG3[:, 0:1024]),
                  inc=True, waits=[ebc])

    e_sqm0 = v_op(lambda eng: eng.memset(sqmax[:, :], 0.0), inc=True,
                  waits=[e_mrb2])
    ln2_evs = []
    rms_mm = [None]
    mx_prev = [e_sqm0]
    for ft in range(16):
        def mk(eng, f=ft):
            eng.tensor_tensor(ltmp2[:, :], t1f[:, f * 1024:(f + 1) * 1024],
                              bcA[:, :], ALU.mult)
            eng.tensor_tensor(ltmp2[:, :], ltmp2[:, :], bcB[:, :], ALU.subtract)
            return eng.tensor_copy(aoT[:, f * 1024:(f + 1) * 1024], ltmp2[:, :])
        e_l = v_op(mk, inc=True, waits=[e_mrb2, sp_evs[ft]])
        e_sq = s_op(lambda eng, f=ft: eng.activation(
            sqtmp[:, 0:1024], aoT[:, f * 1024:(f + 1) * 1024], AF.Square),
            inc=True, waits=[e_l, rms_mm[0], mx_prev[0]])
        t_mm(SM1[0:1, 0:512], onesc[:, :], sqtmp[:, 0:512], ft == 0, ft == 15,
             waits=[e_sq, e_mrb2] if ft == 0 else [e_sq])
        rms_mm[0] = t_mm(SM2[0:1, 0:512], onesc[:, :], sqtmp[:, 512:1024],
                         ft == 0, ft == 15, inc=True)
        mx_prev[0] = v_op(lambda eng: eng.tensor_tensor(
            sqmax[:, :], sqmax[:, :], sqtmp[:, 0:1024], ALU.max),
            inc=True, waits=[e_sq, mx_prev[0]])
        ln2_evs.append(e_l)

    def quant_rows(n_inv, rsqn, rms_mm_ev, fold_ev, bcast_wait):
        def r1(eng):
            t = lrow_f2[0:1, 0:1024]
            eng.tensor_scalar(t[0:1, 0:512], SM1[0:1, 0:512], n_inv, None, ALU.mult)
            eng.tensor_scalar(t[0:1, 512:1024], SM2[0:1, 0:512], n_inv, None, ALU.mult)
            eng.tensor_scalar(t, t, RMS_EPS, None, ALU.add)
            return eng.reciprocal(qrow_f[0:1, 0:1024], t)
        e1 = v_op(r1, inc=True, waits=[rms_mm_ev, fold_ev])
        e2 = s_op(lambda eng: eng.activation(qrow_f[0:1, 0:1024],
                                             qrow_f[0:1, 0:1024], AF.Sqrt),
                  inc=True, waits=[e1])
        return e2

    e_q = quant_rows(INV2048, RSQ2048, rms_mm[0],
                     v_op(pfold(sqmax, ltmp2, 1024, ALU.max), inc=True,
                          waits=[mx_prev[0], ln2_evs[15]]), None)
    e_mxr = v_op(lambda eng: eng.tensor_copy(lrow_f[0:1, 0:1024],
                                             sqmax[0:1, 0:1024]),
                 inc=True, waits=[e_q])
    e_mxs = s_op(lambda eng: eng.activation(lrow_f[0:1, 0:1024],
                                            lrow_f[0:1, 0:1024], AF.Sqrt),
                 inc=True, waits=[e_mxr])

    def q_rows3(eng, rsqn=RSQ2048):
        f = qrow_f[0:1, 0:1024]
        mx = lrow_f[0:1, 0:1024]
        eng.tensor_scalar(f, f, rsqn, None, ALU.mult)
        eng.tensor_tensor(mx, mx, f, ALU.mult)
        eng.tensor_scalar(mx, mx, Q_EPS, None, ALU.max)
        eng.reciprocal(lrow_f2[0:1, 0:1024], mx)
        eng.tensor_scalar(lrow_f2[0:1, 0:1024], lrow_f2[0:1, 0:1024], 127.0,
                          None, ALU.mult)
        eng.tensor_tensor(f, f, lrow_f2[0:1, 0:1024], ALU.mult)
        return eng.tensor_scalar(mx, mx, 1.0 / 127.0, None, ALU.mult)
    e_q3 = v_op(q_rows3, inc=True, waits=[e_mxs])
    ebc = bcast(qrow_f[0:1, 0:1024], 1024, BIG3, lrow_f2, lrow_h, lrow_l,
                waits=[e_q3])
    e_A1 = v_op(lambda eng: eng.tensor_copy(bcA[:, :], BIG3[:, 0:1024]),
                inc=True, waits=[ebc])
    ebc = bcast(lrow_f[0:1, 0:1024], 1024, BIG3, lrow_f2, lrow_h, lrow_l,
                waits=[e_A1])
    e_B1 = v_op(lambda eng: eng.tensor_copy(bcB[:, :], BIG3[:, 0:1024]),
                inc=True, waits=[ebc])

    xq_evs = []
    LT = [ltmp2, ltmp1]
    pendq = None
    for ft in range(16):
        tb = LT[ft % 2]
        e_m = v_op(lambda eng, f=ft, t=tb: eng.tensor_tensor(
            t[:, :], aoT[:, f * 1024:(f + 1) * 1024], bcA[:, :], ALU.mult),
            inc=True, waits=[e_B1, xq_evs[ft - 2] if ft >= 2 else None])
        e_r = s_op(lambda eng, t=tb: (eng.activation(t[:, :], t[:, :],
                                                     AF.Identity, bias=MAGIC),
                                      eng.activation(t[:, :], t[:, :],
                                                     AF.Identity, bias=-MAGIC))[-1],
                   inc=True, waits=[e_m])
        if pendq is not None:
            pf, ptb, per = pendq
            xq_evs.append(v_op(lambda eng, f=pf, t=ptb: eng.tensor_tensor(
                aoT[:, f * 1024:(f + 1) * 1024], t[:, :], bcB[:, :], ALU.mult),
                inc=True, waits=[per]))
        pendq = (ft, tb, e_r)
    pf, ptb, per = pendq
    xq_evs.append(v_op(lambda eng, f=pf, t=ptb: eng.tensor_tensor(
        aoT[:, f * 1024:(f + 1) * 1024], t[:, :], bcB[:, :], ALU.mult),
        inc=True, waits=[per]))

    tap("xq", aoT, 16384, xq_evs[15])
    if "s12" in tap_names:
        y_dma(bass.AP(tap_t["s12"], 0, [[2, 128], [1, 2]]), s12[:, 0:2],
              waits=[e_s12])
    if "preacc" in tap_names:
        y_dma(bass.AP(tap_t["preacc"], 0, [[2, 128], [1, 2]]), pre_acc[:, :],
              waits=[e_h1])

    sg1_prev = [None, None]
    h2_evs = []
    rms2_mm = [None]
    mx2_prev = [None]
    e_sqm2 = v_op(lambda eng: eng.memset(csqmax[:, :], 0.0), inc=True,
                  waits=[e_B1])
    mx2_prev[0] = e_sqm2
    sq_prev = [None]
    for ot in range(64):
        wev = wload(w1db, 2 * 16 * 128, w1, INNER, 0, 16, ot * 128, 128,
                    (ot % 2) * 2048,
                    waits=[sg1_prev[ot % 2]] if ot >= 2 else
                    ([xq_evs[15]] + sp_evs[8:]))
        e_sg = v_op(lambda eng, s2=(ot % 2): (
            eng.tensor_scalar(w1db[:, s2 * 2048:(s2 + 1) * 2048],
                              w1db[:, s2 * 2048:(s2 + 1) * 2048],
                              1e10, 1.0, ALU.mult, ALU.min),
            eng.tensor_scalar(w1db[:, s2 * 2048:(s2 + 1) * 2048],
                              w1db[:, s2 * 2048:(s2 + 1) * 2048],
                              -1.0, None, ALU.max))[-1], inc=True, waits=[wev])
        for half in range(2):
            si = next_slot()
            pt, off = pslots[si]
            for k in range(16):
                w = []
                if k == 0:
                    w = [e_sg, pslot_ev[si]]
                    if ot == 0 and half == 0:
                        w += [e_s12] + xq_evs
                t_mm(pt[:, off:off + 512],
                     w1db[:, (ot % 2) * 2048 + k * 128:(ot % 2) * 2048 + (k + 1) * 128],
                     aoT[:, k * 1024 + half * 512: k * 1024 + half * 512 + 512],
                     k == 0, k == 15, inc=(k == 15), waits=w)
            mev = ("t", cnt["t"])
            if half == 1:
                sg1_prev[ot % 2] = mev
            e_h = s_op(lambda eng, p=pt, o=off, oo=ot, hf=half: eng.activation(
                h2[:, oo * 1024 + hf * 512: oo * 1024 + hf * 512 + 512],
                p[:, o:o + 512], AF.Gelu, bias=bias_sb[:, 64 + oo:65 + oo],
                scale=s12[:, 0:1]), inc=True, waits=[mev])
            pslot_ev[si] = e_h
            h2_evs.append(e_h)
        e_sq = s_op(lambda eng, oo=ot: eng.activation(
            csq[:, 0:1024], h2[:, oo * 1024:(oo + 1) * 1024], AF.Square),
            inc=True, waits=[h2_evs[-1], rms2_mm[0], mx2_prev[0]])
        sq_prev[0] = e_sq
        t_mm(SM1[0:1, 0:512], onesc[:, :], csq[:, 0:512], ot == 0, ot == 63,
             waits=[e_sq, e_B1] if ot == 0 else [e_sq])
        rms2_mm[0] = t_mm(SM2[0:1, 0:512], onesc[:, :], csq[:, 512:1024],
                          ot == 0, ot == 63, inc=True)
        mx2_prev[0] = v_op(lambda eng: eng.tensor_tensor(
            csqmax[:, :], csqmax[:, :], csq[:, 0:1024], ALU.max),
            inc=True, waits=[e_sq, mx2_prev[0]])

    e_fold2 = v_op(pfold(csqmax, ctmp1, 1024, ALU.max), inc=True,
                   waits=[mx2_prev[0]])

    def r2a(eng):
        t = ctmp1[0:1, 0:1024]
        eng.tensor_scalar(t[0:1, 0:512], SM1[0:1, 0:512], 1.0 / 8192.0, None, ALU.mult)
        eng.tensor_scalar(t[0:1, 512:1024], SM2[0:1, 0:512], 1.0 / 8192.0, None, ALU.mult)
        eng.tensor_scalar(t, t, RMS_EPS, None, ALU.add)
        return eng.reciprocal(qrow_f[0:1, 0:1024], t)
    e_ra = v_op(r2a, inc=True, waits=[rms2_mm[0], e_fold2])
    e_rb_ = s_op(lambda eng: eng.activation(qrow_f[0:1, 0:1024],
                                            qrow_f[0:1, 0:1024], AF.Sqrt),
                 inc=True, waits=[e_ra])
    e_mxr2 = v_op(lambda eng: eng.tensor_copy(crow_f[0:1, 0:1024],
                                              csqmax[0:1, 0:1024]),
                  inc=True, waits=[e_rb_])
    e_mxs2 = s_op(lambda eng: eng.activation(crow_f[0:1, 0:1024],
                                             crow_f[0:1, 0:1024], AF.Sqrt),
                  inc=True, waits=[e_mxr2])

    def q2_rows3(eng):
        f = qrow_f[0:1, 0:1024]
        mx = crow_f[0:1, 0:1024]
        eng.tensor_scalar(f, f, RSQ8192, None, ALU.mult)
        eng.tensor_tensor(mx, mx, f, ALU.mult)
        eng.tensor_scalar(mx, mx, Q_EPS, None, ALU.max)
        eng.reciprocal(ctmp1[0:1, 0:1024], mx)
        eng.tensor_scalar(ctmp1[0:1, 0:1024], ctmp1[0:1, 0:1024], 127.0,
                          None, ALU.mult)
        eng.tensor_tensor(f, f, ctmp1[0:1, 0:1024], ALU.mult)
        return eng.tensor_scalar(mx, mx, 1.0 / 127.0, None, ALU.mult)
    e_q23 = v_op(q2_rows3, inc=True, waits=[e_mxs2])
    ebc = bcast(qrow_f[0:1, 0:1024], 1024, BIG3, ctmp1, crow_h, crow_l,
                waits=[e_q23])
    e_A2 = v_op(lambda eng: eng.tensor_copy(bcA[:, :], BIG3[:, 0:1024]),
                inc=True, waits=[ebc])
    ebc = bcast(crow_f[0:1, 0:1024], 1024, BIG3, ctmp1, crow_h, crow_l,
                waits=[e_A2])
    e_B2b = v_op(lambda eng: eng.tensor_copy(bcB[:, :], BIG3[:, 0:1024]),
                 inc=True, waits=[ebc])

    xq2_evs = []
    CT = [ctmp1, ctmp1b]
    pend = None
    for f2 in range(64):
        tb = CT[f2 % 2]
        e_m = v_op(lambda eng, f=f2, t=tb: eng.tensor_tensor(
            t[:, :], h2[:, f * 1024:(f + 1) * 1024], bcA[:, :], ALU.mult),
            inc=True, waits=[e_B2b, h2_evs[2 * f2 + 1],
                             xq2_evs[f2 - 2] if f2 >= 2 else None])
        e_r = s_op(lambda eng, t=tb: (eng.activation(t[:, :], t[:, :],
                                                     AF.Identity, bias=MAGIC),
                                      eng.activation(t[:, :], t[:, :],
                                                     AF.Identity, bias=-MAGIC))[-1],
                   inc=True, waits=[e_m])
        if pend is not None:
            pf, ptb, per = pend
            xq2_evs.append(v_op(lambda eng, f=pf, t=ptb: eng.tensor_tensor(
                h2[:, f * 1024:(f + 1) * 1024], t[:, :], bcB[:, :], ALU.mult),
                inc=True, waits=[per]))
        pend = (f2, tb, e_r)
    pf, ptb, per = pend
    xq2_evs.append(v_op(lambda eng, f=pf, t=ptb: eng.tensor_tensor(
        h2[:, f * 1024:(f + 1) * 1024], t[:, :], bcB[:, :], ALU.mult),
        inc=True, waits=[per]))

    tap("h2q", h2, 65536, xq2_evs[63])

    sg2_prev = [None, None]
    raw_evs = []
    statr = [None]
    sqr_prev = [None]
    for ot2 in range(16):
        siA = next_slot()
        ptA, offA = pslots[siA]
        siB = next_slot()
        ptB, offB = pslots[siB]
        mevA = mevB = None
        for blk in range(4):
            sl = blk % 2
            wev = wload(w2db, 2 * 16 * 128, w2, DM, blk * 16, 16, ot2 * 128,
                        128, sl * 2048,
                        waits=[sg2_prev[sl]] if sg2_prev[sl] else [])
            e_sg = v_op(lambda eng, s2=sl: (
                eng.tensor_scalar(w2db[:, s2 * 2048:(s2 + 1) * 2048],
                                  w2db[:, s2 * 2048:(s2 + 1) * 2048],
                                  1e10, 1.0, ALU.mult, ALU.min),
                eng.tensor_scalar(w2db[:, s2 * 2048:(s2 + 1) * 2048],
                                  w2db[:, s2 * 2048:(s2 + 1) * 2048],
                                  -1.0, None, ALU.max))[-1], inc=True,
                waits=[wev])
            for ki in range(16):
                k = blk * 16 + ki
                wA = []
                wB = []
                if ki == 0:
                    wA = [e_sg]
                    if ot2 == 0:
                        wA.append(xq2_evs[blk * 16 + 15])
                    if k == 0:
                        wA.append(pslot_ev[siA])
                        wB = [pslot_ev[siB]]
                t_mm(ptA[:, offA:offA + 512],
                     w2db[:, sl * 2048 + ki * 128: sl * 2048 + (ki + 1) * 128],
                     h2[:, k * 1024: k * 1024 + 512],
                     k == 0, k == 63, inc=(k == 63), waits=wA)
                if k == 63:
                    mevA = ("t", cnt["t"])
                t_mm(ptB[:, offB:offB + 512],
                     w2db[:, sl * 2048 + ki * 128: sl * 2048 + (ki + 1) * 128],
                     h2[:, k * 1024 + 512:(k + 1) * 1024],
                     k == 0, k == 63, inc=(ki == 15), waits=wB)
                if ki == 15:
                    sg2_prev[sl] = ("t", cnt["t"])
                    if k == 63:
                        mevB = ("t", cnt["t"])
        e_rA = s_op(lambda eng, p=ptA, o=offA, oo=ot2: eng.activation(
            bf2r[:, oo * 1024: oo * 1024 + 512], p[:, o:o + 512], AF.Identity),
            inc=True, waits=[mevA])
        e_rB = s_op(lambda eng, p=ptB, o=offB, oo=ot2: eng.activation(
            bf2r[:, oo * 1024 + 512:(oo + 1) * 1024], p[:, o:o + 512],
            AF.Identity), inc=True, waits=[mevB])
        pslot_ev[siA] = e_rA
        pslot_ev[siB] = e_rB
        raw_evs.append(e_rA)
        raw_evs.append(e_rB)
        e_sq = s_op(lambda eng, oo=ot2: eng.activation(
            csq[:, 0:1024], bf2r[:, oo * 1024:(oo + 1) * 1024], AF.Square),
            inc=True, waits=[e_rB, statr[0], sqr_prev[0]])
        sqr_prev[0] = e_sq
        t_mm(SM1[0:1, 0:512], onesc[:, :], bf2r[:, ot2 * 1024: ot2 * 1024 + 512],
             ot2 == 0, ot2 == 15,
             waits=[e_rA, e_B2b] if ot2 == 0 else [e_rA])
        t_mm(SM2[0:1, 0:512], onesc[:, :],
             bf2r[:, ot2 * 1024 + 512:(ot2 + 1) * 1024], ot2 == 0, ot2 == 15,
             waits=[e_rB])
        t_mm(BIG3[0:1, 0:512], onesc[:, :], csq[:, 0:512], ot2 == 0, ot2 == 15,
             waits=[e_sq, e_B2b] if ot2 == 0 else [e_sq])
        statr[0] = t_mm(BIG3[0:1, 512:1024], onesc[:, :], csq[:, 512:1024],
                        ot2 == 0, ot2 == 15, inc=True)

    tap("bf2r", bf2r, 16384, statr[0])

    e3 = y_dma(fr_sraw[0:1, 0:1024], bass.AP(biasd["b2"], 0, [[1024, 1], [1, 1024]]),
               waits=[statr[0]])
    ec3 = v_op(lambda eng: eng.tensor_copy(b2rb[0:1, 0:1024], fr_sraw[0:1, 0:1024]),
               inc=True, waits=[e3, statr[0]])
    e4 = y_load(fr_sraw[0:1, 0:1024],
                bass.AP(biasd["b2"], 1024, [[1024, 1], [1, 1024]]), waits=[ec3])
    e_b2rb = v_op(lambda eng: eng.tensor_copy(b2rb[0:1, 1024:2048],
                                              fr_sraw[0:1, 0:1024]),
                  inc=True, waits=[e4])


    def cp_rows(eng):
        eng.tensor_copy(fr_sraw[0:1, 0:512], SM1[0:1, 0:512])
        eng.tensor_copy(fr_sraw[0:1, 512:1024], SM2[0:1, 0:512])
        eng.tensor_copy(fr_ssq[0:1, 0:512], BIG3[0:1, 0:512])
        return eng.tensor_copy(fr_ssq[0:1, 512:1024], BIG3[0:1, 512:1024])
    e_cpr = v_op(cp_rows, inc=True, waits=[statr[0], e_b2rb])
    b2r_mm = [None]
    for ot2 in range(16):
        t_mm(SM1[0:1, 0:512], b2cb[:, ot2:ot2 + 1],
             bf2r[:, ot2 * 1024: ot2 * 1024 + 512], ot2 == 0, ot2 == 15,
             waits=[e_cpr, ev_b2cb] if ot2 == 0 else ())
        b2r_mm[0] = t_mm(SM2[0:1, 0:512], b2cb[:, ot2:ot2 + 1],
                         bf2r[:, ot2 * 1024 + 512:(ot2 + 1) * 1024],
                         ot2 == 0, ot2 == 15, inc=(ot2 == 15))

    def cp_b2r(eng):
        eng.tensor_copy(fr_sb2r[0:1, 0:512], SM1[0:1, 0:512])
        return eng.tensor_copy(fr_sb2r[0:1, 512:1024], SM2[0:1, 0:512])
    e_cb2r = v_op(cp_b2r, inc=True, waits=[b2r_mm[0]])

    e_q0 = v_op(lambda eng: eng.tensor_tensor(ctmp2[:, 0:16], bias_sb[:, 128:144],
                                              bias_sb[:, 128:144], ALU.mult),
                inc=True, waits=[ev_bias, e_b2rb, sqr_prev[0], statr[0]])
    e_q1 = v_op(lambda eng: (eng.tensor_reduce(ctmp2[:, 16:17], bias_sb[:, 128:144],
                                               AX.X, ALU.add),
                             eng.tensor_reduce(ctmp2[:, 17:18], ctmp2[:, 0:16],
                                               AX.X, ALU.add))[-1],
                inc=True, waits=[vfence(e_q0)])
    e_q2h = s_op(lambda eng: (eng.copy(csq[:, 0:1], ctmp2[:, 16:17]),
                              eng.copy(csq[:, 1:2], ctmp2[:, 17:18]))[-1],
                 inc=True, waits=[vfence(e_q1)])
    e_q2l = s_op(lambda eng: (eng.activation(ctmp2[:, 18:19], csq[:, 0:1],
                                             AF.Identity, scale=-1.0,
                                             bias=ctmp2[:, 16:17]),
                              eng.activation(ctmp2[:, 19:20], csq[:, 1:2],
                                             AF.Identity, scale=-1.0,
                                             bias=ctmp2[:, 17:18]))[-1],
                 inc=True, waits=[e_q2h])
    e_q2lb = v_op(lambda eng: (eng.tensor_copy(csq[:, 2:3], ctmp2[:, 18:19]),
                               eng.tensor_copy(csq[:, 3:4], ctmp2[:, 19:20]))[-1],
                  inc=True, waits=[sfence(e_q2l)])
    e_b2m = None
    for c in range(2):
        t_mm(SM1[0:1, 16 + c:17 + c], onesc[:, :], csq[:, c:c + 1], True, False,
             waits=[e_q2lb, e_cb2r] if c == 0 else ())
        e_b2m = t_mm(SM1[0:1, 16 + c:17 + c], onesc[:, :], csq[:, 2 + c:3 + c],
                     False, True, inc=True)

    def cp_b2f(eng):
        eng.tensor_copy(ctmp2[0:1, 16:17], SM1[0:1, 16:17])
        return eng.tensor_copy(ctmp2[0:1, 17:18], SM1[0:1, 17:18])
    e_b2f0 = v_op(cp_b2f, inc=True, waits=[e_b2m])
    e_b2f = vfence(e_b2f0)

    e_means = v_op(lambda eng: (eng.tensor_scalar(frow_f[0:1, 0:1],
                                                  ctmp2[0:1, 16:17], INV2048,
                                                  None, ALU.mult),
                                eng.tensor_scalar(frow_f[0:1, 1:2],
                                                  ctmp2[0:1, 17:18], INV2048,
                                                  None, ALU.mult))[-1],
                    inc=True, waits=[e_b2f, sfence(e_b2f)])
    e_meansf = vfence(e_means)

    def f_rows1(eng):
        s2v = s12[0:1, 1:2]
        mp = fr_mp[0:1, 0:1024]
        eng.tensor_scalar(mp, fr_sraw[0:1, 0:1024], INV2048, None, ALU.mult)
        eng.tensor_scalar(mp, mp, s2v, None, ALU.mult)
        eng.tensor_scalar(mp, mp, frow_f[0:1, 0:1], None, ALU.add)
        ey = ftmp1[0:1, 0:1024]
        eng.tensor_scalar(ey, fr_ssq[0:1, 0:1024], INV2048, None, ALU.mult)
        eng.tensor_scalar(ey, ey, s2v, None, ALU.mult)
        eng.tensor_scalar(ey, ey, s2v, None, ALU.mult)
        eng.tensor_scalar(ftmp2[0:1, 0:1024], fr_sb2r[0:1, 0:1024],
                          2.0 * INV2048, None, ALU.mult)
        eng.tensor_scalar(ftmp2[0:1, 0:1024], ftmp2[0:1, 0:1024], s2v, None,
                          ALU.mult)
        eng.tensor_tensor(ey, ey, ftmp2[0:1, 0:1024], ALU.add)
        eng.tensor_scalar(ey, ey, frow_f[0:1, 1:2], None, ALU.add)
        eng.tensor_tensor(ftmp2[0:1, 0:1024], mp, mp, ALU.mult)
        eng.tensor_tensor(ey, ey, ftmp2[0:1, 0:1024], ALU.subtract)
        eng.tensor_scalar(ey, ey, LN_EPS, None, ALU.add)
        return eng.reciprocal(fr_r3[0:1, 0:1024], ey)
    e_f1 = v_op(f_rows1, inc=True, waits=[e_cb2r, sfence(e_meansf), e_s12])
    e_f2 = s_op(lambda eng: eng.activation(fr_r3[0:1, 0:1024],
                                           fr_r3[0:1, 0:1024], AF.Sqrt),
                inc=True, waits=[e_f1])

    def f_rows2(eng):
        s2v = s12[0:1, 1:2]
        r3 = fr_r3[0:1, 0:1024]
        eng.tensor_scalar(fr_c1[0:1, 0:1024], r3, s2v, None, ALU.mult)
        eng.tensor_tensor(ftmp1[0:1, 0:1024], fr_mp[0:1, 0:1024], r3, ALU.mult)
        eng.tensor_scalar(ftmp1[0:1, 0:1024], ftmp1[0:1, 0:1024], -1.0, None,
                          ALU.mult)
        eng.tensor_copy(nb16[0:1, 0:1024], ftmp1[0:1, 0:1024])
        eng.tensor_tensor(ftmp2[0:1, 0:1024], ftmp1[0:1, 0:1024],
                          nb16[0:1, 0:1024], ALU.subtract)
        eng.tensor_copy(nb16[0:1, 1024:2048], ftmp2[0:1, 0:1024])
        eng.tensor_copy(frow_h[0:1, 0:1024], r3)
        eng.tensor_tensor(ftmp2[0:1, 0:1024], r3, frow_h[0:1, 0:1024],
                          ALU.subtract)
        return eng.tensor_copy(frow_l[0:1, 0:1024], ftmp2[0:1, 0:1024])
    e_f3 = v_op(f_rows2, inc=True, waits=[e_f2])
    ebc = bcast(fr_c1[0:1, 0:1024], 1024, BIG3, frow_f, frow_h2, frow_l2,
                waits=[e_f3, statr[0]])
    e_C1 = v_op(lambda eng: eng.tensor_copy(bcC[:, :], BIG3[:, 0:1024]),
                inc=True, waits=[ebc])

    out_evs = []
    for ot2 in range(16):
        trd = y_load(bass.AP(t1rd, (ot2 % 2) * 1024, [[2048, 128], [1, 1024]]),
                     bass.AP(t1sp, ot2 * 128 * T, [[T, 128], [1, 1024]]),
                     waits=[e_f3] if ot2 < 2 else [out_evs[ot2 - 2]])
        si = next_slot()
        pt, off = pslots[si]
        for half in range(2):
            o2 = off if False else None
        si2 = next_slot()
        pt2, off2 = pslots[si2]
        dmm = []
        for (p_, o_, half) in ((pt, off, 0), (pt2, off2, 1)):
            t_mm(p_[:, o_:o_ + 512], b2rb[0:1, ot2 * 128:(ot2 + 1) * 128],
                 frow_h[0:1, half * 512: half * 512 + 512], True, False,
                 waits=[pslot_ev[si if half == 0 else si2], e_f3,
                        e_C1 if ot2 == 0 else None])
            t_mm(p_[:, o_:o_ + 512], b2rb[0:1, ot2 * 128:(ot2 + 1) * 128],
                 frow_l[0:1, half * 512: half * 512 + 512], False, False)
            t_mm(p_[:, o_:o_ + 512], onesr[0:1, :],
                 nb16[0:1, half * 512: half * 512 + 512], False, False)
            dmm.append(t_mm(p_[:, o_:o_ + 512], onesr[0:1, :],
                            nb16[0:1, 1024 + half * 512: 1024 + half * 512 + 512],
                            False, True, inc=True))

        def mk(eng, oo=ot2, pa=pt, oa=off, pb=pt2, ob=off2):
            eng.tensor_tensor(ftmp1[:, :], bf2r[:, oo * 1024:(oo + 1) * 1024],
                              bcC[:, :], ALU.mult)
            eng.tensor_tensor(ftmp1[:, 0:512], ftmp1[:, 0:512],
                              pa[:, oa:oa + 512], ALU.add)
            eng.tensor_tensor(ftmp1[:, 512:1024], ftmp1[:, 512:1024],
                              pb[:, ob:ob + 512], ALU.add)
            return eng.tensor_tensor(
                coutb[:, (oo % 2) * 1024:(oo % 2 + 1) * 1024], ftmp1[:, :],
                t1rd[:, (oo % 2) * 1024:(oo % 2 + 1) * 1024], ALU.add)
        e_o = v_op(mk, inc=True, waits=[trd, dmm[0], dmm[1], e_C1,
                                        raw_evs[2 * ot2 + 1]])
        pslot_ev[si] = e_o
        pslot_ev[si2] = e_o
        out_evs.append(e_o)
        y_dma(bass.AP(outT, ot2 * 128 * T, [[T, 128], [1, 1024]]),
              bass.AP(coutb, (ot2 % 2) * 1024, [[2048, 128], [1, 1024]]),
              waits=[e_o])

    wait("Y", ("ds", cnt["ds"]))

    with nc.Block() as block:
        @block.tensor
        def _(eng):
            for k, fn in ops:
                if k == "T":
                    fn(eng)

        @block.vector
        def _(eng):
            for k, fn in ops:
                if k == "V":
                    fn(eng)

        @block.scalar
        def _(eng):
            for k, fn in ops:
                if k == "S":
                    fn(eng)

        @block.gpsimd
        def _(eng):
            for k, fn in ops:
                if k == "G":
                    fn(eng)

        @block.sync
        def _(eng):
            for k, fn in ops:
                if k == "Y":
                    fn(eng)

    return nc


def kernel(**inputs):
    ip = {k: np.asarray(v) for k, v in inputs.items()}
    x = ip["L0_attn_hidden_states"]

    def tr(a):
        return np.ascontiguousarray(a.T)

    w = {
        "wdkv": tr(ip["L0_attn_W_DKV_w"]), "wuk": tr(ip["L0_attn_W_UK_w"]),
        "wuv": tr(ip["L0_attn_W_UV_w"]), "wdq": tr(ip["L0_attn_W_DQ_w"]),
        "wuq": tr(ip["L0_attn_W_UQ_w"]), "wo": tr(ip["L0_attn_W_O_w"]),
        "w1": tr(ip["L0_ffn_w1"]), "w2": tr(ip["L0_ffn_w2"]),
        "bdkv": np.ascontiguousarray(ip["L0_attn_W_DKV_b"]),
        "buk": np.ascontiguousarray(ip["L0_attn_W_UK_b"]),
        "buv": np.ascontiguousarray(ip["L0_attn_W_UV_b"]),
        "bdq": np.ascontiguousarray(ip["L0_attn_W_DQ_b"]),
        "buq": np.ascontiguousarray(ip["L0_attn_W_UQ_b"]),
        "bo": np.ascontiguousarray(ip["L0_attn_W_O_b"]),
        "b1": np.ascontiguousarray(ip["L0_ffn_b1"]),
        "b2": np.ascontiguousarray(ip["L0_ffn_b2"]),
    }
    xTb = [np.ascontiguousarray(x[b].T) for b in range(2)]
    nc = build_program()
    in_maps = []
    for c in range(8):
        b, qt = c // 4, c % 4
        im = dict(w)
        im["xT"] = xTb[b]
        im["xoT"] = np.ascontiguousarray(xTb[b][:, qt * 1024:(qt + 1) * 1024])
        in_maps.append(im)
    res = run_bass_kernel_spmd(nc, in_maps, core_ids=list(range(8)))
    out = np.empty((2, 4096, 2048), dtype=np.float32)
    for c in range(8):
        b, qt = c // 4, c % 4
        out[b, qt * 1024:(qt + 1) * 1024, :] = res.results[c]["outT"].T
    return out


# version 07
# speedup vs baseline: 1.2261x; 1.0045x over previous
import numpy as np
import concourse.bass as bass
import concourse.mybir as mybir
from concourse.bass_utils import run_bass_kernel_spmd

F32 = mybir.dt.float32
BF16 = mybir.dt.bfloat16
AF = mybir.ActivationFunctionType
ALU = mybir.AluOpType
AX = mybir.AxisListType

MAGIC = float(np.float32(1.5 * 2 ** 23))
EXP_SCALE = 1.0 / float(np.sqrt(128.0))
LN_EPS = 1e-5
RMS_EPS = 1e-6
Q_EPS = 1e-5
INV2048 = 1.0 / 2048.0
RSQ2048 = float(2048.0 ** -0.5)
RSQ8192 = float(8192.0 ** -0.5)

DM, KVC, QC, INNER = 2048, 512, 1536, 8192
S, T = 4096, 1024


class Regions:
    BASE = 17408

    def __init__(self, nc, limit=229376):
        self.nc = nc
        self.limit = limit
        self.items = []

    def add(self, name, off, shape, dt, phases):
        off = off + self.BASE
        esz = 2 if dt == BF16 else 4
        size = int(np.prod(shape[1:])) * esz
        assert off + size <= self.limit, f"{name}: {off}+{size} > {self.limit}"
        for (n2, o2, s2, p2) in self.items:
            if phases & p2 and not (off + size <= o2 or o2 + s2 <= off):
                raise AssertionError(f"overlap {name}[{off}:{off+size}] vs "
                                     f"{n2}[{o2}:{o2+s2}] phases {phases & p2}")
        self.items.append((name, off, size, phases))
        return self.nc.alloc_sbuf_tensor_at(name, shape, dt, offset=off)


def build_program(taps=()):
    nc = bass.Bass("TRN2", target_bir_lowering=False, detect_race_conditions=False)

    xT = nc.dram_tensor("xT", [DM, S], F32, kind="ExternalInput")
    xoT = nc.dram_tensor("xoT", [DM, T], F32, kind="ExternalInput")
    wdkv = nc.dram_tensor("wdkv", [DM, KVC], F32, kind="ExternalInput")
    wuk = nc.dram_tensor("wuk", [KVC, DM], F32, kind="ExternalInput")
    wuv = nc.dram_tensor("wuv", [KVC, DM], F32, kind="ExternalInput")
    wdq = nc.dram_tensor("wdq", [DM, QC], F32, kind="ExternalInput")
    wuq = nc.dram_tensor("wuq", [QC, DM], F32, kind="ExternalInput")
    wo = nc.dram_tensor("wo", [DM, DM], F32, kind="ExternalInput")
    w1 = nc.dram_tensor("w1", [DM, INNER], F32, kind="ExternalInput")
    w2 = nc.dram_tensor("w2", [INNER, DM], F32, kind="ExternalInput")
    biasd = {}
    for nm, n in (("bdkv", KVC), ("buk", DM), ("buv", DM), ("bdq", QC),
                  ("buq", DM), ("bo", DM), ("b1", INNER), ("b2", DM)):
        biasd[nm] = nc.dram_tensor(nm, [n], F32, kind="ExternalInput")
    outT = nc.dram_tensor("outT", [DM, T], F32, kind="ExternalOutput")
    t1sp = nc.dram_tensor("t1sp", [DM, T], F32, kind="Internal")
    tap_t = {nm: nc.dram_tensor("tap_" + nm, sh, F32 if nm in ("s12", "preacc", "dbg") else BF16,
                                 kind="ExternalOutput")
             for nm, sh in taps}

    R = Regions(nc)
    AB = frozenset("AB")
    xbf = R.add("xbf", 0, [128, 2 * 16 * 512], BF16, frozenset("A"))
    arow_f = R.add("arow_f", 32768, [1, 1024], F32, frozenset("A"))
    arb0 = R.add("arb0", 36864, [128, 512], F32, frozenset("A"))
    arb1 = R.add("arb1", 38912, [128, 512], F32, frozenset("A"))
    wdq_sb = R.add("wdq_sb", 40960, [128, 16 * 1536], BF16, frozenset("A"))
    wuq_sb = R.add("wuq_sb", 90112, [128, 2 * 12 * 128], BF16, frozenset("A"))
    wdkv_sb = R.add("wdkv_sb", 96256, [128, 16 * 512], BF16, frozenset("A"))
    cqT = R.add("cqT", 112640, [128, 12 * 512], BF16, frozenset("A"))
    asq = R.add("asq", 124928, [128, 512], BF16, frozenset("A"))
    atmp = R.add("atmp", 125952, [128, 512], F32, frozenset("A"))
    wsdq = R.add("wsdq", 128000, [1, 1536], BF16, frozenset("A"))
    arow_b_0 = R.add("arow_b_0", 196608 + 4608, [1, 512], BF16, frozenset("A"))
    arow_b_1 = R.add("arow_b_1", 196608 + 5632, [1, 512], BF16, frozenset("A"))
    arow_b2 = R.add("arow_b2", 196608 + 6656, [1, 512], BF16, frozenset("A"))
    arow_b2a = R.add("arow_b2a", 196608 + 7680, [1, 512], BF16, frozenset("A"))

    ktb = R.add("ktb", 0, [128, 2 * 8192], BF16, frozenset("B"))
    vtb = R.add("vtb", 32768, [128, 2 * 8192], BF16, frozenset("B"))
    prb = R.add("prb", 65536, [128, 2 * 1024], BF16, frozenset("B"))
    invb = R.add("invb", 69632, [128, 1024], F32, frozenset("B"))
    wukg = R.add("wukg", 73728, [128, 4 * 256], BF16, frozenset("B"))
    wuvg = R.add("wuvg", 75776, [128, 4 * 256], BF16, frozenset("B"))
    pre_f = R.add("pre_f", 77824, [128, 2 * 2048], F32, frozenset("B"))
    brow_f = R.add("brow_f", 94208, [1, 1024], F32, frozenset("B"))
    brow_h = R.add("brow_h", 196608 + 4608, [1, 1024], BF16, frozenset("B"))
    brow_l = R.add("brow_l", 196608 + 6656, [1, 1024], BF16, frozenset("B"))
    attn = R.add("attn", 98304, [128, 16 * 1024], BF16, frozenset("BP"))

    wo_sb = R.add("wo_sb", 73728, [128, 2 * 16 * 128], BF16, frozenset("P"))
    aoT = R.add("aoT", 163840, [128, 16 * 1024], BF16, frozenset("PL"))

    t1f = R.add("t1f", 0, [128, 16 * 1024], F32, frozenset("L"))
    lrow_f = R.add("lrow_f", 65536, [1, 1024], F32, frozenset("PL"))
    lrow_h = R.add("lrow_h", 69632, [1, 1024], BF16, frozenset("PL"))
    lrow_l = R.add("lrow_l", 71680, [1, 1024], BF16, frozenset("PL"))
    lrow_f2 = R.add("lrow_f2", 81920, [1, 1024], F32, frozenset("PL"))
    sqtmp = R.add("sqtmp", 86016, [128, 1024], BF16, frozenset("PL"))
    sqmax = R.add("sqmax", 88064, [128, 1024], F32, frozenset("L"))
    xob = R.add("xob", 92160, [128, 2 * 1024], F32, frozenset("L"))
    ltmp1 = R.add("ltmp1", 100352, [128, 1024], F32, frozenset("L"))
    ltmp2 = R.add("ltmp2", 104448, [128, 1024], F32, frozenset("L"))

    h2 = R.add("h2", 0, [128, 64 * 1024], BF16, frozenset(["C1", "C2"]))
    coutb = R.add("coutb", 0, [128, 2 * 1024], F32, frozenset("F"))
    t1rd = R.add("t1rd", 8192, [128, 2 * 1024], F32, frozenset("F"))
    frow_f = R.add("frow_f", 16384, [1, 1024], F32, frozenset("F"))
    frow_h = R.add("frow_h", 20480, [1, 1024], BF16, frozenset("F"))
    frow_l = R.add("frow_l", 22528, [1, 1024], BF16, frozenset("F"))
    fr_sraw = R.add("fr_sraw", 24576, [1, 1024], F32, frozenset(["F"]))
    fr_ssq = R.add("fr_ssq", 28672, [1, 1024], F32, frozenset(["F"]))
    fr_sb2r = R.add("fr_sb2r", 32768, [1, 1024], F32, frozenset(["F"]))
    fr_mp = R.add("fr_mp", 36864, [1, 1024], F32, frozenset("F"))
    fr_r3 = R.add("fr_r3", 40960, [1, 1024], F32, frozenset("F"))
    fr_c1 = R.add("fr_c1", 45056, [1, 1024], F32, frozenset("F"))
    nb16 = R.add("nb16", 49152, [1, 2048], BF16, frozenset("F"))
    ftmp1 = R.add("ftmp1", 57344, [128, 1024], F32, frozenset("F"))
    ftmp2 = R.add("ftmp2", 61440, [128, 1024], F32, frozenset("F"))
    frow_h2 = R.add("frow_h2", 65536, [1, 1024], BF16, frozenset("F"))
    frow_l2 = R.add("frow_l2", 67584, [1, 1024], BF16, frozenset("F"))

    R1 = 131072
    ckv = R.add("ckv", R1, [128, 4 * 4096], BF16, frozenset("AB"))
    t1b = R.add("t1b", R1, [128, 16 * 1024], BF16, frozenset("L"))
    w1db = R.add("w1db", R1, [128, 2 * 16 * 128], BF16, frozenset("C1"))
    crow_h = R.add("crow_h", R1, [1, 1024], BF16, frozenset(["Cq"]))
    crow_l = R.add("crow_l", R1 + 2048, [1, 1024], BF16, frozenset(["Cq"]))
    b2rb = R.add("b2rb", R1, [1, 2048], BF16, frozenset(["F"]))
    w2db = R.add("w2db", R1 + 8192, [128, 2 * 16 * 128], BF16, frozenset("C2"))
    csq = R.add("csq", R1 + 16384, [128, 1024], BF16, frozenset(["C1", "C2"]))
    csqmax = R.add("csqmax", R1 + 18432, [128, 1024], F32, frozenset(["C1", "Cq"]))
    ctmp1 = R.add("ctmp1", R1 + 22528, [128, 1024], F32,
                  frozenset(["C1", "Cq", "C2", "F"]))
    ctmp2 = R.add("ctmp2", R1 + 26624, [128, 512], F32, frozenset(["C2", "F"]))
    crow_f = R.add("crow_f", R1 + 28672, [1, 1024], F32, frozenset(["Cq"]))

    bf2r = R.add("bf2r", 163840, [128, 16 * 1024], BF16, frozenset(["C2", "F"]))
    qT = R.add("qT", 163840, [128, 16 * 1024], BF16, AB)

    bcA = R.add("bcA", 196608, [128, 1024], F32, frozenset(["L", "Cq", "C1"]))
    bcB = R.add("bcB", 200704, [128, 1024], F32, frozenset(["L", "Cq", "C1"]))
    bcC = R.add("bcC", 204800, [128, 1024], F32, frozenset(["F"]))
    ctmp1b = R.add("ctmp1b", R1 + 4096, [128, 1024], F32, frozenset(["Cq"]))
    dbgB = R.add("dbgB", 204800 - 4096, [128, 16], F32, frozenset(["Bdbg"]))
    qrow_f = R.add("qrow_f", 204800, [1, 1024], F32, frozenset(["L", "Cq"]))

    P0 = 208896
    onesr = R.add("onesr", P0, [1, 128], BF16, frozenset("ABPLC1C2F"))
    onesc = R.add("onesc", P0 + 256, [128, 1], BF16, frozenset("ABPLC1C2F"))
    wsdkv = R.add("wsdkv", P0 + 320, [1, 512], BF16, frozenset("A"))
    bias_sb = R.add("bias_sb", P0 + 1344, [128, 148], F32, frozenset("ABPLC1C2F"))
    s12 = R.add("s12", P0 + 1952, [128, 4], F32, frozenset("BPLC1C2F"))
    b2cb = R.add("b2cb", P0 + 1984, [128, 16], BF16, frozenset("ABPLC1C2F"))
    rowp_f = R.add("rowp_f", P0 + 2016, [1, 128], F32, frozenset("ABPLC1C2F"))
    pre_acc = R.add("pre_acc", P0 + 2528, [128, 2], F32, frozenset("AB"))
    fence_c = R.add("fence_c", P0 + 2624, [1, 8], F32,
                    frozenset("ABPLC1C2F"))
    pre_r = R.add("pre_r", P0 + 2560, [128, 2], F32, frozenset("AB"))
    prehl_b = R.add("prehl_b", P0 + 2592, [128, 4], BF16, frozenset("AB"))
    buvr = R.add("buvr", 196608, [1, 2048], BF16, frozenset("AB"))
    pre_s = R.add("pre_s", 196608 + 4096, [128, 128], F32, frozenset("AB"))


    BIG1 = nc.alloc_psum_tensor("BIG1", [128, 1024], F32)
    BIG2 = nc.alloc_psum_tensor("BIG2", [128, 1024], F32)
    BIG3 = nc.alloc_psum_tensor("BIG3", [128, 1024], F32)
    SM1 = nc.alloc_psum_tensor("SM1", [128, 512], F32)
    SM2 = nc.alloc_psum_tensor("SM2", [128, 512], F32)

    nc.const_aps.register(F32, MAGIC, s12[:, 2:3])
    nc.const_aps.register(F32, -MAGIC, s12[:, 3:4])

    st = nc.alloc_semaphore("st")
    sv = nc.alloc_semaphore("sv")
    sc = nc.alloc_semaphore("sc")
    ds = nc.alloc_semaphore("ds")
    NDG = 8
    dg = [nc.alloc_semaphore("dg%d" % i) for i in range(NDG)]
    NYS = 8
    ys = [nc.alloc_semaphore("ys%d" % i) for i in range(NYS)]

    ops = []
    EMIT = {k: (lambda fn, kk=k: ops.append((kk, fn))) for k in "TVSGY"}
    cnt = {"t": 0, "v": 0, "s": 0, "ds": 0}
    gslot, guse = [0], [0] * NDG
    yslot, yuse = [0], [0] * NYS
    last_wait = {}
    SEMH = {"t": st, "v": sv, "s": sc}

    def wait(ek, ev):
        if ev is None:
            return
        key, val = ev
        if val is None or val <= 0:
            return
        k = (ek, key)
        if last_wait.get(k, 0) >= val:
            return
        last_wait[k] = val
        if key.startswith("dg"):
            semh, tgt = dg[int(key[2:])], val * 16
        elif key.startswith("ys"):
            semh, tgt = ys[int(key[2:])], val * 16
        elif key == "ds":
            semh, tgt = ds, val * 16
        else:
            semh, tgt = SEMH[key], val
        EMIT[ek](lambda eng, s=semh, v=tgt: eng.wait_ge(s, v))

    def g_dma(dst_ap, src_ap, waits=()):
        slot = gslot[0]
        gslot[0] = (slot + 1) % NDG
        prev = guse[slot]
        guse[slot] = prev + 1
        for w in waits:
            wait("G", w)
        if prev > 0:
            wait("G", ("dg%d" % slot, prev))
        EMIT["G"](lambda eng, d=dst_ap, s=src_ap, sm=dg[slot]:
                  eng.dma_start(d, s).then_inc(sm, 16))
        return ("dg%d" % slot, guse[slot])

    def y_dma(dst_ap, src_ap, waits=(), slow=False):
        for w in waits:
            wait("Y", w)
        cnt["ds"] += 1
        EMIT["Y"](lambda eng, d=dst_ap, s=src_ap, sl=slow:
                  eng.dma_start(d, s, allow_slow_non_contiguous=sl).then_inc(ds, 16))
        return ("ds", cnt["ds"])

    def y_load(dst_ap, src_ap, waits=(), slow=False):
        slot = yslot[0]
        yslot[0] = (slot + 1) % NYS
        prev = yuse[slot]
        yuse[slot] = prev + 1
        for w in waits:
            wait("Y", w)
        if prev > 0:
            wait("Y", ("ys%d" % slot, prev))
        EMIT["Y"](lambda eng, d=dst_ap, s=src_ap, sm=ys[slot], sl=slow:
                  eng.dma_start(d, s, allow_slow_non_contiguous=sl).then_inc(sm, 16))
        return ("ys%d" % slot, yuse[slot])

    tap_names = {nm for nm, sh in taps}

    def tap(nm, src_t, ncols, ev):
        if nm in tap_names:
            y_dma(bass.AP(tap_t[nm], 0, [[ncols, 128], [1, ncols]]),
                  bass.AP(src_t, 0, [[ncols, 128], [1, ncols]]), waits=[ev])

    def t_mm(out_ap, lhs_ap, rhs_ap, start, stop, inc=False, waits=()):
        for w in waits:
            wait("T", w)
        if inc:
            cnt["t"] += 1
        EMIT["T"](lambda eng, o=out_ap, l=lhs_ap, r=rhs_ap, a=start, b=stop, i=inc:
                  (eng.matmul(o, l, r, start=a, stop=b).then_inc(st, 1) if i
                   else eng.matmul(o, l, r, start=a, stop=b)))
        return ("t", cnt["t"]) if inc else None

    def v_op(builder, inc=False, waits=()):
        for w in waits:
            wait("V", w)
        if inc:
            cnt["v"] += 1
        EMIT["V"](lambda eng, b=builder, i=inc:
                  (b(eng).then_inc(sv, 1) if i else b(eng)))
        return ("v", cnt["v"]) if inc else None

    def s_op(builder, inc=False, waits=()):
        for w in waits:
            wait("S", w)
        if inc:
            cnt["s"] += 1
        EMIT["S"](lambda eng, b=builder, i=inc:
                  (b(eng).then_inc(sc, 1) if i else b(eng)))
        return ("s", cnt["s"]) if inc else None

    def vfence(ev):
        return s_op(lambda eng: eng.copy(fence_c[0:1, 0:1], fence_c[0:1, 0:1]),
                    inc=True, waits=[ev])

    def sfence(ev):
        return v_op(lambda eng: eng.tensor_copy(fence_c[0:1, 1:2],
                                                fence_c[0:1, 1:2]),
                    inc=True, waits=[ev])

    def bcast(src_row_ap, n, psum_t, scr_f_t, scr_h_t, scr_l_t, waits=(),
              psum_off=0):
        def prep(eng):
            eng.tensor_copy(scr_h_t[0:1, 0:n], src_row_ap)
            eng.tensor_tensor(scr_f_t[0:1, 0:n], src_row_ap, scr_h_t[0:1, 0:n],
                              ALU.subtract)
            return eng.tensor_copy(scr_l_t[0:1, 0:n], scr_f_t[0:1, 0:n])
        pe = v_op(prep, inc=True, waits=waits)
        ev = None
        for c0 in range(0, n, 512):
            w = min(512, n - c0)
            t_mm(psum_t[:, psum_off + c0:psum_off + c0 + w], onesr[0:1, :],
                 scr_h_t[0:1, c0:c0 + w], True, False, waits=[pe])
            ev = t_mm(psum_t[:, psum_off + c0:psum_off + c0 + w], onesr[0:1, :],
                      scr_l_t[0:1, c0:c0 + w], False, True, inc=True)
        return ev

    def pfold(buf, scratch, n, op):
        def fn(eng, b=buf, s=scratch, nn=n, o=op):
            ins = None
            for p in (64, 32):
                eng.tensor_copy(s[0:p, 0:nn], b[p:2 * p, 0:nn])
                ins = eng.tensor_tensor(b[0:p, 0:nn], b[0:p, 0:nn], s[0:p, 0:nn], o)
            p = 16
            while p >= 1:
                mask = [(i + p) % 32 for i in range(32)]
                eng.stream_shuffle(s[0:32, 0:nn], b[0:32, 0:nn], mask)
                ins = eng.tensor_tensor(b[0:32, 0:nn], b[0:32, 0:nn], s[0:32, 0:nn], o)
                p //= 2
            return ins
        return fn

    def init_ones(eng):
        eng.memset(onesr[0:1, :], 1.0)
        eng.memset(fence_c[0:1, 0:8], 0.0)
        eng.memset(s12[:, 2:3], MAGIC)
        eng.memset(s12[:, 3:4], -MAGIC)
        return eng.memset(onesc[:, :], 1.0)
    ev_ones = v_op(init_ones, inc=True)
    ev_bias = None
    for nm, col, k in (("bdkv", 0, 4), ("buk", 4, 16), ("bdq", 20, 12),
                       ("buq", 32, 16), ("bo", 48, 16), ("b1", 64, 64),
                       ("b2", 128, 16)):
        ev_bias = y_load(bass.AP(bias_sb, col, [[148, 128], [1, k]]),
                         bass.AP(biasd[nm], 0, [[1, 128], [128, k]]), slow=True)
    e1 = y_load(arow_f[0:1, 0:1024], bass.AP(biasd["buv"], 0, [[1024, 1], [1, 1024]]))
    ec1 = v_op(lambda eng: eng.tensor_copy(buvr[0:1, 0:1024], arow_f[0:1, 0:1024]),
               inc=True, waits=[e1])
    e2 = y_load(arow_f[0:1, 0:1024],
                bass.AP(biasd["buv"], 1024, [[1024, 1], [1, 1024]]), waits=[ec1])
    ev_buvr = v_op(lambda eng: eng.tensor_copy(buvr[0:1, 1024:2048],
                                               arow_f[0:1, 0:1024]),
                   inc=True, waits=[e2])
    ev_b2cb = v_op(lambda eng: eng.tensor_copy(b2cb[:, :], bias_sb[:, 128:144]),
                   inc=True, waits=[ev_bias])
    ev_pre0 = v_op(lambda eng: eng.memset(fence_c[0:1, 4:6], 0.0), inc=True)

    def wload(dst_sb, dst_row, src_t, src_cols, itile0, ntiles, col0, ncols,
              dst_col0, waits=()):
        dst = bass.AP(dst_sb, dst_col0, [[dst_row, 128], [ncols, ntiles], [1, ncols]])
        src = bass.AP(src_t, itile0 * 128 * src_cols + col0,
                      [[src_cols, 128], [128 * src_cols, ntiles], [1, ncols]])
        return g_dma(dst, src, waits=waits)

    ev_wdkv = wload(wdkv_sb, 8192, wdkv, KVC, 0, 16, 0, KVC, 0)
    ev_wdq = None
    for g in range(4):
        ev_wdq = wload(wdq_sb, 16 * 1536, wdq, QC, 4 * g, 4, 0, QC, 4 * g * 1536)

    ev = None
    for k in range(16):
        ev = t_mm(SM1[0:1, 0:512], onesc[:, :], wdkv_sb[:, k * 512:(k + 1) * 512],
                  k == 0, k == 15, inc=(k == 15),
                  waits=[ev_wdkv, ev_ones] if k == 0 else ())
    ev_wsdkv = v_op(lambda eng: eng.tensor_copy(wsdkv[0:1, :], SM1[0:1, 0:512]),
                    inc=True, waits=[ev])
    rows3 = [(SM2, 0, 0), (BIG3, 0, 512), (BIG3, 512, 1024)]
    evs3 = []
    for (pt, off, dcol) in rows3:
        ev = None
        for k in range(16):
            ev = t_mm(pt[0:1, off:off + 512], onesc[:, :],
                      wdq_sb[:, k * 1536 + dcol: k * 1536 + dcol + 512],
                      k == 0, k == 15, inc=(k == 15),
                      waits=[ev_wdq] if k == 0 else ())
        evs3.append(ev)

    def cp_wsdq(eng):
        eng.tensor_copy(wsdq[0:1, 0:512], SM2[0:1, 0:512])
        eng.tensor_copy(wsdq[0:1, 512:1024], BIG3[0:1, 0:512])
        return eng.tensor_copy(wsdq[0:1, 1024:1536], BIG3[0:1, 512:1024])
    ev_wsdq = v_op(cp_wsdq, inc=True, waits=evs3)

    pslots = [(BIG1, 0), (BIG1, 512), (BIG2, 0), (BIG2, 512)]
    pslot_ev = [None] * 4
    psi = [0]

    def next_slot():
        i = psi[0] % 4
        psi[0] += 1
        return i

    prev_fin = [ev_wsdq]
    prev_rb = [ev_wsdq, ev_wsdq]
    prev_mm2 = [None]
    prev_atmp_rd = [None]

    ARB = [arb0, arb1]
    NEGM = [arow_b_0, arow_b_1]

    def chunk_stats(par, xoff, xev):
        sq_ev = None
        mm2 = prev_mm2[0]
        for k in range(16):
            sq_ev = s_op(lambda eng, kk=k: eng.activation(
                asq[:, 0:512], xbf[:, xoff + kk * 512: xoff + (kk + 1) * 512],
                AF.Square), inc=True, waits=[xev, mm2])
            t_mm(SM1[0:1, 0:512], onesc[:, :],
                 xbf[:, xoff + k * 512: xoff + (k + 1) * 512], k == 0, k == 15,
                 waits=[xev, prev_fin[0]] if k == 0 else ())
            mm2 = t_mm(SM2[0:1, 0:512], onesc[:, :], asq[:, 0:512],
                       k == 0, k == 15, inc=True, waits=[sq_ev])
        prev_mm2[0] = mm2

        def fin(eng):
            m = arow_f[0:1, 0:512]
            v2 = arow_f[0:1, 512:1024]
            eng.tensor_scalar(m, SM1[0:1, 0:512], INV2048, None, ALU.mult)
            eng.tensor_scalar(v2, SM2[0:1, 0:512], INV2048, None, ALU.mult)
            eng.tensor_scalar(atmp[0:1, 0:512], m, -1.0, None, ALU.mult)
            eng.tensor_copy(NEGM[par][0:1, 0:512], atmp[0:1, 0:512])
            eng.tensor_tensor(atmp[0:1, 0:512], m, m, ALU.mult)
            eng.tensor_tensor(v2, v2, atmp[0:1, 0:512], ALU.subtract)
            eng.tensor_scalar(v2, v2, LN_EPS, None, ALU.add)
            return eng.reciprocal(atmp[0:1, 0:512], v2)
        e_f = v_op(fin, inc=True, waits=[mm2, prev_atmp_rd[0]])
        prev_fin[0] = e_f
        e_r = s_op(lambda eng: eng.activation(arow_f[0:1, 0:512],
                                              atmp[0:1, 0:512], AF.Sqrt),
                   inc=True, waits=[e_f])
        ebc = bcast(arow_f[0:1, 0:512], 512, BIG3, atmp, arow_b2,
                    arow_b2a, waits=[e_r, prev_rb[par], prev_atmp_rd[0]],
                    psum_off=par * 512)
        e_rb = v_op(lambda eng, p=par: eng.tensor_copy(ARB[p][:, :],
                                                       BIG3[:, p * 512:(p + 1) * 512]),
                    inc=True, waits=[ebc])
        prev_rb[par] = e_rb
        return e_rb

    a_done = []
    a_last_s = [None]
    last_q_mm = [None]
    qw_prev = [None, None]

    def emit_fold_linear(par, xoff, n_ot, w_sb, w_tcols, ws_t, bias_col,
                         out_cb, e_rb):
        evs = []
        last_mm = [None]
        for ot in range(n_ot):
            si = next_slot()
            pt, off = pslots[si]
            t_mm(pt[:, off:off + 512], ws_t[0:1, ot * 128:(ot + 1) * 128],
                 NEGM[par][0:1, 0:512], True, False,
                 waits=[pslot_ev[si], e_rb if ot == 0 else None])
            for k in range(16):
                t_mm(pt[:, off:off + 512],
                     w_sb[:, k * w_tcols + ot * 128: k * w_tcols + (ot + 1) * 128],
                     xbf[:, xoff + k * 512: xoff + (k + 1) * 512],
                     False, k == 15, inc=(k == 15))
            mev = ("t", cnt["t"])
            last_mm[0] = mev
            e_v = v_op(lambda eng, p=pt, o=off, pr=par: eng.tensor_tensor(
                atmp[:, :], p[:, o:o + 512], ARB[pr][:, :], ALU.mult),
                inc=True, waits=[mev, prev_atmp_rd[0]])
            e_s = s_op(lambda eng, oo=ot, dst=out_cb(ot), bc=bias_col + ot:
                       eng.activation(dst, atmp[:, :], AF.Identity,
                                      bias=bias_sb[:, bc:bc + 1]),
                       inc=True, waits=[e_v])
            prev_atmp_rd[0] = e_s
            pslot_ev[si] = e_s
            evs.append(e_s)
        return evs, last_mm[0]

    consumed = [None] * 10
    xevs = [None] * 10

    def chunk_dma(i):
        par = i % 2
        w = []
        if i >= 2:
            w = [consumed[i - 2]]
        if i < 8:
            src = bass.AP(xT, i * 512, [[S, 128], [128 * S, 16], [1, 512]])
        else:
            src = bass.AP(xoT, (i - 8) * 512, [[T, 128], [128 * T, 16], [1, 512]])
        return g_dma(bass.AP(xbf, par * 8192, [[16384, 128], [512, 16], [1, 512]]),
                     src, waits=w)

    xevs[0] = chunk_dma(0)
    xevs[1] = chunk_dma(1)
    for c in range(8):
        par = c % 2
        xoff = par * 8192
        e_rb = chunk_stats(par, xoff, xevs[c])
        evs, lmm = emit_fold_linear(
            par, xoff, 4, wdkv_sb, 512, wsdkv, 0,
            lambda ot, cc=c: ckv[:, ot * 4096 + cc * 512: ot * 4096 + (cc + 1) * 512],
            e_rb)
        consumed[c] = lmm
        prev_rb[par] = evs[-1]
        a_done.extend(evs)
        a_last_s[0] = evs[-1]
        if c + 2 < 10:
            xevs[c + 2] = chunk_dma(c + 2)

    for oc in range(2):
        i = 8 + oc
        par = i % 2
        xoff = par * 8192
        e_rb = chunk_stats(par, xoff, xevs[i])
        cq_evs = []
        lmm = None
        for ot in range(12):
            si = next_slot()
            pt, off = pslots[si]
            t_mm(pt[:, off:off + 512], wsdq[0:1, ot * 128:(ot + 1) * 128],
                 NEGM[par][0:1, 0:512], True, False,
                 waits=[pslot_ev[si], e_rb if ot == 0 else None])
            for k in range(16):
                t_mm(pt[:, off:off + 512],
                     wdq_sb[:, k * 1536 + ot * 128: k * 1536 + (ot + 1) * 128],
                     xbf[:, xoff + k * 512: xoff + (k + 1) * 512],
                     False, k == 15, inc=(k == 15))
            mev = ("t", cnt["t"])
            lmm = mev
            e_v = v_op(lambda eng, p=pt, o=off, pr=par: eng.tensor_tensor(
                atmp[:, :], p[:, o:o + 512], ARB[pr][:, :], ALU.mult),
                inc=True, waits=[mev, prev_atmp_rd[0]])
            e_s = s_op(lambda eng, oo=ot: eng.activation(
                cqT[:, oo * 512:(oo + 1) * 512], atmp[:, :], AF.Identity,
                bias=bias_sb[:, 20 + oo:21 + oo]), inc=True,
                waits=[e_v, last_q_mm[0] if ot == 0 else None])
            prev_atmp_rd[0] = e_s
            pslot_ev[si] = e_s
            cq_evs.append(e_s)
        consumed[i] = lmm
        prev_rb[par] = cq_evs[-1]
        for ot in range(16):
            wev = wload(wuq_sb, 2 * 12 * 128, wuq, DM, 0, 12, ot * 128, 128,
                        (ot % 2) * 12 * 128, waits=[qw_prev[ot % 2]])
            si = next_slot()
            pt, off = pslots[si]
            for k in range(12):
                t_mm(pt[:, off:off + 512],
                     wuq_sb[:, (ot % 2) * 1536 + k * 128:(ot % 2) * 1536 + (k + 1) * 128],
                     cqT[:, k * 512:(k + 1) * 512], k == 0, k == 11,
                     inc=(k == 11),
                     waits=[wev, pslot_ev[si], cq_evs[k]] if k == 0 else [cq_evs[k]])
            mev = ("t", cnt["t"])
            qw_prev[ot % 2] = mev
            e_s = s_op(lambda eng, p=pt, o=off, oo=ot, occ=oc: eng.activation(
                qT[:, oo * 1024 + occ * 512: oo * 1024 + occ * 512 + 512],
                p[:, o:o + 512], AF.Identity, bias=bias_sb[:, 32 + oo:33 + oo]),
                inc=True, waits=[mev])
            pslot_ev[si] = e_s
            a_last_s[0] = e_s
        last_q_mm[0] = qw_prev[15 % 2]
    a_last = a_last_s[0]
    tap("ckv", ckv, 16384, a_last)
    tap("qT", qT, 16384, a_last)

    kv_wprev = [None, None]

    pre_jobs = [(0, w1, ch) for ch in range(64)] + [(1, w2, ch) for ch in range(64)]
    pre_ptr = [0]
    pre_vev = [a_last, a_last]

    def emit_prestream(n):
        for _ in range(n):
            if pre_ptr[0] >= len(pre_jobs):
                return
            wi, wt, ch = pre_jobs[pre_ptr[0]]
            slot = pre_ptr[0] % 2
            pre_ptr[0] += 1
            dev = y_load(bass.AP(pre_f, slot * 2048, [[4096, 128], [1, 2048]]),
                         bass.AP(wt, ch * 128 * 2048, [[2048, 128], [1, 2048]]),
                         waits=[pre_vev[slot]])

            col = wi * 64 + ch
            pre_vev[slot] = v_op(
                lambda eng, sl=slot, c=col: eng.tensor_reduce(
                    pre_s[:, c:c + 1], pre_f[:, sl * 2048:(sl + 1) * 2048],
                    AX.X, ALU.add, apply_absolute_value=True),
                inc=True, waits=[dev])

    def emit_kvgen(g, first_waits):
        sl = g % 2
        wk = wload(wukg, 4 * 256, wuk, DM, 0, 4, g * 256, 256, 0,
                   waits=(list(first_waits) + [kv_wprev[0]]) if kv_wprev[0]
                   else list(first_waits))
        wv = wload(wuvg, 4 * 256, wuv, DM, 0, 4, g * 256, 256, 0,
                   waits=[kv_wprev[1]] if kv_wprev[1] else [])
        kout = []
        for hh in range(2):
            h = 2 * g + hh
            for cc in range(8):
                off = 0 if len(kout) % 2 == 0 else 512
                for j in range(4):
                    if j == 0:
                        w = ([kout[-2]] if len(kout) >= 2 else
                             list(first_waits) + [wk])
                    else:
                        w = []
                    t_mm(BIG2[:, off:off + 512],
                         wukg[:, j * 256 + hh * 128: j * 256 + (hh + 1) * 128],
                         ckv[:, j * 4096 + cc * 512: j * 4096 + (cc + 1) * 512],
                         j == 0, j == 3, inc=(j == 3), waits=w)
                mev = ("t", cnt["t"])
                if hh == 1 and cc == 7:
                    kv_wprev[0] = mev
                kout.append(v_op(
                    lambda eng, o=off, c2=cc, s=sl, h2_=hh, hb=h:
                    eng.tensor_scalar(
                        ktb[:, s * 8192 + h2_ * 4096 + c2 * 512:
                            s * 8192 + h2_ * 4096 + (c2 + 1) * 512],
                        BIG2[:, o:o + 512], bias_sb[:, 4 + hb:5 + hb],
                        None, ALU.add),
                    inc=True, waits=[mev]))
        vout = []
        for tt in range(32):
            off = 0 if tt % 2 == 0 else 512
            w = (list(first_waits) + [wv, ev_buvr]) if tt < 2 else [vout[tt - 2]]
            t_mm(BIG3[:, off:off + 256], onesr[0:1, :],
                 buvr[0:1, g * 256:(g + 1) * 256], True, False, waits=w)
            for j in range(4):
                t_mm(BIG3[:, off:off + 256],
                     ckv[:, j * 4096 + tt * 128: j * 4096 + (tt + 1) * 128],
                     wuvg[:, j * 256:(j + 1) * 256],
                     False, j == 3, inc=(j == 3))
            mev = ("t", cnt["t"])
            if tt == 31:
                kv_wprev[1] = mev
            vout.append(v_op(lambda eng, o=off, t2=tt, s=sl:
                             eng.tensor_copy(
                                 vtb[:, s * 8192 + t2 * 256: s * 8192 + (t2 + 1) * 256],
                                 BIG3[:, o:o + 256]),
                             inc=True, waits=[mev]))
        return kout, vout

    head_kv = [emit_kvgen(0, [a_last])]
    attn_ev = []
    prev_inv = [prev_fin[0]]
    for h in range(16):
        g, hh = h // 2, h % 2
        k_evs, v_evs = head_kv[g]
        sl = g % 2
        exp_a = [None] * 32
        exp_b = [None] * 32
        pv_evs = [None] * 32
        qk_evs = [None] * 32

        def emit_pvden(j):
            first = j == 0
            pa = [exp_a[j]]
            if first:
                pa.append(attn_ev[-1] if attn_ev else a_last)
            t_mm(BIG3[:, 0:512],
                 vtb[:, sl * 8192 + j * 256 + hh * 128:
                     sl * 8192 + j * 256 + (hh + 1) * 128],
                 prb[:, (j % 2) * 1024:(j % 2) * 1024 + 512],
                 first, j == 31, waits=pa)
            t_mm(BIG3[:, 512:1024],
                 vtb[:, sl * 8192 + j * 256 + hh * 128:
                     sl * 8192 + j * 256 + (hh + 1) * 128],
                 prb[:, (j % 2) * 1024 + 512:(j % 2 + 1) * 1024],
                 first, j == 31, waits=[exp_b[j]])
            t_mm(SM1[0:1, 0:512], onesc[:, :],
                 prb[:, (j % 2) * 1024:(j % 2) * 1024 + 512],
                 first, j == 31, waits=[prev_inv[0]] if first else ())
            pv_evs[j] = t_mm(SM2[0:1, 0:512], onesc[:, :],
                             prb[:, (j % 2) * 1024 + 512:(j % 2 + 1) * 1024],
                             first, j == 31, inc=True)

        for kt in range(32):
            spt = BIG1 if kt % 2 == 0 else BIG2
            if kt < 2:
                w0 = [k_evs[hh * 8 + 7], v_evs[31]]
                if attn_ev:
                    w0.append(attn_ev[-1])
            else:
                w0 = [exp_b[kt - 2]]
            t_mm(spt[:, 0:512],
                 ktb[:, sl * 8192 + hh * 4096 + kt * 128:
                     sl * 8192 + hh * 4096 + (kt + 1) * 128],
                 qT[:, h * 1024: h * 1024 + 512], True, True, waits=w0)
            qk_evs[kt] = t_mm(spt[:, 512:1024],
                              ktb[:, sl * 8192 + hh * 4096 + kt * 128:
                                  sl * 8192 + hh * 4096 + (kt + 1) * 128],
                              qT[:, h * 1024 + 512:(h + 1) * 1024], True, True,
                              inc=True)
            if kt >= 1:
                emit_pvden(kt - 1)
            exp_a[kt] = s_op(
                lambda eng, p=spt, k2=kt: eng.activation(
                    prb[:, (k2 % 2) * 1024:(k2 % 2) * 1024 + 512],
                    p[:, 0:512], AF.Exp, scale=EXP_SCALE),
                inc=True,
                waits=[qk_evs[kt], pv_evs[kt - 2] if kt >= 2 else None])
            exp_b[kt] = s_op(
                lambda eng, p=spt, k2=kt: eng.activation(
                    prb[:, (k2 % 2) * 1024 + 512:(k2 % 2 + 1) * 1024],
                    p[:, 512:1024], AF.Exp, scale=EXP_SCALE),
                inc=True)
        emit_pvden(31)
        exp_evs = exp_b

        def inv_prep(eng):
            eng.reciprocal(brow_f[0:1, 0:512], SM1[0:1, 0:512])
            return eng.reciprocal(brow_f[0:1, 512:1024],
                                              SM2[0:1, 0:512])
        e_inv = v_op(inv_prep, inc=True, waits=[pv_evs[31]])
        prev_inv[0] = e_inv
        ebc = bcast(brow_f[0:1, 0:1024], 1024, BIG1, brow_f,
                    brow_h, brow_l, waits=[e_inv, exp_evs[30]])
        e_cp = v_op(lambda eng: eng.tensor_copy(invb[:, :], BIG1[:, 0:1024]),
                    inc=True, waits=[ebc])
        e_at = v_op(lambda eng, hh=h: eng.tensor_tensor(
            attn[:, hh * 1024:(hh + 1) * 1024], BIG3[:, 0:1024], invb[:, :],
            ALU.mult), inc=True, waits=[e_cp])
        attn_ev.append(e_at)
        if h < 15 and hh == 1:
            head_kv.append(emit_kvgen(g + 1, [exp_evs[31], e_at]))
        emit_prestream(8)
    emit_prestream(len(pre_jobs))

    tap("attn", attn, 16384, attn_ev[15])

    e_pv = s_op(lambda eng: eng.copy(fence_c[0:1, 0:1], fence_c[0:1, 0:1]),
                inc=True, waits=[pre_vev[0], pre_vev[1]])
    e_red = v_op(lambda eng: (eng.tensor_reduce(pre_acc[:, 0:1], pre_s[:, 0:64],
                                                AX.X, ALU.add),
                              eng.tensor_reduce(pre_acc[:, 1:2], pre_s[:, 64:128],
                                                AX.X, ALU.add))[-1],
                 inc=True, waits=[e_pv, attn_ev[15]])
    e_h1 = v_op(lambda eng: (eng.tensor_copy(prehl_b[:, 0:1], pre_acc[:, 0:1]),
                             eng.tensor_copy(prehl_b[:, 1:2], pre_acc[:, 1:2]))[-1],
                inc=True, waits=[vfence(e_red)])
    e_fh1 = vfence(e_h1)
    e_l1 = s_op(lambda eng: (eng.activation(pre_r[:, 0:1], prehl_b[:, 0:1],
                                            AF.Identity, scale=-1.0,
                                            bias=pre_acc[:, 0:1]),
                             eng.activation(pre_r[:, 1:2], prehl_b[:, 1:2],
                                            AF.Identity, scale=-1.0,
                                            bias=pre_acc[:, 1:2]))[-1],
                inc=True, waits=[e_fh1])
    e_l1b = v_op(lambda eng: (eng.tensor_copy(prehl_b[:, 2:3], pre_r[:, 0:1]),
                              eng.tensor_copy(prehl_b[:, 3:4], pre_r[:, 1:2]))[-1],
                 inc=True, waits=[sfence(e_l1)])
    e_sum = None
    for c in range(2):
        t_mm(SM1[0:1, c:c + 1], onesc[:, :], prehl_b[:, c:c + 1], True, False,
             waits=[e_l1b, prev_inv[0]] if c == 0 else ())
        e_sum = t_mm(SM1[0:1, c:c + 1], onesc[:, :], prehl_b[:, 2 + c:3 + c],
                     False, True, inc=True)
    e_rp = v_op(lambda eng: (eng.tensor_scalar(rowp_f[0:1, 0:1], SM1[0:1, 0:1],
                                               1.0 / 16777216.0, None, ALU.mult),
                             eng.tensor_scalar(rowp_f[0:1, 1:2], SM1[0:1, 1:2],
                                               1.0 / 16777216.0, None, ALU.mult))[-1],
                inc=True, waits=[e_sum])
    e_bh = s_op(lambda eng: (eng.copy(brow_h[0:1, 0:1], rowp_f[0:1, 0:1]),
                             eng.copy(brow_h[0:1, 1:2], rowp_f[0:1, 1:2]))[-1],
                inc=True, waits=[vfence(e_rp)])
    e_bl = v_op(lambda eng: (eng.tensor_tensor(rowp_f[0:1, 2:3], rowp_f[0:1, 0:1],
                                               brow_h[0:1, 0:1], ALU.subtract),
                             eng.tensor_tensor(rowp_f[0:1, 3:4], rowp_f[0:1, 1:2],
                                               brow_h[0:1, 1:2], ALU.subtract))[-1],
                inc=True, waits=[sfence(e_bh)])
    e_blc = s_op(lambda eng: (eng.copy(brow_l[0:1, 0:1], rowp_f[0:1, 2:3]),
                              eng.copy(brow_l[0:1, 1:2], rowp_f[0:1, 3:4]))[-1],
                 inc=True, waits=[vfence(e_bl)])
    e_s12m = None
    for c in range(2):
        t_mm(SM2[:, c:c + 1], onesr[0:1, :], brow_h[0:1, c:c + 1], True, False,
             waits=[e_blc] if c == 0 else ())
        e_s12m = t_mm(SM2[:, c:c + 1], onesr[0:1, :], brow_l[0:1, c:c + 1],
                      False, True, inc=True)

    def cp_s12(eng):
        eng.tensor_copy(s12[:, 0:1], SM2[:, 0:1])
        return eng.tensor_copy(s12[:, 1:2], SM2[:, 1:2])
    e_s12 = v_op(cp_s12, inc=True, waits=[e_s12m])
    if "dbg" in tap_names:
        def cp_dbg(eng):
            eng.memset(dbgB[:, :], 0.0)
            for c in range(4):
                eng.tensor_copy(dbgB[:, c:c + 1], prehl_b[:, c:c + 1])
            eng.tensor_copy(dbgB[0:1, 4:5], SM1[0:1, 0:1])
            eng.tensor_copy(dbgB[0:1, 5:6], SM1[0:1, 1:2])
            eng.tensor_copy(dbgB[0:1, 6:7], rowp_f[0:1, 0:1])
            eng.tensor_copy(dbgB[0:1, 7:8], rowp_f[0:1, 1:2])
            eng.tensor_copy(dbgB[0:1, 8:9], brow_h[0:1, 0:1])
            eng.tensor_copy(dbgB[0:1, 9:10], brow_h[0:1, 1:2])
            eng.tensor_copy(dbgB[0:1, 10:11], brow_l[0:1, 0:1])
            eng.tensor_copy(dbgB[0:1, 11:12], brow_l[0:1, 1:2])
            eng.tensor_copy(dbgB[:, 12:13], SM2[:, 0:1])
            return eng.tensor_copy(dbgB[:, 13:14], SM2[:, 1:2])
        e_dbg = v_op(cp_dbg, inc=True, waits=[e_s12])
        y_dma(bass.AP(tap_t["dbg"], 0, [[16, 128], [1, 16]]), dbgB[:, :],
              waits=[e_dbg])

    op_evs = []
    stat_mm = [None]
    wo_prev = [None, None]
    for ot in range(16):
        wev = wload(wo_sb, 2 * 16 * 128, wo, DM, 0, 16, ot * 128, 128,
                    (ot % 2) * 2048,
                    waits=[wo_prev[ot % 2]] if ot >= 2 else
                    [pre_vev[0], pre_vev[1], attn_ev[15], e_s12])
        for half in range(2):
            si = next_slot()
            pt, off = pslots[si]
            for k in range(16):
                w = [wev, pslot_ev[si]] if k == 0 else ()
                t_mm(pt[:, off:off + 512],
                     wo_sb[:, (ot % 2) * 2048 + k * 128:(ot % 2) * 2048 + (k + 1) * 128],
                     attn[:, k * 1024 + half * 512: k * 1024 + half * 512 + 512],
                     k == 0, k == 15, inc=(k == 15), waits=w)
            mev = ("t", cnt["t"])
            if half == 1:
                wo_prev[ot % 2] = mev
            e_s = s_op(lambda eng, p=pt, o=off, oo=ot, hf=half: eng.activation(
                aoT[:, oo * 1024 + hf * 512: oo * 1024 + hf * 512 + 512],
                p[:, o:o + 512], AF.Identity, bias=bias_sb[:, 48 + oo:49 + oo]),
                inc=True, waits=[mev])
            pslot_ev[si] = e_s
            op_evs.append(e_s)
        e_sq = s_op(lambda eng, oo=ot: eng.activation(
            sqtmp[:, 0:1024], aoT[:, oo * 1024:(oo + 1) * 1024], AF.Square),
            inc=True, waits=[op_evs[-1], stat_mm[0]])
        t_mm(SM1[0:1, 0:512], onesc[:, :], aoT[:, ot * 1024: ot * 1024 + 512],
             ot == 0, ot == 15,
             waits=[op_evs[-1], e_s12] if ot == 0 else [op_evs[-1]])
        t_mm(SM2[0:1, 0:512], onesc[:, :],
             aoT[:, ot * 1024 + 512:(ot + 1) * 1024], ot == 0, ot == 15)
        t_mm(BIG3[0:1, 0:512], onesc[:, :], sqtmp[:, 0:512], ot == 0, ot == 15,
             waits=[e_sq, attn_ev[15]] if ot == 0 else [e_sq])
        stat_mm[0] = t_mm(BIG3[0:1, 512:1024], onesc[:, :], sqtmp[:, 512:1024],
                          ot == 0, ot == 15, inc=True)

    def ln_fin(n_inv, eps, r_out, mr_out, waits):
        def fn(eng):
            m = mr_out
            eng.tensor_scalar(m[0:1, 0:512], SM1[0:1, 0:512], n_inv, None, ALU.mult)
            eng.tensor_scalar(m[0:1, 512:1024], SM2[0:1, 0:512], n_inv, None, ALU.mult)
            e2 = lrow_f2[0:1, 0:1024]
            eng.tensor_copy(e2[0:1, 0:512], BIG3[0:1, 0:512])
            eng.tensor_copy(e2[0:1, 512:1024], BIG3[0:1, 512:1024])
            eng.tensor_scalar(e2, e2, n_inv, None, ALU.mult)
            eng.tensor_tensor(r_out, m, m, ALU.mult)
            eng.tensor_tensor(e2, e2, r_out, ALU.subtract)
            eng.tensor_scalar(e2, e2, eps, None, ALU.add)
            return eng.reciprocal(r_out, e2)
        e1 = v_op(fn, inc=True, waits=waits)
        e2 = s_op(lambda eng: eng.activation(r_out, r_out, AF.Sqrt),
                  inc=True, waits=[e1])
        e3 = v_op(lambda eng: eng.tensor_tensor(mr_out, mr_out, r_out, ALU.mult),
                  inc=True, waits=[e2])
        return e3

    r1r = lrow_f[0:1, 0:1024]
    mr1r = ltmp1[0:1, 0:1024]
    e_ln1b = ln_fin(INV2048, LN_EPS, r1r, mr1r, [stat_mm[0]])
    ebc = bcast(r1r, 1024, BIG3, lrow_f2, lrow_h, lrow_l,
                waits=[e_ln1b, stat_mm[0]])
    e_rb1 = v_op(lambda eng: eng.tensor_copy(bcA[:, :], BIG3[:, 0:1024]),
                 inc=True, waits=[ebc])
    ebc = bcast(mr1r, 1024, BIG3, lrow_f2, lrow_h, lrow_l, waits=[e_rb1])
    e_mrb1 = v_op(lambda eng: eng.tensor_copy(bcB[:, :], BIG3[:, 0:1024]),
                  inc=True, waits=[ebc])

    t1_evs = []
    sp_evs = []
    for ft in range(16):
        dev = y_load(bass.AP(xob, (ft % 2) * 1024, [[2048, 128], [1, 1024]]),
                     bass.AP(xoT, ft * 128 * T, [[T, 128], [1, 1024]]),
                     waits=[e_mrb1] if ft < 2 else [t1_evs[ft - 2]])

        def mk(eng, f=ft):
            eng.tensor_tensor(ltmp2[:, :], aoT[:, f * 1024:(f + 1) * 1024],
                              bcA[:, :], ALU.mult)
            eng.tensor_tensor(ltmp2[:, :], ltmp2[:, :], bcB[:, :], ALU.subtract)
            eng.tensor_tensor(t1f[:, f * 1024:(f + 1) * 1024], ltmp2[:, :],
                              xob[:, (f % 2) * 1024:(f % 2 + 1) * 1024], ALU.add)
            return eng.tensor_copy(t1b[:, f * 1024:(f + 1) * 1024],
                                   t1f[:, f * 1024:(f + 1) * 1024])
        e_t1 = v_op(mk, inc=True, waits=[dev, e_mrb1])
        t1_evs.append(e_t1)
        sp_evs.append(y_load(bass.AP(t1sp, ft * 128 * T, [[T, 128], [1, 1024]]),
                             bass.AP(t1f, ft * 1024, [[16384, 128], [1, 1024]]),
                             waits=[e_t1]))

    tap("t1b", t1b, 16384, t1_evs[15])

    stat2 = [None]
    for ft in range(16):
        e_sq = s_op(lambda eng, f=ft: eng.activation(
            sqtmp[:, 0:1024], t1b[:, f * 1024:(f + 1) * 1024], AF.Square),
            inc=True, waits=[t1_evs[ft], stat2[0]])
        t_mm(SM1[0:1, 0:512], onesc[:, :], t1b[:, ft * 1024: ft * 1024 + 512],
             ft == 0, ft == 15,
             waits=[t1_evs[ft], e_mrb1] if ft == 0 else [t1_evs[ft]])
        t_mm(SM2[0:1, 0:512], onesc[:, :],
             t1b[:, ft * 1024 + 512:(ft + 1) * 1024], ft == 0, ft == 15)
        t_mm(BIG3[0:1, 0:512], onesc[:, :], sqtmp[:, 0:512], ft == 0, ft == 15,
             waits=[e_sq, e_mrb1] if ft == 0 else [e_sq])
        stat2[0] = t_mm(BIG3[0:1, 512:1024], onesc[:, :], sqtmp[:, 512:1024],
                        ft == 0, ft == 15, inc=True)
    r2r = lrow_f[0:1, 0:1024]
    mr2r = ltmp1[0:1, 0:1024]
    e_ln2 = ln_fin(INV2048, LN_EPS, r2r, mr2r, [stat2[0], e_rb1])
    ebc = bcast(r2r, 1024, BIG3, lrow_f2, lrow_h, lrow_l, waits=[e_ln2])
    e_rb2 = v_op(lambda eng: eng.tensor_copy(bcA[:, :], BIG3[:, 0:1024]),
                 inc=True, waits=[ebc, t1_evs[15]])
    ebc = bcast(mr2r, 1024, BIG3, lrow_f2, lrow_h, lrow_l, waits=[e_rb2])
    e_mrb2 = v_op(lambda eng: eng.tensor_copy(bcB[:, :], BIG3[:, 0:1024]),
                  inc=True, waits=[ebc])

    e_sqm0 = v_op(lambda eng: eng.memset(sqmax[:, :], 0.0), inc=True,
                  waits=[e_mrb2])
    ln2_evs = []
    rms_mm = [None]
    mx_prev = [e_sqm0]
    for ft in range(16):
        def mk(eng, f=ft):
            eng.tensor_tensor(ltmp2[:, :], t1f[:, f * 1024:(f + 1) * 1024],
                              bcA[:, :], ALU.mult)
            eng.tensor_tensor(ltmp2[:, :], ltmp2[:, :], bcB[:, :], ALU.subtract)
            return eng.tensor_copy(aoT[:, f * 1024:(f + 1) * 1024], ltmp2[:, :])
        e_l = v_op(mk, inc=True, waits=[e_mrb2, sp_evs[ft]])
        e_sq = s_op(lambda eng, f=ft: eng.activation(
            sqtmp[:, 0:1024], aoT[:, f * 1024:(f + 1) * 1024], AF.Square),
            inc=True, waits=[e_l, rms_mm[0], mx_prev[0]])
        t_mm(SM1[0:1, 0:512], onesc[:, :], sqtmp[:, 0:512], ft == 0, ft == 15,
             waits=[e_sq, e_mrb2] if ft == 0 else [e_sq])
        rms_mm[0] = t_mm(SM2[0:1, 0:512], onesc[:, :], sqtmp[:, 512:1024],
                         ft == 0, ft == 15, inc=True)
        mx_prev[0] = v_op(lambda eng: eng.tensor_tensor(
            sqmax[:, :], sqmax[:, :], sqtmp[:, 0:1024], ALU.max),
            inc=True, waits=[e_sq, mx_prev[0]])
        ln2_evs.append(e_l)

    def quant_rows(n_inv, rsqn, rms_mm_ev, fold_ev, bcast_wait):
        def r1(eng):
            t = lrow_f2[0:1, 0:1024]
            eng.tensor_scalar(t[0:1, 0:512], SM1[0:1, 0:512], n_inv, None, ALU.mult)
            eng.tensor_scalar(t[0:1, 512:1024], SM2[0:1, 0:512], n_inv, None, ALU.mult)
            eng.tensor_scalar(t, t, RMS_EPS, None, ALU.add)
            return eng.reciprocal(qrow_f[0:1, 0:1024], t)
        e1 = v_op(r1, inc=True, waits=[rms_mm_ev, fold_ev])
        e2 = s_op(lambda eng: eng.activation(qrow_f[0:1, 0:1024],
                                             qrow_f[0:1, 0:1024], AF.Sqrt),
                  inc=True, waits=[e1])
        return e2

    e_q = quant_rows(INV2048, RSQ2048, rms_mm[0],
                     v_op(pfold(sqmax, ltmp2, 1024, ALU.max), inc=True,
                          waits=[mx_prev[0], ln2_evs[15]]), None)
    e_mxr = v_op(lambda eng: eng.tensor_copy(lrow_f[0:1, 0:1024],
                                             sqmax[0:1, 0:1024]),
                 inc=True, waits=[e_q])
    e_mxs = s_op(lambda eng: eng.activation(lrow_f[0:1, 0:1024],
                                            lrow_f[0:1, 0:1024], AF.Sqrt),
                 inc=True, waits=[e_mxr])

    def q_rows3(eng, rsqn=RSQ2048):
        f = qrow_f[0:1, 0:1024]
        mx = lrow_f[0:1, 0:1024]
        eng.tensor_scalar(f, f, rsqn, None, ALU.mult)
        eng.tensor_tensor(mx, mx, f, ALU.mult)
        eng.tensor_scalar(mx, mx, Q_EPS, None, ALU.max)
        eng.reciprocal(lrow_f2[0:1, 0:1024], mx)
        eng.tensor_scalar(lrow_f2[0:1, 0:1024], lrow_f2[0:1, 0:1024], 127.0,
                          None, ALU.mult)
        eng.tensor_tensor(f, f, lrow_f2[0:1, 0:1024], ALU.mult)
        return eng.tensor_scalar(mx, mx, 1.0 / 127.0, None, ALU.mult)
    e_q3 = v_op(q_rows3, inc=True, waits=[e_mxs])
    ebc = bcast(qrow_f[0:1, 0:1024], 1024, BIG3, lrow_f2, lrow_h, lrow_l,
                waits=[e_q3])
    e_A1 = v_op(lambda eng: eng.tensor_copy(bcA[:, :], BIG3[:, 0:1024]),
                inc=True, waits=[ebc])
    ebc = bcast(lrow_f[0:1, 0:1024], 1024, BIG3, lrow_f2, lrow_h, lrow_l,
                waits=[e_A1])
    e_B1 = v_op(lambda eng: eng.tensor_copy(bcB[:, :], BIG3[:, 0:1024]),
                inc=True, waits=[ebc])

    xq_evs = []
    LT = [ltmp2, ltmp1]
    pendq = None
    for ft in range(16):
        tb = LT[ft % 2]
        e_m = v_op(lambda eng, f=ft, t=tb: eng.tensor_tensor(
            t[:, :], aoT[:, f * 1024:(f + 1) * 1024], bcA[:, :], ALU.mult),
            inc=True, waits=[e_B1, xq_evs[ft - 2] if ft >= 2 else None])
        e_r = s_op(lambda eng, t=tb: (eng.activation(t[:, :], t[:, :],
                                                     AF.Identity, bias=MAGIC),
                                      eng.activation(t[:, :], t[:, :],
                                                     AF.Identity, bias=-MAGIC))[-1],
                   inc=True, waits=[e_m])
        if pendq is not None:
            pf, ptb, per = pendq
            xq_evs.append(v_op(lambda eng, f=pf, t=ptb: eng.tensor_tensor(
                aoT[:, f * 1024:(f + 1) * 1024], t[:, :], bcB[:, :], ALU.mult),
                inc=True, waits=[per]))
        pendq = (ft, tb, e_r)
    pf, ptb, per = pendq
    xq_evs.append(v_op(lambda eng, f=pf, t=ptb: eng.tensor_tensor(
        aoT[:, f * 1024:(f + 1) * 1024], t[:, :], bcB[:, :], ALU.mult),
        inc=True, waits=[per]))

    tap("xq", aoT, 16384, xq_evs[15])
    if "s12" in tap_names:
        y_dma(bass.AP(tap_t["s12"], 0, [[2, 128], [1, 2]]), s12[:, 0:2],
              waits=[e_s12])
    if "preacc" in tap_names:
        y_dma(bass.AP(tap_t["preacc"], 0, [[2, 128], [1, 2]]), pre_acc[:, :],
              waits=[e_h1])

    sg1_prev = [None, None]
    h2_evs = []
    rms2_mm = [None]
    mx2_prev = [None]
    e_sqm2 = v_op(lambda eng: eng.memset(csqmax[:, :], 0.0), inc=True,
                  waits=[e_B1])
    mx2_prev[0] = e_sqm2
    sq_prev = [None]
    for ot in range(64):
        wev = wload(w1db, 2 * 16 * 128, w1, INNER, 0, 16, ot * 128, 128,
                    (ot % 2) * 2048,
                    waits=[sg1_prev[ot % 2]] if ot >= 2 else
                    ([xq_evs[15]] + sp_evs[8:]))
        e_sg = v_op(lambda eng, s2=(ot % 2): (
            eng.tensor_scalar(w1db[:, s2 * 2048:(s2 + 1) * 2048],
                              w1db[:, s2 * 2048:(s2 + 1) * 2048],
                              1e10, 1.0, ALU.mult, ALU.min),
            eng.tensor_scalar(w1db[:, s2 * 2048:(s2 + 1) * 2048],
                              w1db[:, s2 * 2048:(s2 + 1) * 2048],
                              -1.0, None, ALU.max))[-1], inc=True, waits=[wev])
        for half in range(2):
            si = next_slot()
            pt, off = pslots[si]
            for k in range(16):
                w = []
                if k == 0:
                    w = [e_sg, pslot_ev[si]]
                    if ot == 0 and half == 0:
                        w += [e_s12] + xq_evs
                t_mm(pt[:, off:off + 512],
                     w1db[:, (ot % 2) * 2048 + k * 128:(ot % 2) * 2048 + (k + 1) * 128],
                     aoT[:, k * 1024 + half * 512: k * 1024 + half * 512 + 512],
                     k == 0, k == 15, inc=(k == 15), waits=w)
            mev = ("t", cnt["t"])
            if half == 1:
                sg1_prev[ot % 2] = mev
            e_h = s_op(lambda eng, p=pt, o=off, oo=ot, hf=half: eng.activation(
                h2[:, oo * 1024 + hf * 512: oo * 1024 + hf * 512 + 512],
                p[:, o:o + 512], AF.Gelu, bias=bias_sb[:, 64 + oo:65 + oo],
                scale=s12[:, 0:1]), inc=True, waits=[mev])
            pslot_ev[si] = e_h
            h2_evs.append(e_h)
        e_sq = s_op(lambda eng, oo=ot: eng.activation(
            csq[:, 0:1024], h2[:, oo * 1024:(oo + 1) * 1024], AF.Square),
            inc=True, waits=[h2_evs[-1], rms2_mm[0], mx2_prev[0]])
        sq_prev[0] = e_sq
        t_mm(SM1[0:1, 0:512], onesc[:, :], csq[:, 0:512], ot == 0, ot == 63,
             waits=[e_sq, e_B1] if ot == 0 else [e_sq])
        rms2_mm[0] = t_mm(SM2[0:1, 0:512], onesc[:, :], csq[:, 512:1024],
                          ot == 0, ot == 63, inc=True)
        mx2_prev[0] = v_op(lambda eng: eng.tensor_tensor(
            csqmax[:, :], csqmax[:, :], csq[:, 0:1024], ALU.max),
            inc=True, waits=[e_sq, mx2_prev[0]])

    e_fold2 = v_op(pfold(csqmax, ctmp1, 1024, ALU.max), inc=True,
                   waits=[mx2_prev[0]])

    def r2a(eng):
        t = ctmp1[0:1, 0:1024]
        eng.tensor_scalar(t[0:1, 0:512], SM1[0:1, 0:512], 1.0 / 8192.0, None, ALU.mult)
        eng.tensor_scalar(t[0:1, 512:1024], SM2[0:1, 0:512], 1.0 / 8192.0, None, ALU.mult)
        eng.tensor_scalar(t, t, RMS_EPS, None, ALU.add)
        return eng.reciprocal(qrow_f[0:1, 0:1024], t)
    e_ra = v_op(r2a, inc=True, waits=[rms2_mm[0], e_fold2])
    e_rb_ = s_op(lambda eng: eng.activation(qrow_f[0:1, 0:1024],
                                            qrow_f[0:1, 0:1024], AF.Sqrt),
                 inc=True, waits=[e_ra])
    e_mxr2 = v_op(lambda eng: eng.tensor_copy(crow_f[0:1, 0:1024],
                                              csqmax[0:1, 0:1024]),
                  inc=True, waits=[e_rb_])
    e_mxs2 = s_op(lambda eng: eng.activation(crow_f[0:1, 0:1024],
                                             crow_f[0:1, 0:1024], AF.Sqrt),
                  inc=True, waits=[e_mxr2])

    def q2_rows3(eng):
        f = qrow_f[0:1, 0:1024]
        mx = crow_f[0:1, 0:1024]
        eng.tensor_scalar(f, f, RSQ8192, None, ALU.mult)
        eng.tensor_tensor(mx, mx, f, ALU.mult)
        eng.tensor_scalar(mx, mx, Q_EPS, None, ALU.max)
        eng.reciprocal(ctmp1[0:1, 0:1024], mx)
        eng.tensor_scalar(ctmp1[0:1, 0:1024], ctmp1[0:1, 0:1024], 127.0,
                          None, ALU.mult)
        eng.tensor_tensor(f, f, ctmp1[0:1, 0:1024], ALU.mult)
        return eng.tensor_scalar(mx, mx, 1.0 / 127.0, None, ALU.mult)
    e_q23 = v_op(q2_rows3, inc=True, waits=[e_mxs2])
    ebc = bcast(qrow_f[0:1, 0:1024], 1024, BIG3, ctmp1, crow_h, crow_l,
                waits=[e_q23])
    e_A2 = v_op(lambda eng: eng.tensor_copy(bcA[:, :], BIG3[:, 0:1024]),
                inc=True, waits=[ebc])
    ebc = bcast(crow_f[0:1, 0:1024], 1024, BIG3, ctmp1, crow_h, crow_l,
                waits=[e_A2])
    e_B2b = v_op(lambda eng: eng.tensor_copy(bcB[:, :], BIG3[:, 0:1024]),
                 inc=True, waits=[ebc])

    xq2_evs = []
    CT = [ctmp1, ctmp1b]
    pend = None
    for f2 in range(64):
        tb = CT[f2 % 2]
        e_m = v_op(lambda eng, f=f2, t=tb: eng.tensor_tensor(
            t[:, :], h2[:, f * 1024:(f + 1) * 1024], bcA[:, :], ALU.mult),
            inc=True, waits=[e_B2b, h2_evs[2 * f2 + 1],
                             xq2_evs[f2 - 2] if f2 >= 2 else None])
        e_r = s_op(lambda eng, t=tb: (eng.activation(t[:, :], t[:, :],
                                                     AF.Identity, bias=MAGIC),
                                      eng.activation(t[:, :], t[:, :],
                                                     AF.Identity, bias=-MAGIC))[-1],
                   inc=True, waits=[e_m])
        if pend is not None:
            pf, ptb, per = pend
            xq2_evs.append(v_op(lambda eng, f=pf, t=ptb: eng.tensor_tensor(
                h2[:, f * 1024:(f + 1) * 1024], t[:, :], bcB[:, :], ALU.mult),
                inc=True, waits=[per]))
        pend = (f2, tb, e_r)
    pf, ptb, per = pend
    xq2_evs.append(v_op(lambda eng, f=pf, t=ptb: eng.tensor_tensor(
        h2[:, f * 1024:(f + 1) * 1024], t[:, :], bcB[:, :], ALU.mult),
        inc=True, waits=[per]))

    tap("h2q", h2, 65536, xq2_evs[63])

    sg2_prev = [None, None]
    raw_evs = []
    statr = [None]
    sqr_prev = [None]
    for ot2 in range(16):
        siA = next_slot()
        ptA, offA = pslots[siA]
        siB = next_slot()
        ptB, offB = pslots[siB]
        mevA = mevB = None
        for blk in range(4):
            sl = blk % 2
            wev = wload(w2db, 2 * 16 * 128, w2, DM, blk * 16, 16, ot2 * 128,
                        128, sl * 2048,
                        waits=[sg2_prev[sl]] if sg2_prev[sl] else [])
            e_sg = v_op(lambda eng, s2=sl: (
                eng.tensor_scalar(w2db[:, s2 * 2048:(s2 + 1) * 2048],
                                  w2db[:, s2 * 2048:(s2 + 1) * 2048],
                                  1e10, 1.0, ALU.mult, ALU.min),
                eng.tensor_scalar(w2db[:, s2 * 2048:(s2 + 1) * 2048],
                                  w2db[:, s2 * 2048:(s2 + 1) * 2048],
                                  -1.0, None, ALU.max))[-1], inc=True,
                waits=[wev])
            for ki in range(16):
                k = blk * 16 + ki
                wA = []
                wB = []
                if ki == 0:
                    wA = [e_sg]
                    if ot2 == 0:
                        wA.append(xq2_evs[blk * 16 + 15])
                    if k == 0:
                        wA.append(pslot_ev[siA])
                        wB = [pslot_ev[siB]]
                t_mm(ptA[:, offA:offA + 512],
                     w2db[:, sl * 2048 + ki * 128: sl * 2048 + (ki + 1) * 128],
                     h2[:, k * 1024: k * 1024 + 512],
                     k == 0, k == 63, inc=(k == 63), waits=wA)
                if k == 63:
                    mevA = ("t", cnt["t"])
                t_mm(ptB[:, offB:offB + 512],
                     w2db[:, sl * 2048 + ki * 128: sl * 2048 + (ki + 1) * 128],
                     h2[:, k * 1024 + 512:(k + 1) * 1024],
                     k == 0, k == 63, inc=(ki == 15), waits=wB)
                if ki == 15:
                    sg2_prev[sl] = ("t", cnt["t"])
                    if k == 63:
                        mevB = ("t", cnt["t"])
        e_rA = s_op(lambda eng, p=ptA, o=offA, oo=ot2: eng.activation(
            bf2r[:, oo * 1024: oo * 1024 + 512], p[:, o:o + 512], AF.Identity),
            inc=True, waits=[mevA])
        e_rB = s_op(lambda eng, p=ptB, o=offB, oo=ot2: eng.activation(
            bf2r[:, oo * 1024 + 512:(oo + 1) * 1024], p[:, o:o + 512],
            AF.Identity), inc=True, waits=[mevB])
        pslot_ev[siA] = e_rA
        pslot_ev[siB] = e_rB
        raw_evs.append(e_rA)
        raw_evs.append(e_rB)
        e_sq = s_op(lambda eng, oo=ot2: eng.activation(
            csq[:, 0:1024], bf2r[:, oo * 1024:(oo + 1) * 1024], AF.Square),
            inc=True, waits=[e_rB, statr[0], sqr_prev[0]])
        sqr_prev[0] = e_sq
        t_mm(SM1[0:1, 0:512], onesc[:, :], bf2r[:, ot2 * 1024: ot2 * 1024 + 512],
             ot2 == 0, ot2 == 15,
             waits=[e_rA, e_B2b] if ot2 == 0 else [e_rA])
        t_mm(SM2[0:1, 0:512], onesc[:, :],
             bf2r[:, ot2 * 1024 + 512:(ot2 + 1) * 1024], ot2 == 0, ot2 == 15,
             waits=[e_rB])
        t_mm(BIG3[0:1, 0:512], onesc[:, :], csq[:, 0:512], ot2 == 0, ot2 == 15,
             waits=[e_sq, e_B2b] if ot2 == 0 else [e_sq])
        statr[0] = t_mm(BIG3[0:1, 512:1024], onesc[:, :], csq[:, 512:1024],
                        ot2 == 0, ot2 == 15, inc=True)

    tap("bf2r", bf2r, 16384, statr[0])

    e3 = y_dma(fr_sraw[0:1, 0:1024], bass.AP(biasd["b2"], 0, [[1024, 1], [1, 1024]]),
               waits=[statr[0]])
    ec3 = v_op(lambda eng: eng.tensor_copy(b2rb[0:1, 0:1024], fr_sraw[0:1, 0:1024]),
               inc=True, waits=[e3, statr[0]])
    e4 = y_load(fr_sraw[0:1, 0:1024],
                bass.AP(biasd["b2"], 1024, [[1024, 1], [1, 1024]]), waits=[ec3])
    e_b2rb = v_op(lambda eng: eng.tensor_copy(b2rb[0:1, 1024:2048],
                                              fr_sraw[0:1, 0:1024]),
                  inc=True, waits=[e4])


    def cp_rows(eng):
        eng.tensor_copy(fr_sraw[0:1, 0:512], SM1[0:1, 0:512])
        eng.tensor_copy(fr_sraw[0:1, 512:1024], SM2[0:1, 0:512])
        eng.tensor_copy(fr_ssq[0:1, 0:512], BIG3[0:1, 0:512])
        return eng.tensor_copy(fr_ssq[0:1, 512:1024], BIG3[0:1, 512:1024])
    e_cpr = v_op(cp_rows, inc=True, waits=[statr[0], e_b2rb])
    b2r_mm = [None]
    for ot2 in range(16):
        t_mm(SM1[0:1, 0:512], b2cb[:, ot2:ot2 + 1],
             bf2r[:, ot2 * 1024: ot2 * 1024 + 512], ot2 == 0, ot2 == 15,
             waits=[e_cpr, ev_b2cb] if ot2 == 0 else ())
        b2r_mm[0] = t_mm(SM2[0:1, 0:512], b2cb[:, ot2:ot2 + 1],
                         bf2r[:, ot2 * 1024 + 512:(ot2 + 1) * 1024],
                         ot2 == 0, ot2 == 15, inc=(ot2 == 15))

    def cp_b2r(eng):
        eng.tensor_copy(fr_sb2r[0:1, 0:512], SM1[0:1, 0:512])
        return eng.tensor_copy(fr_sb2r[0:1, 512:1024], SM2[0:1, 0:512])
    e_cb2r = v_op(cp_b2r, inc=True, waits=[b2r_mm[0]])

    e_q0 = v_op(lambda eng: eng.tensor_tensor(ctmp2[:, 0:16], bias_sb[:, 128:144],
                                              bias_sb[:, 128:144], ALU.mult),
                inc=True, waits=[ev_bias, e_b2rb, sqr_prev[0], statr[0]])
    e_q1 = v_op(lambda eng: (eng.tensor_reduce(ctmp2[:, 16:17], bias_sb[:, 128:144],
                                               AX.X, ALU.add),
                             eng.tensor_reduce(ctmp2[:, 17:18], ctmp2[:, 0:16],
                                               AX.X, ALU.add))[-1],
                inc=True, waits=[vfence(e_q0)])
    e_q2h = s_op(lambda eng: (eng.copy(csq[:, 0:1], ctmp2[:, 16:17]),
                              eng.copy(csq[:, 1:2], ctmp2[:, 17:18]))[-1],
                 inc=True, waits=[vfence(e_q1)])
    e_q2l = s_op(lambda eng: (eng.activation(ctmp2[:, 18:19], csq[:, 0:1],
                                             AF.Identity, scale=-1.0,
                                             bias=ctmp2[:, 16:17]),
                              eng.activation(ctmp2[:, 19:20], csq[:, 1:2],
                                             AF.Identity, scale=-1.0,
                                             bias=ctmp2[:, 17:18]))[-1],
                 inc=True, waits=[e_q2h])
    e_q2lb = v_op(lambda eng: (eng.tensor_copy(csq[:, 2:3], ctmp2[:, 18:19]),
                               eng.tensor_copy(csq[:, 3:4], ctmp2[:, 19:20]))[-1],
                  inc=True, waits=[sfence(e_q2l)])
    e_b2m = None
    for c in range(2):
        t_mm(SM1[0:1, 16 + c:17 + c], onesc[:, :], csq[:, c:c + 1], True, False,
             waits=[e_q2lb, e_cb2r] if c == 0 else ())
        e_b2m = t_mm(SM1[0:1, 16 + c:17 + c], onesc[:, :], csq[:, 2 + c:3 + c],
                     False, True, inc=True)

    def cp_b2f(eng):
        eng.tensor_copy(ctmp2[0:1, 16:17], SM1[0:1, 16:17])
        return eng.tensor_copy(ctmp2[0:1, 17:18], SM1[0:1, 17:18])
    e_b2f0 = v_op(cp_b2f, inc=True, waits=[e_b2m])
    e_b2f = vfence(e_b2f0)

    e_means = v_op(lambda eng: (eng.tensor_scalar(frow_f[0:1, 0:1],
                                                  ctmp2[0:1, 16:17], INV2048,
                                                  None, ALU.mult),
                                eng.tensor_scalar(frow_f[0:1, 1:2],
                                                  ctmp2[0:1, 17:18], INV2048,
                                                  None, ALU.mult))[-1],
                    inc=True, waits=[e_b2f, sfence(e_b2f)])
    e_meansf = vfence(e_means)

    def f_rows1(eng):
        s2v = s12[0:1, 1:2]
        mp = fr_mp[0:1, 0:1024]
        eng.tensor_scalar(mp, fr_sraw[0:1, 0:1024], INV2048, None, ALU.mult)
        eng.tensor_scalar(mp, mp, s2v, None, ALU.mult)
        eng.tensor_scalar(mp, mp, frow_f[0:1, 0:1], None, ALU.add)
        ey = ftmp1[0:1, 0:1024]
        eng.tensor_scalar(ey, fr_ssq[0:1, 0:1024], INV2048, None, ALU.mult)
        eng.tensor_scalar(ey, ey, s2v, None, ALU.mult)
        eng.tensor_scalar(ey, ey, s2v, None, ALU.mult)
        eng.tensor_scalar(ftmp2[0:1, 0:1024], fr_sb2r[0:1, 0:1024],
                          2.0 * INV2048, None, ALU.mult)
        eng.tensor_scalar(ftmp2[0:1, 0:1024], ftmp2[0:1, 0:1024], s2v, None,
                          ALU.mult)
        eng.tensor_tensor(ey, ey, ftmp2[0:1, 0:1024], ALU.add)
        eng.tensor_scalar(ey, ey, frow_f[0:1, 1:2], None, ALU.add)
        eng.tensor_tensor(ftmp2[0:1, 0:1024], mp, mp, ALU.mult)
        eng.tensor_tensor(ey, ey, ftmp2[0:1, 0:1024], ALU.subtract)
        eng.tensor_scalar(ey, ey, LN_EPS, None, ALU.add)
        return eng.reciprocal(fr_r3[0:1, 0:1024], ey)
    e_f1 = v_op(f_rows1, inc=True, waits=[e_cb2r, sfence(e_meansf), e_s12])
    e_f2 = s_op(lambda eng: eng.activation(fr_r3[0:1, 0:1024],
                                           fr_r3[0:1, 0:1024], AF.Sqrt),
                inc=True, waits=[e_f1])

    def f_rows2(eng):
        s2v = s12[0:1, 1:2]
        r3 = fr_r3[0:1, 0:1024]
        eng.tensor_scalar(fr_c1[0:1, 0:1024], r3, s2v, None, ALU.mult)
        eng.tensor_tensor(ftmp1[0:1, 0:1024], fr_mp[0:1, 0:1024], r3, ALU.mult)
        eng.tensor_scalar(ftmp1[0:1, 0:1024], ftmp1[0:1, 0:1024], -1.0, None,
                          ALU.mult)
        eng.tensor_copy(nb16[0:1, 0:1024], ftmp1[0:1, 0:1024])
        eng.tensor_tensor(ftmp2[0:1, 0:1024], ftmp1[0:1, 0:1024],
                          nb16[0:1, 0:1024], ALU.subtract)
        eng.tensor_copy(nb16[0:1, 1024:2048], ftmp2[0:1, 0:1024])
        eng.tensor_copy(frow_h[0:1, 0:1024], r3)
        eng.tensor_tensor(ftmp2[0:1, 0:1024], r3, frow_h[0:1, 0:1024],
                          ALU.subtract)
        return eng.tensor_copy(frow_l[0:1, 0:1024], ftmp2[0:1, 0:1024])
    e_f3 = v_op(f_rows2, inc=True, waits=[e_f2])
    ebc = bcast(fr_c1[0:1, 0:1024], 1024, BIG3, frow_f, frow_h2, frow_l2,
                waits=[e_f3, statr[0]])
    e_C1 = v_op(lambda eng: eng.tensor_copy(bcC[:, :], BIG3[:, 0:1024]),
                inc=True, waits=[ebc])

    out_evs = []
    for ot2 in range(16):
        trd = y_load(bass.AP(t1rd, (ot2 % 2) * 1024, [[2048, 128], [1, 1024]]),
                     bass.AP(t1sp, ot2 * 128 * T, [[T, 128], [1, 1024]]),
                     waits=[e_f3] if ot2 < 2 else [out_evs[ot2 - 2]])
        si = next_slot()
        pt, off = pslots[si]
        for half in range(2):
            o2 = off if False else None
        si2 = next_slot()
        pt2, off2 = pslots[si2]
        dmm = []
        for (p_, o_, half) in ((pt, off, 0), (pt2, off2, 1)):
            t_mm(p_[:, o_:o_ + 512], b2rb[0:1, ot2 * 128:(ot2 + 1) * 128],
                 frow_h[0:1, half * 512: half * 512 + 512], True, False,
                 waits=[pslot_ev[si if half == 0 else si2], e_f3,
                        e_C1 if ot2 == 0 else None])
            t_mm(p_[:, o_:o_ + 512], b2rb[0:1, ot2 * 128:(ot2 + 1) * 128],
                 frow_l[0:1, half * 512: half * 512 + 512], False, False)
            t_mm(p_[:, o_:o_ + 512], onesr[0:1, :],
                 nb16[0:1, half * 512: half * 512 + 512], False, False)
            dmm.append(t_mm(p_[:, o_:o_ + 512], onesr[0:1, :],
                            nb16[0:1, 1024 + half * 512: 1024 + half * 512 + 512],
                            False, True, inc=True))

        def mk(eng, oo=ot2, pa=pt, oa=off, pb=pt2, ob=off2):
            eng.tensor_tensor(ftmp1[:, :], bf2r[:, oo * 1024:(oo + 1) * 1024],
                              bcC[:, :], ALU.mult)
            eng.tensor_tensor(ftmp1[:, 0:512], ftmp1[:, 0:512],
                              pa[:, oa:oa + 512], ALU.add)
            eng.tensor_tensor(ftmp1[:, 512:1024], ftmp1[:, 512:1024],
                              pb[:, ob:ob + 512], ALU.add)
            return eng.tensor_tensor(
                coutb[:, (oo % 2) * 1024:(oo % 2 + 1) * 1024], ftmp1[:, :],
                t1rd[:, (oo % 2) * 1024:(oo % 2 + 1) * 1024], ALU.add)
        e_o = v_op(mk, inc=True, waits=[trd, dmm[0], dmm[1], e_C1,
                                        raw_evs[2 * ot2 + 1]])
        pslot_ev[si] = e_o
        pslot_ev[si2] = e_o
        out_evs.append(e_o)
        y_dma(bass.AP(outT, ot2 * 128 * T, [[T, 128], [1, 1024]]),
              bass.AP(coutb, (ot2 % 2) * 1024, [[2048, 128], [1, 1024]]),
              waits=[e_o])

    wait("Y", ("ds", cnt["ds"]))

    with nc.Block() as block:
        @block.tensor
        def _(eng):
            for k, fn in ops:
                if k == "T":
                    fn(eng)

        @block.vector
        def _(eng):
            for k, fn in ops:
                if k == "V":
                    fn(eng)

        @block.scalar
        def _(eng):
            for k, fn in ops:
                if k == "S":
                    fn(eng)

        @block.gpsimd
        def _(eng):
            for k, fn in ops:
                if k == "G":
                    fn(eng)

        @block.sync
        def _(eng):
            for k, fn in ops:
                if k == "Y":
                    fn(eng)

    return nc


def kernel(**inputs):
    ip = {k: np.asarray(v) for k, v in inputs.items()}
    x = ip["L0_attn_hidden_states"]

    def tr(a):
        return np.ascontiguousarray(a.T)

    w = {
        "wdkv": tr(ip["L0_attn_W_DKV_w"]), "wuk": tr(ip["L0_attn_W_UK_w"]),
        "wuv": tr(ip["L0_attn_W_UV_w"]), "wdq": tr(ip["L0_attn_W_DQ_w"]),
        "wuq": tr(ip["L0_attn_W_UQ_w"]), "wo": tr(ip["L0_attn_W_O_w"]),
        "w1": tr(ip["L0_ffn_w1"]), "w2": tr(ip["L0_ffn_w2"]),
        "bdkv": np.ascontiguousarray(ip["L0_attn_W_DKV_b"]),
        "buk": np.ascontiguousarray(ip["L0_attn_W_UK_b"]),
        "buv": np.ascontiguousarray(ip["L0_attn_W_UV_b"]),
        "bdq": np.ascontiguousarray(ip["L0_attn_W_DQ_b"]),
        "buq": np.ascontiguousarray(ip["L0_attn_W_UQ_b"]),
        "bo": np.ascontiguousarray(ip["L0_attn_W_O_b"]),
        "b1": np.ascontiguousarray(ip["L0_ffn_b1"]),
        "b2": np.ascontiguousarray(ip["L0_ffn_b2"]),
    }
    xTb = [np.ascontiguousarray(x[b].T) for b in range(2)]
    nc = build_program()
    in_maps = []
    for c in range(8):
        b, qt = c // 4, c % 4
        im = dict(w)
        im["xT"] = xTb[b]
        im["xoT"] = np.ascontiguousarray(xTb[b][:, qt * 1024:(qt + 1) * 1024])
        in_maps.append(im)
    res = run_bass_kernel_spmd(nc, in_maps, core_ids=list(range(8)))
    out = np.empty((2, 4096, 2048), dtype=np.float32)
    for c in range(8):
        b, qt = c // 4, c % 4
        out[b, qt * 1024:(qt + 1) * 1024, :] = res.results[c]["outT"].T
    return out


# version 08
# speedup vs baseline: 1.2291x; 1.0024x over previous
import numpy as np
import concourse.bass as bass
import concourse.mybir as mybir
from concourse.bass_utils import run_bass_kernel_spmd

F32 = mybir.dt.float32
BF16 = mybir.dt.bfloat16
AF = mybir.ActivationFunctionType
ALU = mybir.AluOpType
AX = mybir.AxisListType

MAGIC = float(np.float32(1.5 * 2 ** 23))
EXP_SCALE = 1.0 / float(np.sqrt(128.0))
LN_EPS = 1e-5
RMS_EPS = 1e-6
Q_EPS = 1e-5
INV2048 = 1.0 / 2048.0
RSQ2048 = float(2048.0 ** -0.5)
RSQ8192 = float(8192.0 ** -0.5)

DM, KVC, QC, INNER = 2048, 512, 1536, 8192
S, T = 4096, 1024


class Regions:
    BASE = 17408

    def __init__(self, nc, limit=229376):
        self.nc = nc
        self.limit = limit
        self.items = []

    def add(self, name, off, shape, dt, phases):
        off = off + self.BASE
        esz = 2 if dt == BF16 else 4
        size = int(np.prod(shape[1:])) * esz
        assert off + size <= self.limit, f"{name}: {off}+{size} > {self.limit}"
        for (n2, o2, s2, p2) in self.items:
            if phases & p2 and not (off + size <= o2 or o2 + s2 <= off):
                raise AssertionError(f"overlap {name}[{off}:{off+size}] vs "
                                     f"{n2}[{o2}:{o2+s2}] phases {phases & p2}")
        self.items.append((name, off, size, phases))
        return self.nc.alloc_sbuf_tensor_at(name, shape, dt, offset=off)


def build_program(taps=()):
    nc = bass.Bass("TRN2", target_bir_lowering=False, detect_race_conditions=False)

    xT = nc.dram_tensor("xT", [DM, S], F32, kind="ExternalInput")
    xoT = nc.dram_tensor("xoT", [DM, T], F32, kind="ExternalInput")
    wdkv = nc.dram_tensor("wdkv", [DM, KVC], F32, kind="ExternalInput")
    wuk = nc.dram_tensor("wuk", [KVC, DM], F32, kind="ExternalInput")
    wuv = nc.dram_tensor("wuv", [KVC, DM], F32, kind="ExternalInput")
    wdq = nc.dram_tensor("wdq", [DM, QC], F32, kind="ExternalInput")
    wuq = nc.dram_tensor("wuq", [QC, DM], F32, kind="ExternalInput")
    wo = nc.dram_tensor("wo", [DM, DM], F32, kind="ExternalInput")
    w1 = nc.dram_tensor("w1", [DM, INNER], F32, kind="ExternalInput")
    w2 = nc.dram_tensor("w2", [INNER, DM], F32, kind="ExternalInput")
    biasd = {}
    for nm, n in (("bdkv", KVC), ("buk", DM), ("buv", DM), ("bdq", QC),
                  ("buq", DM), ("bo", DM), ("b1", INNER), ("b2", DM)):
        biasd[nm] = nc.dram_tensor(nm, [n], F32, kind="ExternalInput")
    outT = nc.dram_tensor("outT", [DM, T], F32, kind="ExternalOutput")
    t1sp = nc.dram_tensor("t1sp", [DM, T], F32, kind="Internal")
    tap_t = {nm: nc.dram_tensor("tap_" + nm, sh, F32 if nm in ("s12", "preacc", "dbg") else BF16,
                                 kind="ExternalOutput")
             for nm, sh in taps}

    R = Regions(nc)
    AB = frozenset("AB")
    xbf = R.add("xbf", 0, [128, 2 * 16 * 512], BF16, frozenset("A"))
    arow_f = R.add("arow_f", 32768, [1, 1024], F32, frozenset("A"))
    arb0 = R.add("arb0", 36864, [128, 512], F32, frozenset("A"))
    arb1 = R.add("arb1", 38912, [128, 512], F32, frozenset("A"))
    wdq_sb = R.add("wdq_sb", 40960, [128, 16 * 1536], BF16, frozenset("A"))
    wuq_sb = R.add("wuq_sb", 90112, [128, 2 * 12 * 128], BF16, frozenset("A"))
    wdkv_sb = R.add("wdkv_sb", 96256, [128, 16 * 512], BF16, frozenset("A"))
    cqT = R.add("cqT", 112640, [128, 12 * 512], BF16, frozenset("A"))
    asq = R.add("asq", 124928, [128, 512], BF16, frozenset("A"))
    atmp = R.add("atmp", 125952, [128, 512], F32, frozenset("A"))
    wsdq = R.add("wsdq", 128000, [1, 1536], BF16, frozenset("A"))
    arow_b_0 = R.add("arow_b_0", 196608 + 4608, [1, 512], BF16, frozenset("A"))
    arow_b_1 = R.add("arow_b_1", 196608 + 5632, [1, 512], BF16, frozenset("A"))
    arow_b2 = R.add("arow_b2", 196608 + 6656, [1, 512], BF16, frozenset("A"))
    arow_b2a = R.add("arow_b2a", 196608 + 7680, [1, 512], BF16, frozenset("A"))

    ktb = R.add("ktb", 0, [128, 2 * 8192], BF16, frozenset("B"))
    vtb = R.add("vtb", 32768, [128, 2 * 8192], BF16, frozenset("B"))
    prb = R.add("prb", 65536, [128, 2 * 1024], BF16, frozenset("B"))
    invb = R.add("invb", 69632, [128, 1024], F32, frozenset("B"))
    wukg = R.add("wukg", 73728, [128, 4 * 256], BF16, frozenset("B"))
    wuvg = R.add("wuvg", 75776, [128, 4 * 256], BF16, frozenset("B"))
    pre_f = R.add("pre_f", 77824, [128, 2 * 2048], F32, frozenset("B"))
    brow_f = R.add("brow_f", 94208, [1, 1024], F32, frozenset("B"))
    brow_h = R.add("brow_h", 196608 + 4608, [1, 1024], BF16, frozenset("B"))
    brow_l = R.add("brow_l", 196608 + 6656, [1, 1024], BF16, frozenset("B"))
    attn = R.add("attn", 98304, [128, 16 * 1024], BF16, frozenset("BP"))

    wo_sb = R.add("wo_sb", 73728, [128, 2 * 16 * 128], BF16, frozenset("P"))
    aoT = R.add("aoT", 163840, [128, 16 * 1024], BF16, frozenset("PL"))

    t1f = R.add("t1f", 0, [128, 16 * 1024], F32, frozenset("L"))
    lrow_f = R.add("lrow_f", 65536, [1, 1024], F32, frozenset("PL"))
    lrow_h = R.add("lrow_h", 69632, [1, 1024], BF16, frozenset("PL"))
    lrow_l = R.add("lrow_l", 71680, [1, 1024], BF16, frozenset("PL"))
    lrow_f2 = R.add("lrow_f2", 81920, [1, 1024], F32, frozenset("PL"))
    sqtmp = R.add("sqtmp", 86016, [128, 1024], BF16, frozenset("PL"))
    sqmax = R.add("sqmax", 88064, [128, 1024], F32, frozenset("L"))
    xob = R.add("xob", 92160, [128, 2 * 1024], F32, frozenset("L"))
    ltmp1 = R.add("ltmp1", 100352, [128, 1024], F32, frozenset("L"))
    ltmp2 = R.add("ltmp2", 104448, [128, 1024], F32, frozenset("L"))

    h2 = R.add("h2", 0, [128, 64 * 1024], BF16, frozenset(["C1", "C2"]))
    coutb = R.add("coutb", 0, [128, 2 * 1024], F32, frozenset("F"))
    t1rd = R.add("t1rd", 69632, [128, 4 * 1024], F32, frozenset("F"))
    frow_f = R.add("frow_f", 16384, [1, 1024], F32, frozenset("F"))
    frow_h = R.add("frow_h", 20480, [1, 1024], BF16, frozenset("F"))
    frow_l = R.add("frow_l", 22528, [1, 1024], BF16, frozenset("F"))
    fr_sraw = R.add("fr_sraw", 24576, [1, 1024], F32, frozenset(["F"]))
    fr_ssq = R.add("fr_ssq", 28672, [1, 1024], F32, frozenset(["F"]))
    fr_sb2r = R.add("fr_sb2r", 32768, [1, 1024], F32, frozenset(["F"]))
    fr_mp = R.add("fr_mp", 36864, [1, 1024], F32, frozenset("F"))
    fr_r3 = R.add("fr_r3", 40960, [1, 1024], F32, frozenset("F"))
    fr_c1 = R.add("fr_c1", 45056, [1, 1024], F32, frozenset("F"))
    nb16 = R.add("nb16", 49152, [1, 2048], BF16, frozenset("F"))
    ftmp1 = R.add("ftmp1", 57344, [128, 1024], F32, frozenset("F"))
    ftmp2 = R.add("ftmp2", 61440, [128, 1024], F32, frozenset("F"))
    frow_h2 = R.add("frow_h2", 65536, [1, 1024], BF16, frozenset("F"))
    frow_l2 = R.add("frow_l2", 67584, [1, 1024], BF16, frozenset("F"))

    R1 = 131072
    ckv = R.add("ckv", R1, [128, 4 * 4096], BF16, frozenset("AB"))
    t1b = R.add("t1b", R1, [128, 16 * 1024], BF16, frozenset("L"))
    w1db = R.add("w1db", R1, [128, 2 * 16 * 128], BF16, frozenset("C1"))
    crow_h = R.add("crow_h", R1, [1, 1024], BF16, frozenset(["Cq"]))
    crow_l = R.add("crow_l", R1 + 2048, [1, 1024], BF16, frozenset(["Cq"]))
    b2rb = R.add("b2rb", R1, [1, 2048], BF16, frozenset(["F"]))
    w2db = R.add("w2db", R1 + 8192, [128, 2 * 16 * 128], BF16, frozenset("C2"))
    csq = R.add("csq", R1 + 16384, [128, 1024], BF16, frozenset(["C1", "C2"]))
    csqmax = R.add("csqmax", R1 + 18432, [128, 1024], F32, frozenset(["C1", "Cq"]))
    ctmp1 = R.add("ctmp1", R1 + 22528, [128, 1024], F32,
                  frozenset(["C1", "Cq", "C2", "F"]))
    ctmp2 = R.add("ctmp2", R1 + 26624, [128, 512], F32, frozenset(["C2", "F"]))
    crow_f = R.add("crow_f", R1 + 28672, [1, 1024], F32, frozenset(["Cq"]))

    bf2r = R.add("bf2r", 163840, [128, 16 * 1024], BF16, frozenset(["C2", "F"]))
    qT = R.add("qT", 163840, [128, 16 * 1024], BF16, AB)

    bcA = R.add("bcA", 196608, [128, 1024], F32, frozenset(["L", "Cq", "C1"]))
    bcB = R.add("bcB", 200704, [128, 1024], F32,
                frozenset(["L", "Cq", "C1", "C2"]))
    bcC = R.add("bcC", 204800, [128, 1024], F32, frozenset(["F"]))
    ctmp1b = R.add("ctmp1b", R1 + 4096, [128, 1024], F32, frozenset(["Cq"]))
    dbgB = R.add("dbgB", 204800 - 4096, [128, 16], F32, frozenset(["Bdbg"]))
    qrow_f = R.add("qrow_f", 204800, [1, 1024], F32, frozenset(["L", "Cq"]))

    P0 = 208896
    onesr = R.add("onesr", P0, [1, 128], BF16, frozenset("ABPLC1C2F"))
    onesc = R.add("onesc", P0 + 256, [128, 1], BF16, frozenset("ABPLC1C2F"))
    wsdkv = R.add("wsdkv", P0 + 320, [1, 512], BF16, frozenset("A"))
    bias_sb = R.add("bias_sb", P0 + 1344, [128, 148], F32, frozenset("ABPLC1C2F"))
    s12 = R.add("s12", P0 + 1952, [128, 4], F32, frozenset("BPLC1C2F"))
    b2cb = R.add("b2cb", P0 + 1984, [128, 16], BF16, frozenset("ABPLC1C2F"))
    rowp_f = R.add("rowp_f", P0 + 2016, [1, 128], F32, frozenset("ABPLC1C2F"))
    pre_acc = R.add("pre_acc", P0 + 2528, [128, 2], F32, frozenset("AB"))
    fence_c = R.add("fence_c", P0 + 2624, [1, 8], F32,
                    frozenset("ABPLC1C2F"))
    pre_r = R.add("pre_r", P0 + 2560, [128, 2], F32, frozenset("AB"))
    prehl_b = R.add("prehl_b", P0 + 2592, [128, 4], BF16, frozenset("AB"))
    buvr = R.add("buvr", 196608, [1, 2048], BF16, frozenset("AB"))
    pre_s = R.add("pre_s", 196608 + 4096, [128, 128], F32, frozenset("AB"))


    BIG1 = nc.alloc_psum_tensor("BIG1", [128, 1024], F32)
    BIG2 = nc.alloc_psum_tensor("BIG2", [128, 1024], F32)
    BIG3 = nc.alloc_psum_tensor("BIG3", [128, 1024], F32)
    SM1 = nc.alloc_psum_tensor("SM1", [128, 512], F32)
    SM2 = nc.alloc_psum_tensor("SM2", [128, 512], F32)

    nc.const_aps.register(F32, MAGIC, s12[:, 2:3])
    nc.const_aps.register(F32, -MAGIC, s12[:, 3:4])

    st = nc.alloc_semaphore("st")
    sv = nc.alloc_semaphore("sv")
    sc = nc.alloc_semaphore("sc")
    ds = nc.alloc_semaphore("ds")
    NDG = 8
    dg = [nc.alloc_semaphore("dg%d" % i) for i in range(NDG)]
    NYS = 8
    ys = [nc.alloc_semaphore("ys%d" % i) for i in range(NYS)]

    ops = []
    EMIT = {k: (lambda fn, kk=k: ops.append((kk, fn))) for k in "TVSGY"}
    cnt = {"t": 0, "v": 0, "s": 0, "ds": 0}
    gslot, guse = [0], [0] * NDG
    yslot, yuse = [0], [0] * NYS
    last_wait = {}
    SEMH = {"t": st, "v": sv, "s": sc}

    def wait(ek, ev):
        if ev is None:
            return
        key, val = ev
        if val is None or val <= 0:
            return
        k = (ek, key)
        if last_wait.get(k, 0) >= val:
            return
        last_wait[k] = val
        if key.startswith("dg"):
            semh, tgt = dg[int(key[2:])], val * 16
        elif key.startswith("ys"):
            semh, tgt = ys[int(key[2:])], val * 16
        elif key == "ds":
            semh, tgt = ds, val * 16
        else:
            semh, tgt = SEMH[key], val
        EMIT[ek](lambda eng, s=semh, v=tgt: eng.wait_ge(s, v))

    def g_dma(dst_ap, src_ap, waits=()):
        slot = gslot[0]
        gslot[0] = (slot + 1) % NDG
        prev = guse[slot]
        guse[slot] = prev + 1
        for w in waits:
            wait("G", w)
        if prev > 0:
            wait("G", ("dg%d" % slot, prev))
        EMIT["G"](lambda eng, d=dst_ap, s=src_ap, sm=dg[slot]:
                  eng.dma_start(d, s).then_inc(sm, 16))
        return ("dg%d" % slot, guse[slot])

    def y_dma(dst_ap, src_ap, waits=(), slow=False):
        for w in waits:
            wait("Y", w)
        cnt["ds"] += 1
        EMIT["Y"](lambda eng, d=dst_ap, s=src_ap, sl=slow:
                  eng.dma_start(d, s, allow_slow_non_contiguous=sl).then_inc(ds, 16))
        return ("ds", cnt["ds"])

    def y_load(dst_ap, src_ap, waits=(), slow=False):
        slot = yslot[0]
        yslot[0] = (slot + 1) % NYS
        prev = yuse[slot]
        yuse[slot] = prev + 1
        for w in waits:
            wait("Y", w)
        if prev > 0:
            wait("Y", ("ys%d" % slot, prev))
        EMIT["Y"](lambda eng, d=dst_ap, s=src_ap, sm=ys[slot], sl=slow:
                  eng.dma_start(d, s, allow_slow_non_contiguous=sl).then_inc(sm, 16))
        return ("ys%d" % slot, yuse[slot])

    tap_names = {nm for nm, sh in taps}

    def tap(nm, src_t, ncols, ev):
        if nm in tap_names:
            y_dma(bass.AP(tap_t[nm], 0, [[ncols, 128], [1, ncols]]),
                  bass.AP(src_t, 0, [[ncols, 128], [1, ncols]]), waits=[ev])

    def t_mm(out_ap, lhs_ap, rhs_ap, start, stop, inc=False, waits=()):
        for w in waits:
            wait("T", w)
        if inc:
            cnt["t"] += 1
        EMIT["T"](lambda eng, o=out_ap, l=lhs_ap, r=rhs_ap, a=start, b=stop, i=inc:
                  (eng.matmul(o, l, r, start=a, stop=b).then_inc(st, 1) if i
                   else eng.matmul(o, l, r, start=a, stop=b)))
        return ("t", cnt["t"]) if inc else None

    def v_op(builder, inc=False, waits=()):
        for w in waits:
            wait("V", w)
        if inc:
            cnt["v"] += 1
        EMIT["V"](lambda eng, b=builder, i=inc:
                  (b(eng).then_inc(sv, 1) if i else b(eng)))
        return ("v", cnt["v"]) if inc else None

    def s_op(builder, inc=False, waits=()):
        for w in waits:
            wait("S", w)
        if inc:
            cnt["s"] += 1
        EMIT["S"](lambda eng, b=builder, i=inc:
                  (b(eng).then_inc(sc, 1) if i else b(eng)))
        return ("s", cnt["s"]) if inc else None

    def vfence(ev):
        return s_op(lambda eng: eng.copy(fence_c[0:1, 0:1], fence_c[0:1, 0:1]),
                    inc=True, waits=[ev])

    def sfence(ev):
        return v_op(lambda eng: eng.tensor_copy(fence_c[0:1, 1:2],
                                                fence_c[0:1, 1:2]),
                    inc=True, waits=[ev])

    def bcast(src_row_ap, n, psum_t, scr_f_t, scr_h_t, scr_l_t, waits=(),
              psum_off=0):
        def prep(eng):
            eng.tensor_copy(scr_h_t[0:1, 0:n], src_row_ap)
            eng.tensor_tensor(scr_f_t[0:1, 0:n], src_row_ap, scr_h_t[0:1, 0:n],
                              ALU.subtract)
            return eng.tensor_copy(scr_l_t[0:1, 0:n], scr_f_t[0:1, 0:n])
        pe = v_op(prep, inc=True, waits=waits)
        ev = None
        for c0 in range(0, n, 512):
            w = min(512, n - c0)
            t_mm(psum_t[:, psum_off + c0:psum_off + c0 + w], onesr[0:1, :],
                 scr_h_t[0:1, c0:c0 + w], True, False, waits=[pe])
            ev = t_mm(psum_t[:, psum_off + c0:psum_off + c0 + w], onesr[0:1, :],
                      scr_l_t[0:1, c0:c0 + w], False, True, inc=True)
        return ev

    def pfold(buf, scratch, n, op):
        def fn(eng, b=buf, s=scratch, nn=n, o=op):
            ins = None
            for p in (64, 32):
                eng.tensor_copy(s[0:p, 0:nn], b[p:2 * p, 0:nn])
                ins = eng.tensor_tensor(b[0:p, 0:nn], b[0:p, 0:nn], s[0:p, 0:nn], o)
            p = 16
            while p >= 1:
                mask = [(i + p) % 32 for i in range(32)]
                eng.stream_shuffle(s[0:32, 0:nn], b[0:32, 0:nn], mask)
                ins = eng.tensor_tensor(b[0:32, 0:nn], b[0:32, 0:nn], s[0:32, 0:nn], o)
                p //= 2
            return ins
        return fn

    def init_ones(eng):
        eng.memset(onesr[0:1, :], 1.0)
        eng.memset(fence_c[0:1, 0:8], 0.0)
        eng.memset(s12[:, 2:3], MAGIC)
        eng.memset(s12[:, 3:4], -MAGIC)
        return eng.memset(onesc[:, :], 1.0)
    ev_ones = v_op(init_ones, inc=True)
    ev_bias = None
    for nm, col, k in (("bdkv", 0, 4), ("buk", 4, 16), ("bdq", 20, 12),
                       ("buq", 32, 16), ("bo", 48, 16), ("b1", 64, 64),
                       ("b2", 128, 16)):
        ev_bias = y_load(bass.AP(bias_sb, col, [[148, 128], [1, k]]),
                         bass.AP(biasd[nm], 0, [[1, 128], [128, k]]), slow=True)
    e1 = y_load(arow_f[0:1, 0:1024], bass.AP(biasd["buv"], 0, [[1024, 1], [1, 1024]]))
    ec1 = v_op(lambda eng: eng.tensor_copy(buvr[0:1, 0:1024], arow_f[0:1, 0:1024]),
               inc=True, waits=[e1])
    e2 = y_load(arow_f[0:1, 0:1024],
                bass.AP(biasd["buv"], 1024, [[1024, 1], [1, 1024]]), waits=[ec1])
    ev_buvr = v_op(lambda eng: eng.tensor_copy(buvr[0:1, 1024:2048],
                                               arow_f[0:1, 0:1024]),
                   inc=True, waits=[e2])
    ev_b2cb = v_op(lambda eng: eng.tensor_copy(b2cb[:, :], bias_sb[:, 128:144]),
                   inc=True, waits=[ev_bias])
    ev_pre0 = v_op(lambda eng: eng.memset(fence_c[0:1, 4:6], 0.0), inc=True)

    def wload(dst_sb, dst_row, src_t, src_cols, itile0, ntiles, col0, ncols,
              dst_col0, waits=()):
        dst = bass.AP(dst_sb, dst_col0, [[dst_row, 128], [ncols, ntiles], [1, ncols]])
        src = bass.AP(src_t, itile0 * 128 * src_cols + col0,
                      [[src_cols, 128], [128 * src_cols, ntiles], [1, ncols]])
        return g_dma(dst, src, waits=waits)

    ev_wdkv = wload(wdkv_sb, 8192, wdkv, KVC, 0, 16, 0, KVC, 0)
    ev_wdq = None
    for g in range(4):
        ev_wdq = wload(wdq_sb, 16 * 1536, wdq, QC, 4 * g, 4, 0, QC, 4 * g * 1536)

    ev = None
    for k in range(16):
        ev = t_mm(SM1[0:1, 0:512], onesc[:, :], wdkv_sb[:, k * 512:(k + 1) * 512],
                  k == 0, k == 15, inc=(k == 15),
                  waits=[ev_wdkv, ev_ones] if k == 0 else ())
    ev_wsdkv = v_op(lambda eng: eng.tensor_copy(wsdkv[0:1, :], SM1[0:1, 0:512]),
                    inc=True, waits=[ev])
    rows3 = [(SM2, 0, 0), (BIG3, 0, 512), (BIG3, 512, 1024)]
    evs3 = []
    for (pt, off, dcol) in rows3:
        ev = None
        for k in range(16):
            ev = t_mm(pt[0:1, off:off + 512], onesc[:, :],
                      wdq_sb[:, k * 1536 + dcol: k * 1536 + dcol + 512],
                      k == 0, k == 15, inc=(k == 15),
                      waits=[ev_wdq] if k == 0 else ())
        evs3.append(ev)

    def cp_wsdq(eng):
        eng.tensor_copy(wsdq[0:1, 0:512], SM2[0:1, 0:512])
        eng.tensor_copy(wsdq[0:1, 512:1024], BIG3[0:1, 0:512])
        return eng.tensor_copy(wsdq[0:1, 1024:1536], BIG3[0:1, 512:1024])
    ev_wsdq = v_op(cp_wsdq, inc=True, waits=evs3)

    pslots = [(BIG1, 0), (BIG1, 512), (BIG2, 0), (BIG2, 512)]
    pslot_ev = [None] * 4
    psi = [0]

    def next_slot():
        i = psi[0] % 4
        psi[0] += 1
        return i

    prev_fin = [ev_wsdq]
    prev_rb = [ev_wsdq, ev_wsdq]
    prev_mm2 = [None]
    prev_atmp_rd = [None]

    ARB = [arb0, arb1]
    NEGM = [arow_b_0, arow_b_1]

    def chunk_stats(par, xoff, xev):
        sq_ev = None
        mm2 = prev_mm2[0]
        for k in range(16):
            sq_ev = s_op(lambda eng, kk=k: eng.activation(
                asq[:, 0:512], xbf[:, xoff + kk * 512: xoff + (kk + 1) * 512],
                AF.Square), inc=True, waits=[xev, mm2])
            t_mm(SM1[0:1, 0:512], onesc[:, :],
                 xbf[:, xoff + k * 512: xoff + (k + 1) * 512], k == 0, k == 15,
                 waits=[xev, prev_fin[0]] if k == 0 else ())
            mm2 = t_mm(SM2[0:1, 0:512], onesc[:, :], asq[:, 0:512],
                       k == 0, k == 15, inc=True, waits=[sq_ev])
        prev_mm2[0] = mm2

        def fin(eng):
            m = arow_f[0:1, 0:512]
            v2 = arow_f[0:1, 512:1024]
            eng.tensor_scalar(m, SM1[0:1, 0:512], INV2048, None, ALU.mult)
            eng.tensor_scalar(v2, SM2[0:1, 0:512], INV2048, None, ALU.mult)
            eng.tensor_scalar(atmp[0:1, 0:512], m, -1.0, None, ALU.mult)
            eng.tensor_copy(NEGM[par][0:1, 0:512], atmp[0:1, 0:512])
            eng.tensor_tensor(atmp[0:1, 0:512], m, m, ALU.mult)
            eng.tensor_tensor(v2, v2, atmp[0:1, 0:512], ALU.subtract)
            eng.tensor_scalar(v2, v2, LN_EPS, None, ALU.add)
            return eng.reciprocal(atmp[0:1, 0:512], v2)
        e_f = v_op(fin, inc=True, waits=[mm2, prev_atmp_rd[0]])
        prev_fin[0] = e_f
        e_r = s_op(lambda eng: eng.activation(arow_f[0:1, 0:512],
                                              atmp[0:1, 0:512], AF.Sqrt),
                   inc=True, waits=[e_f])
        ebc = bcast(arow_f[0:1, 0:512], 512, BIG3, atmp, arow_b2,
                    arow_b2a, waits=[e_r, prev_rb[par], prev_atmp_rd[0]],
                    psum_off=par * 512)
        e_rb = v_op(lambda eng, p=par: eng.tensor_copy(ARB[p][:, :],
                                                       BIG3[:, p * 512:(p + 1) * 512]),
                    inc=True, waits=[ebc])
        prev_rb[par] = e_rb
        return e_rb

    a_done = []
    a_last_s = [None]
    last_q_mm = [None]
    qw_prev = [None, None]

    def emit_fold_linear(par, xoff, n_ot, w_sb, w_tcols, ws_t, bias_col,
                         out_cb, e_rb):
        evs = []
        last_mm = [None]
        for ot in range(n_ot):
            si = next_slot()
            pt, off = pslots[si]
            t_mm(pt[:, off:off + 512], ws_t[0:1, ot * 128:(ot + 1) * 128],
                 NEGM[par][0:1, 0:512], True, False,
                 waits=[pslot_ev[si], e_rb if ot == 0 else None])
            for k in range(16):
                t_mm(pt[:, off:off + 512],
                     w_sb[:, k * w_tcols + ot * 128: k * w_tcols + (ot + 1) * 128],
                     xbf[:, xoff + k * 512: xoff + (k + 1) * 512],
                     False, k == 15, inc=(k == 15))
            mev = ("t", cnt["t"])
            last_mm[0] = mev
            e_v = v_op(lambda eng, p=pt, o=off, pr=par: eng.tensor_tensor(
                atmp[:, :], p[:, o:o + 512], ARB[pr][:, :], ALU.mult),
                inc=True, waits=[mev, prev_atmp_rd[0]])
            e_s = s_op(lambda eng, oo=ot, dst=out_cb(ot), bc=bias_col + ot:
                       eng.activation(dst, atmp[:, :], AF.Identity,
                                      bias=bias_sb[:, bc:bc + 1]),
                       inc=True, waits=[e_v])
            prev_atmp_rd[0] = e_s
            pslot_ev[si] = e_s
            evs.append(e_s)
        return evs, last_mm[0]

    consumed = [None] * 10
    xevs = [None] * 10

    def chunk_dma(i):
        par = i % 2
        w = []
        if i >= 2:
            w = [consumed[i - 2]]
        if i < 8:
            src = bass.AP(xT, i * 512, [[S, 128], [128 * S, 16], [1, 512]])
        else:
            src = bass.AP(xoT, (i - 8) * 512, [[T, 128], [128 * T, 16], [1, 512]])
        return g_dma(bass.AP(xbf, par * 8192, [[16384, 128], [512, 16], [1, 512]]),
                     src, waits=w)

    xevs[0] = chunk_dma(0)
    xevs[1] = chunk_dma(1)
    for c in range(8):
        par = c % 2
        xoff = par * 8192
        e_rb = chunk_stats(par, xoff, xevs[c])
        evs, lmm = emit_fold_linear(
            par, xoff, 4, wdkv_sb, 512, wsdkv, 0,
            lambda ot, cc=c: ckv[:, ot * 4096 + cc * 512: ot * 4096 + (cc + 1) * 512],
            e_rb)
        consumed[c] = lmm
        prev_rb[par] = evs[-1]
        a_done.extend(evs)
        a_last_s[0] = evs[-1]
        if c + 2 < 10:
            xevs[c + 2] = chunk_dma(c + 2)

    for oc in range(2):
        i = 8 + oc
        par = i % 2
        xoff = par * 8192
        e_rb = chunk_stats(par, xoff, xevs[i])
        cq_evs = []
        lmm = None
        for ot in range(12):
            si = next_slot()
            pt, off = pslots[si]
            t_mm(pt[:, off:off + 512], wsdq[0:1, ot * 128:(ot + 1) * 128],
                 NEGM[par][0:1, 0:512], True, False,
                 waits=[pslot_ev[si], e_rb if ot == 0 else None])
            for k in range(16):
                t_mm(pt[:, off:off + 512],
                     wdq_sb[:, k * 1536 + ot * 128: k * 1536 + (ot + 1) * 128],
                     xbf[:, xoff + k * 512: xoff + (k + 1) * 512],
                     False, k == 15, inc=(k == 15))
            mev = ("t", cnt["t"])
            lmm = mev
            e_v = v_op(lambda eng, p=pt, o=off, pr=par: eng.tensor_tensor(
                atmp[:, :], p[:, o:o + 512], ARB[pr][:, :], ALU.mult),
                inc=True, waits=[mev, prev_atmp_rd[0]])
            e_s = s_op(lambda eng, oo=ot: eng.activation(
                cqT[:, oo * 512:(oo + 1) * 512], atmp[:, :], AF.Identity,
                bias=bias_sb[:, 20 + oo:21 + oo]), inc=True,
                waits=[e_v, last_q_mm[0] if ot == 0 else None])
            prev_atmp_rd[0] = e_s
            pslot_ev[si] = e_s
            cq_evs.append(e_s)
        consumed[i] = lmm
        prev_rb[par] = cq_evs[-1]
        for ot in range(16):
            wev = wload(wuq_sb, 2 * 12 * 128, wuq, DM, 0, 12, ot * 128, 128,
                        (ot % 2) * 12 * 128, waits=[qw_prev[ot % 2]])
            si = next_slot()
            pt, off = pslots[si]
            for k in range(12):
                t_mm(pt[:, off:off + 512],
                     wuq_sb[:, (ot % 2) * 1536 + k * 128:(ot % 2) * 1536 + (k + 1) * 128],
                     cqT[:, k * 512:(k + 1) * 512], k == 0, k == 11,
                     inc=(k == 11),
                     waits=[wev, pslot_ev[si], cq_evs[k]] if k == 0 else [cq_evs[k]])
            mev = ("t", cnt["t"])
            qw_prev[ot % 2] = mev
            e_s = s_op(lambda eng, p=pt, o=off, oo=ot, occ=oc: eng.activation(
                qT[:, oo * 1024 + occ * 512: oo * 1024 + occ * 512 + 512],
                p[:, o:o + 512], AF.Identity, bias=bias_sb[:, 32 + oo:33 + oo]),
                inc=True, waits=[mev])
            pslot_ev[si] = e_s
            a_last_s[0] = e_s
        last_q_mm[0] = qw_prev[15 % 2]
    a_last = a_last_s[0]
    tap("ckv", ckv, 16384, a_last)
    tap("qT", qT, 16384, a_last)

    kv_wprev = [None, None]

    pre_jobs = [(0, w1, ch) for ch in range(64)] + [(1, w2, ch) for ch in range(64)]
    pre_ptr = [0]
    pre_vev = [a_last, a_last]

    def emit_prestream(n):
        for _ in range(n):
            if pre_ptr[0] >= len(pre_jobs):
                return
            wi, wt, ch = pre_jobs[pre_ptr[0]]
            slot = pre_ptr[0] % 2
            pre_ptr[0] += 1
            dev = y_load(bass.AP(pre_f, slot * 2048, [[4096, 128], [1, 2048]]),
                         bass.AP(wt, ch * 128 * 2048, [[2048, 128], [1, 2048]]),
                         waits=[pre_vev[slot]])

            col = wi * 64 + ch
            pre_vev[slot] = v_op(
                lambda eng, sl=slot, c=col: eng.tensor_reduce(
                    pre_s[:, c:c + 1], pre_f[:, sl * 2048:(sl + 1) * 2048],
                    AX.X, ALU.add, apply_absolute_value=True),
                inc=True, waits=[dev])

    def emit_kvgen(g, first_waits):
        sl = g % 2
        wk = wload(wukg, 4 * 256, wuk, DM, 0, 4, g * 256, 256, 0,
                   waits=(list(first_waits) + [kv_wprev[0]]) if kv_wprev[0]
                   else list(first_waits))
        wv = wload(wuvg, 4 * 256, wuv, DM, 0, 4, g * 256, 256, 0,
                   waits=[kv_wprev[1]] if kv_wprev[1] else [])
        kout = []
        for hh in range(2):
            h = 2 * g + hh
            for cc in range(8):
                off = 0 if len(kout) % 2 == 0 else 512
                for j in range(4):
                    if j == 0:
                        w = ([kout[-2]] if len(kout) >= 2 else
                             list(first_waits) + [wk])
                    else:
                        w = []
                    t_mm(BIG2[:, off:off + 512],
                         wukg[:, j * 256 + hh * 128: j * 256 + (hh + 1) * 128],
                         ckv[:, j * 4096 + cc * 512: j * 4096 + (cc + 1) * 512],
                         j == 0, j == 3, inc=(j == 3), waits=w)
                mev = ("t", cnt["t"])
                if hh == 1 and cc == 7:
                    kv_wprev[0] = mev
                kout.append(v_op(
                    lambda eng, o=off, c2=cc, s=sl, h2_=hh, hb=h:
                    eng.tensor_scalar(
                        ktb[:, s * 8192 + h2_ * 4096 + c2 * 512:
                            s * 8192 + h2_ * 4096 + (c2 + 1) * 512],
                        BIG2[:, o:o + 512], bias_sb[:, 4 + hb:5 + hb],
                        None, ALU.add),
                    inc=True, waits=[mev]))
        vout = []
        for tt in range(32):
            off = 0 if tt % 2 == 0 else 512
            w = (list(first_waits) + [wv, ev_buvr]) if tt < 2 else [vout[tt - 2]]
            t_mm(BIG3[:, off:off + 256], onesr[0:1, :],
                 buvr[0:1, g * 256:(g + 1) * 256], True, False, waits=w)
            for j in range(4):
                t_mm(BIG3[:, off:off + 256],
                     ckv[:, j * 4096 + tt * 128: j * 4096 + (tt + 1) * 128],
                     wuvg[:, j * 256:(j + 1) * 256],
                     False, j == 3, inc=(j == 3))
            mev = ("t", cnt["t"])
            if tt == 31:
                kv_wprev[1] = mev
            vout.append(v_op(lambda eng, o=off, t2=tt, s=sl:
                             eng.tensor_copy(
                                 vtb[:, s * 8192 + t2 * 256: s * 8192 + (t2 + 1) * 256],
                                 BIG3[:, o:o + 256]),
                             inc=True, waits=[mev]))
        return kout, vout

    head_kv = [emit_kvgen(0, [a_last])]
    attn_ev = []
    prev_inv = [prev_fin[0]]
    for h in range(16):
        g, hh = h // 2, h % 2
        k_evs, v_evs = head_kv[g]
        sl = g % 2
        exp_a = [None] * 32
        exp_b = [None] * 32
        pv_evs = [None] * 32
        qk_evs = [None] * 32

        def emit_pvden(j):
            first = j == 0
            pa = [exp_a[j]]
            if first:
                pa.append(attn_ev[-1] if attn_ev else a_last)
            t_mm(BIG3[:, 0:512],
                 vtb[:, sl * 8192 + j * 256 + hh * 128:
                     sl * 8192 + j * 256 + (hh + 1) * 128],
                 prb[:, (j % 2) * 1024:(j % 2) * 1024 + 512],
                 first, j == 31, waits=pa)
            t_mm(BIG3[:, 512:1024],
                 vtb[:, sl * 8192 + j * 256 + hh * 128:
                     sl * 8192 + j * 256 + (hh + 1) * 128],
                 prb[:, (j % 2) * 1024 + 512:(j % 2 + 1) * 1024],
                 first, j == 31, waits=[exp_b[j]])
            t_mm(SM1[0:1, 0:512], onesc[:, :],
                 prb[:, (j % 2) * 1024:(j % 2) * 1024 + 512],
                 first, j == 31, waits=[prev_inv[0]] if first else ())
            pv_evs[j] = t_mm(SM2[0:1, 0:512], onesc[:, :],
                             prb[:, (j % 2) * 1024 + 512:(j % 2 + 1) * 1024],
                             first, j == 31, inc=True)

        for kt in range(32):
            spt = BIG1 if kt % 2 == 0 else BIG2
            if kt < 2:
                w0 = [k_evs[hh * 8 + 7], v_evs[31]]
                if attn_ev:
                    w0.append(attn_ev[-1])
            else:
                w0 = [exp_b[kt - 2]]
            t_mm(spt[:, 0:512],
                 ktb[:, sl * 8192 + hh * 4096 + kt * 128:
                     sl * 8192 + hh * 4096 + (kt + 1) * 128],
                 qT[:, h * 1024: h * 1024 + 512], True, True, waits=w0)
            qk_evs[kt] = t_mm(spt[:, 512:1024],
                              ktb[:, sl * 8192 + hh * 4096 + kt * 128:
                                  sl * 8192 + hh * 4096 + (kt + 1) * 128],
                              qT[:, h * 1024 + 512:(h + 1) * 1024], True, True,
                              inc=True)
            if kt >= 1:
                emit_pvden(kt - 1)
            exp_a[kt] = s_op(
                lambda eng, p=spt, k2=kt: eng.activation(
                    prb[:, (k2 % 2) * 1024:(k2 % 2) * 1024 + 512],
                    p[:, 0:512], AF.Exp, scale=EXP_SCALE),
                inc=True,
                waits=[qk_evs[kt], pv_evs[kt - 2] if kt >= 2 else None])
            exp_b[kt] = s_op(
                lambda eng, p=spt, k2=kt: eng.activation(
                    prb[:, (k2 % 2) * 1024 + 512:(k2 % 2 + 1) * 1024],
                    p[:, 512:1024], AF.Exp, scale=EXP_SCALE),
                inc=True)
        emit_pvden(31)
        exp_evs = exp_b

        def inv_prep(eng):
            eng.reciprocal(brow_f[0:1, 0:512], SM1[0:1, 0:512])
            return eng.reciprocal(brow_f[0:1, 512:1024],
                                              SM2[0:1, 0:512])
        e_inv = v_op(inv_prep, inc=True, waits=[pv_evs[31]])
        prev_inv[0] = e_inv
        ebc = bcast(brow_f[0:1, 0:1024], 1024, BIG1, brow_f,
                    brow_h, brow_l, waits=[e_inv, exp_evs[30]])
        e_cp = v_op(lambda eng: eng.tensor_copy(invb[:, :], BIG1[:, 0:1024]),
                    inc=True, waits=[ebc])
        e_at = v_op(lambda eng, hh=h: eng.tensor_tensor(
            attn[:, hh * 1024:(hh + 1) * 1024], BIG3[:, 0:1024], invb[:, :],
            ALU.mult), inc=True, waits=[e_cp])
        attn_ev.append(e_at)
        if h < 15 and hh == 1:
            head_kv.append(emit_kvgen(g + 1, [exp_evs[31], e_at]))
        emit_prestream(8)
    emit_prestream(len(pre_jobs))

    tap("attn", attn, 16384, attn_ev[15])

    e_pv = s_op(lambda eng: eng.copy(fence_c[0:1, 0:1], fence_c[0:1, 0:1]),
                inc=True, waits=[pre_vev[0], pre_vev[1]])
    e_red = v_op(lambda eng: (eng.tensor_reduce(pre_acc[:, 0:1], pre_s[:, 0:64],
                                                AX.X, ALU.add),
                              eng.tensor_reduce(pre_acc[:, 1:2], pre_s[:, 64:128],
                                                AX.X, ALU.add))[-1],
                 inc=True, waits=[e_pv, attn_ev[15]])
    e_h1 = v_op(lambda eng: (eng.tensor_copy(prehl_b[:, 0:1], pre_acc[:, 0:1]),
                             eng.tensor_copy(prehl_b[:, 1:2], pre_acc[:, 1:2]))[-1],
                inc=True, waits=[vfence(e_red)])
    e_fh1 = vfence(e_h1)
    e_l1 = s_op(lambda eng: (eng.activation(pre_r[:, 0:1], prehl_b[:, 0:1],
                                            AF.Identity, scale=-1.0,
                                            bias=pre_acc[:, 0:1]),
                             eng.activation(pre_r[:, 1:2], prehl_b[:, 1:2],
                                            AF.Identity, scale=-1.0,
                                            bias=pre_acc[:, 1:2]))[-1],
                inc=True, waits=[e_fh1])
    e_l1b = v_op(lambda eng: (eng.tensor_copy(prehl_b[:, 2:3], pre_r[:, 0:1]),
                              eng.tensor_copy(prehl_b[:, 3:4], pre_r[:, 1:2]))[-1],
                 inc=True, waits=[sfence(e_l1)])
    e_sum = None
    for c in range(2):
        t_mm(SM1[0:1, c:c + 1], onesc[:, :], prehl_b[:, c:c + 1], True, False,
             waits=[e_l1b, prev_inv[0]] if c == 0 else ())
        e_sum = t_mm(SM1[0:1, c:c + 1], onesc[:, :], prehl_b[:, 2 + c:3 + c],
                     False, True, inc=True)
    e_rp = v_op(lambda eng: (eng.tensor_scalar(rowp_f[0:1, 0:1], SM1[0:1, 0:1],
                                               1.0 / 16777216.0, None, ALU.mult),
                             eng.tensor_scalar(rowp_f[0:1, 1:2], SM1[0:1, 1:2],
                                               1.0 / 16777216.0, None, ALU.mult))[-1],
                inc=True, waits=[e_sum])
    e_bh = s_op(lambda eng: (eng.copy(brow_h[0:1, 0:1], rowp_f[0:1, 0:1]),
                             eng.copy(brow_h[0:1, 1:2], rowp_f[0:1, 1:2]))[-1],
                inc=True, waits=[vfence(e_rp)])
    e_bl = v_op(lambda eng: (eng.tensor_tensor(rowp_f[0:1, 2:3], rowp_f[0:1, 0:1],
                                               brow_h[0:1, 0:1], ALU.subtract),
                             eng.tensor_tensor(rowp_f[0:1, 3:4], rowp_f[0:1, 1:2],
                                               brow_h[0:1, 1:2], ALU.subtract))[-1],
                inc=True, waits=[sfence(e_bh)])
    e_blc = s_op(lambda eng: (eng.copy(brow_l[0:1, 0:1], rowp_f[0:1, 2:3]),
                              eng.copy(brow_l[0:1, 1:2], rowp_f[0:1, 3:4]))[-1],
                 inc=True, waits=[vfence(e_bl)])
    e_s12m = None
    for c in range(2):
        t_mm(SM2[:, c:c + 1], onesr[0:1, :], brow_h[0:1, c:c + 1], True, False,
             waits=[e_blc] if c == 0 else ())
        e_s12m = t_mm(SM2[:, c:c + 1], onesr[0:1, :], brow_l[0:1, c:c + 1],
                      False, True, inc=True)

    def cp_s12(eng):
        eng.tensor_copy(s12[:, 0:1], SM2[:, 0:1])
        return eng.tensor_copy(s12[:, 1:2], SM2[:, 1:2])
    e_s12 = v_op(cp_s12, inc=True, waits=[e_s12m])
    if "dbg" in tap_names:
        def cp_dbg(eng):
            eng.memset(dbgB[:, :], 0.0)
            for c in range(4):
                eng.tensor_copy(dbgB[:, c:c + 1], prehl_b[:, c:c + 1])
            eng.tensor_copy(dbgB[0:1, 4:5], SM1[0:1, 0:1])
            eng.tensor_copy(dbgB[0:1, 5:6], SM1[0:1, 1:2])
            eng.tensor_copy(dbgB[0:1, 6:7], rowp_f[0:1, 0:1])
            eng.tensor_copy(dbgB[0:1, 7:8], rowp_f[0:1, 1:2])
            eng.tensor_copy(dbgB[0:1, 8:9], brow_h[0:1, 0:1])
            eng.tensor_copy(dbgB[0:1, 9:10], brow_h[0:1, 1:2])
            eng.tensor_copy(dbgB[0:1, 10:11], brow_l[0:1, 0:1])
            eng.tensor_copy(dbgB[0:1, 11:12], brow_l[0:1, 1:2])
            eng.tensor_copy(dbgB[:, 12:13], SM2[:, 0:1])
            return eng.tensor_copy(dbgB[:, 13:14], SM2[:, 1:2])
        e_dbg = v_op(cp_dbg, inc=True, waits=[e_s12])
        y_dma(bass.AP(tap_t["dbg"], 0, [[16, 128], [1, 16]]), dbgB[:, :],
              waits=[e_dbg])

    op_evs = []
    stat_mm = [None]
    wo_prev = [None, None]
    for ot in range(16):
        wev = wload(wo_sb, 2 * 16 * 128, wo, DM, 0, 16, ot * 128, 128,
                    (ot % 2) * 2048,
                    waits=[wo_prev[ot % 2]] if ot >= 2 else
                    [pre_vev[0], pre_vev[1], attn_ev[15], e_s12])
        for half in range(2):
            si = next_slot()
            pt, off = pslots[si]
            for k in range(16):
                w = [wev, pslot_ev[si]] if k == 0 else ()
                t_mm(pt[:, off:off + 512],
                     wo_sb[:, (ot % 2) * 2048 + k * 128:(ot % 2) * 2048 + (k + 1) * 128],
                     attn[:, k * 1024 + half * 512: k * 1024 + half * 512 + 512],
                     k == 0, k == 15, inc=(k == 15), waits=w)
            mev = ("t", cnt["t"])
            if half == 1:
                wo_prev[ot % 2] = mev
            e_s = s_op(lambda eng, p=pt, o=off, oo=ot, hf=half: eng.activation(
                aoT[:, oo * 1024 + hf * 512: oo * 1024 + hf * 512 + 512],
                p[:, o:o + 512], AF.Identity, bias=bias_sb[:, 48 + oo:49 + oo]),
                inc=True, waits=[mev])
            pslot_ev[si] = e_s
            op_evs.append(e_s)
        e_sq = s_op(lambda eng, oo=ot: eng.activation(
            sqtmp[:, 0:1024], aoT[:, oo * 1024:(oo + 1) * 1024], AF.Square),
            inc=True, waits=[op_evs[-1], stat_mm[0]])
        t_mm(SM1[0:1, 0:512], onesc[:, :], aoT[:, ot * 1024: ot * 1024 + 512],
             ot == 0, ot == 15,
             waits=[op_evs[-1], e_s12] if ot == 0 else [op_evs[-1]])
        t_mm(SM2[0:1, 0:512], onesc[:, :],
             aoT[:, ot * 1024 + 512:(ot + 1) * 1024], ot == 0, ot == 15)
        t_mm(BIG3[0:1, 0:512], onesc[:, :], sqtmp[:, 0:512], ot == 0, ot == 15,
             waits=[e_sq, attn_ev[15]] if ot == 0 else [e_sq])
        stat_mm[0] = t_mm(BIG3[0:1, 512:1024], onesc[:, :], sqtmp[:, 512:1024],
                          ot == 0, ot == 15, inc=True)

    def ln_fin(n_inv, eps, r_out, mr_out, waits):
        def fn(eng):
            m = mr_out
            eng.tensor_scalar(m[0:1, 0:512], SM1[0:1, 0:512], n_inv, None, ALU.mult)
            eng.tensor_scalar(m[0:1, 512:1024], SM2[0:1, 0:512], n_inv, None, ALU.mult)
            e2 = lrow_f2[0:1, 0:1024]
            eng.tensor_copy(e2[0:1, 0:512], BIG3[0:1, 0:512])
            eng.tensor_copy(e2[0:1, 512:1024], BIG3[0:1, 512:1024])
            eng.tensor_scalar(e2, e2, n_inv, None, ALU.mult)
            eng.tensor_tensor(r_out, m, m, ALU.mult)
            eng.tensor_tensor(e2, e2, r_out, ALU.subtract)
            eng.tensor_scalar(e2, e2, eps, None, ALU.add)
            return eng.reciprocal(r_out, e2)
        e1 = v_op(fn, inc=True, waits=waits)
        e2 = s_op(lambda eng: eng.activation(r_out, r_out, AF.Sqrt),
                  inc=True, waits=[e1])
        e3 = v_op(lambda eng: eng.tensor_tensor(mr_out, mr_out, r_out, ALU.mult),
                  inc=True, waits=[e2])
        return e3

    r1r = lrow_f[0:1, 0:1024]
    mr1r = ltmp1[0:1, 0:1024]
    e_ln1b = ln_fin(INV2048, LN_EPS, r1r, mr1r, [stat_mm[0]])
    ebc = bcast(r1r, 1024, BIG3, lrow_f2, lrow_h, lrow_l,
                waits=[e_ln1b, stat_mm[0]])
    e_rb1 = v_op(lambda eng: eng.tensor_copy(bcA[:, :], BIG3[:, 0:1024]),
                 inc=True, waits=[ebc])
    ebc = bcast(mr1r, 1024, BIG3, lrow_f2, lrow_h, lrow_l, waits=[e_rb1])
    e_mrb1 = v_op(lambda eng: eng.tensor_copy(bcB[:, :], BIG3[:, 0:1024]),
                  inc=True, waits=[ebc])

    t1_evs = []
    sp_evs = []
    for ft in range(16):
        dev = y_load(bass.AP(xob, (ft % 2) * 1024, [[2048, 128], [1, 1024]]),
                     bass.AP(xoT, ft * 128 * T, [[T, 128], [1, 1024]]),
                     waits=[e_mrb1] if ft < 2 else [t1_evs[ft - 2]])

        def mk(eng, f=ft):
            eng.tensor_tensor(ltmp2[:, :], aoT[:, f * 1024:(f + 1) * 1024],
                              bcA[:, :], ALU.mult)
            eng.tensor_tensor(ltmp2[:, :], ltmp2[:, :], bcB[:, :], ALU.subtract)
            eng.tensor_tensor(t1f[:, f * 1024:(f + 1) * 1024], ltmp2[:, :],
                              xob[:, (f % 2) * 1024:(f % 2 + 1) * 1024], ALU.add)
            return eng.tensor_copy(t1b[:, f * 1024:(f + 1) * 1024],
                                   t1f[:, f * 1024:(f + 1) * 1024])
        e_t1 = v_op(mk, inc=True, waits=[dev, e_mrb1])
        t1_evs.append(e_t1)
        sp_evs.append(y_load(bass.AP(t1sp, ft * 128 * T, [[T, 128], [1, 1024]]),
                             bass.AP(t1f, ft * 1024, [[16384, 128], [1, 1024]]),
                             waits=[e_t1]))

    tap("t1b", t1b, 16384, t1_evs[15])

    stat2 = [None]
    for ft in range(16):
        e_sq = s_op(lambda eng, f=ft: eng.activation(
            sqtmp[:, 0:1024], t1b[:, f * 1024:(f + 1) * 1024], AF.Square),
            inc=True, waits=[t1_evs[ft], stat2[0]])
        t_mm(SM1[0:1, 0:512], onesc[:, :], t1b[:, ft * 1024: ft * 1024 + 512],
             ft == 0, ft == 15,
             waits=[t1_evs[ft], e_mrb1] if ft == 0 else [t1_evs[ft]])
        t_mm(SM2[0:1, 0:512], onesc[:, :],
             t1b[:, ft * 1024 + 512:(ft + 1) * 1024], ft == 0, ft == 15)
        t_mm(BIG3[0:1, 0:512], onesc[:, :], sqtmp[:, 0:512], ft == 0, ft == 15,
             waits=[e_sq, e_mrb1] if ft == 0 else [e_sq])
        stat2[0] = t_mm(BIG3[0:1, 512:1024], onesc[:, :], sqtmp[:, 512:1024],
                        ft == 0, ft == 15, inc=True)
    r2r = lrow_f[0:1, 0:1024]
    mr2r = ltmp1[0:1, 0:1024]
    e_ln2 = ln_fin(INV2048, LN_EPS, r2r, mr2r, [stat2[0], e_rb1])
    ebc = bcast(r2r, 1024, BIG3, lrow_f2, lrow_h, lrow_l, waits=[e_ln2])
    e_rb2 = v_op(lambda eng: eng.tensor_copy(bcA[:, :], BIG3[:, 0:1024]),
                 inc=True, waits=[ebc, t1_evs[15]])
    ebc = bcast(mr2r, 1024, BIG3, lrow_f2, lrow_h, lrow_l, waits=[e_rb2])
    e_mrb2 = v_op(lambda eng: eng.tensor_copy(bcB[:, :], BIG3[:, 0:1024]),
                  inc=True, waits=[ebc])

    e_sqm0 = v_op(lambda eng: eng.memset(sqmax[:, :], 0.0), inc=True,
                  waits=[e_mrb2])
    ln2_evs = []
    rms_mm = [None]
    mx_prev = [e_sqm0]
    for ft in range(16):
        def mk(eng, f=ft):
            eng.tensor_tensor(ltmp2[:, :], t1f[:, f * 1024:(f + 1) * 1024],
                              bcA[:, :], ALU.mult)
            eng.tensor_tensor(ltmp2[:, :], ltmp2[:, :], bcB[:, :], ALU.subtract)
            return eng.tensor_copy(aoT[:, f * 1024:(f + 1) * 1024], ltmp2[:, :])
        e_l = v_op(mk, inc=True, waits=[e_mrb2, sp_evs[ft]])
        e_sq = s_op(lambda eng, f=ft: eng.activation(
            sqtmp[:, 0:1024], aoT[:, f * 1024:(f + 1) * 1024], AF.Square),
            inc=True, waits=[e_l, rms_mm[0], mx_prev[0]])
        t_mm(SM1[0:1, 0:512], onesc[:, :], sqtmp[:, 0:512], ft == 0, ft == 15,
             waits=[e_sq, e_mrb2] if ft == 0 else [e_sq])
        rms_mm[0] = t_mm(SM2[0:1, 0:512], onesc[:, :], sqtmp[:, 512:1024],
                         ft == 0, ft == 15, inc=True)
        mx_prev[0] = v_op(lambda eng: eng.tensor_tensor(
            sqmax[:, :], sqmax[:, :], sqtmp[:, 0:1024], ALU.max),
            inc=True, waits=[e_sq, mx_prev[0]])
        ln2_evs.append(e_l)

    def quant_rows(n_inv, rsqn, rms_mm_ev, fold_ev, bcast_wait):
        def r1(eng):
            t = lrow_f2[0:1, 0:1024]
            eng.tensor_scalar(t[0:1, 0:512], SM1[0:1, 0:512], n_inv, None, ALU.mult)
            eng.tensor_scalar(t[0:1, 512:1024], SM2[0:1, 0:512], n_inv, None, ALU.mult)
            eng.tensor_scalar(t, t, RMS_EPS, None, ALU.add)
            return eng.reciprocal(qrow_f[0:1, 0:1024], t)
        e1 = v_op(r1, inc=True, waits=[rms_mm_ev, fold_ev])
        e2 = s_op(lambda eng: eng.activation(qrow_f[0:1, 0:1024],
                                             qrow_f[0:1, 0:1024], AF.Sqrt),
                  inc=True, waits=[e1])
        return e2

    e_q = quant_rows(INV2048, RSQ2048, rms_mm[0],
                     v_op(pfold(sqmax, ltmp2, 1024, ALU.max), inc=True,
                          waits=[mx_prev[0], ln2_evs[15]]), None)
    e_mxr = v_op(lambda eng: eng.tensor_copy(lrow_f[0:1, 0:1024],
                                             sqmax[0:1, 0:1024]),
                 inc=True, waits=[e_q])
    e_mxs = s_op(lambda eng: eng.activation(lrow_f[0:1, 0:1024],
                                            lrow_f[0:1, 0:1024], AF.Sqrt),
                 inc=True, waits=[e_mxr])

    def q_rows3(eng, rsqn=RSQ2048):
        f = qrow_f[0:1, 0:1024]
        mx = lrow_f[0:1, 0:1024]
        eng.tensor_scalar(f, f, rsqn, None, ALU.mult)
        eng.tensor_tensor(mx, mx, f, ALU.mult)
        eng.tensor_scalar(mx, mx, Q_EPS, None, ALU.max)
        eng.reciprocal(lrow_f2[0:1, 0:1024], mx)
        eng.tensor_scalar(lrow_f2[0:1, 0:1024], lrow_f2[0:1, 0:1024], 127.0,
                          None, ALU.mult)
        eng.tensor_tensor(f, f, lrow_f2[0:1, 0:1024], ALU.mult)
        return eng.tensor_scalar(mx, mx, 1.0 / 127.0, None, ALU.mult)
    e_q3 = v_op(q_rows3, inc=True, waits=[e_mxs])
    ebc = bcast(qrow_f[0:1, 0:1024], 1024, BIG3, lrow_f2, lrow_h, lrow_l,
                waits=[e_q3])
    e_A1 = v_op(lambda eng: eng.tensor_copy(bcA[:, :], BIG3[:, 0:1024]),
                inc=True, waits=[ebc])
    ebc = bcast(lrow_f[0:1, 0:1024], 1024, BIG3, lrow_f2, lrow_h, lrow_l,
                waits=[e_A1])
    e_B1 = v_op(lambda eng: eng.tensor_copy(bcB[:, :], BIG3[:, 0:1024]),
                inc=True, waits=[ebc])

    xq_evs = []
    LT = [ltmp2, ltmp1]
    pendq = None
    for ft in range(16):
        tb = LT[ft % 2]
        e_m = v_op(lambda eng, f=ft, t=tb: eng.tensor_tensor(
            t[:, :], aoT[:, f * 1024:(f + 1) * 1024], bcA[:, :], ALU.mult),
            inc=True, waits=[e_B1, xq_evs[ft - 2] if ft >= 2 else None])
        e_r = s_op(lambda eng, t=tb: (eng.activation(t[:, :], t[:, :],
                                                     AF.Identity, bias=MAGIC),
                                      eng.activation(t[:, :], t[:, :],
                                                     AF.Identity, bias=-MAGIC))[-1],
                   inc=True, waits=[e_m])
        if pendq is not None:
            pf, ptb, per = pendq
            xq_evs.append(v_op(lambda eng, f=pf, t=ptb: eng.tensor_tensor(
                aoT[:, f * 1024:(f + 1) * 1024], t[:, :], bcB[:, :], ALU.mult),
                inc=True, waits=[per]))
        pendq = (ft, tb, e_r)
    pf, ptb, per = pendq
    xq_evs.append(v_op(lambda eng, f=pf, t=ptb: eng.tensor_tensor(
        aoT[:, f * 1024:(f + 1) * 1024], t[:, :], bcB[:, :], ALU.mult),
        inc=True, waits=[per]))

    tap("xq", aoT, 16384, xq_evs[15])
    if "s12" in tap_names:
        y_dma(bass.AP(tap_t["s12"], 0, [[2, 128], [1, 2]]), s12[:, 0:2],
              waits=[e_s12])
    if "preacc" in tap_names:
        y_dma(bass.AP(tap_t["preacc"], 0, [[2, 128], [1, 2]]), pre_acc[:, :],
              waits=[e_h1])

    sg1_prev = [None, None]
    h2_evs = []
    rms2_mm = [None]
    mx2_prev = [None]
    e_sqm2 = v_op(lambda eng: eng.memset(csqmax[:, :], 0.0), inc=True,
                  waits=[e_B1])
    mx2_prev[0] = e_sqm2
    sq_prev = [None]
    for ot in range(64):
        wev = wload(w1db, 2 * 16 * 128, w1, INNER, 0, 16, ot * 128, 128,
                    (ot % 2) * 2048,
                    waits=[sg1_prev[ot % 2]] if ot >= 2 else
                    ([xq_evs[15]] + sp_evs[8:]))
        e_sg = v_op(lambda eng, s2=(ot % 2): (
            eng.tensor_scalar(w1db[:, s2 * 2048:(s2 + 1) * 2048],
                              w1db[:, s2 * 2048:(s2 + 1) * 2048],
                              1e10, 1.0, ALU.mult, ALU.min),
            eng.tensor_scalar(w1db[:, s2 * 2048:(s2 + 1) * 2048],
                              w1db[:, s2 * 2048:(s2 + 1) * 2048],
                              -1.0, None, ALU.max))[-1], inc=True, waits=[wev])
        for half in range(2):
            si = next_slot()
            pt, off = pslots[si]
            for k in range(16):
                w = []
                if k == 0:
                    w = [e_sg, pslot_ev[si]]
                    if ot == 0 and half == 0:
                        w += [e_s12] + xq_evs
                t_mm(pt[:, off:off + 512],
                     w1db[:, (ot % 2) * 2048 + k * 128:(ot % 2) * 2048 + (k + 1) * 128],
                     aoT[:, k * 1024 + half * 512: k * 1024 + half * 512 + 512],
                     k == 0, k == 15, inc=(k == 15), waits=w)
            mev = ("t", cnt["t"])
            if half == 1:
                sg1_prev[ot % 2] = mev
            e_h = s_op(lambda eng, p=pt, o=off, oo=ot, hf=half: eng.activation(
                h2[:, oo * 1024 + hf * 512: oo * 1024 + hf * 512 + 512],
                p[:, o:o + 512], AF.Gelu, bias=bias_sb[:, 64 + oo:65 + oo],
                scale=s12[:, 0:1]), inc=True, waits=[mev])
            pslot_ev[si] = e_h
            h2_evs.append(e_h)
        e_sq = s_op(lambda eng, oo=ot: eng.activation(
            csq[:, 0:1024], h2[:, oo * 1024:(oo + 1) * 1024], AF.Square),
            inc=True, waits=[h2_evs[-1], rms2_mm[0], mx2_prev[0]])
        sq_prev[0] = e_sq
        t_mm(SM1[0:1, 0:512], onesc[:, :], csq[:, 0:512], ot == 0, ot == 63,
             waits=[e_sq, e_B1] if ot == 0 else [e_sq])
        rms2_mm[0] = t_mm(SM2[0:1, 0:512], onesc[:, :], csq[:, 512:1024],
                          ot == 0, ot == 63, inc=True)
        mx2_prev[0] = v_op(lambda eng: eng.tensor_tensor(
            csqmax[:, :], csqmax[:, :], csq[:, 0:1024], ALU.max),
            inc=True, waits=[e_sq, mx2_prev[0]])

    e_fold2 = v_op(pfold(csqmax, ctmp1, 1024, ALU.max), inc=True,
                   waits=[mx2_prev[0]])

    def r2a(eng):
        t = ctmp1[0:1, 0:1024]
        eng.tensor_scalar(t[0:1, 0:512], SM1[0:1, 0:512], 1.0 / 8192.0, None, ALU.mult)
        eng.tensor_scalar(t[0:1, 512:1024], SM2[0:1, 0:512], 1.0 / 8192.0, None, ALU.mult)
        eng.tensor_scalar(t, t, RMS_EPS, None, ALU.add)
        return eng.reciprocal(qrow_f[0:1, 0:1024], t)
    e_ra = v_op(r2a, inc=True, waits=[rms2_mm[0], e_fold2])
    e_rb_ = s_op(lambda eng: eng.activation(qrow_f[0:1, 0:1024],
                                            qrow_f[0:1, 0:1024], AF.Sqrt),
                 inc=True, waits=[e_ra])
    e_mxr2 = v_op(lambda eng: eng.tensor_copy(crow_f[0:1, 0:1024],
                                              csqmax[0:1, 0:1024]),
                  inc=True, waits=[e_rb_])
    e_mxs2 = s_op(lambda eng: eng.activation(crow_f[0:1, 0:1024],
                                             crow_f[0:1, 0:1024], AF.Sqrt),
                  inc=True, waits=[e_mxr2])

    def q2_rows3(eng):
        f = qrow_f[0:1, 0:1024]
        mx = crow_f[0:1, 0:1024]
        eng.tensor_scalar(f, f, RSQ8192, None, ALU.mult)
        eng.tensor_tensor(mx, mx, f, ALU.mult)
        eng.tensor_scalar(mx, mx, Q_EPS, None, ALU.max)
        eng.reciprocal(ctmp1[0:1, 0:1024], mx)
        eng.tensor_scalar(ctmp1[0:1, 0:1024], ctmp1[0:1, 0:1024], 127.0,
                          None, ALU.mult)
        eng.tensor_tensor(f, f, ctmp1[0:1, 0:1024], ALU.mult)
        return eng.tensor_scalar(mx, mx, 1.0 / 127.0, None, ALU.mult)
    e_q23 = v_op(q2_rows3, inc=True, waits=[e_mxs2])
    ebc = bcast(qrow_f[0:1, 0:1024], 1024, BIG3, ctmp1, crow_h, crow_l,
                waits=[e_q23])
    e_A2 = v_op(lambda eng: eng.tensor_copy(bcA[:, :], BIG3[:, 0:1024]),
                inc=True, waits=[ebc])
    ebc = bcast(crow_f[0:1, 0:1024], 1024, BIG3, ctmp1, crow_h, crow_l,
                waits=[e_A2])
    e_B2b = v_op(lambda eng: eng.tensor_copy(bcB[:, :], BIG3[:, 0:1024]),
                 inc=True, waits=[ebc])

    xq2_evs = []
    CT = [ctmp1, ctmp1b]
    for f2 in range(64):
        tb = CT[f2 % 2]
        e_m = v_op(lambda eng, f=f2, t=tb: eng.tensor_tensor(
            t[:, :], h2[:, f * 1024:(f + 1) * 1024], bcA[:, :], ALU.mult),
            inc=True, waits=[e_B2b, h2_evs[2 * f2 + 1],
                             xq2_evs[f2 - 2] if f2 >= 2 else None])
        xq2_evs.append(s_op(lambda eng, f=f2, t=tb: (
            eng.activation(t[:, :], t[:, :], AF.Identity, bias=MAGIC),
            eng.activation(h2[:, f * 1024:(f + 1) * 1024], t[:, :],
                           AF.Identity, bias=-MAGIC))[-1],
            inc=True, waits=[e_m]))

    tap("h2q", h2, 65536, xq2_evs[63])

    sg2_prev = [None, None]
    raw_evs = []
    statr = [None]
    sqr_prev = [None]
    for ot2 in range(16):
        siA = next_slot()
        ptA, offA = pslots[siA]
        siB = next_slot()
        ptB, offB = pslots[siB]
        mevA = mevB = None
        for blk in range(4):
            sl = blk % 2
            wev = wload(w2db, 2 * 16 * 128, w2, DM, blk * 16, 16, ot2 * 128,
                        128, sl * 2048,
                        waits=[sg2_prev[sl]] if sg2_prev[sl] else [])
            e_sg = v_op(lambda eng, s2=sl: (
                eng.tensor_scalar(w2db[:, s2 * 2048:(s2 + 1) * 2048],
                                  w2db[:, s2 * 2048:(s2 + 1) * 2048],
                                  1e10, 1.0, ALU.mult, ALU.min),
                eng.tensor_scalar(w2db[:, s2 * 2048:(s2 + 1) * 2048],
                                  w2db[:, s2 * 2048:(s2 + 1) * 2048],
                                  -1.0, None, ALU.max))[-1], inc=True,
                waits=[wev])
            for ki in range(16):
                k = blk * 16 + ki
                wA = []
                wB = []
                if ki == 0:
                    wA = [e_sg]
                    if ot2 == 0:
                        wA.append(xq2_evs[blk * 16 + 15])
                    if k == 0:
                        wA.append(pslot_ev[siA])
                        wB = [pslot_ev[siB]]
                t_mm(ptA[:, offA:offA + 512],
                     w2db[:, sl * 2048 + ki * 128: sl * 2048 + (ki + 1) * 128],
                     h2[:, k * 1024: k * 1024 + 512],
                     k == 0, k == 63, inc=(k == 63), waits=wA)
                if k == 63:
                    mevA = ("t", cnt["t"])
                t_mm(ptB[:, offB:offB + 512],
                     w2db[:, sl * 2048 + ki * 128: sl * 2048 + (ki + 1) * 128],
                     h2[:, k * 1024 + 512:(k + 1) * 1024],
                     k == 0, k == 63, inc=(ki == 15), waits=wB)
                if ki == 15:
                    sg2_prev[sl] = ("t", cnt["t"])
                    if k == 63:
                        mevB = ("t", cnt["t"])
        e_rA = v_op(lambda eng, p=ptA, o=offA, oo=ot2: eng.tensor_tensor(
            bf2r[:, oo * 1024: oo * 1024 + 512], p[:, o:o + 512],
            bcB[:, 0:512], ALU.mult), inc=True, waits=[mevA])
        e_rB = v_op(lambda eng, p=ptB, o=offB, oo=ot2: eng.tensor_tensor(
            bf2r[:, oo * 1024 + 512:(oo + 1) * 1024], p[:, o:o + 512],
            bcB[:, 512:1024], ALU.mult), inc=True, waits=[mevB])
        pslot_ev[siA] = e_rA
        pslot_ev[siB] = e_rB
        raw_evs.append(e_rA)
        raw_evs.append(e_rB)
        e_sq = s_op(lambda eng, oo=ot2: eng.activation(
            csq[:, 0:1024], bf2r[:, oo * 1024:(oo + 1) * 1024], AF.Square),
            inc=True, waits=[e_rB, statr[0], sqr_prev[0]])
        sqr_prev[0] = e_sq
        t_mm(SM1[0:1, 0:512], onesc[:, :], bf2r[:, ot2 * 1024: ot2 * 1024 + 512],
             ot2 == 0, ot2 == 15,
             waits=[e_rA, e_B2b] if ot2 == 0 else [e_rA])
        t_mm(SM2[0:1, 0:512], onesc[:, :],
             bf2r[:, ot2 * 1024 + 512:(ot2 + 1) * 1024], ot2 == 0, ot2 == 15,
             waits=[e_rB])
        t_mm(BIG3[0:1, 0:512], onesc[:, :], csq[:, 0:512], ot2 == 0, ot2 == 15,
             waits=[e_sq, e_B2b] if ot2 == 0 else [e_sq])
        statr[0] = t_mm(BIG3[0:1, 512:1024], onesc[:, :], csq[:, 512:1024],
                        ot2 == 0, ot2 == 15, inc=True)

    tap("bf2r", bf2r, 16384, statr[0])

    e3 = y_dma(fr_sraw[0:1, 0:1024], bass.AP(biasd["b2"], 0, [[1024, 1], [1, 1024]]),
               waits=[statr[0]])
    ec3 = v_op(lambda eng: eng.tensor_copy(b2rb[0:1, 0:1024], fr_sraw[0:1, 0:1024]),
               inc=True, waits=[e3, statr[0]])
    e4 = y_load(fr_sraw[0:1, 0:1024],
                bass.AP(biasd["b2"], 1024, [[1024, 1], [1, 1024]]), waits=[ec3])
    e_b2rb = v_op(lambda eng: eng.tensor_copy(b2rb[0:1, 1024:2048],
                                              fr_sraw[0:1, 0:1024]),
                  inc=True, waits=[e4])


    def cp_rows(eng):
        eng.tensor_copy(fr_sraw[0:1, 0:512], SM1[0:1, 0:512])
        eng.tensor_copy(fr_sraw[0:1, 512:1024], SM2[0:1, 0:512])
        eng.tensor_copy(fr_ssq[0:1, 0:512], BIG3[0:1, 0:512])
        return eng.tensor_copy(fr_ssq[0:1, 512:1024], BIG3[0:1, 512:1024])
    e_cpr = v_op(cp_rows, inc=True, waits=[statr[0], e_b2rb])
    b2r_mm = [None]
    for ot2 in range(16):
        t_mm(SM1[0:1, 0:512], b2cb[:, ot2:ot2 + 1],
             bf2r[:, ot2 * 1024: ot2 * 1024 + 512], ot2 == 0, ot2 == 15,
             waits=[e_cpr, ev_b2cb] if ot2 == 0 else ())
        b2r_mm[0] = t_mm(SM2[0:1, 0:512], b2cb[:, ot2:ot2 + 1],
                         bf2r[:, ot2 * 1024 + 512:(ot2 + 1) * 1024],
                         ot2 == 0, ot2 == 15, inc=(ot2 == 15))

    def cp_b2r(eng):
        eng.tensor_copy(fr_sb2r[0:1, 0:512], SM1[0:1, 0:512])
        return eng.tensor_copy(fr_sb2r[0:1, 512:1024], SM2[0:1, 0:512])
    e_cb2r = v_op(cp_b2r, inc=True, waits=[b2r_mm[0]])

    e_q0 = v_op(lambda eng: eng.tensor_tensor(ctmp2[:, 0:16], bias_sb[:, 128:144],
                                              bias_sb[:, 128:144], ALU.mult),
                inc=True, waits=[ev_bias, e_b2rb, sqr_prev[0], statr[0]])
    e_q1 = v_op(lambda eng: (eng.tensor_reduce(ctmp2[:, 16:17], bias_sb[:, 128:144],
                                               AX.X, ALU.add),
                             eng.tensor_reduce(ctmp2[:, 17:18], ctmp2[:, 0:16],
                                               AX.X, ALU.add))[-1],
                inc=True, waits=[vfence(e_q0)])
    e_q2h = s_op(lambda eng: (eng.copy(csq[:, 0:1], ctmp2[:, 16:17]),
                              eng.copy(csq[:, 1:2], ctmp2[:, 17:18]))[-1],
                 inc=True, waits=[vfence(e_q1)])
    e_q2l = s_op(lambda eng: (eng.activation(ctmp2[:, 18:19], csq[:, 0:1],
                                             AF.Identity, scale=-1.0,
                                             bias=ctmp2[:, 16:17]),
                              eng.activation(ctmp2[:, 19:20], csq[:, 1:2],
                                             AF.Identity, scale=-1.0,
                                             bias=ctmp2[:, 17:18]))[-1],
                 inc=True, waits=[e_q2h])
    e_q2lb = v_op(lambda eng: (eng.tensor_copy(csq[:, 2:3], ctmp2[:, 18:19]),
                               eng.tensor_copy(csq[:, 3:4], ctmp2[:, 19:20]))[-1],
                  inc=True, waits=[sfence(e_q2l)])
    e_b2m = None
    for c in range(2):
        t_mm(SM1[0:1, 16 + c:17 + c], onesc[:, :], csq[:, c:c + 1], True, False,
             waits=[e_q2lb, e_cb2r] if c == 0 else ())
        e_b2m = t_mm(SM1[0:1, 16 + c:17 + c], onesc[:, :], csq[:, 2 + c:3 + c],
                     False, True, inc=True)

    def cp_b2f(eng):
        eng.tensor_copy(ctmp2[0:1, 16:17], SM1[0:1, 16:17])
        return eng.tensor_copy(ctmp2[0:1, 17:18], SM1[0:1, 17:18])
    e_b2f0 = v_op(cp_b2f, inc=True, waits=[e_b2m])
    e_b2f = vfence(e_b2f0)

    e_means = v_op(lambda eng: (eng.tensor_scalar(frow_f[0:1, 0:1],
                                                  ctmp2[0:1, 16:17], INV2048,
                                                  None, ALU.mult),
                                eng.tensor_scalar(frow_f[0:1, 1:2],
                                                  ctmp2[0:1, 17:18], INV2048,
                                                  None, ALU.mult))[-1],
                    inc=True, waits=[e_b2f, sfence(e_b2f)])
    e_meansf = vfence(e_means)

    def f_rows1(eng):
        s2v = s12[0:1, 1:2]
        mp = fr_mp[0:1, 0:1024]
        eng.tensor_scalar(mp, fr_sraw[0:1, 0:1024], INV2048, None, ALU.mult)
        eng.tensor_scalar(mp, mp, s2v, None, ALU.mult)
        eng.tensor_scalar(mp, mp, frow_f[0:1, 0:1], None, ALU.add)
        ey = ftmp1[0:1, 0:1024]
        eng.tensor_scalar(ey, fr_ssq[0:1, 0:1024], INV2048, None, ALU.mult)
        eng.tensor_scalar(ey, ey, s2v, None, ALU.mult)
        eng.tensor_scalar(ey, ey, s2v, None, ALU.mult)
        eng.tensor_scalar(ftmp2[0:1, 0:1024], fr_sb2r[0:1, 0:1024],
                          2.0 * INV2048, None, ALU.mult)
        eng.tensor_scalar(ftmp2[0:1, 0:1024], ftmp2[0:1, 0:1024], s2v, None,
                          ALU.mult)
        eng.tensor_tensor(ey, ey, ftmp2[0:1, 0:1024], ALU.add)
        eng.tensor_scalar(ey, ey, frow_f[0:1, 1:2], None, ALU.add)
        eng.tensor_tensor(ftmp2[0:1, 0:1024], mp, mp, ALU.mult)
        eng.tensor_tensor(ey, ey, ftmp2[0:1, 0:1024], ALU.subtract)
        eng.tensor_scalar(ey, ey, LN_EPS, None, ALU.add)
        return eng.reciprocal(fr_r3[0:1, 0:1024], ey)
    e_f1 = v_op(f_rows1, inc=True, waits=[e_cb2r, sfence(e_meansf), e_s12])
    e_f2 = s_op(lambda eng: eng.activation(fr_r3[0:1, 0:1024],
                                           fr_r3[0:1, 0:1024], AF.Sqrt),
                inc=True, waits=[e_f1])

    def f_rows2(eng):
        s2v = s12[0:1, 1:2]
        r3 = fr_r3[0:1, 0:1024]
        eng.tensor_scalar(fr_c1[0:1, 0:1024], r3, s2v, None, ALU.mult)
        eng.tensor_tensor(ftmp1[0:1, 0:1024], fr_mp[0:1, 0:1024], r3, ALU.mult)
        eng.tensor_scalar(ftmp1[0:1, 0:1024], ftmp1[0:1, 0:1024], -1.0, None,
                          ALU.mult)
        eng.tensor_copy(nb16[0:1, 0:1024], ftmp1[0:1, 0:1024])
        eng.tensor_tensor(ftmp2[0:1, 0:1024], ftmp1[0:1, 0:1024],
                          nb16[0:1, 0:1024], ALU.subtract)
        eng.tensor_copy(nb16[0:1, 1024:2048], ftmp2[0:1, 0:1024])
        eng.tensor_copy(frow_h[0:1, 0:1024], r3)
        eng.tensor_tensor(ftmp2[0:1, 0:1024], r3, frow_h[0:1, 0:1024],
                          ALU.subtract)
        return eng.tensor_copy(frow_l[0:1, 0:1024], ftmp2[0:1, 0:1024])
    e_f3 = v_op(f_rows2, inc=True, waits=[e_f2])
    ebc = bcast(fr_c1[0:1, 0:1024], 1024, BIG3, frow_f, frow_h2, frow_l2,
                waits=[e_f3, statr[0]])
    e_C1 = v_op(lambda eng: eng.tensor_copy(bcC[:, :], BIG3[:, 0:1024]),
                inc=True, waits=[ebc])

    out_evs = []
    for ot2 in range(16):
        trd = y_load(bass.AP(t1rd, (ot2 % 4) * 1024, [[4096, 128], [1, 1024]]),
                     bass.AP(t1sp, ot2 * 128 * T, [[T, 128], [1, 1024]]),
                     waits=[sp_evs[15]] if ot2 < 4 else [out_evs[ot2 - 4]])
        si = next_slot()
        pt, off = pslots[si]
        for half in range(2):
            o2 = off if False else None
        si2 = next_slot()
        pt2, off2 = pslots[si2]
        dmm = []
        for (p_, o_, half) in ((pt, off, 0), (pt2, off2, 1)):
            t_mm(p_[:, o_:o_ + 512], b2rb[0:1, ot2 * 128:(ot2 + 1) * 128],
                 frow_h[0:1, half * 512: half * 512 + 512], True, False,
                 waits=[pslot_ev[si if half == 0 else si2], e_f3,
                        e_C1 if ot2 == 0 else None])
            t_mm(p_[:, o_:o_ + 512], b2rb[0:1, ot2 * 128:(ot2 + 1) * 128],
                 frow_l[0:1, half * 512: half * 512 + 512], False, False)
            t_mm(p_[:, o_:o_ + 512], onesr[0:1, :],
                 nb16[0:1, half * 512: half * 512 + 512], False, False)
            dmm.append(t_mm(p_[:, o_:o_ + 512], onesr[0:1, :],
                            nb16[0:1, 1024 + half * 512: 1024 + half * 512 + 512],
                            False, True, inc=True))

        def mk(eng, oo=ot2, pa=pt, oa=off, pb=pt2, ob=off2):
            eng.tensor_tensor(ftmp1[:, :], bf2r[:, oo * 1024:(oo + 1) * 1024],
                              bcC[:, :], ALU.mult)
            eng.tensor_tensor(ftmp1[:, 0:512], ftmp1[:, 0:512],
                              pa[:, oa:oa + 512], ALU.add)
            eng.tensor_tensor(ftmp1[:, 512:1024], ftmp1[:, 512:1024],
                              pb[:, ob:ob + 512], ALU.add)
            return eng.tensor_tensor(
                coutb[:, (oo % 2) * 1024:(oo % 2 + 1) * 1024], ftmp1[:, :],
                t1rd[:, (oo % 4) * 1024:(oo % 4 + 1) * 1024], ALU.add)
        e_o = v_op(mk, inc=True, waits=[trd, dmm[0], dmm[1], e_C1,
                                        raw_evs[2 * ot2 + 1]])
        pslot_ev[si] = e_o
        pslot_ev[si2] = e_o
        out_evs.append(e_o)
        y_dma(bass.AP(outT, ot2 * 128 * T, [[T, 128], [1, 1024]]),
              bass.AP(coutb, (ot2 % 2) * 1024, [[2048, 128], [1, 1024]]),
              waits=[e_o])

    wait("Y", ("ds", cnt["ds"]))

    with nc.Block() as block:
        @block.tensor
        def _(eng):
            for k, fn in ops:
                if k == "T":
                    fn(eng)

        @block.vector
        def _(eng):
            for k, fn in ops:
                if k == "V":
                    fn(eng)

        @block.scalar
        def _(eng):
            for k, fn in ops:
                if k == "S":
                    fn(eng)

        @block.gpsimd
        def _(eng):
            for k, fn in ops:
                if k == "G":
                    fn(eng)

        @block.sync
        def _(eng):
            for k, fn in ops:
                if k == "Y":
                    fn(eng)

    return nc


def kernel(**inputs):
    ip = {k: np.asarray(v) for k, v in inputs.items()}
    x = ip["L0_attn_hidden_states"]

    def tr(a):
        return np.ascontiguousarray(a.T)

    w = {
        "wdkv": tr(ip["L0_attn_W_DKV_w"]), "wuk": tr(ip["L0_attn_W_UK_w"]),
        "wuv": tr(ip["L0_attn_W_UV_w"]), "wdq": tr(ip["L0_attn_W_DQ_w"]),
        "wuq": tr(ip["L0_attn_W_UQ_w"]), "wo": tr(ip["L0_attn_W_O_w"]),
        "w1": tr(ip["L0_ffn_w1"]), "w2": tr(ip["L0_ffn_w2"]),
        "bdkv": np.ascontiguousarray(ip["L0_attn_W_DKV_b"]),
        "buk": np.ascontiguousarray(ip["L0_attn_W_UK_b"]),
        "buv": np.ascontiguousarray(ip["L0_attn_W_UV_b"]),
        "bdq": np.ascontiguousarray(ip["L0_attn_W_DQ_b"]),
        "buq": np.ascontiguousarray(ip["L0_attn_W_UQ_b"]),
        "bo": np.ascontiguousarray(ip["L0_attn_W_O_b"]),
        "b1": np.ascontiguousarray(ip["L0_ffn_b1"]),
        "b2": np.ascontiguousarray(ip["L0_ffn_b2"]),
    }
    xTb = [np.ascontiguousarray(x[b].T) for b in range(2)]
    nc = build_program()
    in_maps = []
    for c in range(8):
        b, qt = c // 4, c % 4
        im = dict(w)
        im["xT"] = xTb[b]
        im["xoT"] = np.ascontiguousarray(xTb[b][:, qt * 1024:(qt + 1) * 1024])
        in_maps.append(im)
    res = run_bass_kernel_spmd(nc, in_maps, core_ids=list(range(8)))
    out = np.empty((2, 4096, 2048), dtype=np.float32)
    for c in range(8):
        b, qt = c // 4, c % 4
        out[b, qt * 1024:(qt + 1) * 1024, :] = res.results[c]["outT"].T
    return out
